# version 02
# baseline (speedup 1.0000x reference)
import math
import numpy as np
import concourse.bass as bass
import concourse.mybir as mybir
from concourse.bass_utils import run_bass_kernel_spmd

F32 = mybir.dt.float32
BF16 = mybir.dt.bfloat16
AF = mybir.ActivationFunctionType
ALU = mybir.AluOpType
AX = mybir.AxisListType

B, T, D, NH, HD = 4, 2048, 2048, 16, 128
DFF = 8192
NKC = D // 128
NFC = DFF // 128
OWN = 1024
SHIFT = 12.0
SCALE_S = 1.0 / math.sqrt(HD)

TWO_PI = 2.0 * math.pi
CW1 = float(np.float32(TWO_PI))
CW2 = float(np.float32(np.float64(TWO_PI) - np.float64(np.float32(TWO_PI))))
CW3 = float(np.float64(TWO_PI) - np.float64(np.float32(TWO_PI)) - np.float64(CW2))
R2PI = float(np.float32(1.0 / TWO_PI))
PI_HALF = float(np.float32(math.pi / 2))
PI_HALF_LO = float(np.float64(math.pi / 2) - np.float64(np.float32(math.pi / 2)))
PI_F = float(np.float32(math.pi))

KBY = 1024


class Sem:
    def __init__(self, handle):
        self.h = handle
        self.n = 0

    def inc(self, k=1):
        self.n += k
        return self.n


class Bld:
    def __init__(self, nc):
        self.nc = nc
        self.q = {k: [] for k in ("sync", "gpsimd", "vector", "scalar", "tensor")}
        self._ctxs = []
        self.sems = {}
        self.serial = {}
        self.base = (nc.sbuf_base + 31) & ~31

    def sem(self, name):
        if name not in self.sems:
            cm = self.nc.semaphore(name)
            self.sems[name] = Sem(cm.__enter__())
            self._ctxs.append(cm)
        return self.sems[name]

    def sb(self, name, shape, dtype, off_b):
        t = self.nc.alloc_sbuf_tensor_at(name, shape, dtype, offset=self.base + off_b)
        return t.ap()

    def st(self, eng, fn, waits=(), inc=None, inck=1):
        ser = self.serial.get(eng)
        if ser is not None:
            waits = tuple(waits) + ((ser, ser.n),)
            inc, inck = ser, 1
        waits = tuple((s.h, v) for (s, v) in waits if v > 0)
        if inc is not None:
            inc.inc(inck)

        def run(e, fn=fn, waits=waits, inc=inc, inck=inck):
            for h, v in waits:
                e.wait_ge(h, v)
            r = fn(e)
            if inc is not None:
                r.then_inc(inc.h, inck)
        self.q[eng].append(run)
        return inc.n if inc is not None else None

    def emit(self):
        with self.nc.Block() as block:
            block.sync(lambda e: [f(e) for f in self.q["sync"]])
            block.gpsimd(lambda e: [f(e) for f in self.q["gpsimd"]])
            block.vector(lambda e: [f(e) for f in self.q["vector"]])
            block.scalar(lambda e: [f(e) for f in self.q["scalar"]])
            block.tensor(lambda e: [f(e) for f in self.q["tensor"]])
        for cm in reversed(self._ctxs):
            cm.__exit__(None, None, None)


def build(nc, dbg=(), phases=99, e_head=0):
    dbg = set(dbg)
    K = Bld(nc)
    st = K.st
    KB = lambda x: int(x * KBY)

    xc = nc.dram_tensor("xc", [D, T], F32, kind="ExternalInput").ap()
    cvec_d = nc.dram_tensor("cvec", [128, NKC], F32, kind="ExternalInput").ap()
    pos_d = nc.dram_tensor("pos", [1, T], F32, kind="ExternalInput").ap()
    wmod_d = nc.dram_tensor("wmod", [24, 128, NKC, 512], F32, kind="ExternalInput").ap()
    wq_d = nc.dram_tensor("wq", [NH, 128, NKC, 128], F32, kind="ExternalInput").ap()
    wk_d = nc.dram_tensor("wk", [NH, 128, NKC, 128], F32, kind="ExternalInput").ap()
    wv_d = nc.dram_tensor("wv", [8, 128, NKC, 256], F32, kind="ExternalInput").ap()
    wo_d = nc.dram_tensor("wo", [NKC, 128, NKC, 128], F32, kind="ExternalInput").ap()
    w1_d = nc.dram_tensor("w1", [NFC, 128, NKC, 128], F32, kind="ExternalInput").ap()
    w2_d = nc.dram_tensor("w2", [NFC, 128, NKC, 128], F32, kind="ExternalInput").ap()
    w3_d = nc.dram_tensor("w3", [4, NFC, 128, 4 * 128], F32, kind="ExternalInput").ap()
    out_d = nc.dram_tensor("out", [D, OWN], F32, kind="ExternalOutput").ap()
    bnc_d = nc.dram_tensor("bnc", [64, 2048], F32, kind="Internal").ap()
    x2s_d = nc.dram_tensor("x2s", [D, OWN], F32, kind="Internal").ap()
    hs_d = nc.dram_tensor("hs", [NFC, 128, OWN], BF16, kind="Internal").ap()
    q3s_d = nc.dram_tensor("q3s", [NFC, 128, OWN], BF16, kind="Internal").ap()

    idm4_d = nc.inline_tensor(
        np.ascontiguousarray(np.tile(np.eye(128, dtype=np.float32), (1, 4))), "idm4").ap()
    irow_d = nc.inline_tensor(np.arange(64, dtype=np.float32).reshape(64, 1), "irow").ap()

    DBG = {}

    def dbg_t(name, shape):
        if name in dbg:
            DBG[name] = nc.dram_tensor(name, shape, F32, kind="ExternalOutput").ap()
            return True
        return False

    rows = K.sb("rows", [1, 3072], F32, KB(0))
    rbc = K.sb("rbc", [128, 1024], F32, KB(12))
    modv = K.sb("modv", [128, 128], F32, KB(16))
    consts = K.sb("consts", [128, 8], F32, KB(16.5))
    accw = K.sb("accw", [128, 132], F32, KB(16.6875))
    wsc = K.sb("wsc", [128, 8], F32, KB(17.21875))
    onesb = K.sb("onesb", [128, 128], BF16, KB(17.25))
    sqb2 = K.sb("sqb2", [128, 512], BF16, KB(17.5))
    onesf128 = K.sb("onesf128", [128, 128], F32, KB(18.5))
    xst = K.sb("xst", [128, 2048], F32, KB(19))
    accm = K.sb("accm", [128, 1024], F32, KB(19))
    wfl = K.sb("wfl", [128, 2048], F32, KB(27))
    wfl2 = K.sb("wfl2", [128, 2048], F32, KB(35))
    wsl = K.sb("wsl", [128, 4096], BF16, KB(35))
    ternb = K.sb("ternb", [128, 4096], BF16, KB(35))
    xbb = K.sb("xbb", [128, 1024], BF16, KB(43))
    h_sb = K.sb("h_sb", [128, NKC * T], BF16, KB(45))
    q12 = K.sb("q12", [128, NKC * OWN], BF16, KB(45))
    ysb = K.sb("ysb", [128, NKC * OWN], BF16, KB(111))
    h2sb = K.sb("h2sb", [128, NKC * OWN], BF16, KB(111))
    du1b = K.sb("du1b", [128, 1024], F32, KB(77))
    tmpb = K.sb("tmpb", [128, 1024], F32, KB(81))
    maxac = K.sb("maxac", [128, 1024], F32, KB(85))
    cq3b = K.sb("cq3b", [128, 1024], F32, KB(89))
    do3b = K.sb("do3b", [128, 1024], F32, KB(93))
    terntmp = K.sb("terntmp", [128, 2048], F32, KB(97))
    ternA = K.sb("ternA", [128, 2048], BF16, KB(105))
    ternB = K.sb("ternB", [128, 2048], BF16, KB(109))
    hch = K.sb("hch", [128, 2048], BF16, KB(113))
    q3c = K.sb("q3c", [128, 2048], BF16, KB(117))
    esb = K.sb("esb", [128, NKC * 512], BF16, KB(143))
    vsb = K.sb("vsb", [128, 2 * NKC * 256], BF16, KB(159))
    ksb = K.sb("ksb", [128, 2 * T], BF16, KB(175))
    qsb = K.sb("qsb", [128, 2 * OWN], BF16, KB(183))
    cosT = K.sb("cosT", [64, T], BF16, KB(187))
    ssinT = K.sb("ssinT", [64, T], BF16, KB(191))
    kraw = K.sb("kraw", [128, 512], F32, KB(195))
    krot = K.sb("krot", [64, 512], F32, KB(197))
    tab = K.sb("tab", [64, 8], F32, KB(198))
    lrec = K.sb("lrec", [16, 1024], F32, KB(198.5))
    sm2 = K.sb("sm2", [128, 1024], F32, KB(202.5))
    smr = K.sb("smr", [1, 2048], F32, KB(198.5))
    mring = K.sb("mring", [128, 2 * NKC * 512], BF16, KB(111))
    smat = K.sb("smat", [128, NKC * 128], BF16, KB(143))
    idn4 = K.sb("idn4", [128, 512], F32, KB(147))
    t_ang = K.sb("t_ang", [64, 2048], F32, KB(149.25))
    t_u = K.sb("t_u", [64, 2048], F32, KB(157.25))
    t_r = K.sb("t_r", [64, 2048], F32, KB(165.25))
    t_b = K.sb("t_b", [64, 2048], F32, KB(19))
    wsl2 = K.sb("wsl2", [128, 4096], BF16, KB(19))
    assert K.base + KB(206.5) <= nc.sbuf_top

    pss = [nc.alloc_psum_tensor("ps%d" % i, [128, 512], F32).ap() for i in range(8)]
    psA, psB, psC, psD, psE, psF, psG, psH = pss

    sLD = K.sem("sLD")
    sXL = K.sem("sXL")
    sM0 = K.sem("sM0")
    sM1 = K.sem("sM1")
    sB = [K.sem("sB%d" % i) for i in range(6)]
    sVE = K.sem("sVE")
    sSE = K.sem("sSE")
    sMM = K.sem("sMM")
    sEX = K.sem("sEX")
    sEV = K.sem("sEV")
    sDBG = K.sem("sDBG")
    sDBG2 = K.sem("sDBG2")
    sDBG3 = K.sem("sDBG3")
    sDBG4 = K.sem("sDBG4")
    sDBG5 = K.sem("sDBG5")
    sOUT = K.sem("sOUT")
    sOUT1 = K.sem("sOUT1")
    sA0 = K.sem("sA0")
    sA1 = K.sem("sA1")
    sAC = K.sem("sAC")
    sW0 = K.sem("sW0")
    sW1 = K.sem("sW1")
    sLB = K.sem("sLB")
    sLR = K.sem("sLR")
    sRB0 = K.sem("sRB0")
    sRB1 = K.sem("sRB1")
    sXO0 = K.sem("sXO0")
    sXO1 = K.sem("sXO1")
    sXS0 = K.sem("sXS0")
    sXS1 = K.sem("sXS1")
    sHS = [K.sem("sHS%d" % i) for i in range(4)]
    sHR = [K.sem("sHR%d" % i) for i in range(2)]
    sQS = [K.sem("sQS%d" % i) for i in range(2)]
    K.serial["vector"] = sVE
    K.serial["scalar"] = sSE

    def bounce(pair, row_ap, n, bnc_row, tile_ap, nparts=128, waits=()):
        sw, sr = sB[2 * pair], sB[2 * pair + 1]
        st("sync", lambda e: e.dma_start(out=bnc_d[bnc_row:bnc_row + 1, 0:n], in_=row_ap),
           waits=waits, inc=sw, inck=16)
        w = sw.n
        st("sync", lambda e: e.dma_start(
            out=tile_ap, in_=bnc_d[bnc_row:bnc_row + 1, 0:n].broadcast_to((nparts, n))),
           waits=[(sw, w)], inc=sr, inck=16)
        return (sw, w), (sr, sr.n)

    st("sync", lambda e: e.dma_start(out=modv[:, 96:112], in_=cvec_d[:, :]), inc=sLD, inck=16)
    st("sync", lambda e: e.dma_start(out=idn4[:, :], in_=idm4_d[:, :]), inc=sLD, inck=16)
    st("sync", lambda e: e.dma_start(out=tab[:, 0:1], in_=irow_d[:, :]), inc=sLD, inck=16)
    st("sync", lambda e: e.dma_start(
        out=t_ang[:, :], in_=pos_d[0:1, :].broadcast_to((64, T))), inc=sLD, inck=16)
    w_cvec = w_idn = w_irow = w_pos = sLD.n

    def _consts(e):
        e.memset(consts[:, 0:1], -SHIFT)
        e.memset(consts[:, 1:2], 1.0)
        e.memset(consts[:, 2:3], 1e-5)
        e.memset(wsc[:, :], 0.0)
        e.memset(consts[:, 3:4], 1e-6)
        e.memset(tab[:, 1:2], 10000.0)
        return e.memset(onesb[:, :], 1.0)
    st("vector", _consts)
    w_consts = sVE.n

    for kc in range(NKC):
        st("gpsimd", lambda e, kc=kc: e.dma_start(
            out=h_sb[:, kc * T:(kc + 1) * T], in_=xc[kc * 128:(kc + 1) * 128, :]),
           inc=sXL, inck=16)
    w_xall = sXL.n

    st("scalar", lambda e: e.activation(out=tab[:, 2:3], in_=tab[:, 1:2], func=AF.Ln),
       waits=[(sVE, w_consts)])
    st("scalar", lambda e: e.activation(out=tab[:, 3:4], in_=tab[:, 0:1], func=AF.Identity,
                                        scale=tab[:, 2:3]),
       waits=[(sLD, w_irow), (sSE, sSE.n)])
    st("scalar", lambda e: e.activation(out=tab[:, 4:5], in_=tab[:, 3:4], func=AF.Exp,
                                        scale=-1.0 / 64.0), waits=[(sSE, sSE.n)])
    w_invf = sSE.n
    st("vector", lambda e: e.tensor_scalar_mul(out=t_ang[:, :], in0=t_ang[:, :],
                                               scalar1=tab[:, 4:5]),
       waits=[(sLD, w_pos), (sSE, w_invf)])
    st("vector", lambda e: e.tensor_scalar(out=t_u[:, :], in0=t_ang[:, :],
                                           scalar1=R2PI, scalar2=0.5,
                                           op0=ALU.mult, op1=ALU.add))
    st("vector", lambda e: e.tensor_copy(out=t_r[:, :], in_=t_u[:, :]))
    for k in [256.0, 128.0, 64.0, 32.0, 16.0, 8.0, 4.0, 2.0, 1.0]:
        st("vector", lambda e, k=k: e.tensor_scalar(out=t_b[0:64, :], in0=t_r[:, :],
                                                    scalar1=k, scalar2=k,
                                                    op0=ALU.is_ge, op1=ALU.mult))
        st("vector", lambda e: e.tensor_tensor(out=t_r[:, :], in0=t_r[:, :],
                                               in1=t_b[0:64, :], op=ALU.subtract))
    st("vector", lambda e: e.tensor_tensor(out=t_u[:, :], in0=t_u[:, :], in1=t_r[:, :],
                                           op=ALU.subtract))
    st("vector", lambda e: e.scalar_tensor_tensor(out=t_r[:, :], in0=t_u[:, :],
                                                  scalar=-CW1, in1=t_ang[:, :],
                                                  op0=ALU.mult, op1=ALU.add))
    st("vector", lambda e: e.scalar_tensor_tensor(out=t_r[:, :], in0=t_u[:, :],
                                                  scalar=-CW2, in1=t_r[:, :],
                                                  op0=ALU.mult, op1=ALU.add))
    st("vector", lambda e: e.scalar_tensor_tensor(out=t_r[:, :], in0=t_u[:, :],
                                                  scalar=-CW3, in1=t_r[:, :],
                                                  op0=ALU.mult, op1=ALU.add))
    st("vector", lambda e: e.tensor_scalar(out=t_b[0:64, :], in0=t_r[:, :],
                                           scalar1=PI_F, scalar2=1.0,
                                           op0=ALU.is_gt, op1=ALU.mult))
    st("vector", lambda e: e.scalar_tensor_tensor(
        out=t_r[:, :], in0=t_b[0:64, :], scalar=-CW1, in1=t_r[:, :],
        op0=ALU.mult, op1=ALU.add))
    st("vector", lambda e: e.scalar_tensor_tensor(
        out=t_b[0:64, :], in0=t_b[0:64, :], scalar=-CW2, in1=t_r[:, :],
        op0=ALU.mult, op1=ALU.add))
    w_sinin = sVE.n
    st("scalar", lambda e: e.activation(out=ssinT[:, :], in_=t_b[0:64, :], func=AF.Sin),
       waits=[(sVE, w_sinin)])
    w_sin = sSE.n
    st("vector", lambda e: e.tensor_scalar_mul(out=ssinT[0:32, :], in0=ssinT[0:32, :],
                                               scalar1=-1.0), waits=[(sSE, w_sin)])
    w_ssin = sVE.n
    st("vector", lambda e: e.tensor_scalar_add(out=t_u[:, :], in0=t_r[:, :],
                                               scalar1=PI_HALF), waits=[(sSE, w_sin)])
    st("vector", lambda e: e.tensor_scalar_add(out=t_u[:, :], in0=t_u[:, :],
                                               scalar1=PI_HALF_LO))
    st("vector", lambda e: e.tensor_scalar(out=t_b[0:64, :], in0=t_u[:, :],
                                           scalar1=PI_F, scalar2=1.0,
                                           op0=ALU.is_gt, op1=ALU.mult))
    st("vector", lambda e: e.scalar_tensor_tensor(
        out=t_u[:, :], in0=t_b[0:64, :], scalar=-CW1, in1=t_u[:, :],
        op0=ALU.mult, op1=ALU.add))
    st("vector", lambda e: e.scalar_tensor_tensor(
        out=t_b[0:64, :], in0=t_b[0:64, :], scalar=-CW2, in1=t_u[:, :],
        op0=ALU.mult, op1=ALU.add))
    w_cosin = sVE.n
    st("scalar", lambda e: e.activation(out=cosT[:, :], in_=t_b[0:64, :], func=AF.Sin),
       waits=[(sVE, w_cosin)])
    w_cos = sSE.n
    if dbg_t("d_cs", [64, 2 * T]):
        st("gpsimd", lambda e: e.dma_start(out=DBG["d_cs"][:, 0:T], in_=cosT[:, :]),
           waits=[(sSE, w_cos)], inc=sDBG2, inck=16)
        st("gpsimd", lambda e: e.dma_start(out=DBG["d_cs"][:, T:2 * T], in_=ssinT[:, :]),
           waits=[(sVE, w_ssin)], inc=sDBG2, inck=16)

    st("scalar", lambda e: e.activation(out=modv[:, 112:128], in_=modv[:, 96:112],
                                        func=AF.Silu), waits=[(sLD, w_cvec)])
    w_silu = sSE.n
    for kc in range(NKC):
        st("vector", lambda e, kc=kc: e.tensor_scalar_mul(
            out=smat[:, kc * 128:(kc + 1) * 128], in0=onesb[:, :],
            scalar1=modv[:, 112 + kc:113 + kc]),
           waits=[(sSE, w_silu), (sVE, w_consts)] if kc == 0 else [],
           inc=(sVE if kc == NKC - 1 else None))
    w_smat = sVE.n

    mod_mm_base = sMM.n
    for j in range(24):
        st("gpsimd", lambda e, j=j: e.dma_start(
            out=mring[:, (j % 2) * NKC * 512:((j % 2) + 1) * NKC * 512], in_=wmod_d[j]),
           waits=[(sMM, mod_mm_base + j - 1)] if j >= 2 else [],
           inc=(sM0 if j % 2 == 0 else sM1), inck=16)
    w_modtt = 0
    for j in range(24):
        wld = 16 * (j // 2 + 1)
        sMp = sM0 if j % 2 == 0 else sM1
        slot = mring[:, (j % 2) * NKC * 512:((j % 2) + 1) * NKC * 512]
        for kc in range(NKC):
            st("tensor", lambda e, kc=kc, slot=slot: e.matmul(
                psE[:, :], smat[:, kc * 128:(kc + 1) * 128],
                slot[:, kc * 512:(kc + 1) * 512],
                start=(kc == 0), stop=(kc == NKC - 1)),
               waits=(([(sMp, wld), (sVE, w_smat)] +
                       ([(sVE, w_modtt)] if j >= 1 else []))
                      if kc == 0 else []),
               inc=(sMM if kc == NKC - 1 else None))
        wmm = sMM.n
        w_modtt = st("vector", lambda e: e.tensor_tensor(out=sm2[:, 0:512], in0=psE[:, :],
                                                         in1=idn4[:, :], op=ALU.mult),
                     waits=[(sMM, wmm), (sLD, w_idn)])
        st("vector", lambda e, j=j: e.tensor_reduce(
            out=modv[:, j * 4:(j + 1) * 4],
            in_=sm2[:, 0:512].rearrange("p (b f) -> p b f", f=128),
            axis=AX.X, op=ALU.add))
    w_gmul = sVE.n
    for (cg, nm) in ((32, "ga"), (80, "gb")):
        st("scalar", lambda e, cg=cg: e.activation(out=modv[:, 112:128],
                                                   in_=modv[:, cg:cg + 16],
                                                   func=AF.Sign),
           waits=[(sVE, w_gmul)])
        w_ab_ = st("scalar", lambda e, cg=cg: e.activation(out=modv[:, cg:cg + 16],
                                                           in_=modv[:, cg:cg + 16],
                                                           func=AF.Abs))
        st("vector", lambda e, cg=cg: e.tensor_scalar_max(out=modv[:, cg:cg + 16],
                                                          in0=modv[:, cg:cg + 16],
                                                          scalar1=0.25),
           waits=[(sSE, w_ab_)])
        w_gmul = st("vector", lambda e, cg=cg: e.tensor_tensor(
            out=modv[:, cg:cg + 16], in0=modv[:, cg:cg + 16],
            in1=modv[:, 112:128], op=ALU.mult))
    st("vector", lambda e: e.tensor_scalar_add(out=modv[:, 16:32], in0=modv[:, 16:32],
                                               scalar1=1.0))
    st("vector", lambda e: e.tensor_scalar_add(out=modv[:, 64:80], in0=modv[:, 64:80],
                                               scalar1=1.0))
    w_mod = sVE.n
    if dbg_t("d_mod", [128, 128]):
        st("sync", lambda e: e.dma_start(out=DBG["d_mod"][:, :], in_=modv[:, :]),
           waits=[(sVE, w_mod)], inc=sDBG, inck=16)

    w_rows_evac = 0
    w_hmul_last = 0
    w_rowfree = []
    for tc in range(4):
        for kc in range(NKC):
            st("tensor", lambda e, tc=tc, kc=kc: e.matmul(
                psA[:, :], onesb[:, :],
                h_sb[:, kc * T + tc * 512: kc * T + (tc + 1) * 512],
                start=(kc == 0), stop=(kc == NKC - 1)),
               waits=(([(sXL, w_xall), (sVE, w_consts)] +
                       ([(sVE, w_rows_evac)] if tc > 0 else []))
                      if kc == 0 else []),
               inc=(sMM if kc == NKC - 1 else None))
        w_sum = sMM.n
        for kc in range(NKC):
            w_slot = sMM.n
            st("scalar", lambda e, tc=tc, kc=kc: e.activation(
                out=xbb[:, (kc % 2) * 512:(kc % 2 + 1) * 512],
                in_=h_sb[:, kc * T + tc * 512: kc * T + (tc + 1) * 512],
                func=AF.Square),
               waits=([(sMM, sMM.n - 1)] if kc >= 2 else []) +
                     ([(sXL, w_xall)] if (tc == 0 and kc == 0) else []))
            w_sq = sSE.n
            st("tensor", lambda e, kc=kc: e.matmul(
                psB[:, :], onesb[:, :], xbb[:, (kc % 2) * 512:(kc % 2 + 1) * 512],
                start=(kc == 0), stop=(kc == NKC - 1)),
               waits=[(sSE, w_sq)], inc=sMM)
        w_sq_mm = sMM.n
        r0, r1, r2 = rows[0:1, 0:512], rows[0:1, 1024:1536], rows[0:1, 2048:2560]
        st("vector", lambda e: e.tensor_scalar_mul(out=r0, in0=psA[0:1, :], scalar1=1.0 / D),
           waits=[(sMM, w_sq_mm)] + (w_rowfree if tc > 0 else []))
        w_rows_evac = st("vector", lambda e: e.tensor_scalar_mul(
            out=r1, in0=psB[0:1, :], scalar1=1.0 / D))
        st("vector", lambda e: e.tensor_tensor(out=r2, in0=r0, in1=r0, op=ALU.mult))
        st("vector", lambda e: e.tensor_tensor(out=r1, in0=r1, in1=r2, op=ALU.subtract),
           inc=sVE)
        w_var = sVE.n
        st("scalar", lambda e: e.activation(out=r2, in_=r1, func=AF.Sqrt, bias=consts[0:1, 2:3]),
           waits=[(sVE, w_var)])
        w_std = sSE.n
        st("vector", lambda e: e.reciprocal(out=r1, in_=r2), waits=[(sSE, w_std)])
        st("vector", lambda e: e.tensor_tensor(out=r0, in0=r0, in1=r1, op=ALU.mult))
        st("vector", lambda e: e.tensor_scalar_mul(out=r0, in0=r0, scalar1=-1.0))
        w_ab = sVE.n
        if tc == 0 and dbg_t("d_stats", [2, T]):
            st("sync", lambda e: e.dma_start(out=DBG["d_stats"][0:1, 0:512], in_=r1),
               waits=[(sVE, w_ab)], inc=sDBG, inck=16)
            st("sync", lambda e: e.dma_start(out=DBG["d_stats"][1:2, 0:512], in_=r0),
               waits=[(sVE, w_ab)], inc=sDBG, inck=16)
        wbAw, wbA = bounce(0, r1, 512, 16 + tc, rbc[:, 0:512],
                           waits=[(sVE, w_ab)] + ([(sVE, w_hmul_last)] if tc > 0 else []))
        wbBw, wbB = bounce(1, r0, 512, 20 + tc, rbc[:, 512:1024], waits=[(sVE, w_ab)])
        w_rowfree = [wbAw, wbBw] + ([(sDBG, sDBG.n)] if tc == 0 and "d_stats" in DBG else [])
        for kc in range(NKC):
            sl = h_sb[:, kc * T + tc * 512: kc * T + (tc + 1) * 512]
            s2 = sm2[:, (kc % 2) * 512:(kc % 2 + 1) * 512]
            st("vector", lambda e, sl=sl, s2=s2: e.tensor_tensor(
                out=s2, in0=sl, in1=rbc[:, 0:512], op=ALU.mult),
               waits=([wbA, wbB] if kc == 0 else
                      [(sSE, sSE.n - 1)] if kc >= 2 else []))
            st("vector", lambda e, s2=s2: e.tensor_tensor(
                out=s2, in0=s2, in1=rbc[:, 512:1024], op=ALU.add))
            w_t = sVE.n
            w_hmul_last = w_t
            st("scalar", lambda e, sl=sl, s2=s2, kc=kc: e.activation(
                out=sl, in_=s2, func=AF.Identity,
                scale=modv[:, 16 + kc:17 + kc], bias=modv[:, kc:kc + 1]),
               waits=[(sVE, w_t)] + ([(sVE, w_mod)] if kc == 0 and tc == 0 else []),
               inc=sSE)
    st("scalar", lambda e: e.activation(out=tab[:, 5:6], in_=tab[:, 1:2], func=AF.Identity),
       inc=sSE)
    w_h = sSE.n
    if dbg_t("d_h", [128, NKC * T]):
        st("gpsimd", lambda e: e.dma_start(out=DBG["d_h"][:, :], in_=h_sb[:, :]),
           waits=[(sSE, w_h)], inc=sDBG2, inck=16)

    if phases <= 1:
        for kc in range(NKC):
            st("gpsimd", lambda e, kc=kc: e.dma_start(
                out=out_d[kc * 128:(kc + 1) * 128, :], in_=h_sb[:, kc * T:kc * T + OWN]),
               waits=[(sSE, w_h)] if kc == 0 else [], inc=sOUT, inck=16)
        w_fin = sOUT.n
        st("sync", lambda e: e.nop(), waits=[(sOUT, w_fin), (sDBG, sDBG.n), (sDBG2, sDBG2.n)])
        K.DBG = DBG
        return K

    dbg_t("d_k0", [128, T]); dbg_t("d_q0", [128, OWN]); dbg_t("d_v0", [128, NKC * 256])
    dbg_t("d_e0", [128, NKC * 512]); dbg_t("d_l", [16, OWN]); dbg_t("d_y", [128, NKC * OWN])
    dbg_t("d_x2", [D, OWN])

    scale_chunks = []
    for wd, nel in ((w1_d, DFF * D), (w2_d, DFF * D), (w3_d, DFF * D)):
        flat = wd.rearrange("a b c d -> (a b c d)")
        nch = nel // (128 * 1024)
        for i in range(nch):
            scale_chunks.append(flat[i * 131072:(i + 1) * 131072]
                                .rearrange("(p n) -> p n", p=128))
    n_scale = len(scale_chunks)
    scale_state = {"i": 0, "se": [0] * n_scale, "fold": [], "w_red": 0}

    def emit_fold(j):
        w_last = scale_state["se"][(j + 1) * 128 - 1]
        w_red = st("vector", lambda e: e.reduce_sum(out=accw[:, 128:129],
                                                    in_=accw[:, 0:128], axis=AX.X),
                   waits=[(sSE, w_last)])
        scale_state["w_red"] = w_red
        st("sync", lambda e, j=j: e.dma_start(out=bnc_d[8 + j:9 + j, 0:128],
                                              in_=accw[:, 128:129]),
           waits=[(sVE, w_red)], inc=sB[2], inck=16)
        w1 = sB[2].n
        st("sync", lambda e, j=j: e.dma_start(out=rbc[0:1, 0:128],
                                              in_=bnc_d[8 + j:9 + j, 0:128]),
           waits=[(sB[2], w1)], inc=sB[3], inck=16)
        w2 = sB[3].n
        w3 = st("vector", lambda e: e.reduce_sum(out=rbc[0:1, 128:129],
                                                 in_=rbc[0:1, 0:128], axis=AX.X),
                waits=[(sB[3], w2)])
        st("sync", lambda e, j=j: e.dma_start(out=bnc_d[8 + j:9 + j, 128:129],
                                              in_=rbc[0:1, 128:129]),
           waits=[(sVE, w3)], inc=sB[2], inck=16)
        w4 = sB[2].n
        st("sync", lambda e, j=j: e.dma_start(
            out=wsc[:, 7:8],
            in_=bnc_d[8 + j:9 + j, 128:129].broadcast_to((128, 1))),
           waits=[(sB[2], w4)], inc=sB[3], inck=16)
        w5 = sB[3].n
        st("vector", lambda e, j=j: e.tensor_scalar(
            out=wsc[:, 3 + j:4 + j], in0=wsc[:, 7:8],
            scalar1=1.0 / (DFF * D), scalar2=1e-5, op0=ALU.mult, op1=ALU.max),
           waits=[(sB[3], w5)])
        scale_state["fold"].append(st("vector", lambda e, j=j: e.tensor_scalar_mul(
            out=wsc[:, j:j + 1], in0=wsc[:, 3 + j:4 + j], scalar1=0.5)))

    def emit_scale(nmax):
        for _ in range(nmax):
            i = scale_state["i"]
            if i >= n_scale:
                return
            scale_state["i"] = i + 1
            sl = i % 2
            sW = sW0 if sl == 0 else sW1
            wfree = scale_state["se"][i - 2] if i >= 2 else 0
            st("sync", lambda e, i=i, sl=sl: e.dma_start(
                out=wfl[:, sl * 1024:(sl + 1) * 1024], in_=scale_chunks[i]),
               waits=[(sSE, wfree)], inc=sW, inck=16)
            wld = sW.n
            col = i % 128
            scale_state["se"][i] = st(
                "scalar", lambda e, sl=sl, col=col: e.activation(
                    out=wfl[:, sl * 1024:(sl + 1) * 1024],
                    in_=wfl[:, sl * 1024:(sl + 1) * 1024],
                    func=AF.Abs, accum_out=accw[:, col:col + 1]),
                waits=[(sW, wld)] + ([(sVE, scale_state["w_red"])]
                                     if col == 0 and i > 0 else []))
            if (i + 1) % 128 == 0:
                emit_fold(i // 128)


    events = []
    for g in range(8):
        events.append(("V", g))
        events.append(("KQ", 2 * g))
        events.append(("KQ", 2 * g + 1))
    slot_of = {}
    for i, ev in enumerate(events):
        slot_of[ev] = i % 2
    ev_index = {ev: i for i, ev in enumerate(events)}
    slots = [wsl, wsl2]
    sAs = [sA0, sA1]

    scale_state["w_ones"] = st("vector", lambda e: e.memset(onesf128[:, :], 1.0))
    ev_ld = {}
    ev_done = {}

    def emit_ev_load(ev):
        i = ev_index[ev]
        sl = slot_of[ev]
        sA = sAs[sl]
        wcred = [(sMM, ev_done[events[i - 2]])] if i >= 2 else []
        if ev[0] == "V":
            g = ev[1]
            st("gpsimd", lambda e, sl=sl, g=g: e.dma_start(
                out=slots[sl][:, 0:4096], in_=wv_d[g]), waits=wcred, inc=sA, inck=16)
        else:
            h = ev[1]
            st("gpsimd", lambda e, sl=sl, h=h: e.dma_start(
                out=slots[sl][:, 0:2048], in_=wk_d[h]), waits=wcred, inc=sA, inck=16)
            st("gpsimd", lambda e, sl=sl, h=h: e.dma_start(
                out=slots[sl][:, 2048:4096], in_=wq_d[h]), inc=sA, inck=16)
        ev_ld[ev] = sA.n

    emit_ev_load(events[0])
    emit_ev_load(events[1])

    w_ln1_last_sq = sSE.n
    hrope = {}

    w_vgrp_evac = [0] * 8
    w_yevac = 0
    w_lrowcp = 0
    w_exp = {}
    for h in range(NH):
        g = h // 2
        kq_ev = ("KQ", h)
        v_ev = ("V", g)
        sl_kq = slot_of[kq_ev]
        sl_v = slot_of[v_ev]
        kslot = ksb[:, (h % 2) * T:(h % 2 + 1) * T]
        qslot = qsb[:, (h % 2) * OWN:(h % 2 + 1) * OWN]
        vslot = vsb[:, (g % 2) * NKC * 256:((g % 2) + 1) * NKC * 256]

        if h % 2 == 0:
            for kkc in range(NKC):
                for kc in range(NKC):
                    st("tensor", lambda e, kkc=kkc, kc=kc, sl=sl_v: e.matmul(
                        psE[:, 0:256], h_sb[:, kc * T + kkc * 128: kc * T + kkc * 128 + 128],
                        slots[sl][:, kc * 256:(kc + 1) * 256],
                        start=(kc == 0), stop=(kc == NKC - 1)),
                       waits=(([(sAs[sl_v], ev_ld[v_ev]), (sSE, w_ln1_last_sq)]
                               if kkc == 0 else []) +
                              ([(sVE, w_vgrp_evac[g])] if (kc == 0 and kkc > 0) else
                               [(sVE, w_vgrp_evac[g - 1] if g > 0 else 0)]
                               if kc == 0 else [])),
                       inc=(sMM if kc == NKC - 1 else None))
                w_mm = sMM.n
                w_vgrp_evac[g] = st("vector", lambda e, kkc=kkc, vslot=vslot: e.tensor_copy(
                    out=vslot[:, kkc * 256:(kkc + 1) * 256], in_=psE[:, 0:256]),
                    waits=[(sMM, w_mm)])
            ev_done[v_ev] = sMM.n
            ni = ev_index[v_ev] + 2
            if ni < len(events):
                emit_ev_load(events[ni])
            if h == 0 and "d_v0" in DBG:
                st("gpsimd", lambda e, vslot=vslot: e.dma_start(
                    out=DBG["d_v0"][:, :], in_=vslot[:, :]),
                   waits=[(sVE, w_vgrp_evac[0])], inc=sDBG2, inck=16)

        w_kq_done_mm = 0
        for tkc in range(4):
            pst = psF[:, 0:512]
            for kc in range(NKC):
                st("tensor", lambda e, tkc=tkc, kc=kc, sl=sl_kq: e.matmul(
                    pst, slots[sl][:, kc * 128:(kc + 1) * 128],
                    h_sb[:, kc * T + tkc * 512: kc * T + (tkc + 1) * 512],
                    start=(kc == 0), stop=(kc == NKC - 1)),
                   waits=([(sAs[sl_kq], ev_ld[kq_ev])] +
                          ([(sSE, hrope.get((h, tkc - 1), 0))] if tkc > 0 else
                           [(sSE, hrope.get((h - 1, "q1"), 0))]))
                   if kc == 0 else [],
                   inc=(sMM if kc == NKC - 1 else None))
            w_mm = sMM.n
            prev_v = (hrope.get((h, tkc - 1, "v"), 0) if tkc > 0
                      else hrope.get((h - 1, "q1", "v"), 0))
            hrope[(h, tkc)] = st("scalar", lambda e: e.activation(
                out=kraw[:, 0:512], in_=pst, func=AF.Identity),
                waits=[(sMM, w_mm), (sVE, prev_v)])
            st("vector", lambda e: e.tensor_copy(out=krot[0:32, 0:512],
                                                 in_=kraw[32:64, 0:512]),
               waits=[(sSE, hrope[(h, tkc)])])
            st("vector", lambda e: e.tensor_copy(out=krot[32:64, 0:512],
                                                 in_=kraw[0:32, 0:512]))
            st("vector", lambda e, tkc=tkc: e.tensor_tensor(
                out=sm2[0:64, 0:512], in0=kraw[0:64, 0:512],
                in1=cosT[:, tkc * 512:(tkc + 1) * 512], op=ALU.mult))
            st("vector", lambda e, tkc=tkc: e.tensor_tensor(
                out=krot[0:64, 0:512], in0=krot[0:64, 0:512],
                in1=ssinT[:, tkc * 512:(tkc + 1) * 512], op=ALU.mult))
            st("vector", lambda e, tkc=tkc, kslot=kslot: e.tensor_tensor(
                out=kslot[0:64, tkc * 512:(tkc + 1) * 512], in0=sm2[0:64, 0:512],
                in1=krot[0:64, 0:512], op=ALU.add))
            w_krope_v = st("vector", lambda e, tkc=tkc, kslot=kslot: e.tensor_copy(
                out=kslot[64:128, tkc * 512:(tkc + 1) * 512], in_=kraw[64:128, 0:512]))
            hrope[(h, tkc)] = hrope[(h, tkc)]
            hrope[(h, tkc, "v")] = w_krope_v
        for qkc in range(2):
            pst = psF[:, 0:512]
            for kc in range(NKC):
                st("tensor", lambda e, qkc=qkc, kc=kc, sl=sl_kq: e.matmul(
                    pst, slots[sl][:, 2048 + kc * 128: 2048 + (kc + 1) * 128],
                    h_sb[:, kc * T + qkc * 512: kc * T + (qkc + 1) * 512],
                    start=(kc == 0), stop=(kc == NKC - 1)),
                   waits=[(sSE, hrope[(h, 3)] if qkc == 0 else hrope[(h, "q0")])]
                   if kc == 0 else [],
                   inc=(sMM if kc == NKC - 1 else None))
            w_mm = sMM.n
            key = "q%d" % qkc
            prev_v = hrope[(h, 3, "v")] if qkc == 0 else hrope[(h, "q0", "v")]
            hrope[(h, key)] = st("scalar", lambda e: e.activation(
                out=kraw[:, 0:512], in_=pst, func=AF.Identity),
                waits=[(sMM, w_mm), (sVE, prev_v)])
            st("vector", lambda e: e.tensor_copy(out=krot[0:32, 0:512],
                                                 in_=kraw[32:64, 0:512]),
               waits=[(sSE, hrope[(h, key)])])
            st("vector", lambda e: e.tensor_copy(out=krot[32:64, 0:512],
                                                 in_=kraw[0:32, 0:512]))
            st("vector", lambda e, qkc=qkc: e.tensor_tensor(
                out=sm2[0:64, 0:512], in0=kraw[0:64, 0:512],
                in1=cosT[:, qkc * 512:(qkc + 1) * 512], op=ALU.mult))
            st("vector", lambda e, qkc=qkc: e.tensor_tensor(
                out=krot[0:64, 0:512], in0=krot[0:64, 0:512],
                in1=ssinT[:, qkc * 512:(qkc + 1) * 512], op=ALU.mult))
            st("vector", lambda e, qkc=qkc, qslot=qslot: e.tensor_tensor(
                out=qslot[0:64, qkc * 512:(qkc + 1) * 512], in0=sm2[0:64, 0:512],
                in1=krot[0:64, 0:512], op=ALU.add))
            hrope[(h, key, "v")] = st("vector", lambda e, qkc=qkc, qslot=qslot: e.tensor_copy(
                out=qslot[64:128, qkc * 512:(qkc + 1) * 512], in_=kraw[64:128, 0:512]))
        ev_done[kq_ev] = sMM.n
        ni = ev_index[kq_ev] + 2
        if ni < len(events):
            emit_ev_load(events[ni])
        if h == 0:
            if "d_k0" in DBG:
                st("gpsimd", lambda e, kslot=kslot: e.dma_start(out=DBG["d_k0"][:, :], in_=kslot[:, :]),
                   waits=[(sVE, hrope[(0, 3, "v")])], inc=sDBG2, inck=16)
            if "d_q0" in DBG:
                st("gpsimd", lambda e, qslot=qslot: e.dma_start(out=DBG["d_q0"][:, :], in_=qslot[:, :]),
                   waits=[(sVE, hrope[(0, "q1", "v")])], inc=sDBG2, inck=16)

        w_rope_all = max(hrope[(h, 3, "v")], hrope[(h, "q1", "v")],
                         hrope[(h, 0, "v")], hrope[(h, 1, "v")], hrope[(h, 2, "v")],
                         hrope[(h, "q0", "v")])
        for qh in range(2):
            ps_y = psC if qh == 0 else psD
            ps_l = psG if qh == 0 else psH
            for kkc in range(NKC):
                ps_s = psA if kkc % 2 == 0 else psB
                st("tensor", lambda e, kkc=kkc, qh=qh, ps_s=ps_s, kslot=kslot,
                   qslot=qslot: e.matmul(
                       ps_s[:, 0:512], kslot[:, kkc * 128:(kkc + 1) * 128],
                       qslot[:, qh * 512:(qh + 1) * 512], start=True, stop=True),
                   waits=([(sVE, w_rope_all), (sVE, w_vgrp_evac[g]),
                           (sSE, w_yevac), (sVE, w_lrowcp)] if (qh == 0 and kkc == 0)
                          else [(sSE, w_exp.get((h, qh, kkc - 2),
                                                w_exp.get((h, 0, 14 + kkc), 0)))]),
                   inc=sMM)
                w_smm = sMM.n
                prev_avdone = (w_exp.get(("avdone", h, 0), 0) if qh == 1
                               else w_exp.get(("avdone", h - 1, 1), 0))
                w_exp[(h, qh, kkc)] = st("scalar", lambda e, kkc=kkc, ps_s=ps_s: e.activation(
                    out=esb[:, kkc * 512:(kkc + 1) * 512], in_=ps_s[:, 0:512],
                    func=AF.Exp, bias=consts[:, 0:1], scale=SCALE_S),
                    waits=([(sMM, w_smm), (sMM, prev_avdone), (sDBG2, sDBG2.n)]
                           if kkc == 0 else [(sMM, w_smm)]))
                wexp = w_exp[(h, qh, kkc)]
                st("tensor", lambda e, kkc=kkc, ps_l=ps_l: e.matmul(
                    ps_l[:, 0:512], onesb[:, :], esb[:, kkc * 512:(kkc + 1) * 512],
                    start=(kkc == 0), stop=(kkc == NKC - 1)),
                   waits=[(sSE, wexp)] + ([(sVE, w_lrowcp)] if kkc == 0 else []),
                   inc=(sMM if kkc == NKC - 1 else None))
                st("tensor", lambda e, kkc=kkc, qh=qh, ps_y=ps_y, vslot=vslot,
                   h=h: e.matmul(
                       ps_y[:, 0:512],
                       vslot[:, kkc * 256 + (h % 2) * 128: kkc * 256 + (h % 2) * 128 + 128],
                       esb[:, kkc * 512:(kkc + 1) * 512],
                       start=(kkc == 0), stop=(kkc == NKC - 1)),
                   inc=(sMM if kkc == NKC - 1 else None))
            w_exp[("avdone", h, qh)] = sMM.n
            w_yevac = st("scalar", lambda e, h=h, qh=qh, ps_y=ps_y: e.activation(
                out=ysb[:, h * OWN + qh * 512: h * OWN + (qh + 1) * 512],
                in_=ps_y[:, 0:512], func=AF.Identity), waits=[(sMM, sMM.n)])
            w_lrowcp = st("vector", lambda e, qh=qh, ps_l=ps_l: e.tensor_copy(
                out=rows[0:1, qh * 512:(qh + 1) * 512], in_=ps_l[0:1, 0:512]),
                waits=[(sMM, sMM.n), (sLB, sLB.n)])
        st("sync", lambda e, h=h: e.dma_start(out=bnc_d[32 + h:33 + h, 0:1024],
                                              in_=rows[0:1, 0:1024]),
           waits=[(sVE, w_lrowcp)], inc=sLB, inck=16)
        if h == e_head and "d_e0" in DBG:
            st("gpsimd", lambda e: e.dma_start(out=DBG["d_e0"][:, :], in_=esb[:, :]),
               waits=[(sSE, w_exp[(e_head, 1, NKC - 1)])], inc=sDBG2, inck=16)
        emit_scale(24)

    st("sync", lambda e: e.dma_start(out=lrec[:, :], in_=bnc_d[32:48, 0:1024]),
       waits=[(sLB, sLB.n)], inc=sLR, inck=16)
    w_lrl = sLR.n
    w_rec = st("vector", lambda e: e.reciprocal(out=lrec[:, :], in_=lrec[:, :]),
               waits=[(sLR, w_lrl)])
    st("sync", lambda e: e.dma_start(out=bnc_d[48:64, 0:1024], in_=lrec[:, :]),
       waits=[(sVE, w_rec)], inc=sLR, inck=16)
    w_lrw = sLR.n
    if "d_l" in DBG:
        st("sync", lambda e: e.dma_start(out=DBG["d_l"][:, :], in_=lrec[:, :]),
           waits=[(sVE, w_rec)], inc=sDBG, inck=16)
    w_ynorm = 0
    for h in range(NH):
        rb = rbc if h % 2 == 0 else sm2
        sRB = sRB0 if h % 2 == 0 else sRB1
        st("sync", lambda e, h=h, rb=rb: e.dma_start(
            out=rb[:, 0:1024],
            in_=bnc_d[48 + h:49 + h, 0:1024].broadcast_to((128, 1024))),
           waits=[(sLR, w_lrw)] + ([(sVE, w_ynorm)] if h >= 2 else
                                    [(sVE, scale_state["fold"][-1]
                                      if scale_state["fold"] else 0)]),
           inc=sRB, inck=16)
        wrb = sRB.n
        w_ynorm = st("vector", lambda e, h=h, rb=rb: e.tensor_tensor(
            out=ysb[:, h * OWN:(h + 1) * OWN], in0=ysb[:, h * OWN:(h + 1) * OWN],
            in1=rb[:, 0:1024], op=ALU.mult),
            waits=[(sRB, wrb), (sSE, w_yevac)])
    if "d_y" in DBG:
        st("gpsimd", lambda e: e.dma_start(out=DBG["d_y"][:, :], in_=ysb[:, :]),
           waits=[(sVE, w_ynorm)], inc=sDBG3, inck=16)
    w_dy_tap = sDBG3.n if "d_y" in DBG else 0

    emit_scale(n_scale)
    w_scale_fold = scale_state["fold"]
    if "d_ws" in dbg:
        DBG["d_ws"] = nc.dram_tensor("d_ws", [1, 16], F32, kind="ExternalOutput").ap()
        st("sync", lambda e: e.dma_start(out=DBG["d_ws"][0:1, 0:8], in_=wsc[0:1, :]),
           waits=[(sVE, w_scale_fold[-1])], inc=sDBG4, inck=16)

    wo_slot = [wsl[:, 0:2048], wsl[:, 2048:4096]]
    w_last_attn_ev = ev_done[events[-1]]
    wo_done = {}

    def emit_wo_load(dc):
        sl = dc % 2
        sA = sAs[sl]
        st("gpsimd", lambda e, dc=dc, sl=sl: e.dma_start(
            out=wo_slot[sl], in_=wo_d[dc]),
           waits=[(sMM, wo_done[dc - 2])] if dc >= 2 else
                 [(sMM, ev_done[events[-2 + dc]])], inc=sA, inck=16)
        ev_ld[("WO", dc)] = sA.n

    emit_wo_load(0)
    emit_wo_load(1)
    w_x2 = {}
    w_stmm = {}
    w_x2spill = []
    for dc in range(NKC):
        sl = dc % 2
        sXO = sXO0 if dc % 2 == 0 else sXO1
        xsl = xst[:, (dc % 2) * 1024:(dc % 2 + 1) * 1024]
        st("sync", lambda e, dc=dc, xsl=xsl: e.dma_start(
            out=xsl, in_=xc[dc * 128:(dc + 1) * 128, 0:OWN]),
           waits=[(sVE, w_x2.get(dc - 2, 0))] +
                 ([(sXS0 if dc % 2 == 0 else sXS1, w_x2spill[dc - 2])]
                  if dc >= 2 else []) +
                 ([(sMM, w_last_attn_ev)] if dc < 2 else []), inc=sXO, inck=16)
        w_xo = sXO.n
        for qh in range(2):
            ps_o = (psA if qh == 0 else psB) if dc % 2 == 0 else (psC if qh == 0 else psD)
            for kc in range(NKC):
                st("tensor", lambda e, dc=dc, qh=qh, kc=kc, sl=sl, ps_o=ps_o: e.matmul(
                    ps_o[:, 0:512], wo_slot[sl][:, kc * 128:(kc + 1) * 128],
                    ysb[:, kc * OWN + qh * 512: kc * OWN + (qh + 1) * 512],
                    start=(kc == 0), stop=(kc == NKC - 1)),
                   waits=([(sAs[sl], ev_ld[("WO", dc)]), (sVE, w_ynorm)] +
                          ([(sSE, w_x2.get(("se", dc - 2, qh), 0))] if dc >= 2 else []))
                   if kc == 0 else [],
                   inc=(sMM if kc == NKC - 1 else None))
            w_omm = sMM.n
            w_x2[("se", dc, qh)] = st("scalar", lambda e, dc=dc, qh=qh, ps_o=ps_o: e.activation(
                out=kraw[:, 0:512], in_=ps_o[:, 0:512], func=AF.Identity,
                scale=modv[:, 32 + dc:33 + dc]),
                waits=[(sMM, w_omm)] + ([(sVE, w_x2.get(("tt", dc, 0), 0))]
                                        if qh == 1 else
                                        [(sVE, w_x2.get(("tt", dc - 1, 1), 0))]))
            w_x2[("tt", dc, qh)] = st("vector", lambda e, dc=dc, qh=qh, xsl=xsl: e.tensor_tensor(
                out=xsl[:, qh * 512:(qh + 1) * 512], in0=xsl[:, qh * 512:(qh + 1) * 512],
                in1=kraw[:, 0:512], op=ALU.add),
                waits=[(sXO, w_xo), (sSE, w_x2[("se", dc, qh)])])
            st("vector", lambda e, qh=qh, xsl=xsl: e.tensor_copy(
                out=xbb[:, qh * 512:(qh + 1) * 512], in_=xsl[:, qh * 512:(qh + 1) * 512]),
               waits=[(sMM, w_stmm.get((dc - 1, qh), 0))])
            w_x2[dc] = sVE.n
            w_sq2 = st("scalar", lambda e, qh=qh: e.activation(
                out=sqb2[:, 0:512], in_=xbb[:, qh * 512:(qh + 1) * 512],
                func=AF.Square),
                waits=[(sVE, w_x2[dc]),
                       (sMM, w_stmm.get((dc, qh - 1),
                                        w_stmm.get((dc - 1, 1), 0)))])
            ps_sum = psE if qh == 0 else psF
            ps_sq = psG if qh == 0 else psH
            st("tensor", lambda e, dc=dc, qh=qh, ps_sum=ps_sum: e.matmul(
                ps_sum[:, 0:512], onesb[:, :], xbb[:, qh * 512:(qh + 1) * 512],
                start=(dc == 0), stop=(dc == NKC - 1)),
               waits=[(sVE, w_x2[dc])], inc=sMM)
            st("tensor", lambda e, dc=dc, qh=qh, ps_sq=ps_sq: e.matmul(
                ps_sq[:, 0:512], onesb[:, :], sqb2[:, 0:512],
                start=(dc == 0), stop=(dc == NKC - 1)),
               waits=[(sSE, w_sq2)], inc=sMM)
            w_stmm[(dc, qh)] = sMM.n
        wo_done[dc] = sMM.n
        if dc + 2 < NKC:
            emit_wo_load(dc + 2)
        sXSp = sXS0 if dc % 2 == 0 else sXS1
        st("sync", lambda e, dc=dc, xsl=xsl: e.dma_start(
            out=x2s_d[dc * 128:(dc + 1) * 128, :], in_=xsl),
           waits=[(sVE, w_x2[dc])], inc=sXSp, inck=16)
        w_x2spill.append(sXSp.n)
    w_ln2_mm = sMM.n
    if "d_x2" in DBG:
        st("sync", lambda e: e.dma_start(out=DBG["d_x2"][:, :], in_=x2s_d[:, :])
           if False else e.nop())
        for dc in range(NKC):
            st("gpsimd", lambda e, dc=dc: e.dma_start(
                out=DBG["d_x2"][dc * 128:(dc + 1) * 128, :],
                in_=x2s_d[dc * 128:(dc + 1) * 128, :]),
               waits=[(sXS0 if dc % 2 == 0 else sXS1, w_x2spill[dc])],
               inc=sDBG2, inck=16)

    if phases <= 2:
        for kc in range(NKC):
            st("gpsimd", lambda e, kc=kc: e.dma_start(
                out=out_d[kc * 128:(kc + 1) * 128, :], in_=ysb[:, kc * OWN:kc * OWN + OWN]),
               waits=[(sVE, w_ynorm)] if kc == 0 else [], inc=sOUT, inck=16)
        w_fin = sOUT.n
        st("sync", lambda e: e.nop(),
           waits=[(sOUT, w_fin), (sDBG, sDBG.n), (sDBG2, sDBG2.n),
                  (sXS0, sXS0.n), (sXS1, sXS1.n), (sMM, sMM.n)])
        K.DBG = DBG
        return K

    dbg_t("d_h2", [128, NKC * OWN]); dbg_t("d_q12", [128, NKC * OWN])
    dbg_t("d_hp", [128, 2 * OWN]); dbg_t("d_q3", [128, 2 * OWN])
    dbg_t("d_rows", [1, 2048])
    rr0 = rows[0:1, 0:1024]
    rr1 = rows[0:1, 1024:2048]
    rr2 = rows[0:1, 2048:3072]
    st("vector", lambda e: e.tensor_scalar_mul(out=rows[0:1, 0:512], in0=psE[0:1, 0:512],
                                               scalar1=1.0 / D),
       waits=[(sMM, w_ln2_mm), (sLR, sLR.n), (sRB0, sRB0.n), (sRB1, sRB1.n)])
    st("vector", lambda e: e.tensor_scalar_mul(out=rows[0:1, 512:1024],
                                               in0=psF[0:1, 0:512], scalar1=1.0 / D))
    st("vector", lambda e: e.tensor_scalar_mul(out=rows[0:1, 1024:1536],
                                               in0=psG[0:1, 0:512], scalar1=1.0 / D))
    w_r4 = st("vector", lambda e: e.tensor_scalar_mul(out=rows[0:1, 1536:2048],
                                                      in0=psH[0:1, 0:512], scalar1=1.0 / D))
    st("vector", lambda e: e.tensor_tensor(out=rr2, in0=rr0, in1=rr0, op=ALU.mult))
    st("vector", lambda e: e.tensor_tensor(out=rr1, in0=rr1, in1=rr2, op=ALU.subtract))
    w_var2 = sVE.n
    st("scalar", lambda e: e.activation(out=rr2, in_=rr1, func=AF.Sqrt,
                                        bias=consts[0:1, 2:3]), waits=[(sVE, w_var2)])
    w_std2 = sSE.n
    st("vector", lambda e: e.reciprocal(out=rr1, in_=rr2), waits=[(sSE, w_std2)])
    st("vector", lambda e: e.tensor_tensor(out=rr0, in0=rr0, in1=rr1, op=ALU.mult))
    w_ab2 = st("vector", lambda e: e.tensor_scalar_mul(out=rr0, in0=rr0, scalar1=-1.0))
    wbA2w, wbA2 = bounce(0, rr1, 1024, 16, rbc[:, 0:1024], waits=[(sVE, w_ab2)])
    wbB2w, wbB2 = bounce(1, rr0, 1024, 17, sm2[:, 0:1024], waits=[(sVE, w_ab2)])
    w_h2 = {}
    w_h2mm = {}
    w_maxinit = st("vector", lambda e: e.memset(maxac[:, :], 0.0))
    for dc in range(NKC):
        sXO = sXO0 if dc % 2 == 0 else sXO1
        xsl = xst[:, (dc % 2) * 1024:(dc % 2 + 1) * 1024]
        st("sync", lambda e, dc=dc, xsl=xsl: e.dma_start(
            out=xsl, in_=x2s_d[dc * 128:(dc + 1) * 128, :]),
           waits=([(sVE, w_h2.get(dc - 2, 0)), (sSE, w_h2.get(("aff", dc - 2), 0))]
                  if dc >= 2 else
                  [(sXS0, sXS0.n), (sXS1, sXS1.n)]), inc=sXO, inck=16)
        w_xo = sXO.n
        st("vector", lambda e, xsl=xsl: e.tensor_tensor(
            out=xsl, in0=xsl, in1=rbc[:, 0:1024], op=ALU.mult),
           waits=[(sXO, w_xo), wbA2, wbB2])
        w_t2 = st("vector", lambda e, xsl=xsl: e.tensor_tensor(
            out=xsl, in0=xsl, in1=sm2[:, 0:1024], op=ALU.add))
        w_aff = st("scalar", lambda e, dc=dc, xsl=xsl: e.activation(
            out=h2sb[:, dc * OWN:(dc + 1) * OWN], in_=xsl, func=AF.Identity,
            scale=modv[:, 64 + dc:65 + dc], bias=modv[:, 48 + dc:49 + dc]),
            waits=[(sVE, w_t2)] + ([(sDBG3, w_dy_tap)] if dc == 0 else []))
        w_h2[("aff", dc)] = w_aff
        w_h2[dc] = sVE.n
        for th in range(2):
            w_sqh = st("scalar", lambda e, dc=dc, th=th: e.activation(
                out=sqb2[:, 0:512],
                in_=h2sb[:, dc * OWN + th * 512: dc * OWN + (th + 1) * 512],
                func=AF.Square),
                waits=[(sMM, w_h2mm.get((dc, th - 1),
                                        w_h2mm.get((dc - 1, 1), 0))),
                       (sVE, w_h2.get(("mx", dc, th - 1),
                                      w_h2.get(("mx", dc - 1, 1), 0)))])
            ps_sq = psA if th == 0 else psB
            st("tensor", lambda e, dc=dc, th=th, ps_sq=ps_sq: e.matmul(
                ps_sq[:, 0:512], onesb[:, :], sqb2[:, 0:512],
                start=(dc == 0), stop=(dc == NKC - 1)),
               waits=[(sSE, w_sqh)], inc=sMM)
            w_h2mm[(dc, th)] = sMM.n
            w_h2[("mx", dc, th)] = st("vector", lambda e, th=th: e.tensor_tensor(
                out=maxac[:, th * 512:(th + 1) * 512], in0=maxac[:, th * 512:(th + 1) * 512],
                in1=sqb2[:, 0:512], op=ALU.max),
                waits=[(sSE, w_sqh)] + ([(sVE, w_maxinit)] if dc == 0 else []))
    if "d_h2" in DBG:
        st("gpsimd", lambda e: e.dma_start(out=DBG["d_h2"][:, :], in_=h2sb[:, :]),
           waits=[(sSE, sSE.n)], inc=sDBG2, inck=16)
    st("vector", lambda e: e.tensor_scalar_mul(
        out=rows[0:1, 1024:1536], in0=psA[0:1, 0:512], scalar1=1.0 / D),
       waits=[(sMM, w_h2mm[(NKC - 1, 0)]), wbA2w])
    w_rmsev = st("vector", lambda e: e.tensor_scalar_mul(
        out=rows[0:1, 1536:2048], in0=psB[0:1, 0:512], scalar1=1.0 / D),
        waits=[(sMM, w_h2mm[(NKC - 1, 1)])])
    st("vector", lambda e: e.tensor_copy(out=tmpb[0:64, :], in_=maxac[64:128, :]),
       waits=[(sVE, w_h2[("mx", NKC - 1, 1)])])
    st("vector", lambda e: e.tensor_tensor(out=maxac[0:64, :], in0=maxac[0:64, :],
                                           in1=tmpb[0:64, :], op=ALU.max))
    st("vector", lambda e: e.tensor_copy(out=tmpb[0:32, :], in_=maxac[32:64, :]))
    w_mx32 = st("vector", lambda e: e.tensor_tensor(out=maxac[0:32, :], in0=maxac[0:32, :],
                                                    in1=tmpb[0:32, :], op=ALU.max))
    wsh = w_mx32
    for npart in (16, 8, 4, 2, 1):
        st("sync", lambda e, npart=npart: e.dma_start(
            out=tmpb[0:npart, :], in_=maxac[npart:2 * npart, :]
            if npart >= 32 else maxac[npart:2 * npart, :]),
           waits=[(sVE, wsh)], inc=sB[4], inck=16)
        wdma = sB[4].n
        wsh = st("vector", lambda e, npart=npart: e.tensor_tensor(
            out=maxac[0:npart, :], in0=maxac[0:npart, :], in1=tmpb[0:npart, :],
            op=ALU.max), waits=[(sB[4], wdma)])
    st("scalar", lambda e: e.activation(out=rr2, in_=rr1, func=AF.Sqrt,
                                        bias=consts[0:1, 3:4]),
       waits=[(sVE, wsh), (sVE, w_rmsev)])
    w_s3 = sSE.n
    st("vector", lambda e: e.reciprocal(out=rr1, in_=rr2), waits=[(sSE, w_s3)])
    st("scalar", lambda e: e.activation(out=rr0, in_=maxac[0:1, 0:1024], func=AF.Sqrt),
       waits=[(sVE, sVE.n)])
    w_s4 = sSE.n
    st("vector", lambda e: e.tensor_tensor(out=rr0, in0=rr0, in1=rr1, op=ALU.mult),
       waits=[(sSE, w_s4)])
    st("vector", lambda e: e.tensor_scalar_max(out=rr0, in0=rr0, scalar1=1e-5))
    st("vector", lambda e: e.reciprocal(out=rr2, in_=rr0))
    st("vector", lambda e: e.tensor_tensor(out=rr2, in0=rr2, in1=rr1, op=ALU.mult))
    w_cq = st("vector", lambda e: e.tensor_scalar_mul(out=rr2, in0=rr2, scalar1=127.0))
    st("vector", lambda e: e.tensor_scalar(out=rr1, in0=rr0, scalar1=1.0 / 127.0,
                                           scalar2=wsc[0:1, 3:4],
                                           op0=ALU.mult, op1=ALU.mult))
    w_du1 = sVE.n
    st("vector", lambda e: e.tensor_scalar(out=rr0, in0=rr0, scalar1=1.0 / 127.0,
                                           scalar2=wsc[0:1, 4:5],
                                           op0=ALU.mult, op1=ALU.mult))
    w_du2 = sVE.n
    wbCQw, wbCQ = bounce(2, rr2, 1024, 18, maxac[:, 0:1024], waits=[(sVE, w_cq)])
    wbD1w, wbD1 = bounce(0, rr1, 1024, 19, du1b[:, 0:1024], waits=[(sVE, w_du1), wbA2])
    if "d_rows" in DBG:
        st("sync", lambda e: e.dma_start(out=DBG["d_rows"][0:1, 0:1024], in_=rr2),
           waits=[(sVE, w_cq)], inc=sDBG, inck=16)
        st("sync", lambda e: e.dma_start(out=DBG["d_rows"][0:1, 1024:2048], in_=rr0),
           waits=[(sVE, w_du2)], inc=sDBG, inck=16)
    if "d_cqb" in dbg:
        DBG["d_cqb"] = nc.dram_tensor("d_cqb", [128, 2048], F32, kind="ExternalOutput").ap()
        st("gpsimd", lambda e: e.dma_start(out=DBG["d_cqb"][:, 0:1024], in_=maxac[:, 0:1024]),
           waits=[wbCQ], inc=sDBG5, inck=16)
        st("gpsimd", lambda e: e.dma_start(out=DBG["d_cqb"][:, 1024:2048], in_=du1b[:, 0:1024]),
           waits=[wbD1], inc=sDBG5, inck=16)
    w_q12 = 0
    for dc in range(NKC):
        for th in range(2):
            sl = h2sb[:, dc * OWN + th * 512: dc * OWN + (th + 1) * 512]
            dst = q12[:, dc * OWN + th * 512: dc * OWN + (th + 1) * 512]
            ta = rbc[:, th * 512:(th + 1) * 512]
            tb = sm2[:, th * 512:(th + 1) * 512]
            bb = tmpb[:, th * 512:(th + 1) * 512]
            cqsl = maxac[:, th * 512:(th + 1) * 512]
            st("vector", lambda e, sl=sl, ta=ta, cqsl=cqsl: e.tensor_tensor(
                out=ta, in0=sl, in1=cqsl, op=ALU.mult),
               waits=[wbCQ, wbD1] if (dc == 0 and th == 0) else [])
            st("vector", lambda e, ta=ta, tb=tb: e.tensor_scalar_add(
                out=tb, in0=ta, scalar1=128.5))
            for kk in (128.0, 64.0, 32.0, 16.0, 8.0, 4.0, 2.0, 1.0):
                st("vector", lambda e, tb=tb, bb=bb, kk=kk: e.tensor_scalar(
                    out=bb, in0=tb, scalar1=kk, scalar2=kk,
                    op0=ALU.is_ge, op1=ALU.mult))
                st("vector", lambda e, tb=tb, bb=bb: e.tensor_tensor(
                    out=tb, in0=tb, in1=bb, op=ALU.subtract))
            st("vector", lambda e, ta=ta: e.tensor_scalar_add(out=ta, in0=ta,
                                                              scalar1=128.5))
            st("vector", lambda e, ta=ta, tb=tb: e.tensor_tensor(
                out=ta, in0=ta, in1=tb, op=ALU.subtract))
            w_q12 = st("vector", lambda e, ta=ta, dst=dst: e.tensor_scalar_add(
                out=dst, in0=ta, scalar1=-128.0))
    if "d_q12" in DBG:
        st("gpsimd", lambda e: e.dma_start(out=DBG["d_q12"][:, :], in_=q12[:, :]),
           waits=[(sVE, w_q12)], inc=sDBG2, inck=16)

    st("vector", lambda e: e.tensor_scalar_mul(out=wsc[:, 6:7], in0=wsc[:, 0:1],
                                               scalar1=-1.0),
       waits=[(sVE, w_scale_fold[-1])] + ([(sDBG4, sDBG4.n)] if "d_ws" in DBG else []))
    w_negt = st("vector", lambda e: e.tensor_scalar_mul(out=wsc[:, 7:8], in0=wsc[:, 1:2],
                                                        scalar1=-1.0))
    w_f1mm = {}
    w_tern = {}
    w_hspill = {}
    w_hv = {}
    w_maxi3 = st("vector", lambda e: e.memset(maxac[:, :], 0.0),
                 waits=[(sVE, w_q12)] + ([(sDBG5, 32)] if "d_cqb" in DBG else []))
    for dc2 in range(NFC):
        sW = sW0 if dc2 % 2 == 0 else sW1
        st("sync", lambda e, dc2=dc2: e.dma_start(out=wfl[:, :], in_=w1_d[dc2])
           if False else e.dma_start(out=wfl[:, :] if dc2 % 2 == 0 else wfl2[:, :],
                                     in_=w1_d[dc2]),
           waits=[(sVE, w_tern.get(dc2 - 2, 0))], inc=sW, inck=16)
        st("sync", lambda e, dc2=dc2: e.dma_start(
            out=wfl2[:, :] if dc2 % 2 == 0 else wfl[:, :], in_=w2_d[dc2]),
           inc=sW, inck=16)
        w_wld = sW.n
        wslc1 = wfl if dc2 % 2 == 0 else wfl2
        wslc2 = wfl2 if dc2 % 2 == 0 else wfl
        st("vector", lambda e, wslc1=wslc1: e.tensor_scalar(
            out=terntmp[:, :], in0=wslc1[:, :], scalar1=wsc[:, 0:1], scalar2=-1.0,
            op0=ALU.is_ge, op1=ALU.add),
           waits=[(sW, w_wld), (sVE, w_negt)] +
                 ([(sMM, w_f1mm.get(dc2 - 1, 0))] if dc2 >= 1 else
                  [(sDBG2, sDBG2.n)]))
        st("vector", lambda e, wslc1=wslc1: e.scalar_tensor_tensor(
            out=ternA[:, :], in0=wslc1[:, :], scalar=wsc[:, 6:7], in1=terntmp[:, :],
            op0=ALU.is_gt, op1=ALU.add))
        st("vector", lambda e, wslc2=wslc2: e.tensor_scalar(
            out=terntmp[:, :], in0=wslc2[:, :], scalar1=wsc[:, 1:2], scalar2=-1.0,
            op0=ALU.is_ge, op1=ALU.add))
        w_tern[dc2] = st("vector", lambda e, wslc2=wslc2: e.scalar_tensor_tensor(
            out=ternB[:, :], in0=wslc2[:, :], scalar=wsc[:, 7:8], in1=terntmp[:, :],
            op0=ALU.is_gt, op1=ALU.add))
        for th in range(2):
            for kc in range(NKC):
                st("tensor", lambda e, th=th, kc=kc: e.matmul(
                    (psA if th == 0 else psB)[:, 0:512],
                    ternA[:, kc * 128:(kc + 1) * 128],
                    q12[:, kc * OWN + th * 512: kc * OWN + (th + 1) * 512],
                    start=(kc == 0), stop=(kc == NKC - 1)),
                   waits=([(sVE, w_tern[dc2]), (sVE, w_q12)] +
                          ([(sVE, w_hv.get((dc2 - 1, th), 0))])
                          if kc == 0 else []),
                   inc=(sMM if kc == NKC - 1 else None))
        for th in range(2):
            for kc in range(NKC):
                st("tensor", lambda e, th=th, kc=kc: e.matmul(
                    (psC if th == 0 else psD)[:, 0:512],
                    ternB[:, kc * 128:(kc + 1) * 128],
                    q12[:, kc * OWN + th * 512: kc * OWN + (th + 1) * 512],
                    start=(kc == 0), stop=(kc == NKC - 1)),
                   waits=([(sVE, w_hv.get((dc2 - 1, th), 0))] if kc == 0 else []),
                   inc=(sMM if kc == NKC - 1 else None))
        w_f1mm[dc2] = sMM.n
        for th in range(2):
            psu = psA if th == 0 else psB
            pst2 = psC if th == 0 else psD
            hsl = hch[:, ((2 * dc2 + th) % 4) * 512:(((2 * dc2 + th) % 4) + 1) * 512]
            sHSp = sHS[(2 * dc2 + th) % 4]
            st("vector", lambda e, psu=psu, th=th: e.tensor_tensor(
                out=rbc[:, th * 512:(th + 1) * 512], in0=psu[:, 0:512],
                in1=du1b[:, th * 512:(th + 1) * 512], op=ALU.mult),
               waits=[(sMM, w_f1mm[dc2])] +
                     ([(sHSp, sHSp.n)] if dc2 >= 2 else []))
            w_udq = sVE.n
            st("scalar", lambda e, th=th: e.activation(
                out=sm2[:, th * 512:(th + 1) * 512], in_=rbc[:, th * 512:(th + 1) * 512],
                func=AF.Silu), waits=[(sVE, w_udq)])
            w_sil = sSE.n
            w_hv[(dc2, th)] = st("vector", lambda e, pst2=pst2, th=th, hsl=hsl: e.tensor_tensor(
                out=hsl, in0=sm2[:, th * 512:(th + 1) * 512], in1=pst2[:, 0:512],
                op=ALU.mult), waits=[(sSE, w_sil)])
            st("sync", lambda e, dc2=dc2, th=th, hsl=hsl: e.dma_start(
                out=hs_d[dc2, :, th * 512:(th + 1) * 512], in_=hsl),
               waits=[(sVE, w_hv[(dc2, th)])], inc=sHSp, inck=16)
            w_hspill[(dc2, th)] = sHSp.n
            w_sq3 = st("scalar", lambda e, hsl=hsl: e.activation(
                out=sqb2[:, 0:512], in_=hsl, func=AF.Square),
                waits=[(sVE, w_hv[(dc2, th)]),
                       (sMM, w_f1mm.get(("st", dc2, th - 1),
                                        w_f1mm.get(("st", dc2 - 1, 1), 0))),
                       (sVE, w_hv.get(("mx", dc2, th - 1),
                                      w_hv.get(("mx", dc2 - 1, 1), 0)))])
            ps_sq = psG if th == 0 else psH
            st("tensor", lambda e, dc2=dc2, th=th, ps_sq=ps_sq: e.matmul(
                ps_sq[:, 0:512], onesb[:, :], sqb2[:, 0:512],
                start=(dc2 == 0), stop=(dc2 == NFC - 1)),
               waits=[(sSE, w_sq3)], inc=sMM)
            w_f1mm[("st", dc2, th)] = sMM.n
            w_hv[("mx", dc2, th)] = st("vector", lambda e, th=th: e.tensor_tensor(
                out=maxac[:, th * 512:(th + 1) * 512],
                in0=maxac[:, th * 512:(th + 1) * 512], in1=sqb2[:, 0:512], op=ALU.max),
                waits=[(sSE, w_sq3)] + ([(sVE, w_maxi3)] if dc2 == 0 else []))
    if "d_hp" in DBG:
        st("gpsimd", lambda e: e.dma_start(out=DBG["d_hp"][:, 0:1024],
                                           in_=hs_d[0, :, :]),
           waits=[(sHS[0], w_hspill[(0, 0)]), (sHS[1], w_hspill[(0, 1)])],
           inc=sDBG2, inck=16)

    st("vector", lambda e: e.tensor_scalar_mul(out=rows[0:1, 1024:1536],
                                               in0=psG[0:1, 0:512], scalar1=1.0 / DFF),
       waits=[(sMM, w_f1mm[("st", NFC - 1, 0)]), (sMM, w_f1mm[("st", NFC - 1, 1)]),
              (sDBG, sDBG.n)])
    st("vector", lambda e: e.tensor_scalar_mul(out=rows[0:1, 1536:2048],
                                               in0=psH[0:1, 0:512], scalar1=1.0 / DFF))
    st("vector", lambda e: e.tensor_tensor(out=rr2, in0=rr0, in1=rr0, op=ALU.mult))
    st("vector", lambda e: e.tensor_tensor(out=rr1, in0=rr1, in1=rr2, op=ALU.mult))
    w_e3 = sVE.n
    st("scalar", lambda e: e.activation(out=rr2, in_=rr1, func=AF.Sqrt,
                                        bias=consts[0:1, 3:4]), waits=[(sVE, w_e3)])
    w_s5 = sSE.n
    st("vector", lambda e: e.reciprocal(out=rr1, in_=rr2), waits=[(sSE, w_s5)])
    st("vector", lambda e: e.tensor_copy(out=tmpb[0:64, :], in_=maxac[64:128, :]),
       waits=[(sVE, w_hv[("mx", NFC - 1, 1)])])
    st("vector", lambda e: e.tensor_tensor(out=maxac[0:64, :], in0=maxac[0:64, :],
                                           in1=tmpb[0:64, :], op=ALU.max))
    st("vector", lambda e: e.tensor_copy(out=tmpb[0:32, :], in_=maxac[32:64, :]))
    wsh = st("vector", lambda e: e.tensor_tensor(out=maxac[0:32, :], in0=maxac[0:32, :],
                                                 in1=tmpb[0:32, :], op=ALU.max))
    for npart in (16, 8, 4, 2, 1):
        st("sync", lambda e, npart=npart: e.dma_start(
            out=tmpb[0:npart, :], in_=maxac[npart:2 * npart, :]),
           waits=[(sVE, wsh)], inc=sB[4], inck=16)
        wdma = sB[4].n
        wsh = st("vector", lambda e, npart=npart: e.tensor_tensor(
            out=maxac[0:npart, :], in0=maxac[0:npart, :], in1=tmpb[0:npart, :],
            op=ALU.max), waits=[(sB[4], wdma)])
    st("scalar", lambda e: e.activation(out=rr2, in_=maxac[0:1, 0:1024], func=AF.Sqrt),
       waits=[(sVE, wsh)])
    w_s6 = sSE.n
    st("vector", lambda e: e.tensor_tensor(out=rr2, in0=rr2, in1=rr0, op=ALU.mult),
       waits=[(sSE, w_s6)])
    st("vector", lambda e: e.tensor_tensor(out=rr2, in0=rr2, in1=rr1, op=ALU.mult))
    st("vector", lambda e: e.tensor_scalar_max(out=rr2, in0=rr2, scalar1=1e-5))
    st("vector", lambda e: e.tensor_tensor(out=rr1, in0=rr1, in1=rr0, op=ALU.mult))
    st("vector", lambda e: e.reciprocal(out=rr0, in_=rr2))
    st("vector", lambda e: e.tensor_tensor(out=rr1, in0=rr1, in1=rr0, op=ALU.mult))
    w_cq3 = st("vector", lambda e: e.tensor_scalar_mul(out=rr1, in0=rr1, scalar1=127.0))
    st("vector", lambda e: e.tensor_scalar(out=rr2, in0=rr2, scalar1=1.0 / 127.0,
                                           scalar2=wsc[0:1, 5:6],
                                           op0=ALU.mult, op1=ALU.mult))
    w_do3 = sVE.n
    wbC3w, wbC3 = bounce(2, rr1, 1024, 20, cq3b[:, 0:1024],
                         waits=[(sVE, w_cq3), wbCQ])
    wbD3w, wbD3 = bounce(1, rr2, 1024, 21, do3b[:, 0:1024],
                         waits=[(sVE, w_do3), wbB2])
    w_negt3 = st("vector", lambda e: e.tensor_scalar_mul(out=wsc[:, 6:7],
                                                         in0=wsc[:, 2:3], scalar1=-1.0))

    w_f3 = {}
    w_q3sp = {}
    w_outev = {}
    w_x3st = []
    for dgq in range(4):
        douts = list(range(dgq * 4, (dgq + 1) * 4))
        for k3 in range(NFC):
            qsl = q3c[:, (k3 % 2) * 1024:((k3 % 2) + 1) * 1024]
            sHRp = sHR[k3 % 2]
            if dgq == 0:
                hsl2 = hch[:, (k3 % 2) * 1024:((k3 % 2) + 1) * 1024]
                st("sync", lambda e, k3=k3, hsl2=hsl2: e.dma_start(
                    out=hsl2, in_=hs_d[k3, :, :]),
                   waits=[(sVE, w_f3.get(("lad", k3 - 2), 0)),
                          (sHS[0], sHS[0].n), (sHS[1], sHS[1].n),
                          (sHS[2], sHS[2].n), (sHS[3], sHS[3].n)]
                   if k3 < 2 else [(sVE, w_f3.get(("lad", k3 - 2), 0))],
                   inc=sHRp, inck=16)
                w_hr = sHRp.n
                for th in range(2):
                    ta = rbc[:, th * 512:(th + 1) * 512]
                    tb = sm2[:, th * 512:(th + 1) * 512]
                    bb = tmpb[:, th * 512:(th + 1) * 512]
                    sq_ = hsl2[:, th * 512:(th + 1) * 512]
                    dq_ = qsl[:, th * 512:(th + 1) * 512]
                    st("vector", lambda e, sq_=sq_, ta=ta, th=th: e.tensor_tensor(
                        out=ta, in0=sq_, in1=cq3b[:, th * 512:(th + 1) * 512],
                        op=ALU.mult),
                       waits=[(sHRp, w_hr)] + ([wbC3, wbD3] if k3 == 0 and th == 0
                                               else []) +
                             ([(sMM, w_f3.get(("mm", k3 - 2), 0)),
                               (sQS[k3 % 2], w_q3sp.get(k3 - 2, 0))]
                              if th == 0 else []))
                    st("vector", lambda e, ta=ta, tb=tb: e.tensor_scalar_add(
                        out=tb, in0=ta, scalar1=128.5))
                    for kk in (128.0, 64.0, 32.0, 16.0, 8.0, 4.0, 2.0, 1.0):
                        st("vector", lambda e, tb=tb, bb=bb, kk=kk: e.tensor_scalar(
                            out=bb, in0=tb, scalar1=kk, scalar2=kk,
                            op0=ALU.is_ge, op1=ALU.mult))
                        st("vector", lambda e, tb=tb, bb=bb: e.tensor_tensor(
                            out=tb, in0=tb, in1=bb, op=ALU.subtract))
                    st("vector", lambda e, ta=ta: e.tensor_scalar_add(
                        out=ta, in0=ta, scalar1=128.5))
                    st("vector", lambda e, ta=ta, tb=tb: e.tensor_tensor(
                        out=ta, in0=ta, in1=tb, op=ALU.subtract))
                    w_f3[("lad", k3)] = st("vector", lambda e, ta=ta, dq_=dq_:
                                           e.tensor_scalar_add(out=dq_, in0=ta,
                                                               scalar1=-128.0))
                sQSp = sQS[k3 % 2]
                st("sync", lambda e, k3=k3, qsl=qsl: e.dma_start(
                    out=q3s_d[k3, :, :], in_=qsl),
                   waits=[(sVE, w_f3[("lad", k3)])], inc=sQSp, inck=16)
                w_q3sp[k3] = sQSp.n
                w_qrdy = ("v", w_f3[("lad", k3)])
            else:
                st("sync", lambda e, k3=k3, qsl=qsl: e.dma_start(
                    out=qsl, in_=q3s_d[k3, :, :]),
                   waits=([(sQS[0], sQS[0].n), (sQS[1], sQS[1].n)] if
                          (dgq == 1 and k3 < 2) else []) +
                         [(sMM, w_f3.get(("mm", k3 - 2), 0))],
                   inc=sHRp, inck=16)
                w_qrdy = ("d", sHRp.n, sHRp)
            sW = sW0 if k3 % 2 == 0 else sW1
            wslc = wfl if k3 % 2 == 0 else wfl2
            st("sync", lambda e, dgq=dgq, k3=k3, wslc=wslc: e.dma_start(
                out=wslc[:, 0:512], in_=w3_d[dgq, k3]),
               waits=[(sVE, w_f3.get(("tern", k3 - 2), 0))], inc=sW, inck=16)
            w_w3 = sW.n
            tsl = ternA[:, (k3 % 2) * 512:((k3 % 2) + 1) * 512]
            st("vector", lambda e, wslc=wslc: e.tensor_scalar(
                out=terntmp[:, 0:512], in0=wslc[:, 0:512], scalar1=wsc[:, 2:3],
                scalar2=-1.0, op0=ALU.is_ge, op1=ALU.add),
               waits=[(sW, w_w3), (sMM, w_f3.get(("mm", k3 - 2), 0))] +
                     ([(sVE, w_negt3)] if dgq == 0 and k3 == 0 else []))
            w_f3[("tern", k3)] = st("vector", lambda e, wslc=wslc, tsl=tsl:
                                    e.scalar_tensor_tensor(
                out=tsl, in0=wslc[:, 0:512], scalar=wsc[:, 6:7],
                in1=terntmp[:, 0:512], op0=ALU.is_gt, op1=ALU.add))
            for di, dout in enumerate(douts):
                for th in range(2):
                    psx = pss[di * 2 + th]
                    ww = []
                    if k3 == 0:
                        ww = [(sVE, w_f3[("tern", k3)])]
                        if w_qrdy[0] == "v":
                            ww.append((sVE, w_qrdy[1]))
                        else:
                            ww.append((w_qrdy[2], w_qrdy[1]))
                        if dgq > 0 and di == 0 and th == 0:
                            ww.append((sSE, w_outev.get(dgq - 1, 0)))
                    elif di == 0 and th == 0:
                        ww = [(sVE, w_f3[("tern", k3)])]
                        if w_qrdy[0] == "v":
                            ww.append((sVE, w_qrdy[1]))
                        else:
                            ww.append((w_qrdy[2], w_qrdy[1]))
                    st("tensor", lambda e, di=di, th=th, psx=psx, qsl=qsl, k3=k3,
                       tsl=tsl: e.matmul(
                           psx[:, 0:512], tsl[:, di * 128:(di + 1) * 128],
                           qsl[:, th * 512:(th + 1) * 512],
                           start=(k3 == 0), stop=(k3 == NFC - 1)),
                       waits=ww, inc=(sMM if (di == 3 and th == 1) else None))
            w_f3[("mm", k3)] = sMM.n
        for di, dout in enumerate(douts):
            sXO = sXO0 if dout % 2 == 0 else sXO1
            xsl = xst[:, (dout % 2) * 1024:(dout % 2 + 1) * 1024]
            st("sync", lambda e, dout=dout, xsl=xsl: e.dma_start(
                out=xsl, in_=x2s_d[dout * 128:(dout + 1) * 128, :]),
               waits=[(sVE, w_f3.get(("x3", dout - 2), 0))] +
                     ([w_f3[("ost", dout - 2)]] if ("ost", dout - 2) in w_f3 else []),
               inc=sXO, inck=16)
            w_xo = sXO.n
            for th in range(2):
                psx = pss[di * 2 + th]
                w_ev = st("scalar", lambda e, psx=psx, dout=dout: e.activation(
                    out=kraw[:, 0:512], in_=psx[:, 0:512], func=AF.Identity,
                    scale=modv[:, 80 + dout:81 + dout]),
                    waits=[(sMM, w_f3[("mm", NFC - 1)])] +
                          ([(sVE, w_f3.get(("x3v", dout, th - 1),
                                           w_f3.get(("x3v", dout - 1, 1), 0)))]))
                st("vector", lambda e, th=th: e.tensor_tensor(
                    out=kraw[:, 0:512], in0=kraw[:, 0:512],
                    in1=do3b[:, th * 512:(th + 1) * 512], op=ALU.mult),
                   waits=[(sSE, w_ev)])
                w_f3[("x3v", dout, th)] = st("vector", lambda e, th=th, xsl=xsl:
                                             e.tensor_tensor(
                    out=xsl[:, th * 512:(th + 1) * 512],
                    in0=xsl[:, th * 512:(th + 1) * 512],
                    in1=kraw[:, 0:512], op=ALU.add), waits=[(sXO, w_xo)])
            w_f3[("x3", dout)] = w_f3[("x3v", dout, 1)]
            sOUTp = sOUT if dout % 2 == 0 else sOUT1
            st("sync", lambda e, dout=dout, xsl=xsl: e.dma_start(
                out=out_d[dout * 128:(dout + 1) * 128, :], in_=xsl),
               waits=[(sVE, w_f3[("x3", dout)])], inc=sOUTp, inck=16)
            w_f3[("ost", dout)] = (sOUTp, sOUTp.n)
        w_outev[dgq] = sSE.n
    if "d_q3" in DBG:
        st("gpsimd", lambda e: e.dma_start(out=DBG["d_q3"][:, 0:1024],
                                           in_=q3s_d[0, :, :]),
           waits=[(sQS[0], w_q3sp.get(0, 0))], inc=sDBG2, inck=16)

    st("sync", lambda e: e.nop(),
       waits=[(sOUT, sOUT.n), (sOUT1, sOUT1.n), (sDBG, sDBG.n), (sDBG2, sDBG2.n), (sDBG3, sDBG3.n), (sDBG4, sDBG4.n)])
    K.DBG = DBG
    return K


def _host_prep(inputs, b, half):
    x = inputs["L0_attn_x"][b]
    own = half * OWN
    pos = (np.arange(T) + own) % T
    xfm = np.ascontiguousarray(x[pos].T).astype(np.float32)
    c = inputs["L0_c"][b]
    cvec = np.ascontiguousarray(np.asarray(c).reshape(NKC, 128).T).astype(np.float32)
    return {"xc": xfm, "cvec": cvec, "pos": pos.astype(np.float32).reshape(1, T)}


def _til(w, no, ni):
    return np.ascontiguousarray(
        np.asarray(w).reshape(no // 128, 128, ni // 128, 128).transpose(0, 3, 2, 1)
    ).astype(np.float32)


def _host_weights(inputs):
    wv = np.asarray(inputs["L0_attn_wv"])
    w3 = np.asarray(inputs["L0_ffn_w3"])
    wmod = np.asarray(inputs["L0_wmod"])
    return {
        "wq": _til(inputs["L0_attn_wq"], D, D),
        "wk": _til(inputs["L0_attn_wk"], D, D),
        "wo": _til(inputs["L0_attn_wo"], D, D),
        "wv": np.ascontiguousarray(
            wv.T.reshape(NKC, 128, 8, 256).transpose(2, 1, 0, 3)).astype(np.float32),
        "w1": _til(inputs["L0_ffn_w1"], DFF, D),
        "w2": _til(inputs["L0_ffn_w2"], DFF, D),
        "w3": np.ascontiguousarray(
            w3.reshape(NKC, 128, NFC, 128).transpose(2, 3, 0, 1)
            .reshape(NFC, 128, NKC * 128)
            .reshape(NFC, 128, 4, 4 * 128).transpose(2, 0, 1, 3)).astype(np.float32),
        "wmod": np.ascontiguousarray(
            wmod.reshape(NKC, 128, 24, 512).transpose(2, 1, 0, 3)).astype(np.float32),
    }


def kernel(**inputs):
    nc = bass.Bass("TRN2", target_bir_lowering=False)
    K = build(nc)
    K.emit()
    w = _host_weights(inputs)
    in_maps = []
    for core in range(8):
        b, half = core // 2, core % 2
        m = dict(w)
        m.update(_host_prep(inputs, b, half))
        in_maps.append(m)
    res = run_bass_kernel_spmd(nc, in_maps, core_ids=list(range(8)))
    out = np.zeros((B, T, D), np.float32)
    for core in range(8):
        b, half = core // 2, core % 2
        own = half * OWN
        out[b, own:own + OWN, :] = res.results[core]["out"].T
    return out


# version 05
# speedup vs baseline: 1.0907x; 1.0907x over previous
import math
import numpy as np
import concourse.bass as bass
import concourse.mybir as mybir
from concourse.bass_utils import run_bass_kernel_spmd

F32 = mybir.dt.float32
BF16 = mybir.dt.bfloat16
AF = mybir.ActivationFunctionType
ALU = mybir.AluOpType
AX = mybir.AxisListType

B, T, D, NH, HD = 4, 2048, 2048, 16, 128
DFF = 8192
NKC = D // 128
NFC = DFF // 128
OWN = 1024
SHIFT = 12.0
SCALE_S = 1.0 / math.sqrt(HD)

TWO_PI = 2.0 * math.pi
CW1 = float(np.float32(TWO_PI))
CW2 = float(np.float32(np.float64(TWO_PI) - np.float64(np.float32(TWO_PI))))
CW3 = float(np.float64(TWO_PI) - np.float64(np.float32(TWO_PI)) - np.float64(CW2))
R2PI = float(np.float32(1.0 / TWO_PI))
PI_HALF = float(np.float32(math.pi / 2))
PI_HALF_LO = float(np.float64(math.pi / 2) - np.float64(np.float32(math.pi / 2)))
PI_F = float(np.float32(math.pi))

KBY = 1024


class Sem:
    def __init__(self, handle):
        self.h = handle
        self.n = 0

    def inc(self, k=1):
        self.n += k
        return self.n


class Bld:
    def __init__(self, nc):
        self.nc = nc
        self.q = {k: [] for k in ("sync", "gpsimd", "vector", "scalar", "tensor")}
        self._ctxs = []
        self.sems = {}
        self.serial = {}
        self.base = (nc.sbuf_base + 31) & ~31

    def sem(self, name):
        if name not in self.sems:
            cm = self.nc.semaphore(name)
            self.sems[name] = Sem(cm.__enter__())
            self._ctxs.append(cm)
        return self.sems[name]

    def sb(self, name, shape, dtype, off_b):
        t = self.nc.alloc_sbuf_tensor_at(name, shape, dtype, offset=self.base + off_b)
        return t.ap()

    def st(self, eng, fn, waits=(), inc=None, inck=1):
        ser = self.serial.get(eng)
        if ser is not None:
            waits = tuple(waits) + ((ser, ser.n),)
            inc, inck = ser, 1
        waits = tuple((s.h, v) for (s, v) in waits if v > 0)
        if inc is not None:
            inc.inc(inck)

        def run(e, fn=fn, waits=waits, inc=inc, inck=inck):
            for h, v in waits:
                e.wait_ge(h, v)
            r = fn(e)
            if inc is not None:
                r.then_inc(inc.h, inck)
        self.q[eng].append(run)
        return inc.n if inc is not None else None

    def emit(self):
        with self.nc.Block() as block:
            block.sync(lambda e: [f(e) for f in self.q["sync"]])
            block.gpsimd(lambda e: [f(e) for f in self.q["gpsimd"]])
            block.vector(lambda e: [f(e) for f in self.q["vector"]])
            block.scalar(lambda e: [f(e) for f in self.q["scalar"]])
            block.tensor(lambda e: [f(e) for f in self.q["tensor"]])
        for cm in reversed(self._ctxs):
            cm.__exit__(None, None, None)


def build(nc, dbg=(), phases=99, e_head=0):
    dbg = set(dbg)
    K = Bld(nc)
    st = K.st
    KB = lambda x: int(x * KBY)

    xc = nc.dram_tensor("xc", [D, T], F32, kind="ExternalInput").ap()
    cvec_d = nc.dram_tensor("cvec", [128, NKC], F32, kind="ExternalInput").ap()
    pos_d = nc.dram_tensor("pos", [1, T], F32, kind="ExternalInput").ap()
    wmod_d = nc.dram_tensor("wmod", [24, 128, NKC, 512], F32, kind="ExternalInput").ap()
    wq_d = nc.dram_tensor("wq", [NH, 128, NKC, 128], F32, kind="ExternalInput").ap()
    wk_d = nc.dram_tensor("wk", [NH, 128, NKC, 128], F32, kind="ExternalInput").ap()
    wv_d = nc.dram_tensor("wv", [8, 128, NKC, 256], F32, kind="ExternalInput").ap()
    wo_d = nc.dram_tensor("wo", [NKC, 128, NKC, 128], F32, kind="ExternalInput").ap()
    w1_d = nc.dram_tensor("w1", [NFC, 128, NKC, 128], F32, kind="ExternalInput").ap()
    w2_d = nc.dram_tensor("w2", [NFC, 128, NKC, 128], F32, kind="ExternalInput").ap()
    w3_d = nc.dram_tensor("w3", [4, NFC, 128, 4 * 128], F32, kind="ExternalInput").ap()
    out_d = nc.dram_tensor("out", [D, OWN], F32, kind="ExternalOutput").ap()
    bnc_d = nc.dram_tensor("bnc", [64, 2048], F32, kind="Internal").ap()
    x2s_d = nc.dram_tensor("x2s", [D, OWN], F32, kind="Internal").ap()
    hs_d = nc.dram_tensor("hs", [NFC, 128, OWN], BF16, kind="Internal").ap()
    q3s_d = nc.dram_tensor("q3s", [NFC, 128, OWN], BF16, kind="Internal").ap()

    idm4_d = nc.inline_tensor(
        np.ascontiguousarray(np.tile(np.eye(128, dtype=np.float32), (1, 4))), "idm4").ap()
    irow_d = nc.inline_tensor(np.arange(64, dtype=np.float32).reshape(64, 1), "irow").ap()

    DBG = {}

    def dbg_t(name, shape):
        if name in dbg:
            DBG[name] = nc.dram_tensor(name, shape, F32, kind="ExternalOutput").ap()
            return True
        return False

    rows = K.sb("rows", [1, 3072], F32, KB(0))
    rbc = K.sb("rbc", [128, 1024], F32, KB(12))
    modv = K.sb("modv", [128, 128], F32, KB(16))
    consts = K.sb("consts", [128, 8], F32, KB(16.5))
    accw = K.sb("accw", [128, 132], F32, KB(16.6875))
    wsc = K.sb("wsc", [128, 8], F32, KB(17.21875))
    onesb = K.sb("onesb", [128, 128], BF16, KB(17.25))
    sqb2 = K.sb("sqb2", [128, 512], BF16, KB(17.5))
    onesf128 = K.sb("onesf128", [128, 128], F32, KB(18.5))
    xst = K.sb("xst", [128, 2048], F32, KB(19))
    accm = K.sb("accm", [128, 1024], F32, KB(19))
    wfl = K.sb("wfl", [128, 2048], F32, KB(27))
    wfl2 = K.sb("wfl2", [128, 2048], F32, KB(35))
    wsl = K.sb("wsl", [128, 4096], BF16, KB(35))
    ternb = K.sb("ternb", [128, 4096], BF16, KB(35))
    xbb = K.sb("xbb", [128, 1024], BF16, KB(43))
    h_sb = K.sb("h_sb", [128, NKC * T], BF16, KB(45))
    q12 = K.sb("q12", [128, NKC * OWN], BF16, KB(45))
    ysb = K.sb("ysb", [128, NKC * OWN], BF16, KB(111))
    h2sb = K.sb("h2sb", [128, NKC * OWN], BF16, KB(111))
    du1b = K.sb("du1b", [128, 1024], F32, KB(77))
    tmpb = K.sb("tmpb", [128, 1024], F32, KB(81))
    maxac = K.sb("maxac", [128, 1024], F32, KB(85))
    cq3b = K.sb("cq3b", [128, 1024], F32, KB(89))
    do3b = K.sb("do3b", [128, 1024], F32, KB(93))
    terntmp = K.sb("terntmp", [128, 2048], F32, KB(97))
    ternA = K.sb("ternA", [128, 2048], BF16, KB(105))
    ternB = K.sb("ternB", [128, 2048], BF16, KB(109))
    hch = K.sb("hch", [128, 2048], BF16, KB(113))
    q3c = K.sb("q3c", [128, 2048], BF16, KB(117))
    esb = K.sb("esb", [128, NKC * 512], BF16, KB(143))
    vsb = K.sb("vsb", [128, 2 * NKC * 256], BF16, KB(159))
    ksb = K.sb("ksb", [128, 2 * T], BF16, KB(175))
    qsb = K.sb("qsb", [128, 2 * OWN], BF16, KB(183))
    cosT = K.sb("cosT", [64, T], BF16, KB(187))
    ssinT = K.sb("ssinT", [64, T], BF16, KB(191))
    kraw = K.sb("kraw", [128, 512], F32, KB(195))
    krot = K.sb("krot", [64, 512], F32, KB(197))
    tab = K.sb("tab", [64, 8], F32, KB(198))
    lrec = K.sb("lrec", [16, 1024], F32, KB(198.5))
    sm2 = K.sb("sm2", [128, 1024], F32, KB(202.5))
    smr = K.sb("smr", [1, 2048], F32, KB(198.5))
    mring = K.sb("mring", [128, 2 * NKC * 512], BF16, KB(111))
    smat = K.sb("smat", [128, NKC * 128], BF16, KB(143))
    idn4 = K.sb("idn4", [128, 512], F32, KB(147))
    t_ang = K.sb("t_ang", [64, 2048], F32, KB(149.25))
    t_u = K.sb("t_u", [64, 2048], F32, KB(157.25))
    t_r = K.sb("t_r", [64, 2048], F32, KB(165.25))
    t_b = K.sb("t_b", [64, 2048], F32, KB(19))
    wsl2 = K.sb("wsl2", [128, 4096], BF16, KB(19))
    assert K.base + KB(206.5) <= nc.sbuf_top

    pss = [nc.alloc_psum_tensor("ps%d" % i, [128, 512], F32).ap() for i in range(8)]
    psA, psB, psC, psD, psE, psF, psG, psH = pss

    sLD = K.sem("sLD")
    sXL = K.sem("sXL")
    sM0 = K.sem("sM0")
    sM1 = K.sem("sM1")
    sB = [K.sem("sB%d" % i) for i in range(6)]
    sVE = K.sem("sVE")
    sSE = K.sem("sSE")
    sMM = K.sem("sMM")
    sEX = K.sem("sEX")
    sEV = K.sem("sEV")
    sDBG = K.sem("sDBG")
    sDBG2 = K.sem("sDBG2")
    sDBG3 = K.sem("sDBG3")
    sDBG4 = K.sem("sDBG4")
    sDBG5 = K.sem("sDBG5")
    sOUT = K.sem("sOUT")
    sOUT1 = K.sem("sOUT1")
    sA0 = K.sem("sA0")
    sA1 = K.sem("sA1")
    sAC = K.sem("sAC")
    sW0 = K.sem("sW0")
    sW1 = K.sem("sW1")
    sW2 = K.sem("sW2")
    sW3 = K.sem("sW3")
    sLB = K.sem("sLB")
    sLR = K.sem("sLR")
    sRB0 = K.sem("sRB0")
    sRB1 = K.sem("sRB1")
    sXO0 = K.sem("sXO0")
    sXO1 = K.sem("sXO1")
    sXS0 = K.sem("sXS0")
    sXS1 = K.sem("sXS1")
    sHS = [K.sem("sHS%d" % i) for i in range(4)]
    sHR = [K.sem("sHR%d" % i) for i in range(2)]
    sQS = [K.sem("sQS%d" % i) for i in range(2)]
    K.serial["vector"] = sVE
    K.serial["scalar"] = sSE

    def bounce(pair, row_ap, n, bnc_row, tile_ap, nparts=128, waits=()):
        sw, sr = sB[2 * pair], sB[2 * pair + 1]
        st("sync", lambda e: e.dma_start(out=bnc_d[bnc_row:bnc_row + 1, 0:n], in_=row_ap),
           waits=waits, inc=sw, inck=16)
        w = sw.n
        st("sync", lambda e: e.dma_start(
            out=tile_ap, in_=bnc_d[bnc_row:bnc_row + 1, 0:n].broadcast_to((nparts, n))),
           waits=[(sw, w)], inc=sr, inck=16)
        return (sw, w), (sr, sr.n)

    st("sync", lambda e: e.dma_start(out=modv[:, 96:112], in_=cvec_d[:, :]), inc=sLD, inck=16)
    st("sync", lambda e: e.dma_start(out=idn4[:, :], in_=idm4_d[:, :]), inc=sLD, inck=16)
    st("sync", lambda e: e.dma_start(out=tab[:, 0:1], in_=irow_d[:, :]), inc=sLD, inck=16)
    st("sync", lambda e: e.dma_start(
        out=t_ang[:, :], in_=pos_d[0:1, :].broadcast_to((64, T))), inc=sLD, inck=16)
    w_cvec = w_idn = w_irow = w_pos = sLD.n

    def _consts(e):
        e.memset(consts[:, 0:1], -SHIFT)
        e.memset(consts[:, 1:2], 1.0)
        e.memset(consts[:, 2:3], 1e-5)
        e.memset(wsc[:, :], 0.0)
        e.memset(consts[:, 3:4], 1e-6)
        e.memset(tab[:, 1:2], 10000.0)
        return e.memset(onesb[:, :], 1.0)
    st("vector", _consts)
    w_consts = sVE.n

    for kc in range(NKC):
        st("gpsimd", lambda e, kc=kc: e.dma_start(
            out=h_sb[:, kc * T:(kc + 1) * T], in_=xc[kc * 128:(kc + 1) * 128, :]),
           inc=sXL, inck=16)
    w_xall = sXL.n

    st("scalar", lambda e: e.activation(out=tab[:, 2:3], in_=tab[:, 1:2], func=AF.Ln),
       waits=[(sVE, w_consts)])
    st("scalar", lambda e: e.activation(out=tab[:, 3:4], in_=tab[:, 0:1], func=AF.Identity,
                                        scale=tab[:, 2:3]),
       waits=[(sLD, w_irow), (sSE, sSE.n)])
    st("scalar", lambda e: e.activation(out=tab[:, 4:5], in_=tab[:, 3:4], func=AF.Exp,
                                        scale=-1.0 / 64.0), waits=[(sSE, sSE.n)])
    w_invf = sSE.n
    st("vector", lambda e: e.tensor_scalar_mul(out=t_ang[:, :], in0=t_ang[:, :],
                                               scalar1=tab[:, 4:5]),
       waits=[(sLD, w_pos), (sSE, w_invf)])
    st("vector", lambda e: e.tensor_scalar(out=t_u[:, :], in0=t_ang[:, :],
                                           scalar1=R2PI, scalar2=0.5,
                                           op0=ALU.mult, op1=ALU.add))
    st("vector", lambda e: e.tensor_copy(out=t_r[:, :], in_=t_u[:, :]))
    for k in [256.0, 128.0, 64.0, 32.0, 16.0, 8.0, 4.0, 2.0, 1.0]:
        st("vector", lambda e, k=k: e.tensor_scalar(out=t_b[0:64, :], in0=t_r[:, :],
                                                    scalar1=k, scalar2=k,
                                                    op0=ALU.is_ge, op1=ALU.mult))
        st("vector", lambda e: e.tensor_tensor(out=t_r[:, :], in0=t_r[:, :],
                                               in1=t_b[0:64, :], op=ALU.subtract))
    st("vector", lambda e: e.tensor_tensor(out=t_u[:, :], in0=t_u[:, :], in1=t_r[:, :],
                                           op=ALU.subtract))
    st("vector", lambda e: e.scalar_tensor_tensor(out=t_r[:, :], in0=t_u[:, :],
                                                  scalar=-CW1, in1=t_ang[:, :],
                                                  op0=ALU.mult, op1=ALU.add))
    st("vector", lambda e: e.scalar_tensor_tensor(out=t_r[:, :], in0=t_u[:, :],
                                                  scalar=-CW2, in1=t_r[:, :],
                                                  op0=ALU.mult, op1=ALU.add))
    st("vector", lambda e: e.scalar_tensor_tensor(out=t_r[:, :], in0=t_u[:, :],
                                                  scalar=-CW3, in1=t_r[:, :],
                                                  op0=ALU.mult, op1=ALU.add))
    st("vector", lambda e: e.tensor_scalar(out=t_b[0:64, :], in0=t_r[:, :],
                                           scalar1=PI_F, scalar2=1.0,
                                           op0=ALU.is_gt, op1=ALU.mult))
    st("vector", lambda e: e.scalar_tensor_tensor(
        out=t_r[:, :], in0=t_b[0:64, :], scalar=-CW1, in1=t_r[:, :],
        op0=ALU.mult, op1=ALU.add))
    st("vector", lambda e: e.scalar_tensor_tensor(
        out=t_b[0:64, :], in0=t_b[0:64, :], scalar=-CW2, in1=t_r[:, :],
        op0=ALU.mult, op1=ALU.add))
    w_sinin = sVE.n
    st("scalar", lambda e: e.activation(out=ssinT[:, :], in_=t_b[0:64, :], func=AF.Sin),
       waits=[(sVE, w_sinin)])
    w_sin = sSE.n
    st("vector", lambda e: e.tensor_scalar_mul(out=ssinT[0:32, :], in0=ssinT[0:32, :],
                                               scalar1=-1.0), waits=[(sSE, w_sin)])
    w_ssin = sVE.n
    st("vector", lambda e: e.tensor_scalar_add(out=t_u[:, :], in0=t_r[:, :],
                                               scalar1=PI_HALF), waits=[(sSE, w_sin)])
    st("vector", lambda e: e.tensor_scalar_add(out=t_u[:, :], in0=t_u[:, :],
                                               scalar1=PI_HALF_LO))
    st("vector", lambda e: e.tensor_scalar(out=t_b[0:64, :], in0=t_u[:, :],
                                           scalar1=PI_F, scalar2=1.0,
                                           op0=ALU.is_gt, op1=ALU.mult))
    st("vector", lambda e: e.scalar_tensor_tensor(
        out=t_u[:, :], in0=t_b[0:64, :], scalar=-CW1, in1=t_u[:, :],
        op0=ALU.mult, op1=ALU.add))
    st("vector", lambda e: e.scalar_tensor_tensor(
        out=t_b[0:64, :], in0=t_b[0:64, :], scalar=-CW2, in1=t_u[:, :],
        op0=ALU.mult, op1=ALU.add))
    w_cosin = sVE.n
    st("scalar", lambda e: e.activation(out=cosT[:, :], in_=t_b[0:64, :], func=AF.Sin),
       waits=[(sVE, w_cosin)])
    w_cos = sSE.n
    if dbg_t("d_cs", [64, 2 * T]):
        st("gpsimd", lambda e: e.dma_start(out=DBG["d_cs"][:, 0:T], in_=cosT[:, :]),
           waits=[(sSE, w_cos)], inc=sDBG2, inck=16)
        st("gpsimd", lambda e: e.dma_start(out=DBG["d_cs"][:, T:2 * T], in_=ssinT[:, :]),
           waits=[(sVE, w_ssin)], inc=sDBG2, inck=16)

    st("scalar", lambda e: e.activation(out=modv[:, 112:128], in_=modv[:, 96:112],
                                        func=AF.Silu), waits=[(sLD, w_cvec)])
    w_silu = sSE.n
    for kc in range(NKC):
        st("vector", lambda e, kc=kc: e.tensor_scalar_mul(
            out=smat[:, kc * 128:(kc + 1) * 128], in0=onesb[:, :],
            scalar1=modv[:, 112 + kc:113 + kc]),
           waits=[(sSE, w_silu), (sVE, w_consts)] if kc == 0 else [],
           inc=(sVE if kc == NKC - 1 else None))
    w_smat = sVE.n

    mod_mm_base = sMM.n
    for j in range(24):
        st("gpsimd", lambda e, j=j: e.dma_start(
            out=mring[:, (j % 2) * NKC * 512:((j % 2) + 1) * NKC * 512], in_=wmod_d[j]),
           waits=[(sMM, mod_mm_base + j - 1)] if j >= 2 else [],
           inc=(sM0 if j % 2 == 0 else sM1), inck=16)
    w_modtt = 0
    for j in range(24):
        wld = 16 * (j // 2 + 1)
        sMp = sM0 if j % 2 == 0 else sM1
        slot = mring[:, (j % 2) * NKC * 512:((j % 2) + 1) * NKC * 512]
        for kc in range(NKC):
            st("tensor", lambda e, kc=kc, slot=slot: e.matmul(
                psE[:, :], smat[:, kc * 128:(kc + 1) * 128],
                slot[:, kc * 512:(kc + 1) * 512],
                start=(kc == 0), stop=(kc == NKC - 1)),
               waits=(([(sMp, wld), (sVE, w_smat)] +
                       ([(sVE, w_modtt)] if j >= 1 else []))
                      if kc == 0 else []),
               inc=(sMM if kc == NKC - 1 else None))
        wmm = sMM.n
        w_modtt = st("vector", lambda e: e.tensor_tensor(out=sm2[:, 0:512], in0=psE[:, :],
                                                         in1=idn4[:, :], op=ALU.mult),
                     waits=[(sMM, wmm), (sLD, w_idn)])
        st("vector", lambda e, j=j: e.tensor_reduce(
            out=modv[:, j * 4:(j + 1) * 4],
            in_=sm2[:, 0:512].rearrange("p (b f) -> p b f", f=128),
            axis=AX.X, op=ALU.add))
    w_gmul = sVE.n
    for (cg, nm) in ((32, "ga"), (80, "gb")):
        st("scalar", lambda e, cg=cg: e.activation(out=modv[:, 112:128],
                                                   in_=modv[:, cg:cg + 16],
                                                   func=AF.Sign),
           waits=[(sVE, w_gmul)])
        w_ab_ = st("scalar", lambda e, cg=cg: e.activation(out=modv[:, cg:cg + 16],
                                                           in_=modv[:, cg:cg + 16],
                                                           func=AF.Abs))
        st("vector", lambda e, cg=cg: e.tensor_scalar_max(out=modv[:, cg:cg + 16],
                                                          in0=modv[:, cg:cg + 16],
                                                          scalar1=0.25),
           waits=[(sSE, w_ab_)])
        w_gmul = st("vector", lambda e, cg=cg: e.tensor_tensor(
            out=modv[:, cg:cg + 16], in0=modv[:, cg:cg + 16],
            in1=modv[:, 112:128], op=ALU.mult))
    st("vector", lambda e: e.tensor_scalar_add(out=modv[:, 16:32], in0=modv[:, 16:32],
                                               scalar1=1.0))
    st("vector", lambda e: e.tensor_scalar_add(out=modv[:, 64:80], in0=modv[:, 64:80],
                                               scalar1=1.0))
    w_mod = sVE.n
    if dbg_t("d_mod", [128, 128]):
        st("sync", lambda e: e.dma_start(out=DBG["d_mod"][:, :], in_=modv[:, :]),
           waits=[(sVE, w_mod)], inc=sDBG, inck=16)

    w_rows_evac = 0
    w_hmul_last = 0
    w_rowfree = []
    for tc in range(4):
        for kc in range(NKC):
            st("tensor", lambda e, tc=tc, kc=kc: e.matmul(
                psA[:, :], onesb[:, :],
                h_sb[:, kc * T + tc * 512: kc * T + (tc + 1) * 512],
                start=(kc == 0), stop=(kc == NKC - 1)),
               waits=(([(sXL, w_xall), (sVE, w_consts)] +
                       ([(sVE, w_rows_evac)] if tc > 0 else []))
                      if kc == 0 else []),
               inc=(sMM if kc == NKC - 1 else None))
        w_sum = sMM.n
        for kc in range(NKC):
            w_slot = sMM.n
            st("scalar", lambda e, tc=tc, kc=kc: e.activation(
                out=xbb[:, (kc % 2) * 512:(kc % 2 + 1) * 512],
                in_=h_sb[:, kc * T + tc * 512: kc * T + (tc + 1) * 512],
                func=AF.Square),
               waits=([(sMM, sMM.n - 1)] if kc >= 2 else []) +
                     ([(sXL, w_xall)] if (tc == 0 and kc == 0) else []))
            w_sq = sSE.n
            st("tensor", lambda e, kc=kc: e.matmul(
                psB[:, :], onesb[:, :], xbb[:, (kc % 2) * 512:(kc % 2 + 1) * 512],
                start=(kc == 0), stop=(kc == NKC - 1)),
               waits=[(sSE, w_sq)], inc=sMM)
        w_sq_mm = sMM.n
        r0, r1, r2 = rows[0:1, 0:512], rows[0:1, 1024:1536], rows[0:1, 2048:2560]
        st("vector", lambda e: e.tensor_scalar_mul(out=r0, in0=psA[0:1, :], scalar1=1.0 / D),
           waits=[(sMM, w_sq_mm)] + (w_rowfree if tc > 0 else []))
        w_rows_evac = st("vector", lambda e: e.tensor_scalar_mul(
            out=r1, in0=psB[0:1, :], scalar1=1.0 / D))
        st("vector", lambda e: e.tensor_tensor(out=r2, in0=r0, in1=r0, op=ALU.mult))
        st("vector", lambda e: e.tensor_tensor(out=r1, in0=r1, in1=r2, op=ALU.subtract),
           inc=sVE)
        w_var = sVE.n
        st("scalar", lambda e: e.activation(out=r2, in_=r1, func=AF.Sqrt, bias=consts[0:1, 2:3]),
           waits=[(sVE, w_var)])
        w_std = sSE.n
        st("vector", lambda e: e.reciprocal(out=r1, in_=r2), waits=[(sSE, w_std)])
        st("vector", lambda e: e.tensor_tensor(out=r0, in0=r0, in1=r1, op=ALU.mult))
        st("vector", lambda e: e.tensor_scalar_mul(out=r0, in0=r0, scalar1=-1.0))
        w_ab = sVE.n
        if tc == 0 and dbg_t("d_stats", [2, T]):
            st("sync", lambda e: e.dma_start(out=DBG["d_stats"][0:1, 0:512], in_=r1),
               waits=[(sVE, w_ab)], inc=sDBG, inck=16)
            st("sync", lambda e: e.dma_start(out=DBG["d_stats"][1:2, 0:512], in_=r0),
               waits=[(sVE, w_ab)], inc=sDBG, inck=16)
        wbAw, wbA = bounce(0, r1, 512, 16 + tc, rbc[:, 0:512],
                           waits=[(sVE, w_ab)] + ([(sVE, w_hmul_last)] if tc > 0 else []))
        wbBw, wbB = bounce(1, r0, 512, 20 + tc, rbc[:, 512:1024], waits=[(sVE, w_ab)])
        w_rowfree = [wbAw, wbBw] + ([(sDBG, sDBG.n)] if tc == 0 and "d_stats" in DBG else [])
        for kc in range(NKC):
            sl = h_sb[:, kc * T + tc * 512: kc * T + (tc + 1) * 512]
            s2 = sm2[:, (kc % 2) * 512:(kc % 2 + 1) * 512]
            st("vector", lambda e, sl=sl, s2=s2: e.tensor_tensor(
                out=s2, in0=sl, in1=rbc[:, 0:512], op=ALU.mult),
               waits=([wbA, wbB] if kc == 0 else
                      [(sSE, sSE.n - 1)] if kc >= 2 else []))
            st("vector", lambda e, s2=s2: e.tensor_tensor(
                out=s2, in0=s2, in1=rbc[:, 512:1024], op=ALU.add))
            w_t = sVE.n
            w_hmul_last = w_t
            st("scalar", lambda e, sl=sl, s2=s2, kc=kc: e.activation(
                out=sl, in_=s2, func=AF.Identity,
                scale=modv[:, 16 + kc:17 + kc], bias=modv[:, kc:kc + 1]),
               waits=[(sVE, w_t)] + ([(sVE, w_mod)] if kc == 0 and tc == 0 else []),
               inc=sSE)
    st("scalar", lambda e: e.activation(out=tab[:, 5:6], in_=tab[:, 1:2], func=AF.Identity),
       inc=sSE)
    w_h = sSE.n
    if dbg_t("d_h", [128, NKC * T]):
        st("gpsimd", lambda e: e.dma_start(out=DBG["d_h"][:, :], in_=h_sb[:, :]),
           waits=[(sSE, w_h)], inc=sDBG2, inck=16)

    if phases <= 1:
        for kc in range(NKC):
            st("gpsimd", lambda e, kc=kc: e.dma_start(
                out=out_d[kc * 128:(kc + 1) * 128, :], in_=h_sb[:, kc * T:kc * T + OWN]),
               waits=[(sSE, w_h)] if kc == 0 else [], inc=sOUT, inck=16)
        w_fin = sOUT.n
        st("sync", lambda e: e.nop(), waits=[(sOUT, w_fin), (sDBG, sDBG.n), (sDBG2, sDBG2.n)])
        K.DBG = DBG
        return K

    dbg_t("d_k0", [128, T]); dbg_t("d_q0", [128, OWN]); dbg_t("d_v0", [128, NKC * 256])
    dbg_t("d_e0", [128, NKC * 512]); dbg_t("d_l", [16, OWN]); dbg_t("d_y", [128, NKC * OWN])
    dbg_t("d_x2", [D, OWN])

    scale_chunks = []
    for wd, nel in ((w1_d, DFF * D), (w2_d, DFF * D), (w3_d, DFF * D)):
        flat = wd.rearrange("a b c d -> (a b c d)")
        nch = nel // (128 * 1024)
        for i in range(nch):
            scale_chunks.append(flat[i * 131072:(i + 1) * 131072]
                                .rearrange("(p n) -> p n", p=128))
    n_scale = len(scale_chunks)
    scale_state = {"i": 0, "se": [0] * n_scale, "fold": [], "w_red": 0}

    def emit_fold(j):
        w_last = scale_state["se"][(j + 1) * 128 - 1]
        w_red = st("vector", lambda e: e.reduce_sum(out=accw[:, 128:129],
                                                    in_=accw[:, 0:128], axis=AX.X),
                   waits=[(sSE, w_last)])
        scale_state["w_red"] = w_red
        st("sync", lambda e, j=j: e.dma_start(out=bnc_d[8 + j:9 + j, 0:128],
                                              in_=accw[:, 128:129]),
           waits=[(sVE, w_red)], inc=sB[2], inck=16)
        w1 = sB[2].n
        st("sync", lambda e, j=j: e.dma_start(out=lrec[0:1, 0:128],
                                              in_=bnc_d[8 + j:9 + j, 0:128]),
           waits=[(sB[2], w1)], inc=sB[3], inck=16)
        w2 = sB[3].n
        w3 = st("vector", lambda e: e.reduce_sum(out=lrec[0:1, 128:129],
                                                 in_=lrec[0:1, 0:128], axis=AX.X),
                waits=[(sB[3], w2)])
        st("sync", lambda e, j=j: e.dma_start(out=bnc_d[8 + j:9 + j, 128:129],
                                              in_=lrec[0:1, 128:129]),
           waits=[(sVE, w3)], inc=sB[2], inck=16)
        w4 = sB[2].n
        st("sync", lambda e, j=j: e.dma_start(
            out=wsc[:, 7:8],
            in_=bnc_d[8 + j:9 + j, 128:129].broadcast_to((128, 1))),
           waits=[(sB[2], w4)], inc=sB[3], inck=16)
        w5 = sB[3].n
        st("vector", lambda e, j=j: e.tensor_scalar(
            out=wsc[:, 3 + j:4 + j], in0=wsc[:, 7:8],
            scalar1=1.0 / (DFF * D), scalar2=1e-5, op0=ALU.mult, op1=ALU.max),
           waits=[(sB[3], w5)])
        scale_state["fold"].append(st("vector", lambda e, j=j: e.tensor_scalar_mul(
            out=wsc[:, j:j + 1], in0=wsc[:, 3 + j:4 + j], scalar1=0.5)))

    scale_slots = [wfl[:, 0:1024], wfl[:, 1024:2048], rbc[:, 0:1024]]
    scale_sems = [sW0, sW1, sW2]
    w_ln1_dve_done = sVE.n
    w_ln1_se_done = w_h

    def emit_scale(nmax):
        for _ in range(nmax):
            i = scale_state["i"]
            if i >= n_scale:
                return
            scale_state["i"] = i + 1
            sl = i % 3
            sW = scale_sems[sl]
            slot = scale_slots[sl]
            wfree = scale_state["se"][i - 3] if i >= 3 else 0
            first_w = ([(sSE, wfree)] if i >= 3 else
                       ([(sVE, w_ln1_dve_done), (sSE, w_ln1_se_done)]
                        if sl >= 2 else []))
            for cc in range(2):
                st("sync", lambda e, i=i, slot=slot, cc=cc: e.dma_start(
                    out=slot[:, cc * 512:(cc + 1) * 512],
                    in_=scale_chunks[i][:, cc * 512:(cc + 1) * 512]),
                   waits=(first_w if cc == 0 else []), inc=sW, inck=16)
            wld = sW.n
            col = i % 128
            scale_state["se"][i] = st(
                "scalar", lambda e, slot=slot, col=col: e.activation(
                    out=slot, in_=slot,
                    func=AF.Abs, accum_out=accw[:, col:col + 1]),
                waits=[(sW, wld)] + ([(sVE, scale_state["w_red"])]
                                     if col == 0 and i > 0 else []))
            if (i + 1) % 128 == 0:
                emit_fold(i // 128)


    events = []
    for g in range(8):
        events.append(("V", g))
        events.append(("KQ", 2 * g))
        events.append(("KQ", 2 * g + 1))
    slot_of = {}
    for i, ev in enumerate(events):
        slot_of[ev] = i % 2
    ev_index = {ev: i for i, ev in enumerate(events)}
    slots = [wsl, wsl2]
    sAs = [sA0, sA1]

    scale_state["w_ones"] = st("vector", lambda e: e.memset(onesf128[:, :], 1.0))
    ev_ld = {}
    ev_done = {}

    def emit_ev_load(ev):
        i = ev_index[ev]
        sl = slot_of[ev]
        sA = sAs[sl]
        wcred = [(sMM, ev_done[events[i - 2]])] if i >= 2 else []
        if ev[0] == "V":
            g = ev[1]
            st("gpsimd", lambda e, sl=sl, g=g: e.dma_start(
                out=slots[sl][:, 0:4096], in_=wv_d[g]), waits=wcred, inc=sA, inck=16)
        else:
            h = ev[1]
            st("gpsimd", lambda e, sl=sl, h=h: e.dma_start(
                out=slots[sl][:, 0:2048], in_=wk_d[h]), waits=wcred, inc=sA, inck=16)
            st("gpsimd", lambda e, sl=sl, h=h: e.dma_start(
                out=slots[sl][:, 2048:4096], in_=wq_d[h]), inc=sA, inck=16)
        ev_ld[ev] = sA.n

    emit_ev_load(events[0])
    emit_ev_load(events[1])

    w_ln1_last_sq = sSE.n
    hrope = {}

    w_vgrp_evac = [0] * 8
    w_yevac = 0
    w_lrowcp = 0
    w_exp = {}
    for h in range(NH):
        g = h // 2
        kq_ev = ("KQ", h)
        v_ev = ("V", g)
        sl_kq = slot_of[kq_ev]
        sl_v = slot_of[v_ev]
        kslot = ksb[:, (h % 2) * T:(h % 2 + 1) * T]
        qslot = qsb[:, (h % 2) * OWN:(h % 2 + 1) * OWN]
        vslot = vsb[:, (g % 2) * NKC * 256:((g % 2) + 1) * NKC * 256]

        if h % 2 == 0:
            for kkc in range(NKC):
                for kc in range(NKC):
                    st("tensor", lambda e, kkc=kkc, kc=kc, sl=sl_v: e.matmul(
                        psE[:, 0:256], h_sb[:, kc * T + kkc * 128: kc * T + kkc * 128 + 128],
                        slots[sl][:, kc * 256:(kc + 1) * 256],
                        start=(kc == 0), stop=(kc == NKC - 1)),
                       waits=(([(sAs[sl_v], ev_ld[v_ev]), (sSE, w_ln1_last_sq)]
                               if kkc == 0 else []) +
                              ([(sVE, w_vgrp_evac[g])] if (kc == 0 and kkc > 0) else
                               [(sVE, w_vgrp_evac[g - 1] if g > 0 else 0)]
                               if kc == 0 else [])),
                       inc=(sMM if kc == NKC - 1 else None))
                w_mm = sMM.n
                w_vgrp_evac[g] = st("vector", lambda e, kkc=kkc, vslot=vslot: e.tensor_copy(
                    out=vslot[:, kkc * 256:(kkc + 1) * 256], in_=psE[:, 0:256]),
                    waits=[(sMM, w_mm)])
            ev_done[v_ev] = sMM.n
            ni = ev_index[v_ev] + 2
            if ni < len(events):
                emit_ev_load(events[ni])
            if h == 0 and "d_v0" in DBG:
                st("gpsimd", lambda e, vslot=vslot: e.dma_start(
                    out=DBG["d_v0"][:, :], in_=vslot[:, :]),
                   waits=[(sVE, w_vgrp_evac[0])], inc=sDBG2, inck=16)

        w_kq_done_mm = 0
        for tkc in range(4):
            pst = psF[:, 0:512]
            for kc in range(NKC):
                st("tensor", lambda e, tkc=tkc, kc=kc, sl=sl_kq: e.matmul(
                    pst, slots[sl][:, kc * 128:(kc + 1) * 128],
                    h_sb[:, kc * T + tkc * 512: kc * T + (tkc + 1) * 512],
                    start=(kc == 0), stop=(kc == NKC - 1)),
                   waits=([(sAs[sl_kq], ev_ld[kq_ev])] +
                          ([(sSE, hrope.get((h, tkc - 1), 0))] if tkc > 0 else
                           [(sSE, hrope.get((h - 1, "q1"), 0))]))
                   if kc == 0 else [],
                   inc=(sMM if kc == NKC - 1 else None))
            w_mm = sMM.n
            prev_v = (hrope.get((h, tkc - 1, "v"), 0) if tkc > 0
                      else hrope.get((h - 1, "q1", "v"), 0))
            hrope[(h, tkc)] = st("scalar", lambda e: e.activation(
                out=kraw[:, 0:512], in_=pst, func=AF.Identity),
                waits=[(sMM, w_mm), (sVE, prev_v)])
            st("vector", lambda e: e.tensor_copy(out=krot[0:32, 0:512],
                                                 in_=kraw[32:64, 0:512]),
               waits=[(sSE, hrope[(h, tkc)])])
            st("vector", lambda e: e.tensor_copy(out=krot[32:64, 0:512],
                                                 in_=kraw[0:32, 0:512]))
            st("vector", lambda e, tkc=tkc: e.tensor_tensor(
                out=sm2[0:64, 0:512], in0=kraw[0:64, 0:512],
                in1=cosT[:, tkc * 512:(tkc + 1) * 512], op=ALU.mult))
            st("vector", lambda e, tkc=tkc: e.tensor_tensor(
                out=krot[0:64, 0:512], in0=krot[0:64, 0:512],
                in1=ssinT[:, tkc * 512:(tkc + 1) * 512], op=ALU.mult))
            st("vector", lambda e, tkc=tkc, kslot=kslot: e.tensor_tensor(
                out=kslot[0:64, tkc * 512:(tkc + 1) * 512], in0=sm2[0:64, 0:512],
                in1=krot[0:64, 0:512], op=ALU.add))
            w_krope_v = st("vector", lambda e, tkc=tkc, kslot=kslot: e.tensor_copy(
                out=kslot[64:128, tkc * 512:(tkc + 1) * 512], in_=kraw[64:128, 0:512]))
            hrope[(h, tkc)] = hrope[(h, tkc)]
            hrope[(h, tkc, "v")] = w_krope_v
        for qkc in range(2):
            pst = psF[:, 0:512]
            for kc in range(NKC):
                st("tensor", lambda e, qkc=qkc, kc=kc, sl=sl_kq: e.matmul(
                    pst, slots[sl][:, 2048 + kc * 128: 2048 + (kc + 1) * 128],
                    h_sb[:, kc * T + qkc * 512: kc * T + (qkc + 1) * 512],
                    start=(kc == 0), stop=(kc == NKC - 1)),
                   waits=[(sSE, hrope[(h, 3)] if qkc == 0 else hrope[(h, "q0")])]
                   if kc == 0 else [],
                   inc=(sMM if kc == NKC - 1 else None))
            w_mm = sMM.n
            key = "q%d" % qkc
            prev_v = hrope[(h, 3, "v")] if qkc == 0 else hrope[(h, "q0", "v")]
            hrope[(h, key)] = st("scalar", lambda e: e.activation(
                out=kraw[:, 0:512], in_=pst, func=AF.Identity),
                waits=[(sMM, w_mm), (sVE, prev_v)])
            st("vector", lambda e: e.tensor_copy(out=krot[0:32, 0:512],
                                                 in_=kraw[32:64, 0:512]),
               waits=[(sSE, hrope[(h, key)])])
            st("vector", lambda e: e.tensor_copy(out=krot[32:64, 0:512],
                                                 in_=kraw[0:32, 0:512]))
            st("vector", lambda e, qkc=qkc: e.tensor_tensor(
                out=sm2[0:64, 0:512], in0=kraw[0:64, 0:512],
                in1=cosT[:, qkc * 512:(qkc + 1) * 512], op=ALU.mult))
            st("vector", lambda e, qkc=qkc: e.tensor_tensor(
                out=krot[0:64, 0:512], in0=krot[0:64, 0:512],
                in1=ssinT[:, qkc * 512:(qkc + 1) * 512], op=ALU.mult))
            st("vector", lambda e, qkc=qkc, qslot=qslot: e.tensor_tensor(
                out=qslot[0:64, qkc * 512:(qkc + 1) * 512], in0=sm2[0:64, 0:512],
                in1=krot[0:64, 0:512], op=ALU.add))
            hrope[(h, key, "v")] = st("vector", lambda e, qkc=qkc, qslot=qslot: e.tensor_copy(
                out=qslot[64:128, qkc * 512:(qkc + 1) * 512], in_=kraw[64:128, 0:512]))
        ev_done[kq_ev] = sMM.n
        ni = ev_index[kq_ev] + 2
        if ni < len(events):
            emit_ev_load(events[ni])
        if h == 0:
            if "d_k0" in DBG:
                st("gpsimd", lambda e, kslot=kslot: e.dma_start(out=DBG["d_k0"][:, :], in_=kslot[:, :]),
                   waits=[(sVE, hrope[(0, 3, "v")])], inc=sDBG2, inck=16)
            if "d_q0" in DBG:
                st("gpsimd", lambda e, qslot=qslot: e.dma_start(out=DBG["d_q0"][:, :], in_=qslot[:, :]),
                   waits=[(sVE, hrope[(0, "q1", "v")])], inc=sDBG2, inck=16)

        w_rope_all = max(hrope[(h, 3, "v")], hrope[(h, "q1", "v")],
                         hrope[(h, 0, "v")], hrope[(h, 1, "v")], hrope[(h, 2, "v")],
                         hrope[(h, "q0", "v")])
        for qh in range(2):
            ps_y = psC if qh == 0 else psD
            ps_l = psG if qh == 0 else psH

            def emit_lav(kkc, qh=qh, ps_y=ps_y, ps_l=ps_l):
                wexp = w_exp[(h, qh, kkc)]
                st("tensor", lambda e, kkc=kkc, ps_l=ps_l: e.matmul(
                    ps_l[:, 0:512], onesb[:, :], esb[:, kkc * 512:(kkc + 1) * 512],
                    start=(kkc == 0), stop=(kkc == NKC - 1)),
                   waits=[(sSE, wexp)] + ([(sVE, w_lrowcp)] if kkc == 0 else []),
                   inc=(sMM if kkc == NKC - 1 else None))
                st("tensor", lambda e, kkc=kkc, qh=qh, ps_y=ps_y, vslot=vslot,
                   h=h: e.matmul(
                       ps_y[:, 0:512],
                       vslot[:, kkc * 256 + (h % 2) * 128: kkc * 256 + (h % 2) * 128 + 128],
                       esb[:, kkc * 512:(kkc + 1) * 512],
                       start=(kkc == 0), stop=(kkc == NKC - 1)),
                   inc=(sMM if kkc == NKC - 1 else None))

            for kkc in range(NKC):
                ps_s = psA if kkc % 2 == 0 else psB
                st("tensor", lambda e, kkc=kkc, qh=qh, ps_s=ps_s, kslot=kslot,
                   qslot=qslot: e.matmul(
                       ps_s[:, 0:512], kslot[:, kkc * 128:(kkc + 1) * 128],
                       qslot[:, qh * 512:(qh + 1) * 512], start=True, stop=True),
                   waits=([(sVE, w_rope_all), (sVE, w_vgrp_evac[g]),
                           (sSE, w_yevac), (sVE, w_lrowcp)] if (qh == 0 and kkc == 0)
                          else [(sSE, w_exp.get((h, qh, kkc - 2),
                                                w_exp.get((h, 0, 14 + kkc), 0)))]),
                   inc=sMM)
                w_smm = sMM.n
                prev_avdone = (w_exp.get(("avdone", h, 0), 0) if qh == 1
                               else w_exp.get(("avdone", h - 1, 1), 0))
                w_exp[(h, qh, kkc)] = st("scalar", lambda e, kkc=kkc, ps_s=ps_s: e.activation(
                    out=esb[:, kkc * 512:(kkc + 1) * 512], in_=ps_s[:, 0:512],
                    func=AF.Exp, bias=consts[:, 0:1], scale=SCALE_S),
                    waits=([(sMM, w_smm), (sMM, prev_avdone), (sDBG2, sDBG2.n)]
                           if kkc == 0 else [(sMM, w_smm)]))
                if kkc >= 2:
                    emit_lav(kkc - 2)
                emit_scale(1)
            emit_lav(NKC - 2)
            emit_lav(NKC - 1)
            w_exp[("avdone", h, qh)] = sMM.n
            w_yevac = st("scalar", lambda e, h=h, qh=qh, ps_y=ps_y: e.activation(
                out=ysb[:, h * OWN + qh * 512: h * OWN + (qh + 1) * 512],
                in_=ps_y[:, 0:512], func=AF.Identity), waits=[(sMM, sMM.n)])
            w_lrowcp = st("vector", lambda e, qh=qh, ps_l=ps_l: e.tensor_copy(
                out=rows[0:1, qh * 512:(qh + 1) * 512], in_=ps_l[0:1, 0:512]),
                waits=[(sMM, sMM.n), (sLB, sLB.n)])
        st("sync", lambda e, h=h: e.dma_start(out=bnc_d[32 + h:33 + h, 0:1024],
                                              in_=rows[0:1, 0:1024]),
           waits=[(sVE, w_lrowcp)], inc=sLB, inck=16)
        if h == e_head and "d_e0" in DBG:
            st("gpsimd", lambda e: e.dma_start(out=DBG["d_e0"][:, :], in_=esb[:, :]),
               waits=[(sSE, w_exp[(e_head, 1, NKC - 1)])], inc=sDBG2, inck=16)

    st("sync", lambda e: e.dma_start(out=lrec[:, :], in_=bnc_d[32:48, 0:1024]),
       waits=[(sLB, sLB.n),
              (sVE, scale_state["fold"][-1] if scale_state["fold"] else 0)],
       inc=sLR, inck=16)
    w_lrl = sLR.n
    w_rec = st("vector", lambda e: e.reciprocal(out=lrec[:, :], in_=lrec[:, :]),
               waits=[(sLR, w_lrl)])
    st("sync", lambda e: e.dma_start(out=bnc_d[48:64, 0:1024], in_=lrec[:, :]),
       waits=[(sVE, w_rec)], inc=sLR, inck=16)
    w_lrw = sLR.n
    if "d_l" in DBG:
        st("sync", lambda e: e.dma_start(out=DBG["d_l"][:, :], in_=lrec[:, :]),
           waits=[(sVE, w_rec)], inc=sDBG, inck=16)
    w_ynorm = 0
    for h in range(NH):
        rb = rbc if h % 2 == 0 else sm2
        sRB = sRB0 if h % 2 == 0 else sRB1
        st("sync", lambda e, h=h, rb=rb: e.dma_start(
            out=rb[:, 0:1024],
            in_=bnc_d[48 + h:49 + h, 0:1024].broadcast_to((128, 1024))),
           waits=[(sLR, w_lrw)] + ([(sVE, w_ynorm)] if h >= 2 else
                                    [(sVE, scale_state["fold"][-1]
                                      if scale_state["fold"] else 0)]),
           inc=sRB, inck=16)
        wrb = sRB.n
        w_ynorm = st("vector", lambda e, h=h, rb=rb: e.tensor_tensor(
            out=ysb[:, h * OWN:(h + 1) * OWN], in0=ysb[:, h * OWN:(h + 1) * OWN],
            in1=rb[:, 0:1024], op=ALU.mult),
            waits=[(sRB, wrb), (sSE, w_yevac)])
    if "d_y" in DBG:
        st("gpsimd", lambda e: e.dma_start(out=DBG["d_y"][:, :], in_=ysb[:, :]),
           waits=[(sVE, w_ynorm)], inc=sDBG3, inck=16)
    w_dy_tap = sDBG3.n if "d_y" in DBG else 0

    emit_scale(n_scale)
    w_scale_fold = scale_state["fold"]
    if "d_ws" in dbg:
        DBG["d_ws"] = nc.dram_tensor("d_ws", [1, 16], F32, kind="ExternalOutput").ap()
        st("sync", lambda e: e.dma_start(out=DBG["d_ws"][0:1, 0:8], in_=wsc[0:1, :]),
           waits=[(sVE, w_scale_fold[-1])], inc=sDBG4, inck=16)

    wo_slot = [wsl[:, 0:2048], wsl[:, 2048:4096]]
    w_last_attn_ev = ev_done[events[-1]]
    wo_done = {}

    def emit_wo_load(dc):
        sl = dc % 2
        sA = sAs[sl]
        st("gpsimd", lambda e, dc=dc, sl=sl: e.dma_start(
            out=wo_slot[sl], in_=wo_d[dc]),
           waits=[(sMM, wo_done[dc - 2])] if dc >= 2 else
                 [(sMM, ev_done[events[-2 + dc]])], inc=sA, inck=16)
        ev_ld[("WO", dc)] = sA.n

    emit_wo_load(0)
    emit_wo_load(1)
    w_x2 = {}
    w_stmm = {}
    w_x2spill = []
    for dc in range(NKC):
        sl = dc % 2
        sXO = sXO0 if dc % 2 == 0 else sXO1
        xsl = xst[:, (dc % 2) * 1024:(dc % 2 + 1) * 1024]
        st("sync", lambda e, dc=dc, xsl=xsl: e.dma_start(
            out=xsl, in_=xc[dc * 128:(dc + 1) * 128, 0:OWN]),
           waits=[(sVE, w_x2.get(dc - 2, 0))] +
                 ([(sXS0 if dc % 2 == 0 else sXS1, w_x2spill[dc - 2])]
                  if dc >= 2 else []) +
                 ([(sMM, w_last_attn_ev)] if dc < 2 else []), inc=sXO, inck=16)
        w_xo = sXO.n
        for qh in range(2):
            ps_o = (psA if qh == 0 else psB) if dc % 2 == 0 else (psC if qh == 0 else psD)
            for kc in range(NKC):
                st("tensor", lambda e, dc=dc, qh=qh, kc=kc, sl=sl, ps_o=ps_o: e.matmul(
                    ps_o[:, 0:512], wo_slot[sl][:, kc * 128:(kc + 1) * 128],
                    ysb[:, kc * OWN + qh * 512: kc * OWN + (qh + 1) * 512],
                    start=(kc == 0), stop=(kc == NKC - 1)),
                   waits=([(sAs[sl], ev_ld[("WO", dc)]), (sVE, w_ynorm)] +
                          ([(sSE, w_x2.get(("se", dc - 2, qh), 0))] if dc >= 2 else []))
                   if kc == 0 else [],
                   inc=(sMM if kc == NKC - 1 else None))
            w_omm = sMM.n
            w_x2[("se", dc, qh)] = st("scalar", lambda e, dc=dc, qh=qh, ps_o=ps_o: e.activation(
                out=kraw[:, 0:512], in_=ps_o[:, 0:512], func=AF.Identity,
                scale=modv[:, 32 + dc:33 + dc]),
                waits=[(sMM, w_omm)] + ([(sVE, w_x2.get(("tt", dc, 0), 0))]
                                        if qh == 1 else
                                        [(sVE, w_x2.get(("tt", dc - 1, 1), 0))]))
            w_x2[("tt", dc, qh)] = st("vector", lambda e, dc=dc, qh=qh, xsl=xsl: e.tensor_tensor(
                out=xsl[:, qh * 512:(qh + 1) * 512], in0=xsl[:, qh * 512:(qh + 1) * 512],
                in1=kraw[:, 0:512], op=ALU.add),
                waits=[(sXO, w_xo), (sSE, w_x2[("se", dc, qh)])])
            st("vector", lambda e, qh=qh, xsl=xsl: e.tensor_copy(
                out=xbb[:, qh * 512:(qh + 1) * 512], in_=xsl[:, qh * 512:(qh + 1) * 512]),
               waits=[(sMM, w_stmm.get((dc - 1, qh), 0))])
            w_x2[dc] = sVE.n
            w_sq2 = st("scalar", lambda e, qh=qh: e.activation(
                out=sqb2[:, 0:512], in_=xbb[:, qh * 512:(qh + 1) * 512],
                func=AF.Square),
                waits=[(sVE, w_x2[dc]),
                       (sMM, w_stmm.get((dc, qh - 1),
                                        w_stmm.get((dc - 1, 1), 0)))])
            ps_sum = psE if qh == 0 else psF
            ps_sq = psG if qh == 0 else psH
            st("tensor", lambda e, dc=dc, qh=qh, ps_sum=ps_sum: e.matmul(
                ps_sum[:, 0:512], onesb[:, :], xbb[:, qh * 512:(qh + 1) * 512],
                start=(dc == 0), stop=(dc == NKC - 1)),
               waits=[(sVE, w_x2[dc])], inc=sMM)
            st("tensor", lambda e, dc=dc, qh=qh, ps_sq=ps_sq: e.matmul(
                ps_sq[:, 0:512], onesb[:, :], sqb2[:, 0:512],
                start=(dc == 0), stop=(dc == NKC - 1)),
               waits=[(sSE, w_sq2)], inc=sMM)
            w_stmm[(dc, qh)] = sMM.n
        wo_done[dc] = sMM.n
        if dc + 2 < NKC:
            emit_wo_load(dc + 2)
        sXSp = sXS0 if dc % 2 == 0 else sXS1
        st("sync", lambda e, dc=dc, xsl=xsl: e.dma_start(
            out=x2s_d[dc * 128:(dc + 1) * 128, :], in_=xsl),
           waits=[(sVE, w_x2[dc])], inc=sXSp, inck=16)
        w_x2spill.append(sXSp.n)
    w_ln2_mm = sMM.n
    if "d_x2" in DBG:
        st("sync", lambda e: e.dma_start(out=DBG["d_x2"][:, :], in_=x2s_d[:, :])
           if False else e.nop())
        for dc in range(NKC):
            st("gpsimd", lambda e, dc=dc: e.dma_start(
                out=DBG["d_x2"][dc * 128:(dc + 1) * 128, :],
                in_=x2s_d[dc * 128:(dc + 1) * 128, :]),
               waits=[(sXS0 if dc % 2 == 0 else sXS1, w_x2spill[dc])],
               inc=sDBG2, inck=16)

    if phases <= 2:
        for kc in range(NKC):
            st("gpsimd", lambda e, kc=kc: e.dma_start(
                out=out_d[kc * 128:(kc + 1) * 128, :], in_=ysb[:, kc * OWN:kc * OWN + OWN]),
               waits=[(sVE, w_ynorm)] if kc == 0 else [], inc=sOUT, inck=16)
        w_fin = sOUT.n
        st("sync", lambda e: e.nop(),
           waits=[(sOUT, w_fin), (sDBG, sDBG.n), (sDBG2, sDBG2.n),
                  (sXS0, sXS0.n), (sXS1, sXS1.n), (sMM, sMM.n)])
        K.DBG = DBG
        return K

    dbg_t("d_h2", [128, NKC * OWN]); dbg_t("d_q12", [128, NKC * OWN])
    dbg_t("d_hp", [128, 2 * OWN]); dbg_t("d_q3", [128, 2 * OWN])
    dbg_t("d_rows", [1, 2048])
    rr0 = rows[0:1, 0:1024]
    rr1 = rows[0:1, 1024:2048]
    rr2 = rows[0:1, 2048:3072]
    st("vector", lambda e: e.tensor_scalar_mul(out=rows[0:1, 0:512], in0=psE[0:1, 0:512],
                                               scalar1=1.0 / D),
       waits=[(sMM, w_ln2_mm), (sLR, sLR.n), (sRB0, sRB0.n), (sRB1, sRB1.n)])
    st("vector", lambda e: e.tensor_scalar_mul(out=rows[0:1, 512:1024],
                                               in0=psF[0:1, 0:512], scalar1=1.0 / D))
    st("vector", lambda e: e.tensor_scalar_mul(out=rows[0:1, 1024:1536],
                                               in0=psG[0:1, 0:512], scalar1=1.0 / D))
    w_r4 = st("vector", lambda e: e.tensor_scalar_mul(out=rows[0:1, 1536:2048],
                                                      in0=psH[0:1, 0:512], scalar1=1.0 / D))
    st("vector", lambda e: e.tensor_tensor(out=rr2, in0=rr0, in1=rr0, op=ALU.mult))
    st("vector", lambda e: e.tensor_tensor(out=rr1, in0=rr1, in1=rr2, op=ALU.subtract))
    w_var2 = sVE.n
    st("scalar", lambda e: e.activation(out=rr2, in_=rr1, func=AF.Sqrt,
                                        bias=consts[0:1, 2:3]), waits=[(sVE, w_var2)])
    w_std2 = sSE.n
    st("vector", lambda e: e.reciprocal(out=rr1, in_=rr2), waits=[(sSE, w_std2)])
    st("vector", lambda e: e.tensor_tensor(out=rr0, in0=rr0, in1=rr1, op=ALU.mult))
    w_ab2 = st("vector", lambda e: e.tensor_scalar_mul(out=rr0, in0=rr0, scalar1=-1.0))
    wbA2w, wbA2 = bounce(0, rr1, 1024, 16, rbc[:, 0:1024], waits=[(sVE, w_ab2)])
    wbB2w, wbB2 = bounce(1, rr0, 1024, 17, sm2[:, 0:1024], waits=[(sVE, w_ab2)])
    w_h2 = {}
    w_h2mm = {}
    w_maxinit = st("vector", lambda e: e.memset(maxac[:, :], 0.0))
    for dc in range(NKC):
        sXO = sXO0 if dc % 2 == 0 else sXO1
        xsl = xst[:, (dc % 2) * 1024:(dc % 2 + 1) * 1024]
        st("sync", lambda e, dc=dc, xsl=xsl: e.dma_start(
            out=xsl, in_=x2s_d[dc * 128:(dc + 1) * 128, :]),
           waits=([(sVE, w_h2.get(dc - 2, 0)), (sSE, w_h2.get(("aff", dc - 2), 0))]
                  if dc >= 2 else
                  [(sXS0, sXS0.n), (sXS1, sXS1.n)]), inc=sXO, inck=16)
        w_xo = sXO.n
        st("vector", lambda e, xsl=xsl: e.tensor_tensor(
            out=xsl, in0=xsl, in1=rbc[:, 0:1024], op=ALU.mult),
           waits=[(sXO, w_xo), wbA2, wbB2])
        w_t2 = st("vector", lambda e, xsl=xsl: e.tensor_tensor(
            out=xsl, in0=xsl, in1=sm2[:, 0:1024], op=ALU.add))
        w_aff = st("scalar", lambda e, dc=dc, xsl=xsl: e.activation(
            out=h2sb[:, dc * OWN:(dc + 1) * OWN], in_=xsl, func=AF.Identity,
            scale=modv[:, 64 + dc:65 + dc], bias=modv[:, 48 + dc:49 + dc]),
            waits=[(sVE, w_t2)] + ([(sDBG3, w_dy_tap)] if dc == 0 else []))
        w_h2[("aff", dc)] = w_aff
        w_h2[dc] = sVE.n
        for th in range(2):
            w_sqh = st("scalar", lambda e, dc=dc, th=th: e.activation(
                out=sqb2[:, 0:512],
                in_=h2sb[:, dc * OWN + th * 512: dc * OWN + (th + 1) * 512],
                func=AF.Square),
                waits=[(sMM, w_h2mm.get((dc, th - 1),
                                        w_h2mm.get((dc - 1, 1), 0))),
                       (sVE, w_h2.get(("mx", dc, th - 1),
                                      w_h2.get(("mx", dc - 1, 1), 0)))])
            ps_sq = psA if th == 0 else psB
            st("tensor", lambda e, dc=dc, th=th, ps_sq=ps_sq: e.matmul(
                ps_sq[:, 0:512], onesb[:, :], sqb2[:, 0:512],
                start=(dc == 0), stop=(dc == NKC - 1)),
               waits=[(sSE, w_sqh)], inc=sMM)
            w_h2mm[(dc, th)] = sMM.n
            w_h2[("mx", dc, th)] = st("vector", lambda e, th=th: e.tensor_tensor(
                out=maxac[:, th * 512:(th + 1) * 512], in0=maxac[:, th * 512:(th + 1) * 512],
                in1=sqb2[:, 0:512], op=ALU.max),
                waits=[(sSE, w_sqh)] + ([(sVE, w_maxinit)] if dc == 0 else []))
    if "d_h2" in DBG:
        st("gpsimd", lambda e: e.dma_start(out=DBG["d_h2"][:, :], in_=h2sb[:, :]),
           waits=[(sSE, sSE.n)], inc=sDBG2, inck=16)
    st("vector", lambda e: e.tensor_scalar_mul(
        out=rows[0:1, 1024:1536], in0=psA[0:1, 0:512], scalar1=1.0 / D),
       waits=[(sMM, w_h2mm[(NKC - 1, 0)]), wbA2w])
    w_rmsev = st("vector", lambda e: e.tensor_scalar_mul(
        out=rows[0:1, 1536:2048], in0=psB[0:1, 0:512], scalar1=1.0 / D),
        waits=[(sMM, w_h2mm[(NKC - 1, 1)])])
    st("vector", lambda e: e.tensor_copy(out=tmpb[0:64, :], in_=maxac[64:128, :]),
       waits=[(sVE, w_h2[("mx", NKC - 1, 1)])])
    st("vector", lambda e: e.tensor_tensor(out=maxac[0:64, :], in0=maxac[0:64, :],
                                           in1=tmpb[0:64, :], op=ALU.max))
    st("vector", lambda e: e.tensor_copy(out=tmpb[0:32, :], in_=maxac[32:64, :]))
    w_mx32 = st("vector", lambda e: e.tensor_tensor(out=maxac[0:32, :], in0=maxac[0:32, :],
                                                    in1=tmpb[0:32, :], op=ALU.max))
    wsh = w_mx32
    for npart in (16, 8, 4, 2, 1):
        st("sync", lambda e, npart=npart: e.dma_start(
            out=tmpb[0:npart, :], in_=maxac[npart:2 * npart, :]
            if npart >= 32 else maxac[npart:2 * npart, :]),
           waits=[(sVE, wsh)], inc=sB[4], inck=16)
        wdma = sB[4].n
        wsh = st("vector", lambda e, npart=npart: e.tensor_tensor(
            out=maxac[0:npart, :], in0=maxac[0:npart, :], in1=tmpb[0:npart, :],
            op=ALU.max), waits=[(sB[4], wdma)])
    st("scalar", lambda e: e.activation(out=rr2, in_=rr1, func=AF.Sqrt,
                                        bias=consts[0:1, 3:4]),
       waits=[(sVE, wsh), (sVE, w_rmsev)])
    w_s3 = sSE.n
    st("vector", lambda e: e.reciprocal(out=rr1, in_=rr2), waits=[(sSE, w_s3)])
    st("scalar", lambda e: e.activation(out=rr0, in_=maxac[0:1, 0:1024], func=AF.Sqrt),
       waits=[(sVE, sVE.n)])
    w_s4 = sSE.n
    st("vector", lambda e: e.tensor_tensor(out=rr0, in0=rr0, in1=rr1, op=ALU.mult),
       waits=[(sSE, w_s4)])
    st("vector", lambda e: e.tensor_scalar_max(out=rr0, in0=rr0, scalar1=1e-5))
    st("vector", lambda e: e.reciprocal(out=rr2, in_=rr0))
    st("vector", lambda e: e.tensor_tensor(out=rr2, in0=rr2, in1=rr1, op=ALU.mult))
    w_cq = st("vector", lambda e: e.tensor_scalar_mul(out=rr2, in0=rr2, scalar1=127.0))
    st("vector", lambda e: e.tensor_scalar(out=rr1, in0=rr0, scalar1=1.0 / 127.0,
                                           scalar2=wsc[0:1, 3:4],
                                           op0=ALU.mult, op1=ALU.mult))
    w_du1 = sVE.n
    st("vector", lambda e: e.tensor_scalar(out=rr0, in0=rr0, scalar1=1.0 / 127.0,
                                           scalar2=wsc[0:1, 4:5],
                                           op0=ALU.mult, op1=ALU.mult))
    w_du2 = sVE.n
    wbCQw, wbCQ = bounce(2, rr2, 1024, 18, maxac[:, 0:1024], waits=[(sVE, w_cq)])
    wbD1w, wbD1 = bounce(0, rr1, 1024, 19, du1b[:, 0:1024], waits=[(sVE, w_du1), wbA2])
    if "d_rows" in DBG:
        st("sync", lambda e: e.dma_start(out=DBG["d_rows"][0:1, 0:1024], in_=rr2),
           waits=[(sVE, w_cq)], inc=sDBG, inck=16)
        st("sync", lambda e: e.dma_start(out=DBG["d_rows"][0:1, 1024:2048], in_=rr0),
           waits=[(sVE, w_du2)], inc=sDBG, inck=16)
    if "d_cqb" in dbg:
        DBG["d_cqb"] = nc.dram_tensor("d_cqb", [128, 2048], F32, kind="ExternalOutput").ap()
        st("gpsimd", lambda e: e.dma_start(out=DBG["d_cqb"][:, 0:1024], in_=maxac[:, 0:1024]),
           waits=[wbCQ], inc=sDBG5, inck=16)
        st("gpsimd", lambda e: e.dma_start(out=DBG["d_cqb"][:, 1024:2048], in_=du1b[:, 0:1024]),
           waits=[wbD1], inc=sDBG5, inck=16)
    w_q12 = 0
    for dc in range(NKC):
        for th in range(2):
            sl = h2sb[:, dc * OWN + th * 512: dc * OWN + (th + 1) * 512]
            dst = q12[:, dc * OWN + th * 512: dc * OWN + (th + 1) * 512]
            ta = rbc[:, th * 512:(th + 1) * 512]
            tb = sm2[:, th * 512:(th + 1) * 512]
            bb = tmpb[:, th * 512:(th + 1) * 512]
            cqsl = maxac[:, th * 512:(th + 1) * 512]
            st("vector", lambda e, sl=sl, ta=ta, cqsl=cqsl: e.tensor_tensor(
                out=ta, in0=sl, in1=cqsl, op=ALU.mult),
               waits=[wbCQ, wbD1] if (dc == 0 and th == 0) else [])
            st("vector", lambda e, ta=ta, tb=tb: e.tensor_scalar_add(
                out=tb, in0=ta, scalar1=128.5))
            for kk in (128.0, 64.0, 32.0, 16.0, 8.0, 4.0, 2.0, 1.0):
                st("vector", lambda e, tb=tb, bb=bb, kk=kk: e.tensor_scalar(
                    out=bb, in0=tb, scalar1=kk, scalar2=kk,
                    op0=ALU.is_ge, op1=ALU.mult))
                st("vector", lambda e, tb=tb, bb=bb: e.tensor_tensor(
                    out=tb, in0=tb, in1=bb, op=ALU.subtract))
            st("vector", lambda e, ta=ta: e.tensor_scalar_add(out=ta, in0=ta,
                                                              scalar1=128.5))
            st("vector", lambda e, ta=ta, tb=tb: e.tensor_tensor(
                out=ta, in0=ta, in1=tb, op=ALU.subtract))
            w_q12 = st("vector", lambda e, ta=ta, dst=dst: e.tensor_scalar_add(
                out=dst, in0=ta, scalar1=-128.0))
    if "d_q12" in DBG:
        st("gpsimd", lambda e: e.dma_start(out=DBG["d_q12"][:, :], in_=q12[:, :]),
           waits=[(sVE, w_q12)], inc=sDBG2, inck=16)

    st("vector", lambda e: e.tensor_scalar_mul(out=wsc[:, 6:7], in0=wsc[:, 0:1],
                                               scalar1=-1.0),
       waits=[(sVE, w_scale_fold[-1])] + ([(sDBG4, sDBG4.n)] if "d_ws" in DBG else []))
    w_negt = st("vector", lambda e: e.tensor_scalar_mul(out=wsc[:, 7:8], in0=wsc[:, 1:2],
                                                        scalar1=-1.0))
    w_f1mm = {}
    w_tern = {}
    w_hspill = {}
    w_hv = {}
    w_maxi3 = st("vector", lambda e: e.memset(maxac[:, :], 0.0),
                 waits=[(sVE, w_q12)] + ([(sDBG5, 32)] if "d_cqb" in DBG else []))
    for dc2 in range(NFC):
        sW = sW0 if dc2 % 2 == 0 else sW1
        w_dst1 = wfl if dc2 % 2 == 0 else wfl2
        w_dst2 = wfl2 if dc2 % 2 == 0 else wfl
        for cc in range(4):
            st("sync", lambda e, dc2=dc2, cc=cc, w_dst1=w_dst1: e.dma_start(
                out=w_dst1[:, cc * 512:(cc + 1) * 512],
                in_=w1_d[dc2][:, 4 * cc:4 * cc + 4, :]),
               waits=([(sVE, w_tern.get(dc2 - 2, 0))] if cc == 0 else []),
               inc=sW, inck=16)
            st("sync", lambda e, dc2=dc2, cc=cc, w_dst2=w_dst2: e.dma_start(
                out=w_dst2[:, cc * 512:(cc + 1) * 512],
                in_=w2_d[dc2][:, 4 * cc:4 * cc + 4, :]),
               inc=sW, inck=16)
        w_wld = sW.n
        wslc1 = wfl if dc2 % 2 == 0 else wfl2
        wslc2 = wfl2 if dc2 % 2 == 0 else wfl
        st("vector", lambda e, wslc1=wslc1: e.tensor_scalar(
            out=terntmp[:, :], in0=wslc1[:, :], scalar1=wsc[:, 0:1], scalar2=-1.0,
            op0=ALU.is_ge, op1=ALU.add),
           waits=[(sW, w_wld), (sVE, w_negt)] +
                 ([(sMM, w_f1mm.get(dc2 - 1, 0))] if dc2 >= 1 else
                  [(sDBG2, sDBG2.n)]))
        st("vector", lambda e, wslc1=wslc1: e.scalar_tensor_tensor(
            out=ternA[:, :], in0=wslc1[:, :], scalar=wsc[:, 6:7], in1=terntmp[:, :],
            op0=ALU.is_gt, op1=ALU.add))
        st("vector", lambda e, wslc2=wslc2: e.tensor_scalar(
            out=terntmp[:, :], in0=wslc2[:, :], scalar1=wsc[:, 1:2], scalar2=-1.0,
            op0=ALU.is_ge, op1=ALU.add))
        w_tern[dc2] = st("vector", lambda e, wslc2=wslc2: e.scalar_tensor_tensor(
            out=ternB[:, :], in0=wslc2[:, :], scalar=wsc[:, 7:8], in1=terntmp[:, :],
            op0=ALU.is_gt, op1=ALU.add))
        for th in range(2):
            for kc in range(NKC):
                st("tensor", lambda e, th=th, kc=kc: e.matmul(
                    (psA if th == 0 else psB)[:, 0:512],
                    ternA[:, kc * 128:(kc + 1) * 128],
                    q12[:, kc * OWN + th * 512: kc * OWN + (th + 1) * 512],
                    start=(kc == 0), stop=(kc == NKC - 1)),
                   waits=([(sVE, w_tern[dc2]), (sVE, w_q12)] +
                          ([(sVE, w_hv.get((dc2 - 1, th), 0))])
                          if kc == 0 else []),
                   inc=(sMM if kc == NKC - 1 else None))
        for th in range(2):
            for kc in range(NKC):
                st("tensor", lambda e, th=th, kc=kc: e.matmul(
                    (psC if th == 0 else psD)[:, 0:512],
                    ternB[:, kc * 128:(kc + 1) * 128],
                    q12[:, kc * OWN + th * 512: kc * OWN + (th + 1) * 512],
                    start=(kc == 0), stop=(kc == NKC - 1)),
                   waits=([(sVE, w_hv.get((dc2 - 1, th), 0))] if kc == 0 else []),
                   inc=(sMM if kc == NKC - 1 else None))
        w_f1mm[dc2] = sMM.n
        for th in range(2):
            psu = psA if th == 0 else psB
            pst2 = psC if th == 0 else psD
            hsl = hch[:, ((2 * dc2 + th) % 4) * 512:(((2 * dc2 + th) % 4) + 1) * 512]
            sHSp = sHS[(2 * dc2 + th) % 4]
            st("vector", lambda e, psu=psu, th=th: e.tensor_tensor(
                out=rbc[:, th * 512:(th + 1) * 512], in0=psu[:, 0:512],
                in1=du1b[:, th * 512:(th + 1) * 512], op=ALU.mult),
               waits=[(sMM, w_f1mm[dc2])] +
                     ([(sHSp, sHSp.n)] if dc2 >= 2 else []))
            w_udq = sVE.n
            st("scalar", lambda e, th=th: e.activation(
                out=sm2[:, th * 512:(th + 1) * 512], in_=rbc[:, th * 512:(th + 1) * 512],
                func=AF.Silu), waits=[(sVE, w_udq)])
            w_sil = sSE.n
            w_hv[(dc2, th)] = st("vector", lambda e, pst2=pst2, th=th, hsl=hsl: e.tensor_tensor(
                out=hsl, in0=sm2[:, th * 512:(th + 1) * 512], in1=pst2[:, 0:512],
                op=ALU.mult), waits=[(sSE, w_sil)])
            st("sync", lambda e, dc2=dc2, th=th, hsl=hsl: e.dma_start(
                out=hs_d[dc2, :, th * 512:(th + 1) * 512], in_=hsl),
               waits=[(sVE, w_hv[(dc2, th)])], inc=sHSp, inck=16)
            w_hspill[(dc2, th)] = sHSp.n
            w_sq3 = st("scalar", lambda e, hsl=hsl: e.activation(
                out=sqb2[:, 0:512], in_=hsl, func=AF.Square),
                waits=[(sVE, w_hv[(dc2, th)]),
                       (sMM, w_f1mm.get(("st", dc2, th - 1),
                                        w_f1mm.get(("st", dc2 - 1, 1), 0))),
                       (sVE, w_hv.get(("mx", dc2, th - 1),
                                      w_hv.get(("mx", dc2 - 1, 1), 0)))])
            ps_sq = psG if th == 0 else psH
            st("tensor", lambda e, dc2=dc2, th=th, ps_sq=ps_sq: e.matmul(
                ps_sq[:, 0:512], onesb[:, :], sqb2[:, 0:512],
                start=(dc2 == 0), stop=(dc2 == NFC - 1)),
               waits=[(sSE, w_sq3)], inc=sMM)
            w_f1mm[("st", dc2, th)] = sMM.n
            w_hv[("mx", dc2, th)] = st("vector", lambda e, th=th: e.tensor_tensor(
                out=maxac[:, th * 512:(th + 1) * 512],
                in0=maxac[:, th * 512:(th + 1) * 512], in1=sqb2[:, 0:512], op=ALU.max),
                waits=[(sSE, w_sq3)] + ([(sVE, w_maxi3)] if dc2 == 0 else []))
    if "d_hp" in DBG:
        st("gpsimd", lambda e: e.dma_start(out=DBG["d_hp"][:, 0:1024],
                                           in_=hs_d[0, :, :]),
           waits=[(sHS[0], w_hspill[(0, 0)]), (sHS[1], w_hspill[(0, 1)])],
           inc=sDBG2, inck=16)

    st("vector", lambda e: e.tensor_scalar_mul(out=rows[0:1, 1024:1536],
                                               in0=psG[0:1, 0:512], scalar1=1.0 / DFF),
       waits=[(sMM, w_f1mm[("st", NFC - 1, 0)]), (sMM, w_f1mm[("st", NFC - 1, 1)]),
              (sDBG, sDBG.n)])
    st("vector", lambda e: e.tensor_scalar_mul(out=rows[0:1, 1536:2048],
                                               in0=psH[0:1, 0:512], scalar1=1.0 / DFF))
    st("vector", lambda e: e.tensor_tensor(out=rr2, in0=rr0, in1=rr0, op=ALU.mult))
    st("vector", lambda e: e.tensor_tensor(out=rr1, in0=rr1, in1=rr2, op=ALU.mult))
    w_e3 = sVE.n
    st("scalar", lambda e: e.activation(out=rr2, in_=rr1, func=AF.Sqrt,
                                        bias=consts[0:1, 3:4]), waits=[(sVE, w_e3)])
    w_s5 = sSE.n
    st("vector", lambda e: e.reciprocal(out=rr1, in_=rr2), waits=[(sSE, w_s5)])
    st("vector", lambda e: e.tensor_copy(out=tmpb[0:64, :], in_=maxac[64:128, :]),
       waits=[(sVE, w_hv[("mx", NFC - 1, 1)])])
    st("vector", lambda e: e.tensor_tensor(out=maxac[0:64, :], in0=maxac[0:64, :],
                                           in1=tmpb[0:64, :], op=ALU.max))
    st("vector", lambda e: e.tensor_copy(out=tmpb[0:32, :], in_=maxac[32:64, :]))
    wsh = st("vector", lambda e: e.tensor_tensor(out=maxac[0:32, :], in0=maxac[0:32, :],
                                                 in1=tmpb[0:32, :], op=ALU.max))
    for npart in (16, 8, 4, 2, 1):
        st("sync", lambda e, npart=npart: e.dma_start(
            out=tmpb[0:npart, :], in_=maxac[npart:2 * npart, :]),
           waits=[(sVE, wsh)], inc=sB[4], inck=16)
        wdma = sB[4].n
        wsh = st("vector", lambda e, npart=npart: e.tensor_tensor(
            out=maxac[0:npart, :], in0=maxac[0:npart, :], in1=tmpb[0:npart, :],
            op=ALU.max), waits=[(sB[4], wdma)])
    st("scalar", lambda e: e.activation(out=rr2, in_=maxac[0:1, 0:1024], func=AF.Sqrt),
       waits=[(sVE, wsh)])
    w_s6 = sSE.n
    st("vector", lambda e: e.tensor_tensor(out=rr2, in0=rr2, in1=rr0, op=ALU.mult),
       waits=[(sSE, w_s6)])
    st("vector", lambda e: e.tensor_tensor(out=rr2, in0=rr2, in1=rr1, op=ALU.mult))
    st("vector", lambda e: e.tensor_scalar_max(out=rr2, in0=rr2, scalar1=1e-5))
    st("vector", lambda e: e.tensor_tensor(out=rr1, in0=rr1, in1=rr0, op=ALU.mult))
    st("vector", lambda e: e.reciprocal(out=rr0, in_=rr2))
    st("vector", lambda e: e.tensor_tensor(out=rr1, in0=rr1, in1=rr0, op=ALU.mult))
    w_cq3 = st("vector", lambda e: e.tensor_scalar_mul(out=rr1, in0=rr1, scalar1=127.0))
    st("vector", lambda e: e.tensor_scalar(out=rr2, in0=rr2, scalar1=1.0 / 127.0,
                                           scalar2=wsc[0:1, 5:6],
                                           op0=ALU.mult, op1=ALU.mult))
    w_do3 = sVE.n
    wbC3w, wbC3 = bounce(2, rr1, 1024, 20, cq3b[:, 0:1024],
                         waits=[(sVE, w_cq3), wbCQ])
    wbD3w, wbD3 = bounce(1, rr2, 1024, 21, do3b[:, 0:1024],
                         waits=[(sVE, w_do3), wbB2])
    w_negt3 = st("vector", lambda e: e.tensor_scalar_mul(out=wsc[:, 6:7],
                                                         in0=wsc[:, 2:3], scalar1=-1.0))

    w_f3 = {}
    w_q3sp = {}
    w_outev = {}
    w_x3st = []
    for dgq in range(4):
        douts = list(range(dgq * 4, (dgq + 1) * 4))
        for k3 in range(NFC):
            qsl = q3c[:, (k3 % 2) * 1024:((k3 % 2) + 1) * 1024]
            sHRp = sHR[k3 % 2]
            if dgq == 0:
                hsl2 = hch[:, (k3 % 2) * 1024:((k3 % 2) + 1) * 1024]
                for cc in range(2):
                    st("sync", lambda e, k3=k3, hsl2=hsl2, cc=cc: e.dma_start(
                        out=hsl2[:, cc * 512:(cc + 1) * 512],
                        in_=hs_d[k3, :, cc * 512:(cc + 1) * 512]),
                       waits=([(sVE, w_f3.get(("lad", k3 - 2), 0)),
                               (sHS[0], sHS[0].n), (sHS[1], sHS[1].n),
                               (sHS[2], sHS[2].n), (sHS[3], sHS[3].n)]
                              if k3 < 2 else
                              [(sVE, w_f3.get(("lad", k3 - 2), 0))])
                       if cc == 0 else [],
                       inc=sHRp, inck=16)
                w_hr = sHRp.n
                for th in range(2):
                    ta = rbc[:, th * 512:(th + 1) * 512]
                    tb = sm2[:, th * 512:(th + 1) * 512]
                    bb = tmpb[:, th * 512:(th + 1) * 512]
                    sq_ = hsl2[:, th * 512:(th + 1) * 512]
                    dq_ = qsl[:, th * 512:(th + 1) * 512]
                    st("vector", lambda e, sq_=sq_, ta=ta, th=th: e.tensor_tensor(
                        out=ta, in0=sq_, in1=cq3b[:, th * 512:(th + 1) * 512],
                        op=ALU.mult),
                       waits=[(sHRp, w_hr)] + ([wbC3, wbD3] if k3 == 0 and th == 0
                                               else []) +
                             ([(sMM, w_f3.get(("mm", k3 - 2), 0)),
                               (sQS[k3 % 2], w_q3sp.get(k3 - 2, 0))]
                              if th == 0 else []))
                    st("vector", lambda e, ta=ta, tb=tb: e.tensor_scalar_add(
                        out=tb, in0=ta, scalar1=128.5))
                    for kk in (128.0, 64.0, 32.0, 16.0, 8.0, 4.0, 2.0, 1.0):
                        st("vector", lambda e, tb=tb, bb=bb, kk=kk: e.tensor_scalar(
                            out=bb, in0=tb, scalar1=kk, scalar2=kk,
                            op0=ALU.is_ge, op1=ALU.mult))
                        st("vector", lambda e, tb=tb, bb=bb: e.tensor_tensor(
                            out=tb, in0=tb, in1=bb, op=ALU.subtract))
                    st("vector", lambda e, ta=ta: e.tensor_scalar_add(
                        out=ta, in0=ta, scalar1=128.5))
                    st("vector", lambda e, ta=ta, tb=tb: e.tensor_tensor(
                        out=ta, in0=ta, in1=tb, op=ALU.subtract))
                    w_f3[("lad", k3)] = st("vector", lambda e, ta=ta, dq_=dq_:
                                           e.tensor_scalar_add(out=dq_, in0=ta,
                                                               scalar1=-128.0))
                sQSp = sQS[k3 % 2]
                st("sync", lambda e, k3=k3, qsl=qsl: e.dma_start(
                    out=q3s_d[k3, :, :], in_=qsl),
                   waits=[(sVE, w_f3[("lad", k3)])], inc=sQSp, inck=16)
                w_q3sp[k3] = sQSp.n
                w_qrdy = ("v", w_f3[("lad", k3)])
            else:
                for cc in range(2):
                    st("sync", lambda e, k3=k3, qsl=qsl, cc=cc: e.dma_start(
                        out=qsl[:, cc * 512:(cc + 1) * 512],
                        in_=q3s_d[k3, :, cc * 512:(cc + 1) * 512]),
                       waits=(([(sQS[0], sQS[0].n), (sQS[1], sQS[1].n)] if
                               (dgq == 1 and k3 < 2) else []) +
                              [(sMM, w_f3.get(("mm", k3 - 2), 0))])
                       if cc == 0 else [],
                       inc=sHRp, inck=16)
                w_qrdy = ("d", sHRp.n, sHRp)
            sW = sW0 if k3 % 2 == 0 else sW1
            wslc = wfl if k3 % 2 == 0 else wfl2
            for cc in range(2):
                st("sync", lambda e, dgq=dgq, k3=k3, wslc=wslc, cc=cc: e.dma_start(
                    out=wslc[:, cc * 256:(cc + 1) * 256],
                    in_=w3_d[dgq, k3][:, cc * 256:(cc + 1) * 256]),
                   waits=([(sVE, w_f3.get(("tern", k3 - 2), 0))] if cc == 0 else []),
                   inc=sW, inck=16)
            w_w3 = sW.n
            tsl = ternA[:, (k3 % 2) * 512:((k3 % 2) + 1) * 512]
            st("vector", lambda e, wslc=wslc: e.tensor_scalar(
                out=terntmp[:, 0:512], in0=wslc[:, 0:512], scalar1=wsc[:, 2:3],
                scalar2=-1.0, op0=ALU.is_ge, op1=ALU.add),
               waits=[(sW, w_w3), (sMM, w_f3.get(("mm", k3 - 2), 0))] +
                     ([(sVE, w_negt3)] if dgq == 0 and k3 == 0 else []))
            w_f3[("tern", k3)] = st("vector", lambda e, wslc=wslc, tsl=tsl:
                                    e.scalar_tensor_tensor(
                out=tsl, in0=wslc[:, 0:512], scalar=wsc[:, 6:7],
                in1=terntmp[:, 0:512], op0=ALU.is_gt, op1=ALU.add))
            for di, dout in enumerate(douts):
                for th in range(2):
                    psx = pss[di * 2 + th]
                    ww = []
                    if k3 == 0:
                        ww = [(sVE, w_f3[("tern", k3)])]
                        if w_qrdy[0] == "v":
                            ww.append((sVE, w_qrdy[1]))
                        else:
                            ww.append((w_qrdy[2], w_qrdy[1]))
                        if dgq > 0 and di == 0 and th == 0:
                            ww.append((sSE, w_outev.get(dgq - 1, 0)))
                    elif di == 0 and th == 0:
                        ww = [(sVE, w_f3[("tern", k3)])]
                        if w_qrdy[0] == "v":
                            ww.append((sVE, w_qrdy[1]))
                        else:
                            ww.append((w_qrdy[2], w_qrdy[1]))
                    st("tensor", lambda e, di=di, th=th, psx=psx, qsl=qsl, k3=k3,
                       tsl=tsl: e.matmul(
                           psx[:, 0:512], tsl[:, di * 128:(di + 1) * 128],
                           qsl[:, th * 512:(th + 1) * 512],
                           start=(k3 == 0), stop=(k3 == NFC - 1)),
                       waits=ww, inc=(sMM if (di == 3 and th == 1) else None))
            w_f3[("mm", k3)] = sMM.n
        for di, dout in enumerate(douts):
            sXO = sXO0 if dout % 2 == 0 else sXO1
            xsl = xst[:, (dout % 2) * 1024:(dout % 2 + 1) * 1024]
            st("sync", lambda e, dout=dout, xsl=xsl: e.dma_start(
                out=xsl, in_=x2s_d[dout * 128:(dout + 1) * 128, :]),
               waits=[(sVE, w_f3.get(("x3", dout - 2), 0))] +
                     ([w_f3[("ost", dout - 2)]] if ("ost", dout - 2) in w_f3 else []),
               inc=sXO, inck=16)
            w_xo = sXO.n
            for th in range(2):
                psx = pss[di * 2 + th]
                w_ev = st("scalar", lambda e, psx=psx, dout=dout: e.activation(
                    out=kraw[:, 0:512], in_=psx[:, 0:512], func=AF.Identity,
                    scale=modv[:, 80 + dout:81 + dout]),
                    waits=[(sMM, w_f3[("mm", NFC - 1)])] +
                          ([(sVE, w_f3.get(("x3v", dout, th - 1),
                                           w_f3.get(("x3v", dout - 1, 1), 0)))]))
                st("vector", lambda e, th=th: e.tensor_tensor(
                    out=kraw[:, 0:512], in0=kraw[:, 0:512],
                    in1=do3b[:, th * 512:(th + 1) * 512], op=ALU.mult),
                   waits=[(sSE, w_ev)])
                w_f3[("x3v", dout, th)] = st("vector", lambda e, th=th, xsl=xsl:
                                             e.tensor_tensor(
                    out=xsl[:, th * 512:(th + 1) * 512],
                    in0=xsl[:, th * 512:(th + 1) * 512],
                    in1=kraw[:, 0:512], op=ALU.add), waits=[(sXO, w_xo)])
            w_f3[("x3", dout)] = w_f3[("x3v", dout, 1)]
            sOUTp = sOUT if dout % 2 == 0 else sOUT1
            st("sync", lambda e, dout=dout, xsl=xsl: e.dma_start(
                out=out_d[dout * 128:(dout + 1) * 128, :], in_=xsl),
               waits=[(sVE, w_f3[("x3", dout)])], inc=sOUTp, inck=16)
            w_f3[("ost", dout)] = (sOUTp, sOUTp.n)
        w_outev[dgq] = sSE.n
    if "d_q3" in DBG:
        st("gpsimd", lambda e: e.dma_start(out=DBG["d_q3"][:, 0:1024],
                                           in_=q3s_d[0, :, :]),
           waits=[(sQS[0], w_q3sp.get(0, 0))], inc=sDBG2, inck=16)

    st("sync", lambda e: e.nop(),
       waits=[(sOUT, sOUT.n), (sOUT1, sOUT1.n), (sDBG, sDBG.n), (sDBG2, sDBG2.n), (sDBG3, sDBG3.n), (sDBG4, sDBG4.n)])
    K.DBG = DBG
    return K


def _host_prep(inputs, b, half):
    x = inputs["L0_attn_x"][b]
    own = half * OWN
    pos = (np.arange(T) + own) % T
    xfm = np.ascontiguousarray(x[pos].T).astype(np.float32)
    c = inputs["L0_c"][b]
    cvec = np.ascontiguousarray(np.asarray(c).reshape(NKC, 128).T).astype(np.float32)
    return {"xc": xfm, "cvec": cvec, "pos": pos.astype(np.float32).reshape(1, T)}


def _til(w, no, ni):
    return np.ascontiguousarray(
        np.asarray(w).reshape(no // 128, 128, ni // 128, 128).transpose(0, 3, 2, 1)
    ).astype(np.float32)


def _host_weights(inputs):
    wv = np.asarray(inputs["L0_attn_wv"])
    w3 = np.asarray(inputs["L0_ffn_w3"])
    wmod = np.asarray(inputs["L0_wmod"])
    return {
        "wq": _til(inputs["L0_attn_wq"], D, D),
        "wk": _til(inputs["L0_attn_wk"], D, D),
        "wo": _til(inputs["L0_attn_wo"], D, D),
        "wv": np.ascontiguousarray(
            wv.T.reshape(NKC, 128, 8, 256).transpose(2, 1, 0, 3)).astype(np.float32),
        "w1": _til(inputs["L0_ffn_w1"], DFF, D),
        "w2": _til(inputs["L0_ffn_w2"], DFF, D),
        "w3": np.ascontiguousarray(
            w3.reshape(NKC, 128, NFC, 128).transpose(2, 3, 0, 1)
            .reshape(NFC, 128, NKC * 128)
            .reshape(NFC, 128, 4, 4 * 128).transpose(2, 0, 1, 3)).astype(np.float32),
        "wmod": np.ascontiguousarray(
            wmod.reshape(NKC, 128, 24, 512).transpose(2, 1, 0, 3)).astype(np.float32),
    }


def kernel(**inputs):
    nc = bass.Bass("TRN2", target_bir_lowering=False)
    K = build(nc)
    K.emit()
    w = _host_weights(inputs)
    in_maps = []
    for core in range(8):
        b, half = core // 2, core % 2
        m = dict(w)
        m.update(_host_prep(inputs, b, half))
        in_maps.append(m)
    res = run_bass_kernel_spmd(nc, in_maps, core_ids=list(range(8)))
    out = np.zeros((B, T, D), np.float32)
    for core in range(8):
        b, half = core // 2, core % 2
        own = half * OWN
        out[b, own:own + OWN, :] = res.results[core]["out"].T
    return out


# version 06
# speedup vs baseline: 1.2072x; 1.1068x over previous
import math
import numpy as np
import concourse.bass as bass
import concourse.mybir as mybir
from concourse.bass_utils import run_bass_kernel_spmd

F32 = mybir.dt.float32
BF16 = mybir.dt.bfloat16
AF = mybir.ActivationFunctionType
ALU = mybir.AluOpType
AX = mybir.AxisListType

B, T, D, NH, HD = 4, 2048, 2048, 16, 128
DFF = 8192
NKC = D // 128
NFC = DFF // 128
OWN = 1024
SHIFT = 12.0
SCALE_S = 1.0 / math.sqrt(HD)

TWO_PI = 2.0 * math.pi
CW1 = float(np.float32(TWO_PI))
CW2 = float(np.float32(np.float64(TWO_PI) - np.float64(np.float32(TWO_PI))))
CW3 = float(np.float64(TWO_PI) - np.float64(np.float32(TWO_PI)) - np.float64(CW2))
R2PI = float(np.float32(1.0 / TWO_PI))
PI_HALF = float(np.float32(math.pi / 2))
PI_HALF_LO = float(np.float64(math.pi / 2) - np.float64(np.float32(math.pi / 2)))
PI_F = float(np.float32(math.pi))

KBY = 1024


class Sem:
    def __init__(self, handle):
        self.h = handle
        self.n = 0

    def inc(self, k=1):
        self.n += k
        return self.n


class Bld:
    def __init__(self, nc):
        self.nc = nc
        self.q = {k: [] for k in ("sync", "gpsimd", "vector", "scalar", "tensor")}
        self._ctxs = []
        self.sems = {}
        self.serial = {}
        self.base = (nc.sbuf_base + 31) & ~31

    def sem(self, name):
        if name not in self.sems:
            cm = self.nc.semaphore(name)
            self.sems[name] = Sem(cm.__enter__())
            self._ctxs.append(cm)
        return self.sems[name]

    def sb(self, name, shape, dtype, off_b):
        t = self.nc.alloc_sbuf_tensor_at(name, shape, dtype, offset=self.base + off_b)
        return t.ap()

    def st(self, eng, fn, waits=(), inc=None, inck=1):
        ser = self.serial.get(eng)
        if ser is not None:
            waits = tuple(waits) + ((ser, ser.n),)
            inc, inck = ser, 1
        waits = tuple((s.h, v) for (s, v) in waits if v > 0)
        if inc is not None:
            inc.inc(inck)

        def run(e, fn=fn, waits=waits, inc=inc, inck=inck):
            for h, v in waits:
                e.wait_ge(h, v)
            r = fn(e)
            if inc is not None:
                r.then_inc(inc.h, inck)
        self.q[eng].append(run)
        return inc.n if inc is not None else None

    def emit(self):
        with self.nc.Block() as block:
            block.sync(lambda e: [f(e) for f in self.q["sync"]])
            block.gpsimd(lambda e: [f(e) for f in self.q["gpsimd"]])
            block.vector(lambda e: [f(e) for f in self.q["vector"]])
            block.scalar(lambda e: [f(e) for f in self.q["scalar"]])
            block.tensor(lambda e: [f(e) for f in self.q["tensor"]])
        for cm in reversed(self._ctxs):
            cm.__exit__(None, None, None)


def build(nc, dbg=(), phases=99, e_head=0):
    dbg = set(dbg)
    K = Bld(nc)
    st = K.st
    KB = lambda x: int(x * KBY)

    xc = nc.dram_tensor("xc", [D, T], F32, kind="ExternalInput").ap()
    cvec_d = nc.dram_tensor("cvec", [128, NKC], F32, kind="ExternalInput").ap()
    pos_d = nc.dram_tensor("pos", [1, T], F32, kind="ExternalInput").ap()
    wmod_d = nc.dram_tensor("wmod", [24, 128, NKC, 512], F32, kind="ExternalInput").ap()
    wq_d = nc.dram_tensor("wq", [NH, 128, NKC, 128], F32, kind="ExternalInput").ap()
    wk_d = nc.dram_tensor("wk", [NH, 128, NKC, 128], F32, kind="ExternalInput").ap()
    wv_d = nc.dram_tensor("wv", [8, 128, NKC, 256], F32, kind="ExternalInput").ap()
    wo_d = nc.dram_tensor("wo", [NKC, 128, NKC, 128], F32, kind="ExternalInput").ap()
    w1_d = nc.dram_tensor("w1", [NFC, 128, NKC, 128], F32, kind="ExternalInput").ap()
    w2_d = nc.dram_tensor("w2", [NFC, 128, NKC, 128], F32, kind="ExternalInput").ap()
    w3_d = nc.dram_tensor("w3", [4, NFC, 128, 4 * 128], F32, kind="ExternalInput").ap()
    out_d = nc.dram_tensor("out", [D, OWN], F32, kind="ExternalOutput").ap()
    bnc_d = nc.dram_tensor("bnc", [64, 2048], F32, kind="Internal").ap()
    x2s_d = nc.dram_tensor("x2s", [D, OWN], F32, kind="Internal").ap()
    hs_d = nc.dram_tensor("hs", [NFC, 128, OWN], BF16, kind="Internal").ap()
    q3s_d = nc.dram_tensor("q3s", [NFC, 128, OWN], BF16, kind="Internal").ap()

    idm4_d = nc.inline_tensor(
        np.ascontiguousarray(np.tile(np.eye(128, dtype=np.float32), (1, 4))), "idm4").ap()
    irow_d = nc.inline_tensor(np.arange(64, dtype=np.float32).reshape(64, 1), "irow").ap()

    DBG = {}

    def dbg_t(name, shape):
        if name in dbg:
            DBG[name] = nc.dram_tensor(name, shape, F32, kind="ExternalOutput").ap()
            return True
        return False

    rows = K.sb("rows", [1, 3072], F32, KB(0))
    rbc = K.sb("rbc", [128, 1024], F32, KB(12))
    modv = K.sb("modv", [128, 128], F32, KB(16))
    consts = K.sb("consts", [128, 8], F32, KB(16.5))
    accw = K.sb("accw", [128, 132], F32, KB(16.6875))
    wsc = K.sb("wsc", [128, 8], F32, KB(17.21875))
    onesb = K.sb("onesb", [128, 128], BF16, KB(17.25))
    sqb2 = K.sb("sqb2", [128, 512], BF16, KB(17.5))
    onesf128 = K.sb("onesf128", [128, 128], F32, KB(18.5))
    xst = K.sb("xst", [128, 2048], F32, KB(19))
    accm = K.sb("accm", [128, 1024], F32, KB(19))
    wfl = K.sb("wfl", [128, 2048], F32, KB(27))
    wfl2 = K.sb("wfl2", [128, 2048], F32, KB(35))
    wsl = K.sb("wsl", [128, 4096], BF16, KB(35))
    ternb = K.sb("ternb", [128, 4096], BF16, KB(35))
    xbb = K.sb("xbb", [128, 1024], BF16, KB(43))
    h_sb = K.sb("h_sb", [128, NKC * T], BF16, KB(45))
    q12 = K.sb("q12", [128, NKC * OWN], BF16, KB(45))
    ysb = K.sb("ysb", [128, NKC * OWN], BF16, KB(111))
    h2sb = K.sb("h2sb", [128, NKC * OWN], BF16, KB(111))
    du1b = K.sb("du1b", [128, 1024], F32, KB(77))
    tmpb = K.sb("tmpb", [128, 1024], F32, KB(81))
    maxac = K.sb("maxac", [128, 1024], F32, KB(85))
    cq3b = K.sb("cq3b", [128, 1024], F32, KB(89))
    do3b = K.sb("do3b", [128, 1024], F32, KB(93))
    terntmp = K.sb("terntmp", [128, 2048], F32, KB(97))
    ternA2 = K.sb("ternA2", [128, 2048], BF16, KB(143))
    ternB2 = K.sb("ternB2", [128, 2048], BF16, KB(147))
    wfl3 = K.sb("wfl3", [128, 2048], F32, KB(159))
    wfl4 = K.sb("wfl4", [128, 2048], F32, KB(167))
    ternA = K.sb("ternA", [128, 2048], BF16, KB(105))
    ternB = K.sb("ternB", [128, 2048], BF16, KB(109))
    hch = K.sb("hch", [128, 2048], BF16, KB(113))
    q3c = K.sb("q3c", [128, 2048], BF16, KB(117))
    esb = K.sb("esb", [128, NKC * 512], BF16, KB(143))
    vsb = K.sb("vsb", [128, 2 * NKC * 256], BF16, KB(159))
    ksb = K.sb("ksb", [128, 2 * T], BF16, KB(175))
    qsb = K.sb("qsb", [128, 2 * OWN], BF16, KB(183))
    cosT = K.sb("cosT", [64, T], BF16, KB(187))
    ssinT = K.sb("ssinT", [64, T], BF16, KB(191))
    kraw = K.sb("kraw", [128, 512], F32, KB(195))
    krot = K.sb("krot", [64, 512], F32, KB(197))
    tab = K.sb("tab", [64, 8], F32, KB(198))
    lrec = K.sb("lrec", [16, 1024], F32, KB(198.5))
    sm2 = K.sb("sm2", [128, 1024], F32, KB(202.5))
    smr = K.sb("smr", [1, 2048], F32, KB(198.5))
    mring = K.sb("mring", [128, 2 * NKC * 512], BF16, KB(111))
    smat = K.sb("smat", [128, NKC * 128], BF16, KB(143))
    idn4 = K.sb("idn4", [128, 512], F32, KB(147))
    t_ang = K.sb("t_ang", [64, 2048], F32, KB(149.25))
    t_u = K.sb("t_u", [64, 2048], F32, KB(157.25))
    t_r = K.sb("t_r", [64, 2048], F32, KB(165.25))
    t_b = K.sb("t_b", [64, 2048], F32, KB(19))
    wsl2 = K.sb("wsl2", [128, 4096], BF16, KB(19))
    assert K.base + KB(206.5) <= nc.sbuf_top

    pss = [nc.alloc_psum_tensor("ps%d" % i, [128, 512], F32).ap() for i in range(8)]
    psA, psB, psC, psD, psE, psF, psG, psH = pss

    sLD = K.sem("sLD")
    sXL = K.sem("sXL")
    sM0 = K.sem("sM0")
    sM1 = K.sem("sM1")
    sB = [K.sem("sB%d" % i) for i in range(6)]
    sVE = K.sem("sVE")
    sSE = K.sem("sSE")
    sMM = K.sem("sMM")
    sEX = K.sem("sEX")
    sEV = K.sem("sEV")
    sDBG = K.sem("sDBG")
    sDBG2 = K.sem("sDBG2")
    sDBG3 = K.sem("sDBG3")
    sDBG4 = K.sem("sDBG4")
    sDBG5 = K.sem("sDBG5")
    sOUT = K.sem("sOUT")
    sOUT1 = K.sem("sOUT1")
    sA0 = K.sem("sA0")
    sA1 = K.sem("sA1")
    sAC = K.sem("sAC")
    sW0 = K.sem("sW0")
    sW1 = K.sem("sW1")
    sW2 = K.sem("sW2")
    sW3 = K.sem("sW3")
    sLB = K.sem("sLB")
    sLR = K.sem("sLR")
    sRB0 = K.sem("sRB0")
    sRB1 = K.sem("sRB1")
    sXO0 = K.sem("sXO0")
    sXO1 = K.sem("sXO1")
    sXS0 = K.sem("sXS0")
    sXS1 = K.sem("sXS1")
    sHS = [K.sem("sHS%d" % i) for i in range(4)]
    sHR = [K.sem("sHR%d" % i) for i in range(2)]
    sQS = [K.sem("sQS%d" % i) for i in range(2)]
    K.serial["vector"] = sVE
    K.serial["scalar"] = sSE

    def bounce(pair, row_ap, n, bnc_row, tile_ap, nparts=128, waits=()):
        sw, sr = sB[2 * pair], sB[2 * pair + 1]
        st("sync", lambda e: e.dma_start(out=bnc_d[bnc_row:bnc_row + 1, 0:n], in_=row_ap),
           waits=waits, inc=sw, inck=16)
        w = sw.n
        st("sync", lambda e: e.dma_start(
            out=tile_ap, in_=bnc_d[bnc_row:bnc_row + 1, 0:n].broadcast_to((nparts, n))),
           waits=[(sw, w)], inc=sr, inck=16)
        return (sw, w), (sr, sr.n)

    st("sync", lambda e: e.dma_start(out=modv[:, 96:112], in_=cvec_d[:, :]), inc=sLD, inck=16)
    st("sync", lambda e: e.dma_start(out=idn4[:, :], in_=idm4_d[:, :]), inc=sLD, inck=16)
    st("sync", lambda e: e.dma_start(out=tab[:, 0:1], in_=irow_d[:, :]), inc=sLD, inck=16)
    st("sync", lambda e: e.dma_start(
        out=t_ang[:, :], in_=pos_d[0:1, :].broadcast_to((64, T))), inc=sLD, inck=16)
    w_cvec = w_idn = w_irow = w_pos = sLD.n

    def _consts(e):
        e.memset(consts[:, 0:1], -SHIFT)
        e.memset(consts[:, 1:2], 1.0)
        e.memset(consts[:, 2:3], 1e-5)
        e.memset(wsc[:, :], 0.0)
        e.memset(consts[:, 3:4], 1e-6)
        e.memset(tab[:, 1:2], 10000.0)
        return e.memset(onesb[:, :], 1.0)
    st("vector", _consts)
    w_consts = sVE.n

    for kc in range(NKC):
        st("gpsimd", lambda e, kc=kc: e.dma_start(
            out=h_sb[:, kc * T:(kc + 1) * T], in_=xc[kc * 128:(kc + 1) * 128, :]),
           inc=sXL, inck=16)
    w_xall = sXL.n

    st("scalar", lambda e: e.activation(out=tab[:, 2:3], in_=tab[:, 1:2], func=AF.Ln),
       waits=[(sVE, w_consts)])
    st("scalar", lambda e: e.activation(out=tab[:, 3:4], in_=tab[:, 0:1], func=AF.Identity,
                                        scale=tab[:, 2:3]),
       waits=[(sLD, w_irow), (sSE, sSE.n)])
    st("scalar", lambda e: e.activation(out=tab[:, 4:5], in_=tab[:, 3:4], func=AF.Exp,
                                        scale=-1.0 / 64.0), waits=[(sSE, sSE.n)])
    w_invf = sSE.n
    st("vector", lambda e: e.tensor_scalar_mul(out=t_ang[:, :], in0=t_ang[:, :],
                                               scalar1=tab[:, 4:5]),
       waits=[(sLD, w_pos), (sSE, w_invf)])
    st("vector", lambda e: e.tensor_scalar(out=t_u[:, :], in0=t_ang[:, :],
                                           scalar1=R2PI, scalar2=0.5,
                                           op0=ALU.mult, op1=ALU.add))
    st("vector", lambda e: e.tensor_copy(out=t_r[:, :], in_=t_u[:, :]))
    for k in [256.0, 128.0, 64.0, 32.0, 16.0, 8.0, 4.0, 2.0, 1.0]:
        st("vector", lambda e, k=k: e.tensor_scalar(out=t_b[0:64, :], in0=t_r[:, :],
                                                    scalar1=k, scalar2=k,
                                                    op0=ALU.is_ge, op1=ALU.mult))
        st("vector", lambda e: e.tensor_tensor(out=t_r[:, :], in0=t_r[:, :],
                                               in1=t_b[0:64, :], op=ALU.subtract))
    st("vector", lambda e: e.tensor_tensor(out=t_u[:, :], in0=t_u[:, :], in1=t_r[:, :],
                                           op=ALU.subtract))
    st("vector", lambda e: e.scalar_tensor_tensor(out=t_r[:, :], in0=t_u[:, :],
                                                  scalar=-CW1, in1=t_ang[:, :],
                                                  op0=ALU.mult, op1=ALU.add))
    st("vector", lambda e: e.scalar_tensor_tensor(out=t_r[:, :], in0=t_u[:, :],
                                                  scalar=-CW2, in1=t_r[:, :],
                                                  op0=ALU.mult, op1=ALU.add))
    st("vector", lambda e: e.scalar_tensor_tensor(out=t_r[:, :], in0=t_u[:, :],
                                                  scalar=-CW3, in1=t_r[:, :],
                                                  op0=ALU.mult, op1=ALU.add))
    st("vector", lambda e: e.tensor_scalar(out=t_b[0:64, :], in0=t_r[:, :],
                                           scalar1=PI_F, scalar2=1.0,
                                           op0=ALU.is_gt, op1=ALU.mult))
    st("vector", lambda e: e.scalar_tensor_tensor(
        out=t_r[:, :], in0=t_b[0:64, :], scalar=-CW1, in1=t_r[:, :],
        op0=ALU.mult, op1=ALU.add))
    st("vector", lambda e: e.scalar_tensor_tensor(
        out=t_b[0:64, :], in0=t_b[0:64, :], scalar=-CW2, in1=t_r[:, :],
        op0=ALU.mult, op1=ALU.add))
    w_sinin = sVE.n
    st("scalar", lambda e: e.activation(out=ssinT[:, :], in_=t_b[0:64, :], func=AF.Sin),
       waits=[(sVE, w_sinin)])
    w_sin = sSE.n
    st("vector", lambda e: e.tensor_scalar_mul(out=ssinT[0:32, :], in0=ssinT[0:32, :],
                                               scalar1=-1.0), waits=[(sSE, w_sin)])
    w_ssin = sVE.n
    st("vector", lambda e: e.tensor_scalar_add(out=t_u[:, :], in0=t_r[:, :],
                                               scalar1=PI_HALF), waits=[(sSE, w_sin)])
    st("vector", lambda e: e.tensor_scalar_add(out=t_u[:, :], in0=t_u[:, :],
                                               scalar1=PI_HALF_LO))
    st("vector", lambda e: e.tensor_scalar(out=t_b[0:64, :], in0=t_u[:, :],
                                           scalar1=PI_F, scalar2=1.0,
                                           op0=ALU.is_gt, op1=ALU.mult))
    st("vector", lambda e: e.scalar_tensor_tensor(
        out=t_u[:, :], in0=t_b[0:64, :], scalar=-CW1, in1=t_u[:, :],
        op0=ALU.mult, op1=ALU.add))
    st("vector", lambda e: e.scalar_tensor_tensor(
        out=t_b[0:64, :], in0=t_b[0:64, :], scalar=-CW2, in1=t_u[:, :],
        op0=ALU.mult, op1=ALU.add))
    w_cosin = sVE.n
    st("scalar", lambda e: e.activation(out=cosT[:, :], in_=t_b[0:64, :], func=AF.Sin),
       waits=[(sVE, w_cosin)])
    w_cos = sSE.n
    if dbg_t("d_cs", [64, 2 * T]):
        st("gpsimd", lambda e: e.dma_start(out=DBG["d_cs"][:, 0:T], in_=cosT[:, :]),
           waits=[(sSE, w_cos)], inc=sDBG2, inck=16)
        st("gpsimd", lambda e: e.dma_start(out=DBG["d_cs"][:, T:2 * T], in_=ssinT[:, :]),
           waits=[(sVE, w_ssin)], inc=sDBG2, inck=16)

    st("scalar", lambda e: e.activation(out=modv[:, 112:128], in_=modv[:, 96:112],
                                        func=AF.Silu), waits=[(sLD, w_cvec)])
    w_silu = sSE.n
    for kc in range(NKC):
        st("vector", lambda e, kc=kc: e.tensor_scalar_mul(
            out=smat[:, kc * 128:(kc + 1) * 128], in0=onesb[:, :],
            scalar1=modv[:, 112 + kc:113 + kc]),
           waits=[(sSE, w_silu), (sVE, w_consts)] if kc == 0 else [],
           inc=(sVE if kc == NKC - 1 else None))
    w_smat = sVE.n

    mod_mm_base = sMM.n
    for j in range(24):
        st("gpsimd", lambda e, j=j: e.dma_start(
            out=mring[:, (j % 2) * NKC * 512:((j % 2) + 1) * NKC * 512], in_=wmod_d[j]),
           waits=[(sMM, mod_mm_base + j - 1)] if j >= 2 else [],
           inc=(sM0 if j % 2 == 0 else sM1), inck=16)
    w_modtt = 0
    for j in range(24):
        wld = 16 * (j // 2 + 1)
        sMp = sM0 if j % 2 == 0 else sM1
        slot = mring[:, (j % 2) * NKC * 512:((j % 2) + 1) * NKC * 512]
        for kc in range(NKC):
            st("tensor", lambda e, kc=kc, slot=slot: e.matmul(
                psE[:, :], smat[:, kc * 128:(kc + 1) * 128],
                slot[:, kc * 512:(kc + 1) * 512],
                start=(kc == 0), stop=(kc == NKC - 1)),
               waits=(([(sMp, wld), (sVE, w_smat)] +
                       ([(sVE, w_modtt)] if j >= 1 else []))
                      if kc == 0 else []),
               inc=(sMM if kc == NKC - 1 else None))
        wmm = sMM.n
        w_modtt = st("vector", lambda e: e.tensor_tensor(out=sm2[:, 0:512], in0=psE[:, :],
                                                         in1=idn4[:, :], op=ALU.mult),
                     waits=[(sMM, wmm), (sLD, w_idn)])
        st("vector", lambda e, j=j: e.tensor_reduce(
            out=modv[:, j * 4:(j + 1) * 4],
            in_=sm2[:, 0:512].rearrange("p (b f) -> p b f", f=128),
            axis=AX.X, op=ALU.add))
    w_gmul = sVE.n
    for (cg, nm) in ((32, "ga"), (80, "gb")):
        st("scalar", lambda e, cg=cg: e.activation(out=modv[:, 112:128],
                                                   in_=modv[:, cg:cg + 16],
                                                   func=AF.Sign),
           waits=[(sVE, w_gmul)])
        w_ab_ = st("scalar", lambda e, cg=cg: e.activation(out=modv[:, cg:cg + 16],
                                                           in_=modv[:, cg:cg + 16],
                                                           func=AF.Abs))
        st("vector", lambda e, cg=cg: e.tensor_scalar_max(out=modv[:, cg:cg + 16],
                                                          in0=modv[:, cg:cg + 16],
                                                          scalar1=0.25),
           waits=[(sSE, w_ab_)])
        w_gmul = st("vector", lambda e, cg=cg: e.tensor_tensor(
            out=modv[:, cg:cg + 16], in0=modv[:, cg:cg + 16],
            in1=modv[:, 112:128], op=ALU.mult))
    st("vector", lambda e: e.tensor_scalar_add(out=modv[:, 16:32], in0=modv[:, 16:32],
                                               scalar1=1.0))
    st("vector", lambda e: e.tensor_scalar_add(out=modv[:, 64:80], in0=modv[:, 64:80],
                                               scalar1=1.0))
    w_mod = sVE.n
    if dbg_t("d_mod", [128, 128]):
        st("sync", lambda e: e.dma_start(out=DBG["d_mod"][:, :], in_=modv[:, :]),
           waits=[(sVE, w_mod)], inc=sDBG, inck=16)

    w_rows_evac = 0
    w_hmul_last = 0
    w_rowfree = []
    for tc in range(4):
        for kc in range(NKC):
            st("tensor", lambda e, tc=tc, kc=kc: e.matmul(
                psA[:, :], onesb[:, :],
                h_sb[:, kc * T + tc * 512: kc * T + (tc + 1) * 512],
                start=(kc == 0), stop=(kc == NKC - 1)),
               waits=(([(sXL, w_xall), (sVE, w_consts)] +
                       ([(sVE, w_rows_evac)] if tc > 0 else []))
                      if kc == 0 else []),
               inc=(sMM if kc == NKC - 1 else None))
        w_sum = sMM.n
        for kc in range(NKC):
            w_slot = sMM.n
            st("scalar", lambda e, tc=tc, kc=kc: e.activation(
                out=xbb[:, (kc % 2) * 512:(kc % 2 + 1) * 512],
                in_=h_sb[:, kc * T + tc * 512: kc * T + (tc + 1) * 512],
                func=AF.Square),
               waits=([(sMM, sMM.n - 1)] if kc >= 2 else []) +
                     ([(sXL, w_xall)] if (tc == 0 and kc == 0) else []))
            w_sq = sSE.n
            st("tensor", lambda e, kc=kc: e.matmul(
                psB[:, :], onesb[:, :], xbb[:, (kc % 2) * 512:(kc % 2 + 1) * 512],
                start=(kc == 0), stop=(kc == NKC - 1)),
               waits=[(sSE, w_sq)], inc=sMM)
        w_sq_mm = sMM.n
        r0, r1, r2 = rows[0:1, 0:512], rows[0:1, 1024:1536], rows[0:1, 2048:2560]
        st("vector", lambda e: e.tensor_scalar_mul(out=r0, in0=psA[0:1, :], scalar1=1.0 / D),
           waits=[(sMM, w_sq_mm)] + (w_rowfree if tc > 0 else []))
        w_rows_evac = st("vector", lambda e: e.tensor_scalar_mul(
            out=r1, in0=psB[0:1, :], scalar1=1.0 / D))
        st("vector", lambda e: e.tensor_tensor(out=r2, in0=r0, in1=r0, op=ALU.mult))
        st("vector", lambda e: e.tensor_tensor(out=r1, in0=r1, in1=r2, op=ALU.subtract),
           inc=sVE)
        w_var = sVE.n
        st("scalar", lambda e: e.activation(out=r2, in_=r1, func=AF.Sqrt, bias=consts[0:1, 2:3]),
           waits=[(sVE, w_var)])
        w_std = sSE.n
        st("vector", lambda e: e.reciprocal(out=r1, in_=r2), waits=[(sSE, w_std)])
        st("vector", lambda e: e.tensor_tensor(out=r0, in0=r0, in1=r1, op=ALU.mult))
        st("vector", lambda e: e.tensor_scalar_mul(out=r0, in0=r0, scalar1=-1.0))
        w_ab = sVE.n
        if tc == 0 and dbg_t("d_stats", [2, T]):
            st("sync", lambda e: e.dma_start(out=DBG["d_stats"][0:1, 0:512], in_=r1),
               waits=[(sVE, w_ab)], inc=sDBG, inck=16)
            st("sync", lambda e: e.dma_start(out=DBG["d_stats"][1:2, 0:512], in_=r0),
               waits=[(sVE, w_ab)], inc=sDBG, inck=16)
        wbAw, wbA = bounce(0, r1, 512, 16 + tc, rbc[:, 0:512],
                           waits=[(sVE, w_ab)] + ([(sVE, w_hmul_last)] if tc > 0 else []))
        wbBw, wbB = bounce(1, r0, 512, 20 + tc, rbc[:, 512:1024], waits=[(sVE, w_ab)])
        w_rowfree = [wbAw, wbBw] + ([(sDBG, sDBG.n)] if tc == 0 and "d_stats" in DBG else [])
        for kc in range(NKC):
            sl = h_sb[:, kc * T + tc * 512: kc * T + (tc + 1) * 512]
            s2 = sm2[:, (kc % 2) * 512:(kc % 2 + 1) * 512]
            st("vector", lambda e, sl=sl, s2=s2: e.tensor_tensor(
                out=s2, in0=sl, in1=rbc[:, 0:512], op=ALU.mult),
               waits=([wbA, wbB] if kc == 0 else
                      [(sSE, sSE.n - 1)] if kc >= 2 else []))
            st("vector", lambda e, s2=s2: e.tensor_tensor(
                out=s2, in0=s2, in1=rbc[:, 512:1024], op=ALU.add))
            w_t = sVE.n
            w_hmul_last = w_t
            st("scalar", lambda e, sl=sl, s2=s2, kc=kc: e.activation(
                out=sl, in_=s2, func=AF.Identity,
                scale=modv[:, 16 + kc:17 + kc], bias=modv[:, kc:kc + 1]),
               waits=[(sVE, w_t)] + ([(sVE, w_mod)] if kc == 0 and tc == 0 else []),
               inc=sSE)
    st("scalar", lambda e: e.activation(out=tab[:, 5:6], in_=tab[:, 1:2], func=AF.Identity),
       inc=sSE)
    w_h = sSE.n
    if dbg_t("d_h", [128, NKC * T]):
        st("gpsimd", lambda e: e.dma_start(out=DBG["d_h"][:, :], in_=h_sb[:, :]),
           waits=[(sSE, w_h)], inc=sDBG2, inck=16)

    if phases <= 1:
        for kc in range(NKC):
            st("gpsimd", lambda e, kc=kc: e.dma_start(
                out=out_d[kc * 128:(kc + 1) * 128, :], in_=h_sb[:, kc * T:kc * T + OWN]),
               waits=[(sSE, w_h)] if kc == 0 else [], inc=sOUT, inck=16)
        w_fin = sOUT.n
        st("sync", lambda e: e.nop(), waits=[(sOUT, w_fin), (sDBG, sDBG.n), (sDBG2, sDBG2.n)])
        K.DBG = DBG
        return K

    dbg_t("d_k0", [128, T]); dbg_t("d_q0", [128, OWN]); dbg_t("d_v0", [128, NKC * 256])
    dbg_t("d_e0", [128, NKC * 512]); dbg_t("d_l", [16, OWN]); dbg_t("d_y", [128, NKC * OWN])
    dbg_t("d_x2", [D, OWN])

    scale_chunks = []
    for wd, nel in ((w1_d, DFF * D), (w2_d, DFF * D), (w3_d, DFF * D)):
        flat = wd.rearrange("a b c d -> (a b c d)")
        nch = nel // (128 * 1024)
        for i in range(nch):
            scale_chunks.append(flat[i * 131072:(i + 1) * 131072]
                                .rearrange("(p n) -> p n", p=128))
    n_scale = len(scale_chunks)
    scale_state = {"i": 0, "se": [0] * n_scale, "fold": [], "w_red": 0}

    def emit_fold(j):
        w_last = scale_state["se"][(j + 1) * 128 - 1]
        w_red = st("vector", lambda e: e.reduce_sum(out=accw[:, 128:129],
                                                    in_=accw[:, 0:128], axis=AX.X),
                   waits=[(sSE, w_last)])
        scale_state["w_red"] = w_red
        st("sync", lambda e, j=j: e.dma_start(out=bnc_d[8 + j:9 + j, 0:128],
                                              in_=accw[:, 128:129]),
           waits=[(sVE, w_red)], inc=sB[2], inck=16)
        w1 = sB[2].n
        st("sync", lambda e, j=j: e.dma_start(out=lrec[0:1, 0:128],
                                              in_=bnc_d[8 + j:9 + j, 0:128]),
           waits=[(sB[2], w1)], inc=sB[3], inck=16)
        w2 = sB[3].n
        w3 = st("vector", lambda e: e.reduce_sum(out=lrec[0:1, 128:129],
                                                 in_=lrec[0:1, 0:128], axis=AX.X),
                waits=[(sB[3], w2)])
        st("sync", lambda e, j=j: e.dma_start(out=bnc_d[8 + j:9 + j, 128:129],
                                              in_=lrec[0:1, 128:129]),
           waits=[(sVE, w3)], inc=sB[2], inck=16)
        w4 = sB[2].n
        st("sync", lambda e, j=j: e.dma_start(
            out=wsc[:, 7:8],
            in_=bnc_d[8 + j:9 + j, 128:129].broadcast_to((128, 1))),
           waits=[(sB[2], w4)], inc=sB[3], inck=16)
        w5 = sB[3].n
        st("vector", lambda e, j=j: e.tensor_scalar(
            out=wsc[:, 3 + j:4 + j], in0=wsc[:, 7:8],
            scalar1=1.0 / (DFF * D), scalar2=1e-5, op0=ALU.mult, op1=ALU.max),
           waits=[(sB[3], w5)])
        scale_state["fold"].append(st("vector", lambda e, j=j: e.tensor_scalar_mul(
            out=wsc[:, j:j + 1], in0=wsc[:, 3 + j:4 + j], scalar1=0.5)))

    scale_slots = [wfl[:, 0:1024], wfl[:, 1024:2048], rbc[:, 0:1024]]
    scale_sems = [sW0, sW1, sW2]
    w_ln1_dve_done = sVE.n
    w_ln1_se_done = w_h

    def emit_scale(nmax):
        for _ in range(nmax):
            i = scale_state["i"]
            if i >= n_scale:
                return
            scale_state["i"] = i + 1
            sl = i % 3
            sW = scale_sems[sl]
            slot = scale_slots[sl]
            wfree = scale_state["se"][i - 3] if i >= 3 else 0
            first_w = ([(sSE, wfree)] if i >= 3 else
                       ([(sVE, w_ln1_dve_done), (sSE, w_ln1_se_done)]
                        if sl >= 2 else []))
            for cc in range(2):
                st("sync", lambda e, i=i, slot=slot, cc=cc: e.dma_start(
                    out=slot[:, cc * 512:(cc + 1) * 512],
                    in_=scale_chunks[i][:, cc * 512:(cc + 1) * 512]),
                   waits=(first_w if cc == 0 else []), inc=sW, inck=16)
            wld = sW.n
            col = i % 128
            scale_state["se"][i] = st(
                "scalar", lambda e, slot=slot, col=col: e.activation(
                    out=slot, in_=slot,
                    func=AF.Abs, accum_out=accw[:, col:col + 1]),
                waits=[(sW, wld)] + ([(sVE, scale_state["w_red"])]
                                     if col == 0 and i > 0 else []))
            if (i + 1) % 128 == 0:
                emit_fold(i // 128)


    events = []
    for g in range(8):
        events.append(("V", g))
        events.append(("KQ", 2 * g))
        events.append(("KQ", 2 * g + 1))
    slot_of = {}
    for i, ev in enumerate(events):
        slot_of[ev] = i % 2
    ev_index = {ev: i for i, ev in enumerate(events)}
    slots = [wsl, wsl2]
    sAs = [sA0, sA1]

    scale_state["w_ones"] = st("vector", lambda e: e.memset(onesf128[:, :], 1.0))
    ev_ld = {}
    ev_done = {}

    def emit_ev_load(ev):
        i = ev_index[ev]
        sl = slot_of[ev]
        sA = sAs[sl]
        wcred = [(sMM, ev_done[events[i - 2]])] if i >= 2 else []
        if ev[0] == "V":
            g = ev[1]
            st("gpsimd", lambda e, sl=sl, g=g: e.dma_start(
                out=slots[sl][:, 0:4096], in_=wv_d[g]), waits=wcred, inc=sA, inck=16)
        else:
            h = ev[1]
            st("gpsimd", lambda e, sl=sl, h=h: e.dma_start(
                out=slots[sl][:, 0:2048], in_=wk_d[h]), waits=wcred, inc=sA, inck=16)
            st("gpsimd", lambda e, sl=sl, h=h: e.dma_start(
                out=slots[sl][:, 2048:4096], in_=wq_d[h]), inc=sA, inck=16)
        ev_ld[ev] = sA.n

    emit_ev_load(events[0])
    emit_ev_load(events[1])

    w_ln1_last_sq = sSE.n
    hrope = {}

    w_vgrp_evac = [0] * 8
    w_yevac = 0
    w_lrowcp = 0
    w_exp = {}
    for h in range(NH):
        g = h // 2
        kq_ev = ("KQ", h)
        v_ev = ("V", g)
        sl_kq = slot_of[kq_ev]
        sl_v = slot_of[v_ev]
        kslot = ksb[:, (h % 2) * T:(h % 2 + 1) * T]
        qslot = qsb[:, (h % 2) * OWN:(h % 2 + 1) * OWN]
        vslot = vsb[:, (g % 2) * NKC * 256:((g % 2) + 1) * NKC * 256]

        if h % 2 == 0:
            for kkc in range(NKC):
                for kc in range(NKC):
                    st("tensor", lambda e, kkc=kkc, kc=kc, sl=sl_v: e.matmul(
                        psE[:, 0:256], h_sb[:, kc * T + kkc * 128: kc * T + kkc * 128 + 128],
                        slots[sl][:, kc * 256:(kc + 1) * 256],
                        start=(kc == 0), stop=(kc == NKC - 1)),
                       waits=(([(sAs[sl_v], ev_ld[v_ev]), (sSE, w_ln1_last_sq)]
                               if kkc == 0 else []) +
                              ([(sVE, w_vgrp_evac[g])] if (kc == 0 and kkc > 0) else
                               [(sVE, w_vgrp_evac[g - 1] if g > 0 else 0)]
                               if kc == 0 else [])),
                       inc=(sMM if kc == NKC - 1 else None))
                w_mm = sMM.n
                w_vgrp_evac[g] = st("vector", lambda e, kkc=kkc, vslot=vslot: e.tensor_copy(
                    out=vslot[:, kkc * 256:(kkc + 1) * 256], in_=psE[:, 0:256]),
                    waits=[(sMM, w_mm)])
            ev_done[v_ev] = sMM.n
            ni = ev_index[v_ev] + 2
            if ni < len(events):
                emit_ev_load(events[ni])
            if h == 0 and "d_v0" in DBG:
                st("gpsimd", lambda e, vslot=vslot: e.dma_start(
                    out=DBG["d_v0"][:, :], in_=vslot[:, :]),
                   waits=[(sVE, w_vgrp_evac[0])], inc=sDBG2, inck=16)

        w_kq_done_mm = 0
        for tkc in range(4):
            pst = psF[:, 0:512]
            for kc in range(NKC):
                st("tensor", lambda e, tkc=tkc, kc=kc, sl=sl_kq: e.matmul(
                    pst, slots[sl][:, kc * 128:(kc + 1) * 128],
                    h_sb[:, kc * T + tkc * 512: kc * T + (tkc + 1) * 512],
                    start=(kc == 0), stop=(kc == NKC - 1)),
                   waits=([(sAs[sl_kq], ev_ld[kq_ev])] +
                          ([(sSE, hrope.get((h, tkc - 1), 0))] if tkc > 0 else
                           [(sSE, hrope.get((h - 1, "q1"), 0))]))
                   if kc == 0 else [],
                   inc=(sMM if kc == NKC - 1 else None))
            w_mm = sMM.n
            prev_v = (hrope.get((h, tkc - 1, "v"), 0) if tkc > 0
                      else hrope.get((h - 1, "q1", "v"), 0))
            hrope[(h, tkc)] = st("scalar", lambda e: e.activation(
                out=kraw[:, 0:512], in_=pst, func=AF.Identity),
                waits=[(sMM, w_mm), (sVE, prev_v)])
            st("vector", lambda e: e.tensor_copy(out=krot[0:32, 0:512],
                                                 in_=kraw[32:64, 0:512]),
               waits=[(sSE, hrope[(h, tkc)])])
            st("vector", lambda e: e.tensor_copy(out=krot[32:64, 0:512],
                                                 in_=kraw[0:32, 0:512]))
            st("vector", lambda e, tkc=tkc: e.tensor_tensor(
                out=sm2[0:64, 0:512], in0=kraw[0:64, 0:512],
                in1=cosT[:, tkc * 512:(tkc + 1) * 512], op=ALU.mult))
            st("vector", lambda e, tkc=tkc: e.tensor_tensor(
                out=krot[0:64, 0:512], in0=krot[0:64, 0:512],
                in1=ssinT[:, tkc * 512:(tkc + 1) * 512], op=ALU.mult))
            st("vector", lambda e, tkc=tkc, kslot=kslot: e.tensor_tensor(
                out=kslot[0:64, tkc * 512:(tkc + 1) * 512], in0=sm2[0:64, 0:512],
                in1=krot[0:64, 0:512], op=ALU.add))
            w_krope_v = st("vector", lambda e, tkc=tkc, kslot=kslot: e.tensor_copy(
                out=kslot[64:128, tkc * 512:(tkc + 1) * 512], in_=kraw[64:128, 0:512]))
            hrope[(h, tkc)] = hrope[(h, tkc)]
            hrope[(h, tkc, "v")] = w_krope_v
        for qkc in range(2):
            pst = psF[:, 0:512]
            for kc in range(NKC):
                st("tensor", lambda e, qkc=qkc, kc=kc, sl=sl_kq: e.matmul(
                    pst, slots[sl][:, 2048 + kc * 128: 2048 + (kc + 1) * 128],
                    h_sb[:, kc * T + qkc * 512: kc * T + (qkc + 1) * 512],
                    start=(kc == 0), stop=(kc == NKC - 1)),
                   waits=[(sSE, hrope[(h, 3)] if qkc == 0 else hrope[(h, "q0")])]
                   if kc == 0 else [],
                   inc=(sMM if kc == NKC - 1 else None))
            w_mm = sMM.n
            key = "q%d" % qkc
            prev_v = hrope[(h, 3, "v")] if qkc == 0 else hrope[(h, "q0", "v")]
            hrope[(h, key)] = st("scalar", lambda e: e.activation(
                out=kraw[:, 0:512], in_=pst, func=AF.Identity),
                waits=[(sMM, w_mm), (sVE, prev_v)])
            st("vector", lambda e: e.tensor_copy(out=krot[0:32, 0:512],
                                                 in_=kraw[32:64, 0:512]),
               waits=[(sSE, hrope[(h, key)])])
            st("vector", lambda e: e.tensor_copy(out=krot[32:64, 0:512],
                                                 in_=kraw[0:32, 0:512]))
            st("vector", lambda e, qkc=qkc: e.tensor_tensor(
                out=sm2[0:64, 0:512], in0=kraw[0:64, 0:512],
                in1=cosT[:, qkc * 512:(qkc + 1) * 512], op=ALU.mult))
            st("vector", lambda e, qkc=qkc: e.tensor_tensor(
                out=krot[0:64, 0:512], in0=krot[0:64, 0:512],
                in1=ssinT[:, qkc * 512:(qkc + 1) * 512], op=ALU.mult))
            st("vector", lambda e, qkc=qkc, qslot=qslot: e.tensor_tensor(
                out=qslot[0:64, qkc * 512:(qkc + 1) * 512], in0=sm2[0:64, 0:512],
                in1=krot[0:64, 0:512], op=ALU.add))
            hrope[(h, key, "v")] = st("vector", lambda e, qkc=qkc, qslot=qslot: e.tensor_copy(
                out=qslot[64:128, qkc * 512:(qkc + 1) * 512], in_=kraw[64:128, 0:512]))
        ev_done[kq_ev] = sMM.n
        ni = ev_index[kq_ev] + 2
        if ni < len(events):
            emit_ev_load(events[ni])
        if h == 0:
            if "d_k0" in DBG:
                st("gpsimd", lambda e, kslot=kslot: e.dma_start(out=DBG["d_k0"][:, :], in_=kslot[:, :]),
                   waits=[(sVE, hrope[(0, 3, "v")])], inc=sDBG2, inck=16)
            if "d_q0" in DBG:
                st("gpsimd", lambda e, qslot=qslot: e.dma_start(out=DBG["d_q0"][:, :], in_=qslot[:, :]),
                   waits=[(sVE, hrope[(0, "q1", "v")])], inc=sDBG2, inck=16)

        w_rope_all = max(hrope[(h, 3, "v")], hrope[(h, "q1", "v")],
                         hrope[(h, 0, "v")], hrope[(h, 1, "v")], hrope[(h, 2, "v")],
                         hrope[(h, "q0", "v")])
        for qh in range(2):
            ps_y = psC if qh == 0 else psD
            ps_l = psG if qh == 0 else psH

            def emit_lav(kkc, qh=qh, ps_y=ps_y, ps_l=ps_l):
                wexp = w_exp[(h, qh, kkc)]
                st("tensor", lambda e, kkc=kkc, ps_l=ps_l: e.matmul(
                    ps_l[:, 0:512], onesb[:, :], esb[:, kkc * 512:(kkc + 1) * 512],
                    start=(kkc == 0), stop=(kkc == NKC - 1)),
                   waits=[(sSE, wexp)] + ([(sVE, w_lrowcp)] if kkc == 0 else []),
                   inc=(sMM if kkc == NKC - 1 else None))
                st("tensor", lambda e, kkc=kkc, qh=qh, ps_y=ps_y, vslot=vslot,
                   h=h: e.matmul(
                       ps_y[:, 0:512],
                       vslot[:, kkc * 256 + (h % 2) * 128: kkc * 256 + (h % 2) * 128 + 128],
                       esb[:, kkc * 512:(kkc + 1) * 512],
                       start=(kkc == 0), stop=(kkc == NKC - 1)),
                   inc=(sMM if kkc == NKC - 1 else None))

            for kkc in range(NKC):
                ps_s = psA if kkc % 2 == 0 else psB
                st("tensor", lambda e, kkc=kkc, qh=qh, ps_s=ps_s, kslot=kslot,
                   qslot=qslot: e.matmul(
                       ps_s[:, 0:512], kslot[:, kkc * 128:(kkc + 1) * 128],
                       qslot[:, qh * 512:(qh + 1) * 512], start=True, stop=True),
                   waits=([(sVE, w_rope_all), (sVE, w_vgrp_evac[g]),
                           (sSE, w_yevac), (sVE, w_lrowcp)] if (qh == 0 and kkc == 0)
                          else [(sSE, w_exp.get((h, qh, kkc - 2),
                                                w_exp.get((h, 0, 14 + kkc), 0)))]),
                   inc=sMM)
                w_smm = sMM.n
                prev_avdone = (w_exp.get(("avdone", h, 0), 0) if qh == 1
                               else w_exp.get(("avdone", h - 1, 1), 0))
                w_exp[(h, qh, kkc)] = st("scalar", lambda e, kkc=kkc, ps_s=ps_s: e.activation(
                    out=esb[:, kkc * 512:(kkc + 1) * 512], in_=ps_s[:, 0:512],
                    func=AF.Exp, bias=consts[:, 0:1], scale=SCALE_S),
                    waits=([(sMM, w_smm), (sMM, prev_avdone), (sDBG2, sDBG2.n)]
                           if kkc == 0 else [(sMM, w_smm)]))
                if kkc >= 2:
                    emit_lav(kkc - 2)
                emit_scale(1)
            emit_lav(NKC - 2)
            emit_lav(NKC - 1)
            w_exp[("avdone", h, qh)] = sMM.n
            w_yevac = st("scalar", lambda e, h=h, qh=qh, ps_y=ps_y: e.activation(
                out=ysb[:, h * OWN + qh * 512: h * OWN + (qh + 1) * 512],
                in_=ps_y[:, 0:512], func=AF.Identity), waits=[(sMM, sMM.n)])
            w_lrowcp = st("vector", lambda e, qh=qh, ps_l=ps_l: e.tensor_copy(
                out=rows[0:1, qh * 512:(qh + 1) * 512], in_=ps_l[0:1, 0:512]),
                waits=[(sMM, sMM.n), (sLB, sLB.n)])
        st("sync", lambda e, h=h: e.dma_start(out=bnc_d[32 + h:33 + h, 0:1024],
                                              in_=rows[0:1, 0:1024]),
           waits=[(sVE, w_lrowcp)], inc=sLB, inck=16)
        if h == e_head and "d_e0" in DBG:
            st("gpsimd", lambda e: e.dma_start(out=DBG["d_e0"][:, :], in_=esb[:, :]),
               waits=[(sSE, w_exp[(e_head, 1, NKC - 1)])], inc=sDBG2, inck=16)

    st("sync", lambda e: e.dma_start(out=lrec[:, :], in_=bnc_d[32:48, 0:1024]),
       waits=[(sLB, sLB.n),
              (sVE, scale_state["fold"][-1] if scale_state["fold"] else 0)],
       inc=sLR, inck=16)
    w_lrl = sLR.n
    w_rec = st("vector", lambda e: e.reciprocal(out=lrec[:, :], in_=lrec[:, :]),
               waits=[(sLR, w_lrl)])
    st("sync", lambda e: e.dma_start(out=bnc_d[48:64, 0:1024], in_=lrec[:, :]),
       waits=[(sVE, w_rec)], inc=sLR, inck=16)
    w_lrw = sLR.n
    if "d_l" in DBG:
        st("sync", lambda e: e.dma_start(out=DBG["d_l"][:, :], in_=lrec[:, :]),
           waits=[(sVE, w_rec)], inc=sDBG, inck=16)
    w_ynorm = 0
    for h in range(NH):
        rb = rbc if h % 2 == 0 else sm2
        sRB = sRB0 if h % 2 == 0 else sRB1
        st("sync", lambda e, h=h, rb=rb: e.dma_start(
            out=rb[:, 0:1024],
            in_=bnc_d[48 + h:49 + h, 0:1024].broadcast_to((128, 1024))),
           waits=[(sLR, w_lrw)] + ([(sVE, w_ynorm)] if h >= 2 else
                                    [(sVE, scale_state["fold"][-1]
                                      if scale_state["fold"] else 0)]),
           inc=sRB, inck=16)
        wrb = sRB.n
        w_ynorm = st("vector", lambda e, h=h, rb=rb: e.tensor_tensor(
            out=ysb[:, h * OWN:(h + 1) * OWN], in0=ysb[:, h * OWN:(h + 1) * OWN],
            in1=rb[:, 0:1024], op=ALU.mult),
            waits=[(sRB, wrb), (sSE, w_yevac)])
    if "d_y" in DBG:
        st("gpsimd", lambda e: e.dma_start(out=DBG["d_y"][:, :], in_=ysb[:, :]),
           waits=[(sVE, w_ynorm)], inc=sDBG3, inck=16)
    w_dy_tap = sDBG3.n if "d_y" in DBG else 0

    emit_scale(n_scale)
    w_scale_fold = scale_state["fold"]
    if "d_ws" in dbg:
        DBG["d_ws"] = nc.dram_tensor("d_ws", [1, 16], F32, kind="ExternalOutput").ap()
        st("sync", lambda e: e.dma_start(out=DBG["d_ws"][0:1, 0:8], in_=wsc[0:1, :]),
           waits=[(sVE, w_scale_fold[-1])], inc=sDBG4, inck=16)

    wo_slot = [wsl[:, 0:2048], wsl[:, 2048:4096]]
    w_last_attn_ev = ev_done[events[-1]]
    wo_done = {}

    def emit_wo_load(dc):
        sl = dc % 2
        sA = sAs[sl]
        st("gpsimd", lambda e, dc=dc, sl=sl: e.dma_start(
            out=wo_slot[sl], in_=wo_d[dc]),
           waits=[(sMM, wo_done[dc - 2])] if dc >= 2 else
                 [(sMM, ev_done[events[-2 + dc]])], inc=sA, inck=16)
        ev_ld[("WO", dc)] = sA.n

    emit_wo_load(0)
    emit_wo_load(1)
    w_x2 = {}
    w_stmm = {}
    w_x2spill = []
    for dc in range(NKC):
        sl = dc % 2
        sXO = sXO0 if dc % 2 == 0 else sXO1
        xsl = xst[:, (dc % 2) * 1024:(dc % 2 + 1) * 1024]
        st("sync", lambda e, dc=dc, xsl=xsl: e.dma_start(
            out=xsl, in_=xc[dc * 128:(dc + 1) * 128, 0:OWN]),
           waits=[(sVE, w_x2.get(dc - 2, 0))] +
                 ([(sXS0 if dc % 2 == 0 else sXS1, w_x2spill[dc - 2])]
                  if dc >= 2 else []) +
                 ([(sMM, w_last_attn_ev)] if dc < 2 else []), inc=sXO, inck=16)
        w_xo = sXO.n
        for qh in range(2):
            ps_o = (psA if qh == 0 else psB) if dc % 2 == 0 else (psC if qh == 0 else psD)
            for kc in range(NKC):
                st("tensor", lambda e, dc=dc, qh=qh, kc=kc, sl=sl, ps_o=ps_o: e.matmul(
                    ps_o[:, 0:512], wo_slot[sl][:, kc * 128:(kc + 1) * 128],
                    ysb[:, kc * OWN + qh * 512: kc * OWN + (qh + 1) * 512],
                    start=(kc == 0), stop=(kc == NKC - 1)),
                   waits=([(sAs[sl], ev_ld[("WO", dc)]), (sVE, w_ynorm)] +
                          ([(sSE, w_x2.get(("se", dc - 2, qh), 0))] if dc >= 2 else []))
                   if kc == 0 else [],
                   inc=(sMM if kc == NKC - 1 else None))
            w_omm = sMM.n
            w_x2[("se", dc, qh)] = st("scalar", lambda e, dc=dc, qh=qh, ps_o=ps_o: e.activation(
                out=kraw[:, 0:512], in_=ps_o[:, 0:512], func=AF.Identity,
                scale=modv[:, 32 + dc:33 + dc]),
                waits=[(sMM, w_omm)] + ([(sVE, w_x2.get(("tt", dc, 0), 0))]
                                        if qh == 1 else
                                        [(sVE, w_x2.get(("tt", dc - 1, 1), 0))]))
            w_x2[("tt", dc, qh)] = st("vector", lambda e, dc=dc, qh=qh, xsl=xsl: e.tensor_tensor(
                out=xsl[:, qh * 512:(qh + 1) * 512], in0=xsl[:, qh * 512:(qh + 1) * 512],
                in1=kraw[:, 0:512], op=ALU.add),
                waits=[(sXO, w_xo), (sSE, w_x2[("se", dc, qh)])])
            st("vector", lambda e, qh=qh, xsl=xsl: e.tensor_copy(
                out=xbb[:, qh * 512:(qh + 1) * 512], in_=xsl[:, qh * 512:(qh + 1) * 512]),
               waits=[(sMM, w_stmm.get((dc - 1, qh), 0))])
            w_x2[dc] = sVE.n
            w_sq2 = st("scalar", lambda e, qh=qh: e.activation(
                out=sqb2[:, 0:512], in_=xbb[:, qh * 512:(qh + 1) * 512],
                func=AF.Square),
                waits=[(sVE, w_x2[dc]),
                       (sMM, w_stmm.get((dc, qh - 1),
                                        w_stmm.get((dc - 1, 1), 0)))])
            ps_sum = psE if qh == 0 else psF
            ps_sq = psG if qh == 0 else psH
            st("tensor", lambda e, dc=dc, qh=qh, ps_sum=ps_sum: e.matmul(
                ps_sum[:, 0:512], onesb[:, :], xbb[:, qh * 512:(qh + 1) * 512],
                start=(dc == 0), stop=(dc == NKC - 1)),
               waits=[(sVE, w_x2[dc])], inc=sMM)
            st("tensor", lambda e, dc=dc, qh=qh, ps_sq=ps_sq: e.matmul(
                ps_sq[:, 0:512], onesb[:, :], sqb2[:, 0:512],
                start=(dc == 0), stop=(dc == NKC - 1)),
               waits=[(sSE, w_sq2)], inc=sMM)
            w_stmm[(dc, qh)] = sMM.n
        wo_done[dc] = sMM.n
        if dc + 2 < NKC:
            emit_wo_load(dc + 2)
        sXSp = sXS0 if dc % 2 == 0 else sXS1
        st("sync", lambda e, dc=dc, xsl=xsl: e.dma_start(
            out=x2s_d[dc * 128:(dc + 1) * 128, :], in_=xsl),
           waits=[(sVE, w_x2[dc])], inc=sXSp, inck=16)
        w_x2spill.append(sXSp.n)
    w_ln2_mm = sMM.n
    if "d_x2" in DBG:
        st("sync", lambda e: e.dma_start(out=DBG["d_x2"][:, :], in_=x2s_d[:, :])
           if False else e.nop())
        for dc in range(NKC):
            st("gpsimd", lambda e, dc=dc: e.dma_start(
                out=DBG["d_x2"][dc * 128:(dc + 1) * 128, :],
                in_=x2s_d[dc * 128:(dc + 1) * 128, :]),
               waits=[(sXS0 if dc % 2 == 0 else sXS1, w_x2spill[dc])],
               inc=sDBG2, inck=16)

    if phases <= 2:
        for kc in range(NKC):
            st("gpsimd", lambda e, kc=kc: e.dma_start(
                out=out_d[kc * 128:(kc + 1) * 128, :], in_=ysb[:, kc * OWN:kc * OWN + OWN]),
               waits=[(sVE, w_ynorm)] if kc == 0 else [], inc=sOUT, inck=16)
        w_fin = sOUT.n
        st("sync", lambda e: e.nop(),
           waits=[(sOUT, w_fin), (sDBG, sDBG.n), (sDBG2, sDBG2.n),
                  (sXS0, sXS0.n), (sXS1, sXS1.n), (sMM, sMM.n)])
        K.DBG = DBG
        return K

    dbg_t("d_h2", [128, NKC * OWN]); dbg_t("d_q12", [128, NKC * OWN])
    dbg_t("d_hp", [128, 2 * OWN]); dbg_t("d_q3", [128, 2 * OWN])
    dbg_t("d_rows", [1, 2048])
    rr0 = rows[0:1, 0:1024]
    rr1 = rows[0:1, 1024:2048]
    rr2 = rows[0:1, 2048:3072]
    st("vector", lambda e: e.tensor_scalar_mul(out=rows[0:1, 0:512], in0=psE[0:1, 0:512],
                                               scalar1=1.0 / D),
       waits=[(sMM, w_ln2_mm), (sLR, sLR.n), (sRB0, sRB0.n), (sRB1, sRB1.n)])
    st("vector", lambda e: e.tensor_scalar_mul(out=rows[0:1, 512:1024],
                                               in0=psF[0:1, 0:512], scalar1=1.0 / D))
    st("vector", lambda e: e.tensor_scalar_mul(out=rows[0:1, 1024:1536],
                                               in0=psG[0:1, 0:512], scalar1=1.0 / D))
    w_r4 = st("vector", lambda e: e.tensor_scalar_mul(out=rows[0:1, 1536:2048],
                                                      in0=psH[0:1, 0:512], scalar1=1.0 / D))
    st("vector", lambda e: e.tensor_tensor(out=rr2, in0=rr0, in1=rr0, op=ALU.mult))
    st("vector", lambda e: e.tensor_tensor(out=rr1, in0=rr1, in1=rr2, op=ALU.subtract))
    w_var2 = sVE.n
    st("scalar", lambda e: e.activation(out=rr2, in_=rr1, func=AF.Sqrt,
                                        bias=consts[0:1, 2:3]), waits=[(sVE, w_var2)])
    w_std2 = sSE.n
    st("vector", lambda e: e.reciprocal(out=rr1, in_=rr2), waits=[(sSE, w_std2)])
    st("vector", lambda e: e.tensor_tensor(out=rr0, in0=rr0, in1=rr1, op=ALU.mult))
    w_ab2 = st("vector", lambda e: e.tensor_scalar_mul(out=rr0, in0=rr0, scalar1=-1.0))
    wbA2w, wbA2 = bounce(0, rr1, 1024, 16, rbc[:, 0:1024], waits=[(sVE, w_ab2)])
    wbB2w, wbB2 = bounce(1, rr0, 1024, 17, sm2[:, 0:1024], waits=[(sVE, w_ab2)])
    w_h2 = {}
    w_h2mm = {}
    w_maxinit = st("vector", lambda e: e.memset(maxac[:, :], 0.0))
    for dc in range(NKC):
        sXO = sXO0 if dc % 2 == 0 else sXO1
        xsl = xst[:, (dc % 2) * 1024:(dc % 2 + 1) * 1024]
        st("sync", lambda e, dc=dc, xsl=xsl: e.dma_start(
            out=xsl, in_=x2s_d[dc * 128:(dc + 1) * 128, :]),
           waits=([(sVE, w_h2.get(dc - 2, 0)), (sSE, w_h2.get(("aff", dc - 2), 0))]
                  if dc >= 2 else
                  [(sXS0, sXS0.n), (sXS1, sXS1.n)]), inc=sXO, inck=16)
        w_xo = sXO.n
        st("vector", lambda e, xsl=xsl: e.tensor_tensor(
            out=xsl, in0=xsl, in1=rbc[:, 0:1024], op=ALU.mult),
           waits=[(sXO, w_xo), wbA2, wbB2])
        w_t2 = st("vector", lambda e, xsl=xsl: e.tensor_tensor(
            out=xsl, in0=xsl, in1=sm2[:, 0:1024], op=ALU.add))
        w_aff = st("scalar", lambda e, dc=dc, xsl=xsl: e.activation(
            out=h2sb[:, dc * OWN:(dc + 1) * OWN], in_=xsl, func=AF.Identity,
            scale=modv[:, 64 + dc:65 + dc], bias=modv[:, 48 + dc:49 + dc]),
            waits=[(sVE, w_t2)] + ([(sDBG3, w_dy_tap)] if dc == 0 else []))
        w_h2[("aff", dc)] = w_aff
        w_h2[dc] = sVE.n
        for th in range(2):
            w_sqh = st("scalar", lambda e, dc=dc, th=th: e.activation(
                out=sqb2[:, 0:512],
                in_=h2sb[:, dc * OWN + th * 512: dc * OWN + (th + 1) * 512],
                func=AF.Square),
                waits=[(sMM, w_h2mm.get((dc, th - 1),
                                        w_h2mm.get((dc - 1, 1), 0))),
                       (sVE, w_h2.get(("mx", dc, th - 1),
                                      w_h2.get(("mx", dc - 1, 1), 0)))])
            ps_sq = psA if th == 0 else psB
            st("tensor", lambda e, dc=dc, th=th, ps_sq=ps_sq: e.matmul(
                ps_sq[:, 0:512], onesb[:, :], sqb2[:, 0:512],
                start=(dc == 0), stop=(dc == NKC - 1)),
               waits=[(sSE, w_sqh)], inc=sMM)
            w_h2mm[(dc, th)] = sMM.n
            w_h2[("mx", dc, th)] = st("vector", lambda e, th=th: e.tensor_tensor(
                out=maxac[:, th * 512:(th + 1) * 512], in0=maxac[:, th * 512:(th + 1) * 512],
                in1=sqb2[:, 0:512], op=ALU.max),
                waits=[(sSE, w_sqh)] + ([(sVE, w_maxinit)] if dc == 0 else []))
    if "d_h2" in DBG:
        st("gpsimd", lambda e: e.dma_start(out=DBG["d_h2"][:, :], in_=h2sb[:, :]),
           waits=[(sSE, sSE.n)], inc=sDBG2, inck=16)
    st("vector", lambda e: e.tensor_scalar_mul(
        out=rows[0:1, 1024:1536], in0=psA[0:1, 0:512], scalar1=1.0 / D),
       waits=[(sMM, w_h2mm[(NKC - 1, 0)]), wbA2w])
    w_rmsev = st("vector", lambda e: e.tensor_scalar_mul(
        out=rows[0:1, 1536:2048], in0=psB[0:1, 0:512], scalar1=1.0 / D),
        waits=[(sMM, w_h2mm[(NKC - 1, 1)])])
    st("vector", lambda e: e.tensor_copy(out=tmpb[0:64, :], in_=maxac[64:128, :]),
       waits=[(sVE, w_h2[("mx", NKC - 1, 1)])])
    st("vector", lambda e: e.tensor_tensor(out=maxac[0:64, :], in0=maxac[0:64, :],
                                           in1=tmpb[0:64, :], op=ALU.max))
    st("vector", lambda e: e.tensor_copy(out=tmpb[0:32, :], in_=maxac[32:64, :]))
    w_mx32 = st("vector", lambda e: e.tensor_tensor(out=maxac[0:32, :], in0=maxac[0:32, :],
                                                    in1=tmpb[0:32, :], op=ALU.max))
    wsh = w_mx32
    for npart in (16, 8, 4, 2, 1):
        st("sync", lambda e, npart=npart: e.dma_start(
            out=tmpb[0:npart, :], in_=maxac[npart:2 * npart, :]
            if npart >= 32 else maxac[npart:2 * npart, :]),
           waits=[(sVE, wsh)], inc=sB[4], inck=16)
        wdma = sB[4].n
        wsh = st("vector", lambda e, npart=npart: e.tensor_tensor(
            out=maxac[0:npart, :], in0=maxac[0:npart, :], in1=tmpb[0:npart, :],
            op=ALU.max), waits=[(sB[4], wdma)])
    st("scalar", lambda e: e.activation(out=rr2, in_=rr1, func=AF.Sqrt,
                                        bias=consts[0:1, 3:4]),
       waits=[(sVE, wsh), (sVE, w_rmsev)])
    w_s3 = sSE.n
    st("vector", lambda e: e.reciprocal(out=rr1, in_=rr2), waits=[(sSE, w_s3)])
    st("scalar", lambda e: e.activation(out=rr0, in_=maxac[0:1, 0:1024], func=AF.Sqrt),
       waits=[(sVE, sVE.n)])
    w_s4 = sSE.n
    st("vector", lambda e: e.tensor_tensor(out=rr0, in0=rr0, in1=rr1, op=ALU.mult),
       waits=[(sSE, w_s4)])
    st("vector", lambda e: e.tensor_scalar_max(out=rr0, in0=rr0, scalar1=1e-5))
    st("vector", lambda e: e.reciprocal(out=rr2, in_=rr0))
    st("vector", lambda e: e.tensor_tensor(out=rr2, in0=rr2, in1=rr1, op=ALU.mult))
    w_cq = st("vector", lambda e: e.tensor_scalar_mul(out=rr2, in0=rr2, scalar1=127.0))
    st("vector", lambda e: e.tensor_scalar(out=rr1, in0=rr0, scalar1=1.0 / 127.0,
                                           scalar2=wsc[0:1, 3:4],
                                           op0=ALU.mult, op1=ALU.mult))
    w_du1 = sVE.n
    st("vector", lambda e: e.tensor_scalar(out=rr0, in0=rr0, scalar1=1.0 / 127.0,
                                           scalar2=wsc[0:1, 4:5],
                                           op0=ALU.mult, op1=ALU.mult))
    w_du2 = sVE.n
    wbCQw, wbCQ = bounce(2, rr2, 1024, 18, maxac[:, 0:1024], waits=[(sVE, w_cq)])
    wbD1w, wbD1 = bounce(0, rr1, 1024, 19, du1b[:, 0:1024], waits=[(sVE, w_du1), wbA2])
    if "d_rows" in DBG:
        st("sync", lambda e: e.dma_start(out=DBG["d_rows"][0:1, 0:1024], in_=rr2),
           waits=[(sVE, w_cq)], inc=sDBG, inck=16)
        st("sync", lambda e: e.dma_start(out=DBG["d_rows"][0:1, 1024:2048], in_=rr0),
           waits=[(sVE, w_du2)], inc=sDBG, inck=16)
    if "d_cqb" in dbg:
        DBG["d_cqb"] = nc.dram_tensor("d_cqb", [128, 2048], F32, kind="ExternalOutput").ap()
        st("gpsimd", lambda e: e.dma_start(out=DBG["d_cqb"][:, 0:1024], in_=maxac[:, 0:1024]),
           waits=[wbCQ], inc=sDBG5, inck=16)
        st("gpsimd", lambda e: e.dma_start(out=DBG["d_cqb"][:, 1024:2048], in_=du1b[:, 0:1024]),
           waits=[wbD1], inc=sDBG5, inck=16)
    w_q12 = 0
    for dc in range(NKC):
        for th in range(2):
            sl = h2sb[:, dc * OWN + th * 512: dc * OWN + (th + 1) * 512]
            dst = q12[:, dc * OWN + th * 512: dc * OWN + (th + 1) * 512]
            ta = rbc[:, th * 512:(th + 1) * 512]
            tb = sm2[:, th * 512:(th + 1) * 512]
            bb = tmpb[:, th * 512:(th + 1) * 512]
            cqsl = maxac[:, th * 512:(th + 1) * 512]
            st("vector", lambda e, sl=sl, ta=ta, cqsl=cqsl: e.tensor_tensor(
                out=ta, in0=sl, in1=cqsl, op=ALU.mult),
               waits=[wbCQ, wbD1] if (dc == 0 and th == 0) else [])
            st("vector", lambda e, ta=ta, tb=tb: e.tensor_scalar_add(
                out=tb, in0=ta, scalar1=128.5))
            for kk in (128.0, 64.0, 32.0, 16.0, 8.0, 4.0, 2.0, 1.0):
                st("vector", lambda e, tb=tb, bb=bb, kk=kk: e.tensor_scalar(
                    out=bb, in0=tb, scalar1=kk, scalar2=kk,
                    op0=ALU.is_ge, op1=ALU.mult))
                st("vector", lambda e, tb=tb, bb=bb: e.tensor_tensor(
                    out=tb, in0=tb, in1=bb, op=ALU.subtract))
            st("vector", lambda e, ta=ta: e.tensor_scalar_add(out=ta, in0=ta,
                                                              scalar1=128.5))
            st("vector", lambda e, ta=ta, tb=tb: e.tensor_tensor(
                out=ta, in0=ta, in1=tb, op=ALU.subtract))
            w_q12 = st("vector", lambda e, ta=ta, dst=dst: e.tensor_scalar_add(
                out=dst, in0=ta, scalar1=-128.0))
    if "d_q12" in DBG:
        st("gpsimd", lambda e: e.dma_start(out=DBG["d_q12"][:, :], in_=q12[:, :]),
           waits=[(sVE, w_q12)], inc=sDBG2, inck=16)

    st("vector", lambda e: e.tensor_scalar_mul(out=wsc[:, 6:7], in0=wsc[:, 0:1],
                                               scalar1=-1.0),
       waits=[(sVE, w_scale_fold[-1])] + ([(sDBG4, sDBG4.n)] if "d_ws" in DBG else []))
    w_negt = st("vector", lambda e: e.tensor_scalar_mul(out=wsc[:, 7:8], in0=wsc[:, 1:2],
                                                        scalar1=-1.0))
    w_f1mm = {}
    w_tern = {}
    w_hspill = {}
    w_hv = {}
    w_maxi3 = st("vector", lambda e: e.memset(maxac[:, :], 0.0),
                 waits=[(sVE, w_q12)] + ([(sDBG5, 32)] if "d_cqb" in DBG else []))
    def f1_loads(dc2):
        sW = sW0 if dc2 % 2 == 0 else sW1
        w_dst1 = wfl if dc2 % 2 == 0 else wfl3
        w_dst2 = wfl2 if dc2 % 2 == 0 else wfl4
        for cc in range(4):
            st("sync", lambda e, dc2=dc2, cc=cc, w_dst1=w_dst1: e.dma_start(
                out=w_dst1[:, cc * 512:(cc + 1) * 512],
                in_=w1_d[dc2][:, 4 * cc:4 * cc + 4, :]),
               waits=([(sVE, w_tern.get(dc2 - 2, 0))] if cc == 0 else []),
               inc=sW, inck=16)
            st("sync", lambda e, dc2=dc2, cc=cc, w_dst2=w_dst2: e.dma_start(
                out=w_dst2[:, cc * 512:(cc + 1) * 512],
                in_=w2_d[dc2][:, 4 * cc:4 * cc + 4, :]),
               inc=sW, inck=16)
        return sW.n

    def f1_tern(dc2, w_wld):
        sW = sW0 if dc2 % 2 == 0 else sW1
        wslc1 = wfl if dc2 % 2 == 0 else wfl3
        wslc2 = wfl2 if dc2 % 2 == 0 else wfl4
        tA = ternA if dc2 % 2 == 0 else ternA2
        tB = ternB if dc2 % 2 == 0 else ternB2
        st("vector", lambda e, wslc1=wslc1: e.tensor_scalar(
            out=terntmp[:, :], in0=wslc1[:, :], scalar1=wsc[:, 0:1], scalar2=-1.0,
            op0=ALU.is_ge, op1=ALU.add),
           waits=[(sW, w_wld), (sVE, w_negt)] +
                 ([(sMM, w_f1mm.get(dc2 - 2, 0))] if dc2 >= 2 else
                  [(sDBG2, sDBG2.n)]))
        st("vector", lambda e, wslc1=wslc1, tA=tA: e.scalar_tensor_tensor(
            out=tA[:, :], in0=wslc1[:, :], scalar=wsc[:, 6:7], in1=terntmp[:, :],
            op0=ALU.is_gt, op1=ALU.add))
        st("vector", lambda e, wslc2=wslc2: e.tensor_scalar(
            out=terntmp[:, :], in0=wslc2[:, :], scalar1=wsc[:, 1:2], scalar2=-1.0,
            op0=ALU.is_ge, op1=ALU.add))
        w_tern[dc2] = st("vector", lambda e, wslc2=wslc2, tB=tB: e.scalar_tensor_tensor(
            out=tB[:, :], in0=wslc2[:, :], scalar=wsc[:, 7:8], in1=terntmp[:, :],
            op0=ALU.is_gt, op1=ALU.add))

    w_ld = {0: f1_loads(0)}
    f1_tern(0, w_ld[0])
    for dc2 in range(NFC):
        if dc2 + 1 < NFC:
            w_ld[dc2 + 1] = f1_loads(dc2 + 1)
            f1_tern(dc2 + 1, w_ld[dc2 + 1])
        tA = ternA if dc2 % 2 == 0 else ternA2
        tB = ternB if dc2 % 2 == 0 else ternB2
        psU0 = psA if dc2 % 2 == 0 else psE
        psU1 = psB if dc2 % 2 == 0 else psF
        for th in range(2):
            for kc in range(NKC):
                st("tensor", lambda e, th=th, kc=kc, tA=tA, psU0=psU0, psU1=psU1:
                   e.matmul(
                       (psU0 if th == 0 else psU1)[:, 0:512],
                       tA[:, kc * 128:(kc + 1) * 128],
                       q12[:, kc * OWN + th * 512: kc * OWN + (th + 1) * 512],
                       start=(kc == 0), stop=(kc == NKC - 1)),
                   waits=([(sVE, w_tern[dc2]), (sVE, w_q12)] +
                          ([(sVE, w_hv.get(("udq", dc2 - 2, th), 0))])
                          if kc == 0 else []),
                   inc=(sMM if kc == NKC - 1 else None))
        for th in range(2):
            for kc in range(NKC):
                st("tensor", lambda e, th=th, kc=kc, tB=tB: e.matmul(
                    (psC if th == 0 else psD)[:, 0:512],
                    tB[:, kc * 128:(kc + 1) * 128],
                    q12[:, kc * OWN + th * 512: kc * OWN + (th + 1) * 512],
                    start=(kc == 0), stop=(kc == NKC - 1)),
                   waits=([(sVE, w_hv.get((dc2 - 1, th), 0))] if kc == 0 else []),
                   inc=(sMM if kc == NKC - 1 else None))
        w_f1mm[dc2] = sMM.n
        for th in range(2):
            psu = psU0 if th == 0 else psU1
            pst2 = psC if th == 0 else psD
            hsl = hch[:, ((2 * dc2 + th) % 4) * 512:(((2 * dc2 + th) % 4) + 1) * 512]
            sHSp = sHS[(2 * dc2 + th) % 4]
            st("vector", lambda e, psu=psu, th=th: e.tensor_tensor(
                out=rbc[:, th * 512:(th + 1) * 512], in0=psu[:, 0:512],
                in1=du1b[:, th * 512:(th + 1) * 512], op=ALU.mult),
               waits=[(sMM, w_f1mm[dc2])] +
                     ([(sHSp, sHSp.n)] if dc2 >= 2 else []))
            w_udq = sVE.n
            w_hv[("udq", dc2, th)] = w_udq
            st("scalar", lambda e, th=th: e.activation(
                out=sm2[:, th * 512:(th + 1) * 512], in_=rbc[:, th * 512:(th + 1) * 512],
                func=AF.Silu), waits=[(sVE, w_udq)])
            w_sil = sSE.n
            w_hv[(dc2, th)] = st("vector", lambda e, pst2=pst2, th=th, hsl=hsl: e.tensor_tensor(
                out=hsl, in0=sm2[:, th * 512:(th + 1) * 512], in1=pst2[:, 0:512],
                op=ALU.mult), waits=[(sSE, w_sil)])
            st("sync", lambda e, dc2=dc2, th=th, hsl=hsl: e.dma_start(
                out=hs_d[dc2, :, th * 512:(th + 1) * 512], in_=hsl),
               waits=[(sVE, w_hv[(dc2, th)])], inc=sHSp, inck=16)
            w_hspill[(dc2, th)] = sHSp.n
            w_sq3 = st("scalar", lambda e, hsl=hsl: e.activation(
                out=sqb2[:, 0:512], in_=hsl, func=AF.Square),
                waits=[(sVE, w_hv[(dc2, th)]),
                       (sMM, w_f1mm.get(("st", dc2, th - 1),
                                        w_f1mm.get(("st", dc2 - 1, 1), 0))),
                       (sVE, w_hv.get(("mx", dc2, th - 1),
                                      w_hv.get(("mx", dc2 - 1, 1), 0)))])
            ps_sq = psG if th == 0 else psH
            st("tensor", lambda e, dc2=dc2, th=th, ps_sq=ps_sq: e.matmul(
                ps_sq[:, 0:512], onesb[:, :], sqb2[:, 0:512],
                start=(dc2 == 0), stop=(dc2 == NFC - 1)),
               waits=[(sSE, w_sq3)], inc=sMM)
            w_f1mm[("st", dc2, th)] = sMM.n
            w_hv[("mx", dc2, th)] = st("vector", lambda e, th=th: e.tensor_tensor(
                out=maxac[:, th * 512:(th + 1) * 512],
                in0=maxac[:, th * 512:(th + 1) * 512], in1=sqb2[:, 0:512], op=ALU.max),
                waits=[(sSE, w_sq3)] + ([(sVE, w_maxi3)] if dc2 == 0 else []))
    if "d_hp" in DBG:
        st("gpsimd", lambda e: e.dma_start(out=DBG["d_hp"][:, 0:1024],
                                           in_=hs_d[0, :, :]),
           waits=[(sHS[0], w_hspill[(0, 0)]), (sHS[1], w_hspill[(0, 1)])],
           inc=sDBG2, inck=16)

    st("vector", lambda e: e.tensor_scalar_mul(out=rows[0:1, 1024:1536],
                                               in0=psG[0:1, 0:512], scalar1=1.0 / DFF),
       waits=[(sMM, w_f1mm[("st", NFC - 1, 0)]), (sMM, w_f1mm[("st", NFC - 1, 1)]),
              (sDBG, sDBG.n)])
    st("vector", lambda e: e.tensor_scalar_mul(out=rows[0:1, 1536:2048],
                                               in0=psH[0:1, 0:512], scalar1=1.0 / DFF))
    st("vector", lambda e: e.tensor_tensor(out=rr2, in0=rr0, in1=rr0, op=ALU.mult))
    st("vector", lambda e: e.tensor_tensor(out=rr1, in0=rr1, in1=rr2, op=ALU.mult))
    w_e3 = sVE.n
    st("scalar", lambda e: e.activation(out=rr2, in_=rr1, func=AF.Sqrt,
                                        bias=consts[0:1, 3:4]), waits=[(sVE, w_e3)])
    w_s5 = sSE.n
    st("vector", lambda e: e.reciprocal(out=rr1, in_=rr2), waits=[(sSE, w_s5)])
    st("vector", lambda e: e.tensor_copy(out=tmpb[0:64, :], in_=maxac[64:128, :]),
       waits=[(sVE, w_hv[("mx", NFC - 1, 1)])])
    st("vector", lambda e: e.tensor_tensor(out=maxac[0:64, :], in0=maxac[0:64, :],
                                           in1=tmpb[0:64, :], op=ALU.max))
    st("vector", lambda e: e.tensor_copy(out=tmpb[0:32, :], in_=maxac[32:64, :]))
    wsh = st("vector", lambda e: e.tensor_tensor(out=maxac[0:32, :], in0=maxac[0:32, :],
                                                 in1=tmpb[0:32, :], op=ALU.max))
    for npart in (16, 8, 4, 2, 1):
        st("sync", lambda e, npart=npart: e.dma_start(
            out=tmpb[0:npart, :], in_=maxac[npart:2 * npart, :]),
           waits=[(sVE, wsh)], inc=sB[4], inck=16)
        wdma = sB[4].n
        wsh = st("vector", lambda e, npart=npart: e.tensor_tensor(
            out=maxac[0:npart, :], in0=maxac[0:npart, :], in1=tmpb[0:npart, :],
            op=ALU.max), waits=[(sB[4], wdma)])
    st("scalar", lambda e: e.activation(out=rr2, in_=maxac[0:1, 0:1024], func=AF.Sqrt),
       waits=[(sVE, wsh)])
    w_s6 = sSE.n
    st("vector", lambda e: e.tensor_tensor(out=rr2, in0=rr2, in1=rr0, op=ALU.mult),
       waits=[(sSE, w_s6)])
    st("vector", lambda e: e.tensor_tensor(out=rr2, in0=rr2, in1=rr1, op=ALU.mult))
    st("vector", lambda e: e.tensor_scalar_max(out=rr2, in0=rr2, scalar1=1e-5))
    st("vector", lambda e: e.tensor_tensor(out=rr1, in0=rr1, in1=rr0, op=ALU.mult))
    st("vector", lambda e: e.reciprocal(out=rr0, in_=rr2))
    st("vector", lambda e: e.tensor_tensor(out=rr1, in0=rr1, in1=rr0, op=ALU.mult))
    w_cq3 = st("vector", lambda e: e.tensor_scalar_mul(out=rr1, in0=rr1, scalar1=127.0))
    st("vector", lambda e: e.tensor_scalar(out=rr2, in0=rr2, scalar1=1.0 / 127.0,
                                           scalar2=wsc[0:1, 5:6],
                                           op0=ALU.mult, op1=ALU.mult))
    w_do3 = sVE.n
    wbC3w, wbC3 = bounce(2, rr1, 1024, 20, cq3b[:, 0:1024],
                         waits=[(sVE, w_cq3), wbCQ])
    wbD3w, wbD3 = bounce(1, rr2, 1024, 21, do3b[:, 0:1024],
                         waits=[(sVE, w_do3), wbB2])
    w_negt3 = st("vector", lambda e: e.tensor_scalar_mul(out=wsc[:, 6:7],
                                                         in0=wsc[:, 2:3], scalar1=-1.0))

    w_f3 = {}
    w_q3sp = {}
    w_outev = {}
    w_x3st = []
    for dgq in range(4):
        douts = list(range(dgq * 4, (dgq + 1) * 4))
        for k3 in range(NFC):
            qsl = q3c[:, (k3 % 2) * 1024:((k3 % 2) + 1) * 1024]
            sHRp = sHR[k3 % 2]
            if dgq == 0:
                hsl2 = hch[:, (k3 % 2) * 1024:((k3 % 2) + 1) * 1024]
                for cc in range(2):
                    st("sync", lambda e, k3=k3, hsl2=hsl2, cc=cc: e.dma_start(
                        out=hsl2[:, cc * 512:(cc + 1) * 512],
                        in_=hs_d[k3, :, cc * 512:(cc + 1) * 512]),
                       waits=([(sVE, w_f3.get(("lad", k3 - 2), 0)),
                               (sHS[0], sHS[0].n), (sHS[1], sHS[1].n),
                               (sHS[2], sHS[2].n), (sHS[3], sHS[3].n)]
                              if k3 < 2 else
                              [(sVE, w_f3.get(("lad", k3 - 2), 0))])
                       if cc == 0 else [],
                       inc=sHRp, inck=16)
                w_hr = sHRp.n
                for th in range(2):
                    ta = rbc[:, th * 512:(th + 1) * 512]
                    tb = sm2[:, th * 512:(th + 1) * 512]
                    bb = tmpb[:, th * 512:(th + 1) * 512]
                    sq_ = hsl2[:, th * 512:(th + 1) * 512]
                    dq_ = qsl[:, th * 512:(th + 1) * 512]
                    st("vector", lambda e, sq_=sq_, ta=ta, th=th: e.tensor_tensor(
                        out=ta, in0=sq_, in1=cq3b[:, th * 512:(th + 1) * 512],
                        op=ALU.mult),
                       waits=[(sHRp, w_hr)] + ([wbC3, wbD3] if k3 == 0 and th == 0
                                               else []) +
                             ([(sMM, w_f3.get(("mm", k3 - 2), 0)),
                               (sQS[k3 % 2], w_q3sp.get(k3 - 2, 0))]
                              if th == 0 else []))
                    st("vector", lambda e, ta=ta, tb=tb: e.tensor_scalar_add(
                        out=tb, in0=ta, scalar1=128.5))
                    for kk in (128.0, 64.0, 32.0, 16.0, 8.0, 4.0, 2.0, 1.0):
                        st("vector", lambda e, tb=tb, bb=bb, kk=kk: e.tensor_scalar(
                            out=bb, in0=tb, scalar1=kk, scalar2=kk,
                            op0=ALU.is_ge, op1=ALU.mult))
                        st("vector", lambda e, tb=tb, bb=bb: e.tensor_tensor(
                            out=tb, in0=tb, in1=bb, op=ALU.subtract))
                    st("vector", lambda e, ta=ta: e.tensor_scalar_add(
                        out=ta, in0=ta, scalar1=128.5))
                    st("vector", lambda e, ta=ta, tb=tb: e.tensor_tensor(
                        out=ta, in0=ta, in1=tb, op=ALU.subtract))
                    w_f3[("lad", k3)] = st("vector", lambda e, ta=ta, dq_=dq_:
                                           e.tensor_scalar_add(out=dq_, in0=ta,
                                                               scalar1=-128.0))
                sQSp = sQS[k3 % 2]
                st("sync", lambda e, k3=k3, qsl=qsl: e.dma_start(
                    out=q3s_d[k3, :, :], in_=qsl),
                   waits=[(sVE, w_f3[("lad", k3)])], inc=sQSp, inck=16)
                w_q3sp[k3] = sQSp.n
                w_qrdy = ("v", w_f3[("lad", k3)])
            else:
                for cc in range(2):
                    st("sync", lambda e, k3=k3, qsl=qsl, cc=cc: e.dma_start(
                        out=qsl[:, cc * 512:(cc + 1) * 512],
                        in_=q3s_d[k3, :, cc * 512:(cc + 1) * 512]),
                       waits=(([(sQS[0], sQS[0].n), (sQS[1], sQS[1].n)] if
                               (dgq == 1 and k3 < 2) else []) +
                              [(sMM, w_f3.get(("mm", k3 - 2), 0))])
                       if cc == 0 else [],
                       inc=sHRp, inck=16)
                w_qrdy = ("d", sHRp.n, sHRp)
            sW = sW0 if k3 % 2 == 0 else sW1
            wslc = wfl if k3 % 2 == 0 else wfl2
            for cc in range(2):
                st("sync", lambda e, dgq=dgq, k3=k3, wslc=wslc, cc=cc: e.dma_start(
                    out=wslc[:, cc * 256:(cc + 1) * 256],
                    in_=w3_d[dgq, k3][:, cc * 256:(cc + 1) * 256]),
                   waits=([(sVE, w_f3.get(("tern", k3 - 2), 0))] if cc == 0 else []),
                   inc=sW, inck=16)
            w_w3 = sW.n
            tsl = ternA[:, (k3 % 2) * 512:((k3 % 2) + 1) * 512]
            st("vector", lambda e, wslc=wslc: e.tensor_scalar(
                out=terntmp[:, 0:512], in0=wslc[:, 0:512], scalar1=wsc[:, 2:3],
                scalar2=-1.0, op0=ALU.is_ge, op1=ALU.add),
               waits=[(sW, w_w3), (sMM, w_f3.get(("mm", k3 - 2), 0))] +
                     ([(sVE, w_negt3)] if dgq == 0 and k3 == 0 else []))
            w_f3[("tern", k3)] = st("vector", lambda e, wslc=wslc, tsl=tsl:
                                    e.scalar_tensor_tensor(
                out=tsl, in0=wslc[:, 0:512], scalar=wsc[:, 6:7],
                in1=terntmp[:, 0:512], op0=ALU.is_gt, op1=ALU.add))
            for di, dout in enumerate(douts):
                for th in range(2):
                    psx = pss[di * 2 + th]
                    ww = []
                    if k3 == 0:
                        ww = [(sVE, w_f3[("tern", k3)])]
                        if w_qrdy[0] == "v":
                            ww.append((sVE, w_qrdy[1]))
                        else:
                            ww.append((w_qrdy[2], w_qrdy[1]))
                        if dgq > 0 and di == 0 and th == 0:
                            ww.append((sSE, w_outev.get(dgq - 1, 0)))
                    elif di == 0 and th == 0:
                        ww = [(sVE, w_f3[("tern", k3)])]
                        if w_qrdy[0] == "v":
                            ww.append((sVE, w_qrdy[1]))
                        else:
                            ww.append((w_qrdy[2], w_qrdy[1]))
                    st("tensor", lambda e, di=di, th=th, psx=psx, qsl=qsl, k3=k3,
                       tsl=tsl: e.matmul(
                           psx[:, 0:512], tsl[:, di * 128:(di + 1) * 128],
                           qsl[:, th * 512:(th + 1) * 512],
                           start=(k3 == 0), stop=(k3 == NFC - 1)),
                       waits=ww, inc=(sMM if (di == 3 and th == 1) else None))
            w_f3[("mm", k3)] = sMM.n
        for di, dout in enumerate(douts):
            sXO = sXO0 if dout % 2 == 0 else sXO1
            xsl = xst[:, (dout % 2) * 1024:(dout % 2 + 1) * 1024]
            st("sync", lambda e, dout=dout, xsl=xsl: e.dma_start(
                out=xsl, in_=x2s_d[dout * 128:(dout + 1) * 128, :]),
               waits=[(sVE, w_f3.get(("x3", dout - 2), 0))] +
                     ([w_f3[("ost", dout - 2)]] if ("ost", dout - 2) in w_f3 else []),
               inc=sXO, inck=16)
            w_xo = sXO.n
            for th in range(2):
                psx = pss[di * 2 + th]
                w_ev = st("scalar", lambda e, psx=psx, dout=dout: e.activation(
                    out=kraw[:, 0:512], in_=psx[:, 0:512], func=AF.Identity,
                    scale=modv[:, 80 + dout:81 + dout]),
                    waits=[(sMM, w_f3[("mm", NFC - 1)])] +
                          ([(sVE, w_f3.get(("x3v", dout, th - 1),
                                           w_f3.get(("x3v", dout - 1, 1), 0)))]))
                st("vector", lambda e, th=th: e.tensor_tensor(
                    out=kraw[:, 0:512], in0=kraw[:, 0:512],
                    in1=do3b[:, th * 512:(th + 1) * 512], op=ALU.mult),
                   waits=[(sSE, w_ev)])
                w_f3[("x3v", dout, th)] = st("vector", lambda e, th=th, xsl=xsl:
                                             e.tensor_tensor(
                    out=xsl[:, th * 512:(th + 1) * 512],
                    in0=xsl[:, th * 512:(th + 1) * 512],
                    in1=kraw[:, 0:512], op=ALU.add), waits=[(sXO, w_xo)])
            w_f3[("x3", dout)] = w_f3[("x3v", dout, 1)]
            sOUTp = sOUT if dout % 2 == 0 else sOUT1
            st("sync", lambda e, dout=dout, xsl=xsl: e.dma_start(
                out=out_d[dout * 128:(dout + 1) * 128, :], in_=xsl),
               waits=[(sVE, w_f3[("x3", dout)])], inc=sOUTp, inck=16)
            w_f3[("ost", dout)] = (sOUTp, sOUTp.n)
        w_outev[dgq] = sSE.n
    if "d_q3" in DBG:
        st("gpsimd", lambda e: e.dma_start(out=DBG["d_q3"][:, 0:1024],
                                           in_=q3s_d[0, :, :]),
           waits=[(sQS[0], w_q3sp.get(0, 0))], inc=sDBG2, inck=16)

    st("sync", lambda e: e.nop(),
       waits=[(sOUT, sOUT.n), (sOUT1, sOUT1.n), (sDBG, sDBG.n), (sDBG2, sDBG2.n), (sDBG3, sDBG3.n), (sDBG4, sDBG4.n)])
    K.DBG = DBG
    return K


def _host_prep(inputs, b, half):
    x = inputs["L0_attn_x"][b]
    own = half * OWN
    pos = (np.arange(T) + own) % T
    xfm = np.ascontiguousarray(x[pos].T).astype(np.float32)
    c = inputs["L0_c"][b]
    cvec = np.ascontiguousarray(np.asarray(c).reshape(NKC, 128).T).astype(np.float32)
    return {"xc": xfm, "cvec": cvec, "pos": pos.astype(np.float32).reshape(1, T)}


def _til(w, no, ni):
    return np.ascontiguousarray(
        np.asarray(w).reshape(no // 128, 128, ni // 128, 128).transpose(0, 3, 2, 1)
    ).astype(np.float32)


def _host_weights(inputs):
    wv = np.asarray(inputs["L0_attn_wv"])
    w3 = np.asarray(inputs["L0_ffn_w3"])
    wmod = np.asarray(inputs["L0_wmod"])
    return {
        "wq": _til(inputs["L0_attn_wq"], D, D),
        "wk": _til(inputs["L0_attn_wk"], D, D),
        "wo": _til(inputs["L0_attn_wo"], D, D),
        "wv": np.ascontiguousarray(
            wv.T.reshape(NKC, 128, 8, 256).transpose(2, 1, 0, 3)).astype(np.float32),
        "w1": _til(inputs["L0_ffn_w1"], DFF, D),
        "w2": _til(inputs["L0_ffn_w2"], DFF, D),
        "w3": np.ascontiguousarray(
            w3.reshape(NKC, 128, NFC, 128).transpose(2, 3, 0, 1)
            .reshape(NFC, 128, NKC * 128)
            .reshape(NFC, 128, 4, 4 * 128).transpose(2, 0, 1, 3)).astype(np.float32),
        "wmod": np.ascontiguousarray(
            wmod.reshape(NKC, 128, 24, 512).transpose(2, 1, 0, 3)).astype(np.float32),
    }


def kernel(**inputs):
    nc = bass.Bass("TRN2", target_bir_lowering=False)
    K = build(nc)
    K.emit()
    w = _host_weights(inputs)
    in_maps = []
    for core in range(8):
        b, half = core // 2, core % 2
        m = dict(w)
        m.update(_host_prep(inputs, b, half))
        in_maps.append(m)
    res = run_bass_kernel_spmd(nc, in_maps, core_ids=list(range(8)))
    out = np.zeros((B, T, D), np.float32)
    for core in range(8):
        b, half = core // 2, core % 2
        own = half * OWN
        out[b, own:own + OWN, :] = res.results[core]["out"].T
    return out


# version 07
# speedup vs baseline: 1.2636x; 1.0467x over previous
import math
import numpy as np
import concourse.bass as bass
import concourse.mybir as mybir
from concourse.bass_utils import run_bass_kernel_spmd

F32 = mybir.dt.float32
BF16 = mybir.dt.bfloat16
AF = mybir.ActivationFunctionType
ALU = mybir.AluOpType
AX = mybir.AxisListType

B, T, D, NH, HD = 4, 2048, 2048, 16, 128
DFF = 8192
NKC = D // 128
NFC = DFF // 128
OWN = 1024
SHIFT = 12.0
SCALE_S = 1.0 / math.sqrt(HD)

TWO_PI = 2.0 * math.pi
CW1 = float(np.float32(TWO_PI))
CW2 = float(np.float32(np.float64(TWO_PI) - np.float64(np.float32(TWO_PI))))
CW3 = float(np.float64(TWO_PI) - np.float64(np.float32(TWO_PI)) - np.float64(CW2))
R2PI = float(np.float32(1.0 / TWO_PI))
PI_HALF = float(np.float32(math.pi / 2))
PI_HALF_LO = float(np.float64(math.pi / 2) - np.float64(np.float32(math.pi / 2)))
PI_F = float(np.float32(math.pi))

KBY = 1024


class Sem:
    def __init__(self, handle):
        self.h = handle
        self.n = 0

    def inc(self, k=1):
        self.n += k
        return self.n


class Bld:
    def __init__(self, nc):
        self.nc = nc
        self.q = {k: [] for k in ("sync", "gpsimd", "vector", "scalar", "tensor")}
        self._ctxs = []
        self.sems = {}
        self.serial = {}
        self.base = (nc.sbuf_base + 31) & ~31

    def sem(self, name):
        if name not in self.sems:
            cm = self.nc.semaphore(name)
            self.sems[name] = Sem(cm.__enter__())
            self._ctxs.append(cm)
        return self.sems[name]

    def sb(self, name, shape, dtype, off_b):
        t = self.nc.alloc_sbuf_tensor_at(name, shape, dtype, offset=self.base + off_b)
        return t.ap()

    def st(self, eng, fn, waits=(), inc=None, inck=1):
        ser = self.serial.get(eng)
        if ser is not None:
            waits = tuple(waits) + ((ser, ser.n),)
            inc, inck = ser, 1
        waits = tuple((s.h, v) for (s, v) in waits if v > 0)
        if inc is not None:
            inc.inc(inck)

        def run(e, fn=fn, waits=waits, inc=inc, inck=inck):
            for h, v in waits:
                e.wait_ge(h, v)
            r = fn(e)
            if inc is not None:
                r.then_inc(inc.h, inck)
        self.q[eng].append(run)
        return inc.n if inc is not None else None

    def emit(self):
        with self.nc.Block() as block:
            block.sync(lambda e: [f(e) for f in self.q["sync"]])
            block.gpsimd(lambda e: [f(e) for f in self.q["gpsimd"]])
            block.vector(lambda e: [f(e) for f in self.q["vector"]])
            block.scalar(lambda e: [f(e) for f in self.q["scalar"]])
            block.tensor(lambda e: [f(e) for f in self.q["tensor"]])
        for cm in reversed(self._ctxs):
            cm.__exit__(None, None, None)


def build(nc, dbg=(), phases=99, e_head=0):
    dbg = set(dbg)
    K = Bld(nc)
    st = K.st
    KB = lambda x: int(x * KBY)

    xc = nc.dram_tensor("xc", [D, T], F32, kind="ExternalInput").ap()
    cvec_d = nc.dram_tensor("cvec", [128, NKC], F32, kind="ExternalInput").ap()
    pos_d = nc.dram_tensor("pos", [1, T], F32, kind="ExternalInput").ap()
    wmod_d = nc.dram_tensor("wmod", [24, 128, NKC, 512], F32, kind="ExternalInput").ap()
    wq_d = nc.dram_tensor("wq", [NH, 128, NKC, 128], F32, kind="ExternalInput").ap()
    wk_d = nc.dram_tensor("wk", [NH, 128, NKC, 128], F32, kind="ExternalInput").ap()
    wv_d = nc.dram_tensor("wv", [8, 128, NKC, 256], F32, kind="ExternalInput").ap()
    wo_d = nc.dram_tensor("wo", [NKC, 128, NKC, 128], F32, kind="ExternalInput").ap()
    w1_d = nc.dram_tensor("w1", [NFC, 128, NKC, 128], F32, kind="ExternalInput").ap()
    w2_d = nc.dram_tensor("w2", [NFC, 128, NKC, 128], F32, kind="ExternalInput").ap()
    w3_d = nc.dram_tensor("w3", [4, NFC, 128, 4 * 128], F32, kind="ExternalInput").ap()
    out_d = nc.dram_tensor("out", [D, OWN], F32, kind="ExternalOutput").ap()
    bnc_d = nc.dram_tensor("bnc", [64, 2048], F32, kind="Internal").ap()
    x2s_d = nc.dram_tensor("x2s", [D, OWN], F32, kind="Internal").ap()
    hs_d = nc.dram_tensor("hs", [NFC, 128, OWN], BF16, kind="Internal").ap()
    q3s_d = nc.dram_tensor("q3s", [NFC, 128, OWN], BF16, kind="Internal").ap()

    idm4_d = nc.inline_tensor(
        np.ascontiguousarray(np.tile(np.eye(128, dtype=np.float32), (1, 4))), "idm4").ap()
    irow_d = nc.inline_tensor(np.arange(64, dtype=np.float32).reshape(64, 1), "irow").ap()

    DBG = {}

    def dbg_t(name, shape):
        if name in dbg:
            DBG[name] = nc.dram_tensor(name, shape, F32, kind="ExternalOutput").ap()
            return True
        return False

    rows = K.sb("rows", [1, 3072], F32, KB(0))
    rbc = K.sb("rbc", [128, 1024], F32, KB(12))
    modv = K.sb("modv", [128, 128], F32, KB(16))
    consts = K.sb("consts", [128, 16], F32, KB(16.5))
    accw = K.sb("accw", [128, 132], F32, KB(16.6875))
    wsc = K.sb("wsc", [128, 8], F32, KB(17.21875))
    onesb = K.sb("onesb", [128, 128], BF16, KB(17.25))
    sqb2 = K.sb("sqb2", [128, 512], BF16, KB(17.5))
    onesf128 = K.sb("onesf128", [128, 128], F32, KB(18.5))
    xst = K.sb("xst", [128, 2048], F32, KB(19))
    accm = K.sb("accm", [128, 1024], F32, KB(19))
    wfl = K.sb("wfl", [128, 2048], F32, KB(27))
    wfl2 = K.sb("wfl2", [128, 2048], F32, KB(35))
    wsl = K.sb("wsl", [128, 4096], BF16, KB(35))
    ternb = K.sb("ternb", [128, 4096], BF16, KB(35))
    xbb = K.sb("xbb", [128, 1024], BF16, KB(43))
    h_sb = K.sb("h_sb", [128, NKC * T], BF16, KB(45))
    q12 = K.sb("q12", [128, NKC * OWN], BF16, KB(45))
    ysb = K.sb("ysb", [128, NKC * OWN], BF16, KB(111))
    h2sb = K.sb("h2sb", [128, NKC * OWN], BF16, KB(111))
    du1b = K.sb("du1b", [128, 1024], F32, KB(77))
    tmpb = K.sb("tmpb", [128, 1024], F32, KB(81))
    maxac = K.sb("maxac", [128, 1024], F32, KB(85))
    cq3b = K.sb("cq3b", [128, 1024], F32, KB(89))
    do3b = K.sb("do3b", [128, 1024], F32, KB(93))
    terntmp = K.sb("terntmp", [128, 2048], F32, KB(97))
    ternA2 = K.sb("ternA2", [128, 2048], BF16, KB(143))
    ternB2 = K.sb("ternB2", [128, 2048], BF16, KB(147))
    sgntmp = K.sb("sgntmp", [128, 2048], F32, KB(151))
    wfl3 = K.sb("wfl3", [128, 2048], F32, KB(159))
    wfl4 = K.sb("wfl4", [128, 2048], F32, KB(167))
    ternA = K.sb("ternA", [128, 2048], BF16, KB(105))
    ternB = K.sb("ternB", [128, 2048], BF16, KB(109))
    hch = K.sb("hch", [128, 2048], BF16, KB(113))
    q3c = K.sb("q3c", [128, 2048], BF16, KB(117))
    esb = K.sb("esb", [128, NKC * 512], BF16, KB(143))
    vsb = K.sb("vsb", [128, 2 * NKC * 256], BF16, KB(159))
    ksb = K.sb("ksb", [128, 2 * T], BF16, KB(175))
    qsb = K.sb("qsb", [128, 2 * OWN], BF16, KB(183))
    cosT = K.sb("cosT", [64, T], BF16, KB(187))
    ssinT = K.sb("ssinT", [64, T], BF16, KB(191))
    kraw = K.sb("kraw", [128, 512], F32, KB(195))
    krot = K.sb("krot", [64, 512], F32, KB(197))
    tab = K.sb("tab", [64, 8], F32, KB(198))
    lrec = K.sb("lrec", [16, 1024], F32, KB(198.5))
    sm2 = K.sb("sm2", [128, 1024], F32, KB(202.5))
    smr = K.sb("smr", [1, 2048], F32, KB(198.5))
    mring = K.sb("mring", [128, 2 * NKC * 512], BF16, KB(111))
    smat = K.sb("smat", [128, NKC * 128], BF16, KB(143))
    idn4 = K.sb("idn4", [128, 512], F32, KB(147))
    t_ang = K.sb("t_ang", [64, 2048], F32, KB(149.25))
    t_u = K.sb("t_u", [64, 2048], F32, KB(157.25))
    t_r = K.sb("t_r", [64, 2048], F32, KB(165.25))
    t_b = K.sb("t_b", [64, 2048], F32, KB(19))
    wsl2 = K.sb("wsl2", [128, 4096], BF16, KB(19))
    assert K.base + KB(206.5) <= nc.sbuf_top

    pss = [nc.alloc_psum_tensor("ps%d" % i, [128, 512], F32).ap() for i in range(8)]
    psA, psB, psC, psD, psE, psF, psG, psH = pss

    sLD = K.sem("sLD")
    sXL = K.sem("sXL")
    sM0 = K.sem("sM0")
    sM1 = K.sem("sM1")
    sB = [K.sem("sB%d" % i) for i in range(6)]
    sVE = K.sem("sVE")
    sSE = K.sem("sSE")
    sMM = K.sem("sMM")
    sEX = K.sem("sEX")
    sEV = K.sem("sEV")
    sDBG = K.sem("sDBG")
    sDBG2 = K.sem("sDBG2")
    sDBG3 = K.sem("sDBG3")
    sDBG4 = K.sem("sDBG4")
    sDBG5 = K.sem("sDBG5")
    sOUT = K.sem("sOUT")
    sOUT1 = K.sem("sOUT1")
    sA0 = K.sem("sA0")
    sA1 = K.sem("sA1")
    sAC = K.sem("sAC")
    sW0 = K.sem("sW0")
    sW1 = K.sem("sW1")
    sW2 = K.sem("sW2")
    sW3 = K.sem("sW3")
    sLB = K.sem("sLB")
    sLR = K.sem("sLR")
    sRB0 = K.sem("sRB0")
    sRB1 = K.sem("sRB1")
    sXO0 = K.sem("sXO0")
    sXO1 = K.sem("sXO1")
    sXS0 = K.sem("sXS0")
    sXS1 = K.sem("sXS1")
    sHS = [K.sem("sHS%d" % i) for i in range(4)]
    sHR = [K.sem("sHR%d" % i) for i in range(2)]
    sQS = [K.sem("sQS%d" % i) for i in range(2)]
    K.serial["vector"] = sVE
    K.serial["scalar"] = sSE

    def bounce(pair, row_ap, n, bnc_row, tile_ap, nparts=128, waits=()):
        sw, sr = sB[2 * pair], sB[2 * pair + 1]
        st("sync", lambda e: e.dma_start(out=bnc_d[bnc_row:bnc_row + 1, 0:n], in_=row_ap),
           waits=waits, inc=sw, inck=16)
        w = sw.n
        st("sync", lambda e: e.dma_start(
            out=tile_ap, in_=bnc_d[bnc_row:bnc_row + 1, 0:n].broadcast_to((nparts, n))),
           waits=[(sw, w)], inc=sr, inck=16)
        return (sw, w), (sr, sr.n)

    def emit_ladder(chains):
        vws = []
        for ch in chains:
            vws.append(st("vector", lambda e, ch=ch: e.tensor_tensor(
                out=ch["ta"], in0=ch["src"], in1=ch["cq"], op=ALU.mult),
                waits=ch.get("mw", [])))
        sws = [0] * len(chains)
        for i, k in enumerate((128.0, 64.0, 32.0, 16.0, 8.0, 4.0, 2.0, 1.0)):
            for j, ch in enumerate(chains):
                src_u = ch["ta"] if i == 0 else ch["tb"]
                sws[j] = st("scalar", lambda e, ch=ch, src_u=src_u: e.activation(
                    out=ch["bb"], in_=src_u, func=AF.Sign, bias=consts[:, 8:9]),
                    waits=[(sVE, vws[j])])
            for j, ch in enumerate(chains):
                src_u = ch["ta"] if i == 0 else ch["tb"]
                vws[j] = st("vector", lambda e, ch=ch, src_u=src_u, k=k:
                            e.scalar_tensor_tensor(
                    out=ch["tb"], in0=ch["bb"], scalar=-(k / 2.0), in1=src_u,
                    op0=ALU.mult, op1=ALU.add),
                    waits=[(sSE, sws[j])])
        last = 0
        for ch in chains:
            st("vector", lambda e, ch=ch: e.tensor_tensor(
                out=ch["ta"], in0=ch["ta"], in1=ch["tb"], op=ALU.subtract))
            last = st("vector", lambda e, ch=ch: e.tensor_scalar_add(
                out=ch["dst"], in0=ch["ta"], scalar1=-0.5))
        return last

    st("sync", lambda e: e.dma_start(out=modv[:, 96:112], in_=cvec_d[:, :]), inc=sLD, inck=16)
    st("sync", lambda e: e.dma_start(out=idn4[:, :], in_=idm4_d[:, :]), inc=sLD, inck=16)
    st("sync", lambda e: e.dma_start(out=tab[:, 0:1], in_=irow_d[:, :]), inc=sLD, inck=16)
    st("sync", lambda e: e.dma_start(
        out=t_ang[:, :], in_=pos_d[0:1, :].broadcast_to((64, T))), inc=sLD, inck=16)
    w_cvec = w_idn = w_irow = w_pos = sLD.n

    def _consts(e):
        e.memset(consts[:, 0:1], -SHIFT)
        e.memset(consts[:, 1:2], 1.0)
        e.memset(consts[:, 2:3], 1e-5)
        e.memset(wsc[:, :], 0.0)
        e.memset(consts[:, 3:4], 1e-6)
        e.memset(consts[:, 8:9], 0.5)
        e.memset(tab[:, 1:2], 10000.0)
        return e.memset(onesb[:, :], 1.0)
    st("vector", _consts)
    w_consts = sVE.n

    for kc in range(NKC):
        st("gpsimd", lambda e, kc=kc: e.dma_start(
            out=h_sb[:, kc * T:(kc + 1) * T], in_=xc[kc * 128:(kc + 1) * 128, :]),
           inc=sXL, inck=16)
    w_xall = sXL.n

    st("scalar", lambda e: e.activation(out=tab[:, 2:3], in_=tab[:, 1:2], func=AF.Ln),
       waits=[(sVE, w_consts)])
    st("scalar", lambda e: e.activation(out=tab[:, 3:4], in_=tab[:, 0:1], func=AF.Identity,
                                        scale=tab[:, 2:3]),
       waits=[(sLD, w_irow), (sSE, sSE.n)])
    st("scalar", lambda e: e.activation(out=tab[:, 4:5], in_=tab[:, 3:4], func=AF.Exp,
                                        scale=-1.0 / 64.0), waits=[(sSE, sSE.n)])
    w_invf = sSE.n
    st("vector", lambda e: e.tensor_scalar_mul(out=t_ang[:, :], in0=t_ang[:, :],
                                               scalar1=tab[:, 4:5]),
       waits=[(sLD, w_pos), (sSE, w_invf)])
    st("vector", lambda e: e.tensor_scalar(out=t_u[:, :], in0=t_ang[:, :],
                                           scalar1=R2PI, scalar2=0.5,
                                           op0=ALU.mult, op1=ALU.add))
    st("vector", lambda e: e.tensor_copy(out=t_r[:, :], in_=t_u[:, :]))
    for k in [256.0, 128.0, 64.0, 32.0, 16.0, 8.0, 4.0, 2.0, 1.0]:
        st("vector", lambda e, k=k: e.tensor_scalar(out=t_b[0:64, :], in0=t_r[:, :],
                                                    scalar1=k, scalar2=k,
                                                    op0=ALU.is_ge, op1=ALU.mult))
        st("vector", lambda e: e.tensor_tensor(out=t_r[:, :], in0=t_r[:, :],
                                               in1=t_b[0:64, :], op=ALU.subtract))
    st("vector", lambda e: e.tensor_tensor(out=t_u[:, :], in0=t_u[:, :], in1=t_r[:, :],
                                           op=ALU.subtract))
    st("vector", lambda e: e.scalar_tensor_tensor(out=t_r[:, :], in0=t_u[:, :],
                                                  scalar=-CW1, in1=t_ang[:, :],
                                                  op0=ALU.mult, op1=ALU.add))
    st("vector", lambda e: e.scalar_tensor_tensor(out=t_r[:, :], in0=t_u[:, :],
                                                  scalar=-CW2, in1=t_r[:, :],
                                                  op0=ALU.mult, op1=ALU.add))
    st("vector", lambda e: e.scalar_tensor_tensor(out=t_r[:, :], in0=t_u[:, :],
                                                  scalar=-CW3, in1=t_r[:, :],
                                                  op0=ALU.mult, op1=ALU.add))
    st("vector", lambda e: e.tensor_scalar(out=t_b[0:64, :], in0=t_r[:, :],
                                           scalar1=PI_F, scalar2=1.0,
                                           op0=ALU.is_gt, op1=ALU.mult))
    st("vector", lambda e: e.scalar_tensor_tensor(
        out=t_r[:, :], in0=t_b[0:64, :], scalar=-CW1, in1=t_r[:, :],
        op0=ALU.mult, op1=ALU.add))
    st("vector", lambda e: e.scalar_tensor_tensor(
        out=t_b[0:64, :], in0=t_b[0:64, :], scalar=-CW2, in1=t_r[:, :],
        op0=ALU.mult, op1=ALU.add))
    w_sinin = sVE.n
    st("scalar", lambda e: e.activation(out=ssinT[:, :], in_=t_b[0:64, :], func=AF.Sin),
       waits=[(sVE, w_sinin)])
    w_sin = sSE.n
    st("vector", lambda e: e.tensor_scalar_mul(out=ssinT[0:32, :], in0=ssinT[0:32, :],
                                               scalar1=-1.0), waits=[(sSE, w_sin)])
    w_ssin = sVE.n
    st("vector", lambda e: e.tensor_scalar_add(out=t_u[:, :], in0=t_r[:, :],
                                               scalar1=PI_HALF), waits=[(sSE, w_sin)])
    st("vector", lambda e: e.tensor_scalar_add(out=t_u[:, :], in0=t_u[:, :],
                                               scalar1=PI_HALF_LO))
    st("vector", lambda e: e.tensor_scalar(out=t_b[0:64, :], in0=t_u[:, :],
                                           scalar1=PI_F, scalar2=1.0,
                                           op0=ALU.is_gt, op1=ALU.mult))
    st("vector", lambda e: e.scalar_tensor_tensor(
        out=t_u[:, :], in0=t_b[0:64, :], scalar=-CW1, in1=t_u[:, :],
        op0=ALU.mult, op1=ALU.add))
    st("vector", lambda e: e.scalar_tensor_tensor(
        out=t_b[0:64, :], in0=t_b[0:64, :], scalar=-CW2, in1=t_u[:, :],
        op0=ALU.mult, op1=ALU.add))
    w_cosin = sVE.n
    st("scalar", lambda e: e.activation(out=cosT[:, :], in_=t_b[0:64, :], func=AF.Sin),
       waits=[(sVE, w_cosin)])
    w_cos = sSE.n
    if dbg_t("d_cs", [64, 2 * T]):
        st("gpsimd", lambda e: e.dma_start(out=DBG["d_cs"][:, 0:T], in_=cosT[:, :]),
           waits=[(sSE, w_cos)], inc=sDBG2, inck=16)
        st("gpsimd", lambda e: e.dma_start(out=DBG["d_cs"][:, T:2 * T], in_=ssinT[:, :]),
           waits=[(sVE, w_ssin)], inc=sDBG2, inck=16)

    st("scalar", lambda e: e.activation(out=modv[:, 112:128], in_=modv[:, 96:112],
                                        func=AF.Silu), waits=[(sLD, w_cvec)])
    w_silu = sSE.n
    for kc in range(NKC):
        st("vector", lambda e, kc=kc: e.tensor_scalar_mul(
            out=smat[:, kc * 128:(kc + 1) * 128], in0=onesb[:, :],
            scalar1=modv[:, 112 + kc:113 + kc]),
           waits=[(sSE, w_silu), (sVE, w_consts)] if kc == 0 else [],
           inc=(sVE if kc == NKC - 1 else None))
    w_smat = sVE.n

    mod_mm_base = sMM.n
    for j in range(24):
        st("gpsimd", lambda e, j=j: e.dma_start(
            out=mring[:, (j % 2) * NKC * 512:((j % 2) + 1) * NKC * 512], in_=wmod_d[j]),
           waits=[(sMM, mod_mm_base + j - 1)] if j >= 2 else [],
           inc=(sM0 if j % 2 == 0 else sM1), inck=16)
    w_modtt = 0
    for j in range(24):
        wld = 16 * (j // 2 + 1)
        sMp = sM0 if j % 2 == 0 else sM1
        slot = mring[:, (j % 2) * NKC * 512:((j % 2) + 1) * NKC * 512]
        for kc in range(NKC):
            st("tensor", lambda e, kc=kc, slot=slot: e.matmul(
                psE[:, :], smat[:, kc * 128:(kc + 1) * 128],
                slot[:, kc * 512:(kc + 1) * 512],
                start=(kc == 0), stop=(kc == NKC - 1)),
               waits=(([(sMp, wld), (sVE, w_smat)] +
                       ([(sVE, w_modtt)] if j >= 1 else []))
                      if kc == 0 else []),
               inc=(sMM if kc == NKC - 1 else None))
        wmm = sMM.n
        w_modtt = st("vector", lambda e: e.tensor_tensor(out=sm2[:, 0:512], in0=psE[:, :],
                                                         in1=idn4[:, :], op=ALU.mult),
                     waits=[(sMM, wmm), (sLD, w_idn)])
        st("vector", lambda e, j=j: e.tensor_reduce(
            out=modv[:, j * 4:(j + 1) * 4],
            in_=sm2[:, 0:512].rearrange("p (b f) -> p b f", f=128),
            axis=AX.X, op=ALU.add))
    w_gmul = sVE.n
    for (cg, nm) in ((32, "ga"), (80, "gb")):
        st("scalar", lambda e, cg=cg: e.activation(out=modv[:, 112:128],
                                                   in_=modv[:, cg:cg + 16],
                                                   func=AF.Sign),
           waits=[(sVE, w_gmul)])
        w_ab_ = st("scalar", lambda e, cg=cg: e.activation(out=modv[:, cg:cg + 16],
                                                           in_=modv[:, cg:cg + 16],
                                                           func=AF.Abs))
        st("vector", lambda e, cg=cg: e.tensor_scalar_max(out=modv[:, cg:cg + 16],
                                                          in0=modv[:, cg:cg + 16],
                                                          scalar1=0.25),
           waits=[(sSE, w_ab_)])
        w_gmul = st("vector", lambda e, cg=cg: e.tensor_tensor(
            out=modv[:, cg:cg + 16], in0=modv[:, cg:cg + 16],
            in1=modv[:, 112:128], op=ALU.mult))
    st("vector", lambda e: e.tensor_scalar_add(out=modv[:, 16:32], in0=modv[:, 16:32],
                                               scalar1=1.0))
    st("vector", lambda e: e.tensor_scalar_add(out=modv[:, 64:80], in0=modv[:, 64:80],
                                               scalar1=1.0))
    w_mod = sVE.n
    if dbg_t("d_mod", [128, 128]):
        st("sync", lambda e: e.dma_start(out=DBG["d_mod"][:, :], in_=modv[:, :]),
           waits=[(sVE, w_mod)], inc=sDBG, inck=16)

    w_rows_evac = 0
    w_hmul_last = 0
    w_rowfree = []
    for tc in range(4):
        for kc in range(NKC):
            st("tensor", lambda e, tc=tc, kc=kc: e.matmul(
                psA[:, :], onesb[:, :],
                h_sb[:, kc * T + tc * 512: kc * T + (tc + 1) * 512],
                start=(kc == 0), stop=(kc == NKC - 1)),
               waits=(([(sXL, w_xall), (sVE, w_consts)] +
                       ([(sVE, w_rows_evac)] if tc > 0 else []))
                      if kc == 0 else []),
               inc=(sMM if kc == NKC - 1 else None))
        w_sum = sMM.n
        for kc in range(NKC):
            w_slot = sMM.n
            st("scalar", lambda e, tc=tc, kc=kc: e.activation(
                out=xbb[:, (kc % 2) * 512:(kc % 2 + 1) * 512],
                in_=h_sb[:, kc * T + tc * 512: kc * T + (tc + 1) * 512],
                func=AF.Square),
               waits=([(sMM, sMM.n - 1)] if kc >= 2 else []) +
                     ([(sXL, w_xall)] if (tc == 0 and kc == 0) else []))
            w_sq = sSE.n
            st("tensor", lambda e, kc=kc: e.matmul(
                psB[:, :], onesb[:, :], xbb[:, (kc % 2) * 512:(kc % 2 + 1) * 512],
                start=(kc == 0), stop=(kc == NKC - 1)),
               waits=[(sSE, w_sq)], inc=sMM)
        w_sq_mm = sMM.n
        r0, r1, r2 = rows[0:1, 0:512], rows[0:1, 1024:1536], rows[0:1, 2048:2560]
        st("vector", lambda e: e.tensor_scalar_mul(out=r0, in0=psA[0:1, :], scalar1=1.0 / D),
           waits=[(sMM, w_sq_mm)] + (w_rowfree if tc > 0 else []))
        w_rows_evac = st("vector", lambda e: e.tensor_scalar_mul(
            out=r1, in0=psB[0:1, :], scalar1=1.0 / D))
        st("vector", lambda e: e.tensor_tensor(out=r2, in0=r0, in1=r0, op=ALU.mult))
        st("vector", lambda e: e.tensor_tensor(out=r1, in0=r1, in1=r2, op=ALU.subtract),
           inc=sVE)
        w_var = sVE.n
        st("scalar", lambda e: e.activation(out=r2, in_=r1, func=AF.Sqrt, bias=consts[0:1, 2:3]),
           waits=[(sVE, w_var)])
        w_std = sSE.n
        st("vector", lambda e: e.reciprocal(out=r1, in_=r2), waits=[(sSE, w_std)])
        st("vector", lambda e: e.tensor_tensor(out=r0, in0=r0, in1=r1, op=ALU.mult))
        st("vector", lambda e: e.tensor_scalar_mul(out=r0, in0=r0, scalar1=-1.0))
        w_ab = sVE.n
        if tc == 0 and dbg_t("d_stats", [2, T]):
            st("sync", lambda e: e.dma_start(out=DBG["d_stats"][0:1, 0:512], in_=r1),
               waits=[(sVE, w_ab)], inc=sDBG, inck=16)
            st("sync", lambda e: e.dma_start(out=DBG["d_stats"][1:2, 0:512], in_=r0),
               waits=[(sVE, w_ab)], inc=sDBG, inck=16)
        wbAw, wbA = bounce(0, r1, 512, 16 + tc, rbc[:, 0:512],
                           waits=[(sVE, w_ab)] + ([(sVE, w_hmul_last)] if tc > 0 else []))
        wbBw, wbB = bounce(1, r0, 512, 20 + tc, rbc[:, 512:1024], waits=[(sVE, w_ab)])
        w_rowfree = [wbAw, wbBw] + ([(sDBG, sDBG.n)] if tc == 0 and "d_stats" in DBG else [])
        for kc in range(NKC):
            sl = h_sb[:, kc * T + tc * 512: kc * T + (tc + 1) * 512]
            s2 = sm2[:, (kc % 2) * 512:(kc % 2 + 1) * 512]
            st("vector", lambda e, sl=sl, s2=s2: e.tensor_tensor(
                out=s2, in0=sl, in1=rbc[:, 0:512], op=ALU.mult),
               waits=([wbA, wbB] if kc == 0 else
                      [(sSE, sSE.n - 1)] if kc >= 2 else []))
            st("vector", lambda e, s2=s2: e.tensor_tensor(
                out=s2, in0=s2, in1=rbc[:, 512:1024], op=ALU.add))
            w_t = sVE.n
            w_hmul_last = w_t
            st("scalar", lambda e, sl=sl, s2=s2, kc=kc: e.activation(
                out=sl, in_=s2, func=AF.Identity,
                scale=modv[:, 16 + kc:17 + kc], bias=modv[:, kc:kc + 1]),
               waits=[(sVE, w_t)] + ([(sVE, w_mod)] if kc == 0 and tc == 0 else []),
               inc=sSE)
    st("scalar", lambda e: e.activation(out=tab[:, 5:6], in_=tab[:, 1:2], func=AF.Identity),
       inc=sSE)
    w_h = sSE.n
    if dbg_t("d_h", [128, NKC * T]):
        st("gpsimd", lambda e: e.dma_start(out=DBG["d_h"][:, :], in_=h_sb[:, :]),
           waits=[(sSE, w_h)], inc=sDBG2, inck=16)

    if phases <= 1:
        for kc in range(NKC):
            st("gpsimd", lambda e, kc=kc: e.dma_start(
                out=out_d[kc * 128:(kc + 1) * 128, :], in_=h_sb[:, kc * T:kc * T + OWN]),
               waits=[(sSE, w_h)] if kc == 0 else [], inc=sOUT, inck=16)
        w_fin = sOUT.n
        st("sync", lambda e: e.nop(), waits=[(sOUT, w_fin), (sDBG, sDBG.n), (sDBG2, sDBG2.n)])
        K.DBG = DBG
        return K

    dbg_t("d_k0", [128, T]); dbg_t("d_q0", [128, OWN]); dbg_t("d_v0", [128, NKC * 256])
    dbg_t("d_e0", [128, NKC * 512]); dbg_t("d_l", [16, OWN]); dbg_t("d_y", [128, NKC * OWN])
    dbg_t("d_x2", [D, OWN])

    scale_chunks = []
    for wd, nel in ((w1_d, DFF * D), (w2_d, DFF * D), (w3_d, DFF * D)):
        flat = wd.rearrange("a b c d -> (a b c d)")
        nch = nel // (128 * 1024)
        for i in range(nch):
            scale_chunks.append(flat[i * 131072:(i + 1) * 131072]
                                .rearrange("(p n) -> p n", p=128))
    n_scale = len(scale_chunks)
    scale_state = {"i": 0, "se": [0] * n_scale, "fold": [], "w_red": 0}

    def emit_fold(j):
        w_last = scale_state["se"][(j + 1) * 128 - 1]
        w_red = st("vector", lambda e: e.reduce_sum(out=accw[:, 128:129],
                                                    in_=accw[:, 0:128], axis=AX.X),
                   waits=[(sSE, w_last)])
        scale_state["w_red"] = w_red
        st("sync", lambda e, j=j: e.dma_start(out=bnc_d[8 + j:9 + j, 0:128],
                                              in_=accw[:, 128:129]),
           waits=[(sVE, w_red)], inc=sB[2], inck=16)
        w1 = sB[2].n
        st("sync", lambda e, j=j: e.dma_start(out=lrec[0:1, 0:128],
                                              in_=bnc_d[8 + j:9 + j, 0:128]),
           waits=[(sB[2], w1)], inc=sB[3], inck=16)
        w2 = sB[3].n
        w3 = st("vector", lambda e: e.reduce_sum(out=lrec[0:1, 128:129],
                                                 in_=lrec[0:1, 0:128], axis=AX.X),
                waits=[(sB[3], w2)])
        st("sync", lambda e, j=j: e.dma_start(out=bnc_d[8 + j:9 + j, 128:129],
                                              in_=lrec[0:1, 128:129]),
           waits=[(sVE, w3)], inc=sB[2], inck=16)
        w4 = sB[2].n
        st("sync", lambda e, j=j: e.dma_start(
            out=wsc[:, 7:8],
            in_=bnc_d[8 + j:9 + j, 128:129].broadcast_to((128, 1))),
           waits=[(sB[2], w4)], inc=sB[3], inck=16)
        w5 = sB[3].n
        st("vector", lambda e, j=j: e.tensor_scalar(
            out=wsc[:, 3 + j:4 + j], in0=wsc[:, 7:8],
            scalar1=1.0 / (DFF * D), scalar2=1e-5, op0=ALU.mult, op1=ALU.max),
           waits=[(sB[3], w5)])
        scale_state["fold"].append(st("vector", lambda e, j=j: e.tensor_scalar_mul(
            out=wsc[:, j:j + 1], in0=wsc[:, 3 + j:4 + j], scalar1=0.5)))

    scale_slots = [wfl[:, 0:1024], wfl[:, 1024:2048], rbc[:, 0:1024]]
    scale_sems = [sW0, sW1, sW2]
    w_ln1_dve_done = sVE.n
    w_ln1_se_done = w_h

    def emit_scale(nmax):
        for _ in range(nmax):
            i = scale_state["i"]
            if i >= n_scale:
                return
            scale_state["i"] = i + 1
            sl = i % 3
            sW = scale_sems[sl]
            slot = scale_slots[sl]
            wfree = scale_state["se"][i - 3] if i >= 3 else 0
            first_w = ([(sSE, wfree)] if i >= 3 else
                       ([(sVE, w_ln1_dve_done), (sSE, w_ln1_se_done)]
                        if sl >= 2 else []))
            for cc in range(2):
                st("sync", lambda e, i=i, slot=slot, cc=cc: e.dma_start(
                    out=slot[:, cc * 512:(cc + 1) * 512],
                    in_=scale_chunks[i][:, cc * 512:(cc + 1) * 512]),
                   waits=(first_w if cc == 0 else []), inc=sW, inck=16)
            wld = sW.n
            col = i % 128
            scale_state["se"][i] = st(
                "scalar", lambda e, slot=slot, col=col: e.activation(
                    out=slot, in_=slot,
                    func=AF.Abs, accum_out=accw[:, col:col + 1]),
                waits=[(sW, wld)] + ([(sVE, scale_state["w_red"])]
                                     if col == 0 and i > 0 else []))
            if (i + 1) % 128 == 0:
                emit_fold(i // 128)


    events = []
    for g in range(8):
        events.append(("V", g))
        events.append(("KQ", 2 * g))
        events.append(("KQ", 2 * g + 1))
    slot_of = {}
    for i, ev in enumerate(events):
        slot_of[ev] = i % 2
    ev_index = {ev: i for i, ev in enumerate(events)}
    slots = [wsl, wsl2]
    sAs = [sA0, sA1]

    scale_state["w_ones"] = st("vector", lambda e: e.memset(onesf128[:, :], 1.0))
    ev_ld = {}
    ev_done = {}

    def emit_ev_load(ev):
        i = ev_index[ev]
        sl = slot_of[ev]
        sA = sAs[sl]
        wcred = [(sMM, ev_done[events[i - 2]])] if i >= 2 else []
        if ev[0] == "V":
            g = ev[1]
            st("gpsimd", lambda e, sl=sl, g=g: e.dma_start(
                out=slots[sl][:, 0:4096], in_=wv_d[g]), waits=wcred, inc=sA, inck=16)
        else:
            h = ev[1]
            st("gpsimd", lambda e, sl=sl, h=h: e.dma_start(
                out=slots[sl][:, 0:2048], in_=wk_d[h]), waits=wcred, inc=sA, inck=16)
            st("gpsimd", lambda e, sl=sl, h=h: e.dma_start(
                out=slots[sl][:, 2048:4096], in_=wq_d[h]), inc=sA, inck=16)
        ev_ld[ev] = sA.n

    emit_ev_load(events[0])
    emit_ev_load(events[1])

    w_ln1_last_sq = sSE.n
    hrope = {}

    w_vgrp_evac = [0] * 8
    w_yevac = 0
    w_lrowcp = 0
    w_exp = {}
    for h in range(NH):
        g = h // 2
        kq_ev = ("KQ", h)
        v_ev = ("V", g)
        sl_kq = slot_of[kq_ev]
        sl_v = slot_of[v_ev]
        kslot = ksb[:, (h % 2) * T:(h % 2 + 1) * T]
        qslot = qsb[:, (h % 2) * OWN:(h % 2 + 1) * OWN]
        vslot = vsb[:, (g % 2) * NKC * 256:((g % 2) + 1) * NKC * 256]

        if h % 2 == 0:
            for kkc in range(NKC):
                for kc in range(NKC):
                    st("tensor", lambda e, kkc=kkc, kc=kc, sl=sl_v: e.matmul(
                        psE[:, 0:256], h_sb[:, kc * T + kkc * 128: kc * T + kkc * 128 + 128],
                        slots[sl][:, kc * 256:(kc + 1) * 256],
                        start=(kc == 0), stop=(kc == NKC - 1)),
                       waits=(([(sAs[sl_v], ev_ld[v_ev]), (sSE, w_ln1_last_sq)]
                               if kkc == 0 else []) +
                              ([(sVE, w_vgrp_evac[g])] if (kc == 0 and kkc > 0) else
                               [(sVE, w_vgrp_evac[g - 1] if g > 0 else 0)]
                               if kc == 0 else [])),
                       inc=(sMM if kc == NKC - 1 else None))
                w_mm = sMM.n
                w_vgrp_evac[g] = st("vector", lambda e, kkc=kkc, vslot=vslot: e.tensor_copy(
                    out=vslot[:, kkc * 256:(kkc + 1) * 256], in_=psE[:, 0:256]),
                    waits=[(sMM, w_mm)])
            ev_done[v_ev] = sMM.n
            ni = ev_index[v_ev] + 2
            if ni < len(events):
                emit_ev_load(events[ni])
            if h == 0 and "d_v0" in DBG:
                st("gpsimd", lambda e, vslot=vslot: e.dma_start(
                    out=DBG["d_v0"][:, :], in_=vslot[:, :]),
                   waits=[(sVE, w_vgrp_evac[0])], inc=sDBG2, inck=16)

        w_kq_done_mm = 0
        for tkc in range(4):
            pst = psF[:, 0:512]
            for kc in range(NKC):
                st("tensor", lambda e, tkc=tkc, kc=kc, sl=sl_kq: e.matmul(
                    pst, slots[sl][:, kc * 128:(kc + 1) * 128],
                    h_sb[:, kc * T + tkc * 512: kc * T + (tkc + 1) * 512],
                    start=(kc == 0), stop=(kc == NKC - 1)),
                   waits=([(sAs[sl_kq], ev_ld[kq_ev])] +
                          ([(sSE, hrope.get((h, tkc - 1), 0))] if tkc > 0 else
                           [(sSE, hrope.get((h - 1, "q1"), 0))]))
                   if kc == 0 else [],
                   inc=(sMM if kc == NKC - 1 else None))
            w_mm = sMM.n
            prev_v = (hrope.get((h, tkc - 1, "v"), 0) if tkc > 0
                      else hrope.get((h - 1, "q1", "v"), 0))
            hrope[(h, tkc)] = st("scalar", lambda e: e.activation(
                out=kraw[:, 0:512], in_=pst, func=AF.Identity),
                waits=[(sMM, w_mm), (sVE, prev_v)])
            st("vector", lambda e: e.tensor_copy(out=krot[0:32, 0:512],
                                                 in_=kraw[32:64, 0:512]),
               waits=[(sSE, hrope[(h, tkc)])])
            st("vector", lambda e: e.tensor_copy(out=krot[32:64, 0:512],
                                                 in_=kraw[0:32, 0:512]))
            st("vector", lambda e, tkc=tkc: e.tensor_tensor(
                out=sm2[0:64, 0:512], in0=kraw[0:64, 0:512],
                in1=cosT[:, tkc * 512:(tkc + 1) * 512], op=ALU.mult))
            st("vector", lambda e, tkc=tkc: e.tensor_tensor(
                out=krot[0:64, 0:512], in0=krot[0:64, 0:512],
                in1=ssinT[:, tkc * 512:(tkc + 1) * 512], op=ALU.mult))
            st("vector", lambda e, tkc=tkc, kslot=kslot: e.tensor_tensor(
                out=kslot[0:64, tkc * 512:(tkc + 1) * 512], in0=sm2[0:64, 0:512],
                in1=krot[0:64, 0:512], op=ALU.add))
            w_krope_v = st("vector", lambda e, tkc=tkc, kslot=kslot: e.tensor_copy(
                out=kslot[64:128, tkc * 512:(tkc + 1) * 512], in_=kraw[64:128, 0:512]))
            hrope[(h, tkc)] = hrope[(h, tkc)]
            hrope[(h, tkc, "v")] = w_krope_v
        for qkc in range(2):
            pst = psF[:, 0:512]
            for kc in range(NKC):
                st("tensor", lambda e, qkc=qkc, kc=kc, sl=sl_kq: e.matmul(
                    pst, slots[sl][:, 2048 + kc * 128: 2048 + (kc + 1) * 128],
                    h_sb[:, kc * T + qkc * 512: kc * T + (qkc + 1) * 512],
                    start=(kc == 0), stop=(kc == NKC - 1)),
                   waits=[(sSE, hrope[(h, 3)] if qkc == 0 else hrope[(h, "q0")])]
                   if kc == 0 else [],
                   inc=(sMM if kc == NKC - 1 else None))
            w_mm = sMM.n
            key = "q%d" % qkc
            prev_v = hrope[(h, 3, "v")] if qkc == 0 else hrope[(h, "q0", "v")]
            hrope[(h, key)] = st("scalar", lambda e: e.activation(
                out=kraw[:, 0:512], in_=pst, func=AF.Identity),
                waits=[(sMM, w_mm), (sVE, prev_v)])
            st("vector", lambda e: e.tensor_copy(out=krot[0:32, 0:512],
                                                 in_=kraw[32:64, 0:512]),
               waits=[(sSE, hrope[(h, key)])])
            st("vector", lambda e: e.tensor_copy(out=krot[32:64, 0:512],
                                                 in_=kraw[0:32, 0:512]))
            st("vector", lambda e, qkc=qkc: e.tensor_tensor(
                out=sm2[0:64, 0:512], in0=kraw[0:64, 0:512],
                in1=cosT[:, qkc * 512:(qkc + 1) * 512], op=ALU.mult))
            st("vector", lambda e, qkc=qkc: e.tensor_tensor(
                out=krot[0:64, 0:512], in0=krot[0:64, 0:512],
                in1=ssinT[:, qkc * 512:(qkc + 1) * 512], op=ALU.mult))
            st("vector", lambda e, qkc=qkc, qslot=qslot: e.tensor_tensor(
                out=qslot[0:64, qkc * 512:(qkc + 1) * 512], in0=sm2[0:64, 0:512],
                in1=krot[0:64, 0:512], op=ALU.add))
            hrope[(h, key, "v")] = st("vector", lambda e, qkc=qkc, qslot=qslot: e.tensor_copy(
                out=qslot[64:128, qkc * 512:(qkc + 1) * 512], in_=kraw[64:128, 0:512]))
        ev_done[kq_ev] = sMM.n
        ni = ev_index[kq_ev] + 2
        if ni < len(events):
            emit_ev_load(events[ni])
        if h == 0:
            if "d_k0" in DBG:
                st("gpsimd", lambda e, kslot=kslot: e.dma_start(out=DBG["d_k0"][:, :], in_=kslot[:, :]),
                   waits=[(sVE, hrope[(0, 3, "v")])], inc=sDBG2, inck=16)
            if "d_q0" in DBG:
                st("gpsimd", lambda e, qslot=qslot: e.dma_start(out=DBG["d_q0"][:, :], in_=qslot[:, :]),
                   waits=[(sVE, hrope[(0, "q1", "v")])], inc=sDBG2, inck=16)

        w_rope_all = max(hrope[(h, 3, "v")], hrope[(h, "q1", "v")],
                         hrope[(h, 0, "v")], hrope[(h, 1, "v")], hrope[(h, 2, "v")],
                         hrope[(h, "q0", "v")])
        for qh in range(2):
            ps_y = psC if qh == 0 else psD
            ps_l = psG if qh == 0 else psH

            def emit_lav(kkc, qh=qh, ps_y=ps_y, ps_l=ps_l):
                wexp = w_exp[(h, qh, kkc)]
                st("tensor", lambda e, kkc=kkc, ps_l=ps_l: e.matmul(
                    ps_l[:, 0:512], onesb[:, :], esb[:, kkc * 512:(kkc + 1) * 512],
                    start=(kkc == 0), stop=(kkc == NKC - 1)),
                   waits=[(sSE, wexp)] + ([(sVE, w_lrowcp)] if kkc == 0 else []),
                   inc=(sMM if kkc == NKC - 1 else None))
                st("tensor", lambda e, kkc=kkc, qh=qh, ps_y=ps_y, vslot=vslot,
                   h=h: e.matmul(
                       ps_y[:, 0:512],
                       vslot[:, kkc * 256 + (h % 2) * 128: kkc * 256 + (h % 2) * 128 + 128],
                       esb[:, kkc * 512:(kkc + 1) * 512],
                       start=(kkc == 0), stop=(kkc == NKC - 1)),
                   inc=(sMM if kkc == NKC - 1 else None))

            for kkc in range(NKC):
                ps_s = psA if kkc % 2 == 0 else psB
                st("tensor", lambda e, kkc=kkc, qh=qh, ps_s=ps_s, kslot=kslot,
                   qslot=qslot: e.matmul(
                       ps_s[:, 0:512], kslot[:, kkc * 128:(kkc + 1) * 128],
                       qslot[:, qh * 512:(qh + 1) * 512], start=True, stop=True),
                   waits=([(sVE, w_rope_all), (sVE, w_vgrp_evac[g]),
                           (sSE, w_yevac), (sVE, w_lrowcp)] if (qh == 0 and kkc == 0)
                          else [(sSE, w_exp.get((h, qh, kkc - 2),
                                                w_exp.get((h, 0, 14 + kkc), 0)))]),
                   inc=sMM)
                w_smm = sMM.n
                prev_avdone = (w_exp.get(("avdone", h, 0), 0) if qh == 1
                               else w_exp.get(("avdone", h - 1, 1), 0))
                w_exp[(h, qh, kkc)] = st("scalar", lambda e, kkc=kkc, ps_s=ps_s: e.activation(
                    out=esb[:, kkc * 512:(kkc + 1) * 512], in_=ps_s[:, 0:512],
                    func=AF.Exp, bias=consts[:, 0:1], scale=SCALE_S),
                    waits=([(sMM, w_smm), (sMM, prev_avdone), (sDBG2, sDBG2.n)]
                           if kkc == 0 else [(sMM, w_smm)]))
                if kkc >= 2:
                    emit_lav(kkc - 2)
                emit_scale(1)
            emit_lav(NKC - 2)
            emit_lav(NKC - 1)
            w_exp[("avdone", h, qh)] = sMM.n
            w_yevac = st("scalar", lambda e, h=h, qh=qh, ps_y=ps_y: e.activation(
                out=ysb[:, h * OWN + qh * 512: h * OWN + (qh + 1) * 512],
                in_=ps_y[:, 0:512], func=AF.Identity), waits=[(sMM, sMM.n)])
            w_lrowcp = st("vector", lambda e, qh=qh, ps_l=ps_l: e.tensor_copy(
                out=rows[0:1, qh * 512:(qh + 1) * 512], in_=ps_l[0:1, 0:512]),
                waits=[(sMM, sMM.n), (sLB, sLB.n)])
        st("sync", lambda e, h=h: e.dma_start(out=bnc_d[32 + h:33 + h, 0:1024],
                                              in_=rows[0:1, 0:1024]),
           waits=[(sVE, w_lrowcp)], inc=sLB, inck=16)
        if h == e_head and "d_e0" in DBG:
            st("gpsimd", lambda e: e.dma_start(out=DBG["d_e0"][:, :], in_=esb[:, :]),
               waits=[(sSE, w_exp[(e_head, 1, NKC - 1)])], inc=sDBG2, inck=16)

    st("sync", lambda e: e.dma_start(out=lrec[:, :], in_=bnc_d[32:48, 0:1024]),
       waits=[(sLB, sLB.n),
              (sVE, scale_state["fold"][-1] if scale_state["fold"] else 0)],
       inc=sLR, inck=16)
    w_lrl = sLR.n
    w_rec = st("vector", lambda e: e.reciprocal(out=lrec[:, :], in_=lrec[:, :]),
               waits=[(sLR, w_lrl)])
    st("sync", lambda e: e.dma_start(out=bnc_d[48:64, 0:1024], in_=lrec[:, :]),
       waits=[(sVE, w_rec)], inc=sLR, inck=16)
    w_lrw = sLR.n
    if "d_l" in DBG:
        st("sync", lambda e: e.dma_start(out=DBG["d_l"][:, :], in_=lrec[:, :]),
           waits=[(sVE, w_rec)], inc=sDBG, inck=16)
    w_ynorm = 0
    for h in range(NH):
        rb = rbc if h % 2 == 0 else sm2
        sRB = sRB0 if h % 2 == 0 else sRB1
        st("sync", lambda e, h=h, rb=rb: e.dma_start(
            out=rb[:, 0:1024],
            in_=bnc_d[48 + h:49 + h, 0:1024].broadcast_to((128, 1024))),
           waits=[(sLR, w_lrw)] + ([(sVE, w_ynorm)] if h >= 2 else
                                    [(sVE, scale_state["fold"][-1]
                                      if scale_state["fold"] else 0)]),
           inc=sRB, inck=16)
        wrb = sRB.n
        w_ynorm = st("vector", lambda e, h=h, rb=rb: e.tensor_tensor(
            out=ysb[:, h * OWN:(h + 1) * OWN], in0=ysb[:, h * OWN:(h + 1) * OWN],
            in1=rb[:, 0:1024], op=ALU.mult),
            waits=[(sRB, wrb), (sSE, w_yevac)])
    if "d_y" in DBG:
        st("gpsimd", lambda e: e.dma_start(out=DBG["d_y"][:, :], in_=ysb[:, :]),
           waits=[(sVE, w_ynorm)], inc=sDBG3, inck=16)
    w_dy_tap = sDBG3.n if "d_y" in DBG else 0

    emit_scale(n_scale)
    w_scale_fold = scale_state["fold"]
    if "d_ws" in dbg:
        DBG["d_ws"] = nc.dram_tensor("d_ws", [1, 16], F32, kind="ExternalOutput").ap()
        st("sync", lambda e: e.dma_start(out=DBG["d_ws"][0:1, 0:8], in_=wsc[0:1, :]),
           waits=[(sVE, w_scale_fold[-1])], inc=sDBG4, inck=16)

    wo_slot = [wsl[:, 0:2048], wsl[:, 2048:4096]]
    w_last_attn_ev = ev_done[events[-1]]
    wo_done = {}

    def emit_wo_load(dc):
        sl = dc % 2
        sA = sAs[sl]
        st("gpsimd", lambda e, dc=dc, sl=sl: e.dma_start(
            out=wo_slot[sl], in_=wo_d[dc]),
           waits=[(sMM, wo_done[dc - 2])] if dc >= 2 else
                 [(sMM, ev_done[events[-2 + dc]])], inc=sA, inck=16)
        ev_ld[("WO", dc)] = sA.n

    emit_wo_load(0)
    emit_wo_load(1)
    w_x2 = {}
    w_stmm = {}
    w_x2spill = []
    for dc in range(NKC):
        sl = dc % 2
        sXO = sXO0 if dc % 2 == 0 else sXO1
        xsl = xst[:, (dc % 2) * 1024:(dc % 2 + 1) * 1024]
        st("sync", lambda e, dc=dc, xsl=xsl: e.dma_start(
            out=xsl, in_=xc[dc * 128:(dc + 1) * 128, 0:OWN]),
           waits=[(sVE, w_x2.get(dc - 2, 0))] +
                 ([(sXS0 if dc % 2 == 0 else sXS1, w_x2spill[dc - 2])]
                  if dc >= 2 else []) +
                 ([(sMM, w_last_attn_ev)] if dc < 2 else []), inc=sXO, inck=16)
        w_xo = sXO.n
        for qh in range(2):
            ps_o = (psA if qh == 0 else psB) if dc % 2 == 0 else (psC if qh == 0 else psD)
            for kc in range(NKC):
                st("tensor", lambda e, dc=dc, qh=qh, kc=kc, sl=sl, ps_o=ps_o: e.matmul(
                    ps_o[:, 0:512], wo_slot[sl][:, kc * 128:(kc + 1) * 128],
                    ysb[:, kc * OWN + qh * 512: kc * OWN + (qh + 1) * 512],
                    start=(kc == 0), stop=(kc == NKC - 1)),
                   waits=([(sAs[sl], ev_ld[("WO", dc)]), (sVE, w_ynorm)] +
                          ([(sSE, w_x2.get(("se", dc - 2, qh), 0))] if dc >= 2 else []))
                   if kc == 0 else [],
                   inc=(sMM if kc == NKC - 1 else None))
            w_omm = sMM.n
            w_x2[("se", dc, qh)] = st("scalar", lambda e, dc=dc, qh=qh, ps_o=ps_o: e.activation(
                out=kraw[:, 0:512], in_=ps_o[:, 0:512], func=AF.Identity,
                scale=modv[:, 32 + dc:33 + dc]),
                waits=[(sMM, w_omm)] + ([(sVE, w_x2.get(("tt", dc, 0), 0))]
                                        if qh == 1 else
                                        [(sVE, w_x2.get(("tt", dc - 1, 1), 0))]))
            w_x2[("tt", dc, qh)] = st("vector", lambda e, dc=dc, qh=qh, xsl=xsl: e.tensor_tensor(
                out=xsl[:, qh * 512:(qh + 1) * 512], in0=xsl[:, qh * 512:(qh + 1) * 512],
                in1=kraw[:, 0:512], op=ALU.add),
                waits=[(sXO, w_xo), (sSE, w_x2[("se", dc, qh)])])
            st("vector", lambda e, qh=qh, xsl=xsl: e.tensor_copy(
                out=xbb[:, qh * 512:(qh + 1) * 512], in_=xsl[:, qh * 512:(qh + 1) * 512]),
               waits=[(sMM, w_stmm.get((dc - 1, qh), 0))])
            w_x2[dc] = sVE.n
            w_sq2 = st("scalar", lambda e, qh=qh: e.activation(
                out=sqb2[:, 0:512], in_=xbb[:, qh * 512:(qh + 1) * 512],
                func=AF.Square),
                waits=[(sVE, w_x2[dc]),
                       (sMM, w_stmm.get((dc, qh - 1),
                                        w_stmm.get((dc - 1, 1), 0)))])
            ps_sum = psE if qh == 0 else psF
            ps_sq = psG if qh == 0 else psH
            st("tensor", lambda e, dc=dc, qh=qh, ps_sum=ps_sum: e.matmul(
                ps_sum[:, 0:512], onesb[:, :], xbb[:, qh * 512:(qh + 1) * 512],
                start=(dc == 0), stop=(dc == NKC - 1)),
               waits=[(sVE, w_x2[dc])], inc=sMM)
            st("tensor", lambda e, dc=dc, qh=qh, ps_sq=ps_sq: e.matmul(
                ps_sq[:, 0:512], onesb[:, :], sqb2[:, 0:512],
                start=(dc == 0), stop=(dc == NKC - 1)),
               waits=[(sSE, w_sq2)], inc=sMM)
            w_stmm[(dc, qh)] = sMM.n
        wo_done[dc] = sMM.n
        if dc + 2 < NKC:
            emit_wo_load(dc + 2)
        sXSp = sXS0 if dc % 2 == 0 else sXS1
        st("sync", lambda e, dc=dc, xsl=xsl: e.dma_start(
            out=x2s_d[dc * 128:(dc + 1) * 128, :], in_=xsl),
           waits=[(sVE, w_x2[dc])], inc=sXSp, inck=16)
        w_x2spill.append(sXSp.n)
    w_ln2_mm = sMM.n
    if "d_x2" in DBG:
        st("sync", lambda e: e.dma_start(out=DBG["d_x2"][:, :], in_=x2s_d[:, :])
           if False else e.nop())
        for dc in range(NKC):
            st("gpsimd", lambda e, dc=dc: e.dma_start(
                out=DBG["d_x2"][dc * 128:(dc + 1) * 128, :],
                in_=x2s_d[dc * 128:(dc + 1) * 128, :]),
               waits=[(sXS0 if dc % 2 == 0 else sXS1, w_x2spill[dc])],
               inc=sDBG2, inck=16)

    if phases <= 2:
        for kc in range(NKC):
            st("gpsimd", lambda e, kc=kc: e.dma_start(
                out=out_d[kc * 128:(kc + 1) * 128, :], in_=ysb[:, kc * OWN:kc * OWN + OWN]),
               waits=[(sVE, w_ynorm)] if kc == 0 else [], inc=sOUT, inck=16)
        w_fin = sOUT.n
        st("sync", lambda e: e.nop(),
           waits=[(sOUT, w_fin), (sDBG, sDBG.n), (sDBG2, sDBG2.n),
                  (sXS0, sXS0.n), (sXS1, sXS1.n), (sMM, sMM.n)])
        K.DBG = DBG
        return K

    dbg_t("d_h2", [128, NKC * OWN]); dbg_t("d_q12", [128, NKC * OWN])
    dbg_t("d_hp", [128, 2 * OWN]); dbg_t("d_q3", [128, 2 * OWN])
    dbg_t("d_rows", [1, 2048])
    rr0 = rows[0:1, 0:1024]
    rr1 = rows[0:1, 1024:2048]
    rr2 = rows[0:1, 2048:3072]
    st("vector", lambda e: e.tensor_scalar_mul(out=rows[0:1, 0:512], in0=psE[0:1, 0:512],
                                               scalar1=1.0 / D),
       waits=[(sMM, w_ln2_mm), (sLR, sLR.n), (sRB0, sRB0.n), (sRB1, sRB1.n)])
    st("vector", lambda e: e.tensor_scalar_mul(out=rows[0:1, 512:1024],
                                               in0=psF[0:1, 0:512], scalar1=1.0 / D))
    st("vector", lambda e: e.tensor_scalar_mul(out=rows[0:1, 1024:1536],
                                               in0=psG[0:1, 0:512], scalar1=1.0 / D))
    w_r4 = st("vector", lambda e: e.tensor_scalar_mul(out=rows[0:1, 1536:2048],
                                                      in0=psH[0:1, 0:512], scalar1=1.0 / D))
    st("vector", lambda e: e.tensor_tensor(out=rr2, in0=rr0, in1=rr0, op=ALU.mult))
    st("vector", lambda e: e.tensor_tensor(out=rr1, in0=rr1, in1=rr2, op=ALU.subtract))
    w_var2 = sVE.n
    st("scalar", lambda e: e.activation(out=rr2, in_=rr1, func=AF.Sqrt,
                                        bias=consts[0:1, 2:3]), waits=[(sVE, w_var2)])
    w_std2 = sSE.n
    st("vector", lambda e: e.reciprocal(out=rr1, in_=rr2), waits=[(sSE, w_std2)])
    st("vector", lambda e: e.tensor_tensor(out=rr0, in0=rr0, in1=rr1, op=ALU.mult))
    w_ab2 = st("vector", lambda e: e.tensor_scalar_mul(out=rr0, in0=rr0, scalar1=-1.0))
    wbA2w, wbA2 = bounce(0, rr1, 1024, 16, rbc[:, 0:1024], waits=[(sVE, w_ab2)])
    wbB2w, wbB2 = bounce(1, rr0, 1024, 17, sm2[:, 0:1024], waits=[(sVE, w_ab2)])
    w_h2 = {}
    w_h2mm = {}
    w_maxinit = st("vector", lambda e: e.memset(maxac[:, :], 0.0))
    for dc in range(NKC):
        sXO = sXO0 if dc % 2 == 0 else sXO1
        xsl = xst[:, (dc % 2) * 1024:(dc % 2 + 1) * 1024]
        st("sync", lambda e, dc=dc, xsl=xsl: e.dma_start(
            out=xsl, in_=x2s_d[dc * 128:(dc + 1) * 128, :]),
           waits=([(sVE, w_h2.get(dc - 2, 0)), (sSE, w_h2.get(("aff", dc - 2), 0))]
                  if dc >= 2 else
                  [(sXS0, sXS0.n), (sXS1, sXS1.n)]), inc=sXO, inck=16)
        w_xo = sXO.n
        st("vector", lambda e, xsl=xsl: e.tensor_tensor(
            out=xsl, in0=xsl, in1=rbc[:, 0:1024], op=ALU.mult),
           waits=[(sXO, w_xo), wbA2, wbB2])
        w_t2 = st("vector", lambda e, xsl=xsl: e.tensor_tensor(
            out=xsl, in0=xsl, in1=sm2[:, 0:1024], op=ALU.add))
        w_aff = st("scalar", lambda e, dc=dc, xsl=xsl: e.activation(
            out=h2sb[:, dc * OWN:(dc + 1) * OWN], in_=xsl, func=AF.Identity,
            scale=modv[:, 64 + dc:65 + dc], bias=modv[:, 48 + dc:49 + dc]),
            waits=[(sVE, w_t2)] + ([(sDBG3, w_dy_tap)] if dc == 0 else []))
        w_h2[("aff", dc)] = w_aff
        w_h2[dc] = sVE.n
        for th in range(2):
            w_sqh = st("scalar", lambda e, dc=dc, th=th: e.activation(
                out=sqb2[:, 0:512],
                in_=h2sb[:, dc * OWN + th * 512: dc * OWN + (th + 1) * 512],
                func=AF.Square),
                waits=[(sMM, w_h2mm.get((dc, th - 1),
                                        w_h2mm.get((dc - 1, 1), 0))),
                       (sVE, w_h2.get(("mx", dc, th - 1),
                                      w_h2.get(("mx", dc - 1, 1), 0)))])
            ps_sq = psA if th == 0 else psB
            st("tensor", lambda e, dc=dc, th=th, ps_sq=ps_sq: e.matmul(
                ps_sq[:, 0:512], onesb[:, :], sqb2[:, 0:512],
                start=(dc == 0), stop=(dc == NKC - 1)),
               waits=[(sSE, w_sqh)], inc=sMM)
            w_h2mm[(dc, th)] = sMM.n
            w_h2[("mx", dc, th)] = st("vector", lambda e, th=th: e.tensor_tensor(
                out=maxac[:, th * 512:(th + 1) * 512], in0=maxac[:, th * 512:(th + 1) * 512],
                in1=sqb2[:, 0:512], op=ALU.max),
                waits=[(sSE, w_sqh)] + ([(sVE, w_maxinit)] if dc == 0 else []))
    if "d_h2" in DBG:
        st("gpsimd", lambda e: e.dma_start(out=DBG["d_h2"][:, :], in_=h2sb[:, :]),
           waits=[(sSE, sSE.n)], inc=sDBG2, inck=16)
    st("vector", lambda e: e.tensor_scalar_mul(
        out=rows[0:1, 1024:1536], in0=psA[0:1, 0:512], scalar1=1.0 / D),
       waits=[(sMM, w_h2mm[(NKC - 1, 0)]), wbA2w])
    w_rmsev = st("vector", lambda e: e.tensor_scalar_mul(
        out=rows[0:1, 1536:2048], in0=psB[0:1, 0:512], scalar1=1.0 / D),
        waits=[(sMM, w_h2mm[(NKC - 1, 1)])])
    st("vector", lambda e: e.tensor_copy(out=tmpb[0:64, :], in_=maxac[64:128, :]),
       waits=[(sVE, w_h2[("mx", NKC - 1, 1)])])
    st("vector", lambda e: e.tensor_tensor(out=maxac[0:64, :], in0=maxac[0:64, :],
                                           in1=tmpb[0:64, :], op=ALU.max))
    st("vector", lambda e: e.tensor_copy(out=tmpb[0:32, :], in_=maxac[32:64, :]))
    w_mx32 = st("vector", lambda e: e.tensor_tensor(out=maxac[0:32, :], in0=maxac[0:32, :],
                                                    in1=tmpb[0:32, :], op=ALU.max))
    wsh = w_mx32
    for npart in (16, 8, 4, 2, 1):
        st("sync", lambda e, npart=npart: e.dma_start(
            out=tmpb[0:npart, :], in_=maxac[npart:2 * npart, :]
            if npart >= 32 else maxac[npart:2 * npart, :]),
           waits=[(sVE, wsh)], inc=sB[4], inck=16)
        wdma = sB[4].n
        wsh = st("vector", lambda e, npart=npart: e.tensor_tensor(
            out=maxac[0:npart, :], in0=maxac[0:npart, :], in1=tmpb[0:npart, :],
            op=ALU.max), waits=[(sB[4], wdma)])
    st("scalar", lambda e: e.activation(out=rr2, in_=rr1, func=AF.Sqrt,
                                        bias=consts[0:1, 3:4]),
       waits=[(sVE, wsh), (sVE, w_rmsev)])
    w_s3 = sSE.n
    st("vector", lambda e: e.reciprocal(out=rr1, in_=rr2), waits=[(sSE, w_s3)])
    st("scalar", lambda e: e.activation(out=rr0, in_=maxac[0:1, 0:1024], func=AF.Sqrt),
       waits=[(sVE, sVE.n)])
    w_s4 = sSE.n
    st("vector", lambda e: e.tensor_tensor(out=rr0, in0=rr0, in1=rr1, op=ALU.mult),
       waits=[(sSE, w_s4)])
    st("vector", lambda e: e.tensor_scalar_max(out=rr0, in0=rr0, scalar1=1e-5))
    st("vector", lambda e: e.reciprocal(out=rr2, in_=rr0))
    st("vector", lambda e: e.tensor_tensor(out=rr2, in0=rr2, in1=rr1, op=ALU.mult))
    w_cq = st("vector", lambda e: e.tensor_scalar_mul(out=rr2, in0=rr2, scalar1=127.0))
    st("vector", lambda e: e.tensor_scalar(out=rr1, in0=rr0, scalar1=0.5 / 127.0,
                                           scalar2=wsc[0:1, 3:4],
                                           op0=ALU.mult, op1=ALU.mult))
    w_du1 = sVE.n
    st("vector", lambda e: e.tensor_scalar(out=rr0, in0=rr0, scalar1=0.5 / 127.0,
                                           scalar2=wsc[0:1, 4:5],
                                           op0=ALU.mult, op1=ALU.mult))
    w_du2 = sVE.n
    wbCQw, wbCQ = bounce(2, rr2, 1024, 18, maxac[:, 0:1024], waits=[(sVE, w_cq)])
    wbD1w, wbD1 = bounce(0, rr1, 1024, 19, du1b[:, 0:1024], waits=[(sVE, w_du1), wbA2])
    if "d_rows" in DBG:
        st("sync", lambda e: e.dma_start(out=DBG["d_rows"][0:1, 0:1024], in_=rr2),
           waits=[(sVE, w_cq)], inc=sDBG, inck=16)
        st("sync", lambda e: e.dma_start(out=DBG["d_rows"][0:1, 1024:2048], in_=rr0),
           waits=[(sVE, w_du2)], inc=sDBG, inck=16)
    if "d_cqb" in dbg:
        DBG["d_cqb"] = nc.dram_tensor("d_cqb", [128, 2048], F32, kind="ExternalOutput").ap()
        st("gpsimd", lambda e: e.dma_start(out=DBG["d_cqb"][:, 0:1024], in_=maxac[:, 0:1024]),
           waits=[wbCQ], inc=sDBG5, inck=16)
        st("gpsimd", lambda e: e.dma_start(out=DBG["d_cqb"][:, 1024:2048], in_=du1b[:, 0:1024]),
           waits=[wbD1], inc=sDBG5, inck=16)
    w_q12 = 0
    for dc in range(NKC):
        chains = []
        for th in range(2):
            chains.append(dict(
                src=h2sb[:, dc * OWN + th * 512: dc * OWN + (th + 1) * 512],
                cq=maxac[:, th * 512:(th + 1) * 512],
                ta=rbc[:, th * 512:(th + 1) * 512],
                tb=sm2[:, th * 512:(th + 1) * 512],
                bb=tmpb[:, th * 512:(th + 1) * 512],
                dst=q12[:, dc * OWN + th * 512: dc * OWN + (th + 1) * 512],
                mw=([wbCQ, wbD1] if (dc == 0 and th == 0) else [])))
        w_q12 = emit_ladder(chains)
    if "d_q12" in DBG:
        st("gpsimd", lambda e: e.dma_start(out=DBG["d_q12"][:, :], in_=q12[:, :]),
           waits=[(sVE, w_q12)], inc=sDBG2, inck=16)

    st("vector", lambda e: e.tensor_scalar_mul(out=wsc[:, 6:7], in0=wsc[:, 0:1],
                                               scalar1=-1.0),
       waits=[(sVE, w_scale_fold[-1])] + ([(sDBG4, sDBG4.n)] if "d_ws" in DBG else []))
    w_negt = st("vector", lambda e: e.tensor_scalar_mul(out=wsc[:, 7:8], in0=wsc[:, 1:2],
                                                        scalar1=-1.0))
    w_f1mm = {}
    w_tern = {}
    w_ternse = {}
    w_hspill = {}
    w_hv = {}
    w_maxi3 = st("vector", lambda e: e.memset(maxac[:, :], 0.0),
                 waits=[(sVE, w_q12)] + ([(sDBG5, 32)] if "d_cqb" in DBG else []))
    def f1_loads(dc2):
        sW = sW0 if dc2 % 2 == 0 else sW1
        w_dst1 = wfl if dc2 % 2 == 0 else wfl3
        w_dst2 = wfl2 if dc2 % 2 == 0 else wfl4
        for cc in range(4):
            st("sync", lambda e, dc2=dc2, cc=cc, w_dst1=w_dst1: e.dma_start(
                out=w_dst1[:, cc * 512:(cc + 1) * 512],
                in_=w1_d[dc2][:, 4 * cc:4 * cc + 4, :]),
               waits=([(sSE, w_ternse.get(dc2 - 2, 0))] if cc == 0 else []),
               inc=sW, inck=16)
            st("sync", lambda e, dc2=dc2, cc=cc, w_dst2=w_dst2: e.dma_start(
                out=w_dst2[:, cc * 512:(cc + 1) * 512],
                in_=w2_d[dc2][:, 4 * cc:4 * cc + 4, :]),
               inc=sW, inck=16)
        return sW.n

    def f1_tern(dc2, w_wld):
        sW = sW0 if dc2 % 2 == 0 else sW1
        wslc1 = wfl if dc2 % 2 == 0 else wfl3
        wslc2 = wfl2 if dc2 % 2 == 0 else wfl4
        tA = ternA if dc2 % 2 == 0 else ternA2
        tB = ternB if dc2 % 2 == 0 else ternB2
        st("scalar", lambda e, wslc1=wslc1: e.activation(
            out=terntmp[:, :], in_=wslc1[:, :], func=AF.Sign, bias=wsc[:, 6:7]),
           waits=[(sW, w_wld), (sVE, w_negt), (sVE, w_tern.get(dc2 - 1, 0))] +
                 ([(sDBG2, sDBG2.n)] if dc2 < 2 else []))
        st("scalar", lambda e, wslc1=wslc1: e.activation(
            out=sgntmp[:, :], in_=wslc1[:, :], func=AF.Sign, bias=wsc[:, 0:1]))
        wta = st("vector", lambda e, tA=tA: e.tensor_tensor(
            out=tA[:, :], in0=terntmp[:, :], in1=sgntmp[:, :], op=ALU.add),
            waits=[(sSE, sSE.n), (sMM, w_f1mm.get(dc2 - 2, 0))])
        st("scalar", lambda e, wslc2=wslc2: e.activation(
            out=terntmp[:, :], in_=wslc2[:, :], func=AF.Sign, bias=wsc[:, 7:8]),
           waits=[(sVE, wta)])
        w_ternse[dc2] = st("scalar", lambda e, wslc2=wslc2: e.activation(
            out=sgntmp[:, :], in_=wslc2[:, :], func=AF.Sign, bias=wsc[:, 1:2]))
        w_tern[dc2] = st("vector", lambda e, tB=tB: e.tensor_tensor(
            out=tB[:, :], in0=terntmp[:, :], in1=sgntmp[:, :], op=ALU.add),
            waits=[(sSE, w_ternse[dc2])])

    w_ld = {0: f1_loads(0)}
    f1_tern(0, w_ld[0])
    for dc2 in range(NFC):
        if dc2 + 1 < NFC:
            w_ld[dc2 + 1] = f1_loads(dc2 + 1)
            f1_tern(dc2 + 1, w_ld[dc2 + 1])
        tA = ternA if dc2 % 2 == 0 else ternA2
        tB = ternB if dc2 % 2 == 0 else ternB2
        psU0 = psA if dc2 % 2 == 0 else psE
        psU1 = psB if dc2 % 2 == 0 else psF
        for th in range(2):
            for kc in range(NKC):
                st("tensor", lambda e, th=th, kc=kc, tA=tA, psU0=psU0, psU1=psU1:
                   e.matmul(
                       (psU0 if th == 0 else psU1)[:, 0:512],
                       tA[:, kc * 128:(kc + 1) * 128],
                       q12[:, kc * OWN + th * 512: kc * OWN + (th + 1) * 512],
                       start=(kc == 0), stop=(kc == NKC - 1)),
                   waits=([(sVE, w_tern[dc2]), (sVE, w_q12)] +
                          ([(sVE, w_hv.get(("udq", dc2 - 2, th), 0))])
                          if kc == 0 else []),
                   inc=(sMM if kc == NKC - 1 else None))
        for th in range(2):
            for kc in range(NKC):
                st("tensor", lambda e, th=th, kc=kc, tB=tB: e.matmul(
                    (psC if th == 0 else psD)[:, 0:512],
                    tB[:, kc * 128:(kc + 1) * 128],
                    q12[:, kc * OWN + th * 512: kc * OWN + (th + 1) * 512],
                    start=(kc == 0), stop=(kc == NKC - 1)),
                   waits=([(sVE, w_hv.get((dc2 - 1, th), 0))] if kc == 0 else []),
                   inc=(sMM if kc == NKC - 1 else None))
        w_f1mm[dc2] = sMM.n
        for th in range(2):
            psu = psU0 if th == 0 else psU1
            pst2 = psC if th == 0 else psD
            hsl = hch[:, ((2 * dc2 + th) % 4) * 512:(((2 * dc2 + th) % 4) + 1) * 512]
            sHSp = sHS[(2 * dc2 + th) % 4]
            st("vector", lambda e, psu=psu, th=th: e.tensor_tensor(
                out=rbc[:, th * 512:(th + 1) * 512], in0=psu[:, 0:512],
                in1=du1b[:, th * 512:(th + 1) * 512], op=ALU.mult),
               waits=[(sMM, w_f1mm[dc2])] +
                     ([(sHSp, sHSp.n)] if dc2 >= 2 else []))
            w_udq = sVE.n
            w_hv[("udq", dc2, th)] = w_udq
            st("scalar", lambda e, th=th: e.activation(
                out=sm2[:, th * 512:(th + 1) * 512], in_=rbc[:, th * 512:(th + 1) * 512],
                func=AF.Silu), waits=[(sVE, w_udq)])
            w_sil = sSE.n
            w_hv[(dc2, th)] = st("vector", lambda e, pst2=pst2, th=th, hsl=hsl: e.tensor_tensor(
                out=hsl, in0=sm2[:, th * 512:(th + 1) * 512], in1=pst2[:, 0:512],
                op=ALU.mult), waits=[(sSE, w_sil)])
            st("sync", lambda e, dc2=dc2, th=th, hsl=hsl: e.dma_start(
                out=hs_d[dc2, :, th * 512:(th + 1) * 512], in_=hsl),
               waits=[(sVE, w_hv[(dc2, th)])], inc=sHSp, inck=16)
            w_hspill[(dc2, th)] = sHSp.n
            w_sq3 = st("scalar", lambda e, hsl=hsl: e.activation(
                out=sqb2[:, 0:512], in_=hsl, func=AF.Square),
                waits=[(sVE, w_hv[(dc2, th)]),
                       (sMM, w_f1mm.get(("st", dc2, th - 1),
                                        w_f1mm.get(("st", dc2 - 1, 1), 0))),
                       (sVE, w_hv.get(("mx", dc2, th - 1),
                                      w_hv.get(("mx", dc2 - 1, 1), 0)))])
            ps_sq = psG if th == 0 else psH
            st("tensor", lambda e, dc2=dc2, th=th, ps_sq=ps_sq: e.matmul(
                ps_sq[:, 0:512], onesb[:, :], sqb2[:, 0:512],
                start=(dc2 == 0), stop=(dc2 == NFC - 1)),
               waits=[(sSE, w_sq3)], inc=sMM)
            w_f1mm[("st", dc2, th)] = sMM.n
            w_hv[("mx", dc2, th)] = st("vector", lambda e, th=th: e.tensor_tensor(
                out=maxac[:, th * 512:(th + 1) * 512],
                in0=maxac[:, th * 512:(th + 1) * 512], in1=sqb2[:, 0:512], op=ALU.max),
                waits=[(sSE, w_sq3)] + ([(sVE, w_maxi3)] if dc2 == 0 else []))
    if "d_hp" in DBG:
        st("gpsimd", lambda e: e.dma_start(out=DBG["d_hp"][:, 0:1024],
                                           in_=hs_d[0, :, :]),
           waits=[(sHS[0], w_hspill[(0, 0)]), (sHS[1], w_hspill[(0, 1)])],
           inc=sDBG2, inck=16)

    st("vector", lambda e: e.tensor_scalar_mul(out=rows[0:1, 1024:1536],
                                               in0=psG[0:1, 0:512], scalar1=1.0 / DFF),
       waits=[(sMM, w_f1mm[("st", NFC - 1, 0)]), (sMM, w_f1mm[("st", NFC - 1, 1)]),
              (sDBG, sDBG.n)])
    st("vector", lambda e: e.tensor_scalar_mul(out=rows[0:1, 1536:2048],
                                               in0=psH[0:1, 0:512], scalar1=1.0 / DFF))
    st("vector", lambda e: e.tensor_tensor(out=rr2, in0=rr0, in1=rr0, op=ALU.mult))
    st("vector", lambda e: e.tensor_tensor(out=rr1, in0=rr1, in1=rr2, op=ALU.mult))
    w_e3 = sVE.n
    st("scalar", lambda e: e.activation(out=rr2, in_=rr1, func=AF.Sqrt,
                                        bias=consts[0:1, 3:4]), waits=[(sVE, w_e3)])
    w_s5 = sSE.n
    st("vector", lambda e: e.reciprocal(out=rr1, in_=rr2), waits=[(sSE, w_s5)])
    st("vector", lambda e: e.tensor_copy(out=tmpb[0:64, :], in_=maxac[64:128, :]),
       waits=[(sVE, w_hv[("mx", NFC - 1, 1)])])
    st("vector", lambda e: e.tensor_tensor(out=maxac[0:64, :], in0=maxac[0:64, :],
                                           in1=tmpb[0:64, :], op=ALU.max))
    st("vector", lambda e: e.tensor_copy(out=tmpb[0:32, :], in_=maxac[32:64, :]))
    wsh = st("vector", lambda e: e.tensor_tensor(out=maxac[0:32, :], in0=maxac[0:32, :],
                                                 in1=tmpb[0:32, :], op=ALU.max))
    for npart in (16, 8, 4, 2, 1):
        st("sync", lambda e, npart=npart: e.dma_start(
            out=tmpb[0:npart, :], in_=maxac[npart:2 * npart, :]),
           waits=[(sVE, wsh)], inc=sB[4], inck=16)
        wdma = sB[4].n
        wsh = st("vector", lambda e, npart=npart: e.tensor_tensor(
            out=maxac[0:npart, :], in0=maxac[0:npart, :], in1=tmpb[0:npart, :],
            op=ALU.max), waits=[(sB[4], wdma)])
    st("scalar", lambda e: e.activation(out=rr2, in_=maxac[0:1, 0:1024], func=AF.Sqrt),
       waits=[(sVE, wsh)])
    w_s6 = sSE.n
    st("vector", lambda e: e.tensor_tensor(out=rr2, in0=rr2, in1=rr0, op=ALU.mult),
       waits=[(sSE, w_s6)])
    st("vector", lambda e: e.tensor_tensor(out=rr2, in0=rr2, in1=rr1, op=ALU.mult))
    st("vector", lambda e: e.tensor_scalar_max(out=rr2, in0=rr2, scalar1=1e-5))
    st("vector", lambda e: e.tensor_tensor(out=rr1, in0=rr1, in1=rr0, op=ALU.mult))
    st("vector", lambda e: e.reciprocal(out=rr0, in_=rr2))
    st("vector", lambda e: e.tensor_tensor(out=rr1, in0=rr1, in1=rr0, op=ALU.mult))
    w_cq3 = st("vector", lambda e: e.tensor_scalar_mul(out=rr1, in0=rr1, scalar1=127.0))
    st("vector", lambda e: e.tensor_scalar(out=rr2, in0=rr2, scalar1=0.5 / 127.0,
                                           scalar2=wsc[0:1, 5:6],
                                           op0=ALU.mult, op1=ALU.mult))
    w_do3 = sVE.n
    wbC3w, wbC3 = bounce(2, rr1, 1024, 20, cq3b[:, 0:1024],
                         waits=[(sVE, w_cq3), wbCQ])
    wbD3w, wbD3 = bounce(1, rr2, 1024, 21, do3b[:, 0:1024],
                         waits=[(sVE, w_do3), wbB2])
    w_negt3 = st("vector", lambda e: e.tensor_scalar_mul(out=wsc[:, 6:7],
                                                         in0=wsc[:, 2:3], scalar1=-1.0),
                 waits=[(sSE, w_ternse[NFC - 1])])

    w_f3 = {}
    w_q3sp = {}
    w_outev = {}
    w_x3st = []
    for dgq in range(4):
        douts = list(range(dgq * 4, (dgq + 1) * 4))
        for k3 in range(NFC):
            qsl = q3c[:, (k3 % 2) * 1024:((k3 % 2) + 1) * 1024]
            sHRp = sHR[k3 % 2]
            if dgq == 0:
                hsl2 = hch[:, (k3 % 2) * 1024:((k3 % 2) + 1) * 1024]
                for cc in range(2):
                    st("sync", lambda e, k3=k3, hsl2=hsl2, cc=cc: e.dma_start(
                        out=hsl2[:, cc * 512:(cc + 1) * 512],
                        in_=hs_d[k3, :, cc * 512:(cc + 1) * 512]),
                       waits=([(sVE, w_f3.get(("lad", k3 - 2), 0)),
                               (sHS[0], sHS[0].n), (sHS[1], sHS[1].n),
                               (sHS[2], sHS[2].n), (sHS[3], sHS[3].n)]
                              if k3 < 2 else
                              [(sVE, w_f3.get(("lad", k3 - 2), 0))])
                       if cc == 0 else [],
                       inc=sHRp, inck=16)
                w_hr = sHRp.n
                chains = []
                for th in range(2):
                    chains.append(dict(
                        src=hsl2[:, th * 512:(th + 1) * 512],
                        cq=cq3b[:, th * 512:(th + 1) * 512],
                        ta=rbc[:, th * 512:(th + 1) * 512],
                        tb=sm2[:, th * 512:(th + 1) * 512],
                        bb=tmpb[:, th * 512:(th + 1) * 512],
                        dst=qsl[:, th * 512:(th + 1) * 512],
                        mw=[(sHRp, w_hr)] +
                           ([wbC3, wbD3] if k3 == 0 and th == 0 else []) +
                           ([(sMM, w_f3.get(("mm", k3 - 2), 0)),
                             (sQS[k3 % 2], w_q3sp.get(k3 - 2, 0))]
                            if th == 0 else [])))
                w_f3[("lad", k3)] = emit_ladder(chains)
                sQSp = sQS[k3 % 2]
                st("sync", lambda e, k3=k3, qsl=qsl: e.dma_start(
                    out=q3s_d[k3, :, :], in_=qsl),
                   waits=[(sVE, w_f3[("lad", k3)])], inc=sQSp, inck=16)
                w_q3sp[k3] = sQSp.n
                w_qrdy = ("v", w_f3[("lad", k3)])
            else:
                for cc in range(2):
                    st("sync", lambda e, k3=k3, qsl=qsl, cc=cc: e.dma_start(
                        out=qsl[:, cc * 512:(cc + 1) * 512],
                        in_=q3s_d[k3, :, cc * 512:(cc + 1) * 512]),
                       waits=(([(sQS[0], sQS[0].n), (sQS[1], sQS[1].n)] if
                               (dgq == 1 and k3 < 2) else []) +
                              [(sMM, w_f3.get(("mm", k3 - 2), 0))])
                       if cc == 0 else [],
                       inc=sHRp, inck=16)
                w_qrdy = ("d", sHRp.n, sHRp)
            sW = sW0 if k3 % 2 == 0 else sW1
            wslc = wfl if k3 % 2 == 0 else wfl2
            for cc in range(2):
                st("sync", lambda e, dgq=dgq, k3=k3, wslc=wslc, cc=cc: e.dma_start(
                    out=wslc[:, cc * 256:(cc + 1) * 256],
                    in_=w3_d[dgq, k3][:, cc * 256:(cc + 1) * 256]),
                   waits=([(sSE, w_f3.get(("ternse", k3 - 2), 0))] if cc == 0 else []),
                   inc=sW, inck=16)
            w_w3 = sW.n
            tsl = ternA[:, (k3 % 2) * 512:((k3 % 2) + 1) * 512]
            st("scalar", lambda e, wslc=wslc: e.activation(
                out=terntmp[:, 0:512], in_=wslc[:, 0:512], func=AF.Sign,
                bias=wsc[:, 6:7]),
               waits=[(sW, w_w3), (sVE, w_f3.get(("tern", k3 - 1), 0))] +
                     ([(sVE, w_negt3)] if dgq == 0 and k3 == 0 else []))
            w_f3[("ternse", k3)] = st("scalar", lambda e, wslc=wslc: e.activation(
                out=sgntmp[:, 0:512], in_=wslc[:, 0:512], func=AF.Sign,
                bias=wsc[:, 2:3]))
            w_f3[("tern", k3)] = st("vector", lambda e, tsl=tsl: e.tensor_tensor(
                out=tsl, in0=terntmp[:, 0:512], in1=sgntmp[:, 0:512], op=ALU.add),
                waits=[(sSE, w_f3[("ternse", k3)]),
                       (sMM, w_f3.get(("mm", k3 - 2), 0))])
            for di, dout in enumerate(douts):
                for th in range(2):
                    psx = pss[di * 2 + th]
                    ww = []
                    if k3 == 0:
                        ww = [(sVE, w_f3[("tern", k3)])]
                        if w_qrdy[0] == "v":
                            ww.append((sVE, w_qrdy[1]))
                        else:
                            ww.append((w_qrdy[2], w_qrdy[1]))
                        if dgq > 0 and di == 0 and th == 0:
                            ww.append((sSE, w_outev.get(dgq - 1, 0)))
                    elif di == 0 and th == 0:
                        ww = [(sVE, w_f3[("tern", k3)])]
                        if w_qrdy[0] == "v":
                            ww.append((sVE, w_qrdy[1]))
                        else:
                            ww.append((w_qrdy[2], w_qrdy[1]))
                    st("tensor", lambda e, di=di, th=th, psx=psx, qsl=qsl, k3=k3,
                       tsl=tsl: e.matmul(
                           psx[:, 0:512], tsl[:, di * 128:(di + 1) * 128],
                           qsl[:, th * 512:(th + 1) * 512],
                           start=(k3 == 0), stop=(k3 == NFC - 1)),
                       waits=ww, inc=(sMM if (di == 3 and th == 1) else None))
            w_f3[("mm", k3)] = sMM.n
        for di, dout in enumerate(douts):
            sXO = sXO0 if dout % 2 == 0 else sXO1
            xsl = xst[:, (dout % 2) * 1024:(dout % 2 + 1) * 1024]
            st("sync", lambda e, dout=dout, xsl=xsl: e.dma_start(
                out=xsl, in_=x2s_d[dout * 128:(dout + 1) * 128, :]),
               waits=[(sVE, w_f3.get(("x3", dout - 2), 0))] +
                     ([w_f3[("ost", dout - 2)]] if ("ost", dout - 2) in w_f3 else []),
               inc=sXO, inck=16)
            w_xo = sXO.n
            for th in range(2):
                psx = pss[di * 2 + th]
                w_ev = st("scalar", lambda e, psx=psx, dout=dout: e.activation(
                    out=kraw[:, 0:512], in_=psx[:, 0:512], func=AF.Identity,
                    scale=modv[:, 80 + dout:81 + dout]),
                    waits=[(sMM, w_f3[("mm", NFC - 1)])] +
                          ([(sVE, w_f3.get(("x3v", dout, th - 1),
                                           w_f3.get(("x3v", dout - 1, 1), 0)))]))
                st("vector", lambda e, th=th: e.tensor_tensor(
                    out=kraw[:, 0:512], in0=kraw[:, 0:512],
                    in1=do3b[:, th * 512:(th + 1) * 512], op=ALU.mult),
                   waits=[(sSE, w_ev)])
                w_f3[("x3v", dout, th)] = st("vector", lambda e, th=th, xsl=xsl:
                                             e.tensor_tensor(
                    out=xsl[:, th * 512:(th + 1) * 512],
                    in0=xsl[:, th * 512:(th + 1) * 512],
                    in1=kraw[:, 0:512], op=ALU.add), waits=[(sXO, w_xo)])
            w_f3[("x3", dout)] = w_f3[("x3v", dout, 1)]
            sOUTp = sOUT if dout % 2 == 0 else sOUT1
            st("sync", lambda e, dout=dout, xsl=xsl: e.dma_start(
                out=out_d[dout * 128:(dout + 1) * 128, :], in_=xsl),
               waits=[(sVE, w_f3[("x3", dout)])], inc=sOUTp, inck=16)
            w_f3[("ost", dout)] = (sOUTp, sOUTp.n)
        w_outev[dgq] = sSE.n
    if "d_q3" in DBG:
        st("gpsimd", lambda e: e.dma_start(out=DBG["d_q3"][:, 0:1024],
                                           in_=q3s_d[0, :, :]),
           waits=[(sQS[0], w_q3sp.get(0, 0))], inc=sDBG2, inck=16)

    st("sync", lambda e: e.nop(),
       waits=[(sOUT, sOUT.n), (sOUT1, sOUT1.n), (sDBG, sDBG.n), (sDBG2, sDBG2.n), (sDBG3, sDBG3.n), (sDBG4, sDBG4.n)])
    K.DBG = DBG
    return K


def _host_prep(inputs, b, half):
    x = inputs["L0_attn_x"][b]
    own = half * OWN
    pos = (np.arange(T) + own) % T
    xfm = np.ascontiguousarray(x[pos].T).astype(np.float32)
    c = inputs["L0_c"][b]
    cvec = np.ascontiguousarray(np.asarray(c).reshape(NKC, 128).T).astype(np.float32)
    return {"xc": xfm, "cvec": cvec, "pos": pos.astype(np.float32).reshape(1, T)}


def _til(w, no, ni):
    return np.ascontiguousarray(
        np.asarray(w).reshape(no // 128, 128, ni // 128, 128).transpose(0, 3, 2, 1)
    ).astype(np.float32)


def _host_weights(inputs):
    wv = np.asarray(inputs["L0_attn_wv"])
    w3 = np.asarray(inputs["L0_ffn_w3"])
    wmod = np.asarray(inputs["L0_wmod"])
    return {
        "wq": _til(inputs["L0_attn_wq"], D, D),
        "wk": _til(inputs["L0_attn_wk"], D, D),
        "wo": _til(inputs["L0_attn_wo"], D, D),
        "wv": np.ascontiguousarray(
            wv.T.reshape(NKC, 128, 8, 256).transpose(2, 1, 0, 3)).astype(np.float32),
        "w1": _til(inputs["L0_ffn_w1"], DFF, D),
        "w2": _til(inputs["L0_ffn_w2"], DFF, D),
        "w3": np.ascontiguousarray(
            w3.reshape(NKC, 128, NFC, 128).transpose(2, 3, 0, 1)
            .reshape(NFC, 128, NKC * 128)
            .reshape(NFC, 128, 4, 4 * 128).transpose(2, 0, 1, 3)).astype(np.float32),
        "wmod": np.ascontiguousarray(
            wmod.reshape(NKC, 128, 24, 512).transpose(2, 1, 0, 3)).astype(np.float32),
    }


def kernel(**inputs):
    nc = bass.Bass("TRN2", target_bir_lowering=False)
    K = build(nc)
    K.emit()
    w = _host_weights(inputs)
    in_maps = []
    for core in range(8):
        b, half = core // 2, core % 2
        m = dict(w)
        m.update(_host_prep(inputs, b, half))
        in_maps.append(m)
    res = run_bass_kernel_spmd(nc, in_maps, core_ids=list(range(8)))
    out = np.zeros((B, T, D), np.float32)
    for core in range(8):
        b, half = core // 2, core % 2
        own = half * OWN
        out[b, own:own + OWN, :] = res.results[core]["out"].T
    return out


# version 08
# speedup vs baseline: 1.2669x; 1.0026x over previous
import math
import numpy as np
import concourse.bass as bass
import concourse.mybir as mybir
from concourse.bass_utils import run_bass_kernel_spmd

F32 = mybir.dt.float32
BF16 = mybir.dt.bfloat16
AF = mybir.ActivationFunctionType
ALU = mybir.AluOpType
AX = mybir.AxisListType

B, T, D, NH, HD = 4, 2048, 2048, 16, 128
DFF = 8192
NKC = D // 128
NFC = DFF // 128
OWN = 1024
SHIFT = 12.0
SCALE_S = 1.0 / math.sqrt(HD)

TWO_PI = 2.0 * math.pi
CW1 = float(np.float32(TWO_PI))
CW2 = float(np.float32(np.float64(TWO_PI) - np.float64(np.float32(TWO_PI))))
CW3 = float(np.float64(TWO_PI) - np.float64(np.float32(TWO_PI)) - np.float64(CW2))
R2PI = float(np.float32(1.0 / TWO_PI))
PI_HALF = float(np.float32(math.pi / 2))
PI_HALF_LO = float(np.float64(math.pi / 2) - np.float64(np.float32(math.pi / 2)))
PI_F = float(np.float32(math.pi))

KBY = 1024


class Sem:
    def __init__(self, handle):
        self.h = handle
        self.n = 0

    def inc(self, k=1):
        self.n += k
        return self.n


class Bld:
    def __init__(self, nc):
        self.nc = nc
        self.q = {k: [] for k in ("sync", "gpsimd", "vector", "scalar", "tensor")}
        self._ctxs = []
        self.sems = {}
        self.serial = {}
        self.base = (nc.sbuf_base + 31) & ~31

    def sem(self, name):
        if name not in self.sems:
            cm = self.nc.semaphore(name)
            self.sems[name] = Sem(cm.__enter__())
            self._ctxs.append(cm)
        return self.sems[name]

    def sb(self, name, shape, dtype, off_b):
        t = self.nc.alloc_sbuf_tensor_at(name, shape, dtype, offset=self.base + off_b)
        return t.ap()

    def st(self, eng, fn, waits=(), inc=None, inck=1):
        ser = self.serial.get(eng)
        if ser is not None:
            waits = tuple(waits) + ((ser, ser.n),)
            inc, inck = ser, 1
        waits = tuple((s.h, v) for (s, v) in waits if v > 0)
        if inc is not None:
            inc.inc(inck)

        def run(e, fn=fn, waits=waits, inc=inc, inck=inck):
            for h, v in waits:
                e.wait_ge(h, v)
            r = fn(e)
            if inc is not None:
                r.then_inc(inc.h, inck)
        self.q[eng].append(run)
        return inc.n if inc is not None else None

    def emit(self):
        with self.nc.Block() as block:
            block.sync(lambda e: [f(e) for f in self.q["sync"]])
            block.gpsimd(lambda e: [f(e) for f in self.q["gpsimd"]])
            block.vector(lambda e: [f(e) for f in self.q["vector"]])
            block.scalar(lambda e: [f(e) for f in self.q["scalar"]])
            block.tensor(lambda e: [f(e) for f in self.q["tensor"]])
        for cm in reversed(self._ctxs):
            cm.__exit__(None, None, None)


def build(nc, dbg=(), phases=99, e_head=0):
    dbg = set(dbg)
    K = Bld(nc)
    st = K.st
    KB = lambda x: int(x * KBY)

    xc = nc.dram_tensor("xc", [D, T], F32, kind="ExternalInput").ap()
    cvec_d = nc.dram_tensor("cvec", [128, NKC], F32, kind="ExternalInput").ap()
    pos_d = nc.dram_tensor("pos", [1, T], F32, kind="ExternalInput").ap()
    wmod_d = nc.dram_tensor("wmod", [24, 128, NKC, 512], F32, kind="ExternalInput").ap()
    wq_d = nc.dram_tensor("wq", [NH, 128, NKC, 128], F32, kind="ExternalInput").ap()
    wk_d = nc.dram_tensor("wk", [NH, 128, NKC, 128], F32, kind="ExternalInput").ap()
    wv_d = nc.dram_tensor("wv", [8, 128, NKC, 256], F32, kind="ExternalInput").ap()
    wo_d = nc.dram_tensor("wo", [NKC, 128, NKC, 128], F32, kind="ExternalInput").ap()
    w1_d = nc.dram_tensor("w1", [NFC, 128, NKC, 128], F32, kind="ExternalInput").ap()
    w2_d = nc.dram_tensor("w2", [NFC, 128, NKC, 128], F32, kind="ExternalInput").ap()
    w3_d = nc.dram_tensor("w3", [4, NFC, 128, 4 * 128], F32, kind="ExternalInput").ap()
    out_d = nc.dram_tensor("out", [D, OWN], F32, kind="ExternalOutput").ap()
    bnc_d = nc.dram_tensor("bnc", [64, 2048], F32, kind="Internal").ap()
    x2s_d = nc.dram_tensor("x2s", [D, OWN], F32, kind="Internal").ap()
    hs_d = nc.dram_tensor("hs", [NFC, 128, OWN], BF16, kind="Internal").ap()
    q3s_d = nc.dram_tensor("q3s", [NFC, 128, OWN], BF16, kind="Internal").ap()

    idm4_d = nc.inline_tensor(
        np.ascontiguousarray(np.tile(np.eye(128, dtype=np.float32), (1, 4))), "idm4").ap()
    irow_d = nc.inline_tensor(np.arange(64, dtype=np.float32).reshape(64, 1), "irow").ap()

    DBG = {}

    def dbg_t(name, shape):
        if name in dbg:
            DBG[name] = nc.dram_tensor(name, shape, F32, kind="ExternalOutput").ap()
            return True
        return False

    rows = K.sb("rows", [1, 3072], F32, KB(0))
    rbc = K.sb("rbc", [128, 1024], F32, KB(12))
    modv = K.sb("modv", [128, 128], F32, KB(16))
    consts = K.sb("consts", [128, 16], F32, KB(16.5))
    accw = K.sb("accw", [128, 132], F32, KB(16.6875))
    wsc = K.sb("wsc", [128, 8], F32, KB(17.21875))
    onesb = K.sb("onesb", [128, 128], BF16, KB(17.25))
    sqb2 = K.sb("sqb2", [128, 512], BF16, KB(17.5))
    onesf128 = K.sb("onesf128", [128, 128], F32, KB(18.5))
    xst = K.sb("xst", [128, 2048], F32, KB(19))
    accm = K.sb("accm", [128, 1024], F32, KB(19))
    wfl = K.sb("wfl", [128, 2048], F32, KB(27))
    wfl2 = K.sb("wfl2", [128, 2048], F32, KB(35))
    wsl = K.sb("wsl", [128, 4096], BF16, KB(35))
    ternb = K.sb("ternb", [128, 4096], BF16, KB(35))
    xbb = K.sb("xbb", [128, 1024], BF16, KB(43))
    h_sb = K.sb("h_sb", [128, NKC * T], BF16, KB(45))
    q12 = K.sb("q12", [128, NKC * OWN], BF16, KB(45))
    ysb = K.sb("ysb", [128, NKC * OWN], BF16, KB(111))
    h2sb = K.sb("h2sb", [128, NKC * OWN], BF16, KB(111))
    du1b = K.sb("du1b", [128, 1024], F32, KB(77))
    tmpb = K.sb("tmpb", [128, 1024], F32, KB(81))
    maxac = K.sb("maxac", [128, 1024], F32, KB(85))
    cq3b = K.sb("cq3b", [128, 1024], F32, KB(89))
    do3b = K.sb("do3b", [128, 1024], F32, KB(93))
    terntmp = K.sb("terntmp", [128, 2048], F32, KB(97))
    ternA2 = K.sb("ternA2", [128, 2048], BF16, KB(143))
    ternB2 = K.sb("ternB2", [128, 2048], BF16, KB(147))
    sgntmp = K.sb("sgntmp", [128, 2048], F32, KB(151))
    sgntmp2 = K.sb("sgntmp2", [128, 2048], F32, KB(175))
    sgntmp3 = K.sb("sgntmp3", [128, 2048], F32, KB(183))
    wfl3 = K.sb("wfl3", [128, 2048], F32, KB(159))
    wfl4 = K.sb("wfl4", [128, 2048], F32, KB(167))
    ternA = K.sb("ternA", [128, 2048], BF16, KB(105))
    ternB = K.sb("ternB", [128, 2048], BF16, KB(109))
    hch = K.sb("hch", [128, 2048], BF16, KB(113))
    q3c = K.sb("q3c", [128, 2048], BF16, KB(117))
    esb = K.sb("esb", [128, NKC * 512], BF16, KB(143))
    vsb = K.sb("vsb", [128, 2 * NKC * 256], BF16, KB(159))
    ksb = K.sb("ksb", [128, 2 * T], BF16, KB(175))
    qsb = K.sb("qsb", [128, 2 * OWN], BF16, KB(183))
    cosT = K.sb("cosT", [64, T], BF16, KB(187))
    ssinT = K.sb("ssinT", [64, T], BF16, KB(191))
    kraw = K.sb("kraw", [128, 512], F32, KB(195))
    krot = K.sb("krot", [64, 512], F32, KB(197))
    tab = K.sb("tab", [64, 8], F32, KB(198))
    lrec = K.sb("lrec", [16, 1024], F32, KB(198.5))
    sm2 = K.sb("sm2", [128, 1024], F32, KB(202.5))
    smr = K.sb("smr", [1, 2048], F32, KB(198.5))
    mring = K.sb("mring", [128, 2 * NKC * 512], BF16, KB(111))
    smat = K.sb("smat", [128, NKC * 128], BF16, KB(143))
    idn4 = K.sb("idn4", [128, 512], F32, KB(147))
    t_ang = K.sb("t_ang", [64, 2048], F32, KB(149.25))
    t_u = K.sb("t_u", [64, 2048], F32, KB(157.25))
    t_r = K.sb("t_r", [64, 2048], F32, KB(165.25))
    t_b = K.sb("t_b", [64, 2048], F32, KB(19))
    wsl2 = K.sb("wsl2", [128, 4096], BF16, KB(19))
    assert K.base + KB(206.5) <= nc.sbuf_top

    pss = [nc.alloc_psum_tensor("ps%d" % i, [128, 512], F32).ap() for i in range(8)]
    psA, psB, psC, psD, psE, psF, psG, psH = pss

    sLD = K.sem("sLD")
    sXL = K.sem("sXL")
    sM0 = K.sem("sM0")
    sM1 = K.sem("sM1")
    sB = [K.sem("sB%d" % i) for i in range(6)]
    sVE = K.sem("sVE")
    sSE = K.sem("sSE")
    sMM = K.sem("sMM")
    sEX = K.sem("sEX")
    sEV = K.sem("sEV")
    sDBG = K.sem("sDBG")
    sDBG2 = K.sem("sDBG2")
    sDBG3 = K.sem("sDBG3")
    sDBG4 = K.sem("sDBG4")
    sDBG5 = K.sem("sDBG5")
    sOUT = K.sem("sOUT")
    sOUT1 = K.sem("sOUT1")
    sA0 = K.sem("sA0")
    sA1 = K.sem("sA1")
    sAC = K.sem("sAC")
    sW0 = K.sem("sW0")
    sW1 = K.sem("sW1")
    sW2 = K.sem("sW2")
    sW3 = K.sem("sW3")
    sLB = K.sem("sLB")
    sLR = K.sem("sLR")
    sRB0 = K.sem("sRB0")
    sRB1 = K.sem("sRB1")
    sXO0 = K.sem("sXO0")
    sXO1 = K.sem("sXO1")
    sXS0 = K.sem("sXS0")
    sXS1 = K.sem("sXS1")
    sHS = [K.sem("sHS%d" % i) for i in range(4)]
    sHR = [K.sem("sHR%d" % i) for i in range(2)]
    sQS = [K.sem("sQS%d" % i) for i in range(2)]
    K.serial["vector"] = sVE
    K.serial["scalar"] = sSE

    def bounce(pair, row_ap, n, bnc_row, tile_ap, nparts=128, waits=()):
        sw, sr = sB[2 * pair], sB[2 * pair + 1]
        st("sync", lambda e: e.dma_start(out=bnc_d[bnc_row:bnc_row + 1, 0:n], in_=row_ap),
           waits=waits, inc=sw, inck=16)
        w = sw.n
        st("sync", lambda e: e.dma_start(
            out=tile_ap, in_=bnc_d[bnc_row:bnc_row + 1, 0:n].broadcast_to((nparts, n))),
           waits=[(sw, w)], inc=sr, inck=16)
        return (sw, w), (sr, sr.n)

    def emit_ladder(chains):
        vws = []
        for ch in chains:
            vws.append(st("vector", lambda e, ch=ch: e.tensor_tensor(
                out=ch["ta"], in0=ch["src"], in1=ch["cq"], op=ALU.mult),
                waits=ch.get("mw", [])))
        sws = [0] * len(chains)
        for i, k in enumerate((128.0, 64.0, 32.0, 16.0, 8.0, 4.0, 2.0, 1.0)):
            for j, ch in enumerate(chains):
                src_u = ch["ta"] if i == 0 else ch["tb"]
                sws[j] = st("scalar", lambda e, ch=ch, src_u=src_u: e.activation(
                    out=ch["bb"], in_=src_u, func=AF.Sign, bias=consts[:, 8:9]),
                    waits=[(sVE, vws[j])])
            for j, ch in enumerate(chains):
                src_u = ch["ta"] if i == 0 else ch["tb"]
                vws[j] = st("vector", lambda e, ch=ch, src_u=src_u, k=k:
                            e.scalar_tensor_tensor(
                    out=ch["tb"], in0=ch["bb"], scalar=-(k / 2.0), in1=src_u,
                    op0=ALU.mult, op1=ALU.add),
                    waits=[(sSE, sws[j])])
        last = 0
        for ch in chains:
            st("vector", lambda e, ch=ch: e.tensor_tensor(
                out=ch["ta"], in0=ch["ta"], in1=ch["tb"], op=ALU.subtract))
            last = st("vector", lambda e, ch=ch: e.tensor_scalar_add(
                out=ch["dst"], in0=ch["ta"], scalar1=-0.5))
        return last

    st("sync", lambda e: e.dma_start(out=modv[:, 96:112], in_=cvec_d[:, :]), inc=sLD, inck=16)
    st("sync", lambda e: e.dma_start(out=idn4[:, :], in_=idm4_d[:, :]), inc=sLD, inck=16)
    st("sync", lambda e: e.dma_start(out=tab[:, 0:1], in_=irow_d[:, :]), inc=sLD, inck=16)
    st("sync", lambda e: e.dma_start(
        out=t_ang[:, :], in_=pos_d[0:1, :].broadcast_to((64, T))), inc=sLD, inck=16)
    w_cvec = w_idn = w_irow = w_pos = sLD.n

    def _consts(e):
        e.memset(consts[:, 0:1], -SHIFT)
        e.memset(consts[:, 1:2], 1.0)
        e.memset(consts[:, 2:3], 1e-5)
        e.memset(wsc[:, :], 0.0)
        e.memset(consts[:, 3:4], 1e-6)
        e.memset(consts[:, 8:9], 0.5)
        e.memset(tab[:, 1:2], 10000.0)
        return e.memset(onesb[:, :], 1.0)
    st("vector", _consts)
    w_consts = sVE.n

    for kc in range(NKC):
        st("gpsimd", lambda e, kc=kc: e.dma_start(
            out=h_sb[:, kc * T:(kc + 1) * T], in_=xc[kc * 128:(kc + 1) * 128, :]),
           inc=sXL, inck=16)
    w_xall = sXL.n

    st("scalar", lambda e: e.activation(out=tab[:, 2:3], in_=tab[:, 1:2], func=AF.Ln),
       waits=[(sVE, w_consts)])
    st("scalar", lambda e: e.activation(out=tab[:, 3:4], in_=tab[:, 0:1], func=AF.Identity,
                                        scale=tab[:, 2:3]),
       waits=[(sLD, w_irow), (sSE, sSE.n)])
    st("scalar", lambda e: e.activation(out=tab[:, 4:5], in_=tab[:, 3:4], func=AF.Exp,
                                        scale=-1.0 / 64.0), waits=[(sSE, sSE.n)])
    w_invf = sSE.n
    st("vector", lambda e: e.tensor_scalar_mul(out=t_ang[:, :], in0=t_ang[:, :],
                                               scalar1=tab[:, 4:5]),
       waits=[(sLD, w_pos), (sSE, w_invf)])
    st("vector", lambda e: e.tensor_scalar(out=t_u[:, :], in0=t_ang[:, :],
                                           scalar1=R2PI, scalar2=0.5,
                                           op0=ALU.mult, op1=ALU.add))
    st("vector", lambda e: e.tensor_copy(out=t_r[:, :], in_=t_u[:, :]))
    for k in [256.0, 128.0, 64.0, 32.0, 16.0, 8.0, 4.0, 2.0, 1.0]:
        st("vector", lambda e, k=k: e.tensor_scalar(out=t_b[0:64, :], in0=t_r[:, :],
                                                    scalar1=k, scalar2=k,
                                                    op0=ALU.is_ge, op1=ALU.mult))
        st("vector", lambda e: e.tensor_tensor(out=t_r[:, :], in0=t_r[:, :],
                                               in1=t_b[0:64, :], op=ALU.subtract))
    st("vector", lambda e: e.tensor_tensor(out=t_u[:, :], in0=t_u[:, :], in1=t_r[:, :],
                                           op=ALU.subtract))
    st("vector", lambda e: e.scalar_tensor_tensor(out=t_r[:, :], in0=t_u[:, :],
                                                  scalar=-CW1, in1=t_ang[:, :],
                                                  op0=ALU.mult, op1=ALU.add))
    st("vector", lambda e: e.scalar_tensor_tensor(out=t_r[:, :], in0=t_u[:, :],
                                                  scalar=-CW2, in1=t_r[:, :],
                                                  op0=ALU.mult, op1=ALU.add))
    st("vector", lambda e: e.scalar_tensor_tensor(out=t_r[:, :], in0=t_u[:, :],
                                                  scalar=-CW3, in1=t_r[:, :],
                                                  op0=ALU.mult, op1=ALU.add))
    st("vector", lambda e: e.tensor_scalar(out=t_b[0:64, :], in0=t_r[:, :],
                                           scalar1=PI_F, scalar2=1.0,
                                           op0=ALU.is_gt, op1=ALU.mult))
    st("vector", lambda e: e.scalar_tensor_tensor(
        out=t_r[:, :], in0=t_b[0:64, :], scalar=-CW1, in1=t_r[:, :],
        op0=ALU.mult, op1=ALU.add))
    st("vector", lambda e: e.scalar_tensor_tensor(
        out=t_b[0:64, :], in0=t_b[0:64, :], scalar=-CW2, in1=t_r[:, :],
        op0=ALU.mult, op1=ALU.add))
    w_sinin = sVE.n
    st("scalar", lambda e: e.activation(out=ssinT[:, :], in_=t_b[0:64, :], func=AF.Sin),
       waits=[(sVE, w_sinin)])
    w_sin = sSE.n
    st("vector", lambda e: e.tensor_scalar_mul(out=ssinT[0:32, :], in0=ssinT[0:32, :],
                                               scalar1=-1.0), waits=[(sSE, w_sin)])
    w_ssin = sVE.n
    st("vector", lambda e: e.tensor_scalar_add(out=t_u[:, :], in0=t_r[:, :],
                                               scalar1=PI_HALF), waits=[(sSE, w_sin)])
    st("vector", lambda e: e.tensor_scalar_add(out=t_u[:, :], in0=t_u[:, :],
                                               scalar1=PI_HALF_LO))
    st("vector", lambda e: e.tensor_scalar(out=t_b[0:64, :], in0=t_u[:, :],
                                           scalar1=PI_F, scalar2=1.0,
                                           op0=ALU.is_gt, op1=ALU.mult))
    st("vector", lambda e: e.scalar_tensor_tensor(
        out=t_u[:, :], in0=t_b[0:64, :], scalar=-CW1, in1=t_u[:, :],
        op0=ALU.mult, op1=ALU.add))
    st("vector", lambda e: e.scalar_tensor_tensor(
        out=t_b[0:64, :], in0=t_b[0:64, :], scalar=-CW2, in1=t_u[:, :],
        op0=ALU.mult, op1=ALU.add))
    w_cosin = sVE.n
    st("scalar", lambda e: e.activation(out=cosT[:, :], in_=t_b[0:64, :], func=AF.Sin),
       waits=[(sVE, w_cosin)])
    w_cos = sSE.n
    if dbg_t("d_cs", [64, 2 * T]):
        st("gpsimd", lambda e: e.dma_start(out=DBG["d_cs"][:, 0:T], in_=cosT[:, :]),
           waits=[(sSE, w_cos)], inc=sDBG2, inck=16)
        st("gpsimd", lambda e: e.dma_start(out=DBG["d_cs"][:, T:2 * T], in_=ssinT[:, :]),
           waits=[(sVE, w_ssin)], inc=sDBG2, inck=16)

    st("scalar", lambda e: e.activation(out=modv[:, 112:128], in_=modv[:, 96:112],
                                        func=AF.Silu), waits=[(sLD, w_cvec)])
    w_silu = sSE.n
    for kc in range(NKC):
        st("vector", lambda e, kc=kc: e.tensor_scalar_mul(
            out=smat[:, kc * 128:(kc + 1) * 128], in0=onesb[:, :],
            scalar1=modv[:, 112 + kc:113 + kc]),
           waits=[(sSE, w_silu), (sVE, w_consts)] if kc == 0 else [],
           inc=(sVE if kc == NKC - 1 else None))
    w_smat = sVE.n

    mod_mm_base = sMM.n
    for j in range(24):
        st("gpsimd", lambda e, j=j: e.dma_start(
            out=mring[:, (j % 2) * NKC * 512:((j % 2) + 1) * NKC * 512], in_=wmod_d[j]),
           waits=[(sMM, mod_mm_base + j - 1)] if j >= 2 else [],
           inc=(sM0 if j % 2 == 0 else sM1), inck=16)
    w_modtt = 0
    for j in range(24):
        wld = 16 * (j // 2 + 1)
        sMp = sM0 if j % 2 == 0 else sM1
        slot = mring[:, (j % 2) * NKC * 512:((j % 2) + 1) * NKC * 512]
        for kc in range(NKC):
            st("tensor", lambda e, kc=kc, slot=slot: e.matmul(
                psE[:, :], smat[:, kc * 128:(kc + 1) * 128],
                slot[:, kc * 512:(kc + 1) * 512],
                start=(kc == 0), stop=(kc == NKC - 1)),
               waits=(([(sMp, wld), (sVE, w_smat)] +
                       ([(sVE, w_modtt)] if j >= 1 else []))
                      if kc == 0 else []),
               inc=(sMM if kc == NKC - 1 else None))
        wmm = sMM.n
        w_modtt = st("vector", lambda e: e.tensor_tensor(out=sm2[:, 0:512], in0=psE[:, :],
                                                         in1=idn4[:, :], op=ALU.mult),
                     waits=[(sMM, wmm), (sLD, w_idn)])
        st("vector", lambda e, j=j: e.tensor_reduce(
            out=modv[:, j * 4:(j + 1) * 4],
            in_=sm2[:, 0:512].rearrange("p (b f) -> p b f", f=128),
            axis=AX.X, op=ALU.add))
    w_gmul = sVE.n
    for (cg, nm) in ((32, "ga"), (80, "gb")):
        st("scalar", lambda e, cg=cg: e.activation(out=modv[:, 112:128],
                                                   in_=modv[:, cg:cg + 16],
                                                   func=AF.Sign),
           waits=[(sVE, w_gmul)])
        w_ab_ = st("scalar", lambda e, cg=cg: e.activation(out=modv[:, cg:cg + 16],
                                                           in_=modv[:, cg:cg + 16],
                                                           func=AF.Abs))
        st("vector", lambda e, cg=cg: e.tensor_scalar_max(out=modv[:, cg:cg + 16],
                                                          in0=modv[:, cg:cg + 16],
                                                          scalar1=0.25),
           waits=[(sSE, w_ab_)])
        w_gmul = st("vector", lambda e, cg=cg: e.tensor_tensor(
            out=modv[:, cg:cg + 16], in0=modv[:, cg:cg + 16],
            in1=modv[:, 112:128], op=ALU.mult))
    st("vector", lambda e: e.tensor_scalar_add(out=modv[:, 16:32], in0=modv[:, 16:32],
                                               scalar1=1.0))
    st("vector", lambda e: e.tensor_scalar_add(out=modv[:, 64:80], in0=modv[:, 64:80],
                                               scalar1=1.0))
    w_mod = sVE.n
    if dbg_t("d_mod", [128, 128]):
        st("sync", lambda e: e.dma_start(out=DBG["d_mod"][:, :], in_=modv[:, :]),
           waits=[(sVE, w_mod)], inc=sDBG, inck=16)

    w_rows_evac = 0
    w_hmul_last = 0
    w_rowfree = []
    for tc in range(4):
        for kc in range(NKC):
            st("tensor", lambda e, tc=tc, kc=kc: e.matmul(
                psA[:, :], onesb[:, :],
                h_sb[:, kc * T + tc * 512: kc * T + (tc + 1) * 512],
                start=(kc == 0), stop=(kc == NKC - 1)),
               waits=(([(sXL, w_xall), (sVE, w_consts)] +
                       ([(sVE, w_rows_evac)] if tc > 0 else []))
                      if kc == 0 else []),
               inc=(sMM if kc == NKC - 1 else None))
        w_sum = sMM.n
        for kc in range(NKC):
            w_slot = sMM.n
            st("scalar", lambda e, tc=tc, kc=kc: e.activation(
                out=xbb[:, (kc % 2) * 512:(kc % 2 + 1) * 512],
                in_=h_sb[:, kc * T + tc * 512: kc * T + (tc + 1) * 512],
                func=AF.Square),
               waits=([(sMM, sMM.n - 1)] if kc >= 2 else []) +
                     ([(sXL, w_xall)] if (tc == 0 and kc == 0) else []))
            w_sq = sSE.n
            st("tensor", lambda e, kc=kc: e.matmul(
                psB[:, :], onesb[:, :], xbb[:, (kc % 2) * 512:(kc % 2 + 1) * 512],
                start=(kc == 0), stop=(kc == NKC - 1)),
               waits=[(sSE, w_sq)], inc=sMM)
        w_sq_mm = sMM.n
        r0, r1, r2 = rows[0:1, 0:512], rows[0:1, 1024:1536], rows[0:1, 2048:2560]
        st("vector", lambda e: e.tensor_scalar_mul(out=r0, in0=psA[0:1, :], scalar1=1.0 / D),
           waits=[(sMM, w_sq_mm)] + (w_rowfree if tc > 0 else []))
        w_rows_evac = st("vector", lambda e: e.tensor_scalar_mul(
            out=r1, in0=psB[0:1, :], scalar1=1.0 / D))
        st("vector", lambda e: e.tensor_tensor(out=r2, in0=r0, in1=r0, op=ALU.mult))
        st("vector", lambda e: e.tensor_tensor(out=r1, in0=r1, in1=r2, op=ALU.subtract),
           inc=sVE)
        w_var = sVE.n
        st("scalar", lambda e: e.activation(out=r2, in_=r1, func=AF.Sqrt, bias=consts[0:1, 2:3]),
           waits=[(sVE, w_var)])
        w_std = sSE.n
        st("vector", lambda e: e.reciprocal(out=r1, in_=r2), waits=[(sSE, w_std)])
        st("vector", lambda e: e.tensor_tensor(out=r0, in0=r0, in1=r1, op=ALU.mult))
        st("vector", lambda e: e.tensor_scalar_mul(out=r0, in0=r0, scalar1=-1.0))
        w_ab = sVE.n
        if tc == 0 and dbg_t("d_stats", [2, T]):
            st("sync", lambda e: e.dma_start(out=DBG["d_stats"][0:1, 0:512], in_=r1),
               waits=[(sVE, w_ab)], inc=sDBG, inck=16)
            st("sync", lambda e: e.dma_start(out=DBG["d_stats"][1:2, 0:512], in_=r0),
               waits=[(sVE, w_ab)], inc=sDBG, inck=16)
        wbAw, wbA = bounce(0, r1, 512, 16 + tc, rbc[:, 0:512],
                           waits=[(sVE, w_ab)] + ([(sVE, w_hmul_last)] if tc > 0 else []))
        wbBw, wbB = bounce(1, r0, 512, 20 + tc, rbc[:, 512:1024], waits=[(sVE, w_ab)])
        w_rowfree = [wbAw, wbBw] + ([(sDBG, sDBG.n)] if tc == 0 and "d_stats" in DBG else [])
        for kc in range(NKC):
            sl = h_sb[:, kc * T + tc * 512: kc * T + (tc + 1) * 512]
            s2 = sm2[:, (kc % 2) * 512:(kc % 2 + 1) * 512]
            st("vector", lambda e, sl=sl, s2=s2: e.tensor_tensor(
                out=s2, in0=sl, in1=rbc[:, 0:512], op=ALU.mult),
               waits=([wbA, wbB] if kc == 0 else
                      [(sSE, sSE.n - 1)] if kc >= 2 else []))
            st("vector", lambda e, s2=s2: e.tensor_tensor(
                out=s2, in0=s2, in1=rbc[:, 512:1024], op=ALU.add))
            w_t = sVE.n
            w_hmul_last = w_t
            st("scalar", lambda e, sl=sl, s2=s2, kc=kc: e.activation(
                out=sl, in_=s2, func=AF.Identity,
                scale=modv[:, 16 + kc:17 + kc], bias=modv[:, kc:kc + 1]),
               waits=[(sVE, w_t)] + ([(sVE, w_mod)] if kc == 0 and tc == 0 else []),
               inc=sSE)
    st("scalar", lambda e: e.activation(out=tab[:, 5:6], in_=tab[:, 1:2], func=AF.Identity),
       inc=sSE)
    w_h = sSE.n
    if dbg_t("d_h", [128, NKC * T]):
        st("gpsimd", lambda e: e.dma_start(out=DBG["d_h"][:, :], in_=h_sb[:, :]),
           waits=[(sSE, w_h)], inc=sDBG2, inck=16)

    if phases <= 1:
        for kc in range(NKC):
            st("gpsimd", lambda e, kc=kc: e.dma_start(
                out=out_d[kc * 128:(kc + 1) * 128, :], in_=h_sb[:, kc * T:kc * T + OWN]),
               waits=[(sSE, w_h)] if kc == 0 else [], inc=sOUT, inck=16)
        w_fin = sOUT.n
        st("sync", lambda e: e.nop(), waits=[(sOUT, w_fin), (sDBG, sDBG.n), (sDBG2, sDBG2.n)])
        K.DBG = DBG
        return K

    dbg_t("d_k0", [128, T]); dbg_t("d_q0", [128, OWN]); dbg_t("d_v0", [128, NKC * 256])
    dbg_t("d_e0", [128, NKC * 512]); dbg_t("d_l", [16, OWN]); dbg_t("d_y", [128, NKC * OWN])
    dbg_t("d_x2", [D, OWN])

    scale_chunks = []
    for wd, nel in ((w1_d, DFF * D), (w2_d, DFF * D), (w3_d, DFF * D)):
        flat = wd.rearrange("a b c d -> (a b c d)")
        nch = nel // (128 * 1024)
        for i in range(nch):
            scale_chunks.append(flat[i * 131072:(i + 1) * 131072]
                                .rearrange("(p n) -> p n", p=128))
    n_scale = len(scale_chunks)
    scale_state = {"i": 0, "se": [0] * n_scale, "fold": [], "w_red": 0}

    def emit_fold(j):
        w_last = scale_state["se"][(j + 1) * 128 - 1]
        w_red = st("vector", lambda e: e.reduce_sum(out=accw[:, 128:129],
                                                    in_=accw[:, 0:128], axis=AX.X),
                   waits=[(sSE, w_last)])
        scale_state["w_red"] = w_red
        st("sync", lambda e, j=j: e.dma_start(out=bnc_d[8 + j:9 + j, 0:128],
                                              in_=accw[:, 128:129]),
           waits=[(sVE, w_red)], inc=sB[2], inck=16)
        w1 = sB[2].n
        st("sync", lambda e, j=j: e.dma_start(out=lrec[0:1, 0:128],
                                              in_=bnc_d[8 + j:9 + j, 0:128]),
           waits=[(sB[2], w1)], inc=sB[3], inck=16)
        w2 = sB[3].n
        w3 = st("vector", lambda e: e.reduce_sum(out=lrec[0:1, 128:129],
                                                 in_=lrec[0:1, 0:128], axis=AX.X),
                waits=[(sB[3], w2)])
        st("sync", lambda e, j=j: e.dma_start(out=bnc_d[8 + j:9 + j, 128:129],
                                              in_=lrec[0:1, 128:129]),
           waits=[(sVE, w3)], inc=sB[2], inck=16)
        w4 = sB[2].n
        st("sync", lambda e, j=j: e.dma_start(
            out=wsc[:, 7:8],
            in_=bnc_d[8 + j:9 + j, 128:129].broadcast_to((128, 1))),
           waits=[(sB[2], w4)], inc=sB[3], inck=16)
        w5 = sB[3].n
        st("vector", lambda e, j=j: e.tensor_scalar(
            out=wsc[:, 3 + j:4 + j], in0=wsc[:, 7:8],
            scalar1=1.0 / (DFF * D), scalar2=1e-5, op0=ALU.mult, op1=ALU.max),
           waits=[(sB[3], w5)])
        scale_state["fold"].append(st("vector", lambda e, j=j: e.tensor_scalar_mul(
            out=wsc[:, j:j + 1], in0=wsc[:, 3 + j:4 + j], scalar1=0.5)))

    scale_slots = [wfl[:, 0:1024], wfl[:, 1024:2048], rbc[:, 0:1024]]
    scale_sems = [sW0, sW1, sW2]
    w_ln1_dve_done = sVE.n
    w_ln1_se_done = w_h

    def emit_scale(nmax):
        for _ in range(nmax):
            i = scale_state["i"]
            if i >= n_scale:
                return
            scale_state["i"] = i + 1
            sl = i % 3
            sW = scale_sems[sl]
            slot = scale_slots[sl]
            wfree = scale_state["se"][i - 3] if i >= 3 else 0
            first_w = ([(sSE, wfree)] if i >= 3 else
                       ([(sVE, w_ln1_dve_done), (sSE, w_ln1_se_done)]
                        if sl >= 2 else []))
            for cc in range(2):
                st("sync", lambda e, i=i, slot=slot, cc=cc: e.dma_start(
                    out=slot[:, cc * 512:(cc + 1) * 512],
                    in_=scale_chunks[i][:, cc * 512:(cc + 1) * 512]),
                   waits=(first_w if cc == 0 else []), inc=sW, inck=16)
            wld = sW.n
            col = i % 128
            scale_state["se"][i] = st(
                "scalar", lambda e, slot=slot, col=col: e.activation(
                    out=slot, in_=slot,
                    func=AF.Abs, accum_out=accw[:, col:col + 1]),
                waits=[(sW, wld)] + ([(sVE, scale_state["w_red"])]
                                     if col == 0 and i > 0 else []))
            if (i + 1) % 128 == 0:
                emit_fold(i // 128)


    events = []
    for g in range(8):
        events.append(("V", g))
        events.append(("KQ", 2 * g))
        events.append(("KQ", 2 * g + 1))
    slot_of = {}
    for i, ev in enumerate(events):
        slot_of[ev] = i % 2
    ev_index = {ev: i for i, ev in enumerate(events)}
    slots = [wsl, wsl2]
    sAs = [sA0, sA1]

    scale_state["w_ones"] = st("vector", lambda e: e.memset(onesf128[:, :], 1.0))
    ev_ld = {}
    ev_done = {}

    def emit_ev_load(ev):
        i = ev_index[ev]
        sl = slot_of[ev]
        sA = sAs[sl]
        wcred = [(sMM, ev_done[events[i - 2]])] if i >= 2 else []
        if ev[0] == "V":
            g = ev[1]
            st("gpsimd", lambda e, sl=sl, g=g: e.dma_start(
                out=slots[sl][:, 0:4096], in_=wv_d[g]), waits=wcred, inc=sA, inck=16)
        else:
            h = ev[1]
            st("gpsimd", lambda e, sl=sl, h=h: e.dma_start(
                out=slots[sl][:, 0:2048], in_=wk_d[h]), waits=wcred, inc=sA, inck=16)
            st("gpsimd", lambda e, sl=sl, h=h: e.dma_start(
                out=slots[sl][:, 2048:4096], in_=wq_d[h]), inc=sA, inck=16)
        ev_ld[ev] = sA.n

    emit_ev_load(events[0])
    emit_ev_load(events[1])

    w_ln1_last_sq = sSE.n
    hrope = {}

    w_vgrp_evac = [0] * 8
    w_yevac = 0
    w_lrowcp = 0
    w_exp = {}
    for h in range(NH):
        g = h // 2
        kq_ev = ("KQ", h)
        v_ev = ("V", g)
        sl_kq = slot_of[kq_ev]
        sl_v = slot_of[v_ev]
        kslot = ksb[:, (h % 2) * T:(h % 2 + 1) * T]
        qslot = qsb[:, (h % 2) * OWN:(h % 2 + 1) * OWN]
        vslot = vsb[:, (g % 2) * NKC * 256:((g % 2) + 1) * NKC * 256]

        if h % 2 == 0:
            for kkc in range(NKC):
                for kc in range(NKC):
                    st("tensor", lambda e, kkc=kkc, kc=kc, sl=sl_v: e.matmul(
                        psE[:, 0:256], h_sb[:, kc * T + kkc * 128: kc * T + kkc * 128 + 128],
                        slots[sl][:, kc * 256:(kc + 1) * 256],
                        start=(kc == 0), stop=(kc == NKC - 1)),
                       waits=(([(sAs[sl_v], ev_ld[v_ev]), (sSE, w_ln1_last_sq)]
                               if kkc == 0 else []) +
                              ([(sVE, w_vgrp_evac[g])] if (kc == 0 and kkc > 0) else
                               [(sVE, w_vgrp_evac[g - 1] if g > 0 else 0)]
                               if kc == 0 else [])),
                       inc=(sMM if kc == NKC - 1 else None))
                w_mm = sMM.n
                w_vgrp_evac[g] = st("vector", lambda e, kkc=kkc, vslot=vslot: e.tensor_copy(
                    out=vslot[:, kkc * 256:(kkc + 1) * 256], in_=psE[:, 0:256]),
                    waits=[(sMM, w_mm)])
            ev_done[v_ev] = sMM.n
            ni = ev_index[v_ev] + 2
            if ni < len(events):
                emit_ev_load(events[ni])
            if h == 0 and "d_v0" in DBG:
                st("gpsimd", lambda e, vslot=vslot: e.dma_start(
                    out=DBG["d_v0"][:, :], in_=vslot[:, :]),
                   waits=[(sVE, w_vgrp_evac[0])], inc=sDBG2, inck=16)

        w_kq_done_mm = 0
        for tkc in range(4):
            pst = psF[:, 0:512]
            for kc in range(NKC):
                st("tensor", lambda e, tkc=tkc, kc=kc, sl=sl_kq: e.matmul(
                    pst, slots[sl][:, kc * 128:(kc + 1) * 128],
                    h_sb[:, kc * T + tkc * 512: kc * T + (tkc + 1) * 512],
                    start=(kc == 0), stop=(kc == NKC - 1)),
                   waits=([(sAs[sl_kq], ev_ld[kq_ev])] +
                          ([(sSE, hrope.get((h, tkc - 1), 0))] if tkc > 0 else
                           [(sSE, hrope.get((h - 1, "q1"), 0))]))
                   if kc == 0 else [],
                   inc=(sMM if kc == NKC - 1 else None))
            w_mm = sMM.n
            prev_v = (hrope.get((h, tkc - 1, "v"), 0) if tkc > 0
                      else hrope.get((h - 1, "q1", "v"), 0))
            hrope[(h, tkc)] = st("scalar", lambda e: e.activation(
                out=kraw[:, 0:512], in_=pst, func=AF.Identity),
                waits=[(sMM, w_mm), (sVE, prev_v)])
            st("vector", lambda e: e.tensor_copy(out=krot[0:32, 0:512],
                                                 in_=kraw[32:64, 0:512]),
               waits=[(sSE, hrope[(h, tkc)])])
            st("vector", lambda e: e.tensor_copy(out=krot[32:64, 0:512],
                                                 in_=kraw[0:32, 0:512]))
            st("vector", lambda e, tkc=tkc: e.tensor_tensor(
                out=sm2[0:64, 0:512], in0=kraw[0:64, 0:512],
                in1=cosT[:, tkc * 512:(tkc + 1) * 512], op=ALU.mult))
            st("vector", lambda e, tkc=tkc: e.tensor_tensor(
                out=krot[0:64, 0:512], in0=krot[0:64, 0:512],
                in1=ssinT[:, tkc * 512:(tkc + 1) * 512], op=ALU.mult))
            st("vector", lambda e, tkc=tkc, kslot=kslot: e.tensor_tensor(
                out=kslot[0:64, tkc * 512:(tkc + 1) * 512], in0=sm2[0:64, 0:512],
                in1=krot[0:64, 0:512], op=ALU.add))
            w_krope_v = st("vector", lambda e, tkc=tkc, kslot=kslot: e.tensor_copy(
                out=kslot[64:128, tkc * 512:(tkc + 1) * 512], in_=kraw[64:128, 0:512]))
            hrope[(h, tkc)] = hrope[(h, tkc)]
            hrope[(h, tkc, "v")] = w_krope_v
        for qkc in range(2):
            pst = psF[:, 0:512]
            for kc in range(NKC):
                st("tensor", lambda e, qkc=qkc, kc=kc, sl=sl_kq: e.matmul(
                    pst, slots[sl][:, 2048 + kc * 128: 2048 + (kc + 1) * 128],
                    h_sb[:, kc * T + qkc * 512: kc * T + (qkc + 1) * 512],
                    start=(kc == 0), stop=(kc == NKC - 1)),
                   waits=[(sSE, hrope[(h, 3)] if qkc == 0 else hrope[(h, "q0")])]
                   if kc == 0 else [],
                   inc=(sMM if kc == NKC - 1 else None))
            w_mm = sMM.n
            key = "q%d" % qkc
            prev_v = hrope[(h, 3, "v")] if qkc == 0 else hrope[(h, "q0", "v")]
            hrope[(h, key)] = st("scalar", lambda e: e.activation(
                out=kraw[:, 0:512], in_=pst, func=AF.Identity),
                waits=[(sMM, w_mm), (sVE, prev_v)])
            st("vector", lambda e: e.tensor_copy(out=krot[0:32, 0:512],
                                                 in_=kraw[32:64, 0:512]),
               waits=[(sSE, hrope[(h, key)])])
            st("vector", lambda e: e.tensor_copy(out=krot[32:64, 0:512],
                                                 in_=kraw[0:32, 0:512]))
            st("vector", lambda e, qkc=qkc: e.tensor_tensor(
                out=sm2[0:64, 0:512], in0=kraw[0:64, 0:512],
                in1=cosT[:, qkc * 512:(qkc + 1) * 512], op=ALU.mult))
            st("vector", lambda e, qkc=qkc: e.tensor_tensor(
                out=krot[0:64, 0:512], in0=krot[0:64, 0:512],
                in1=ssinT[:, qkc * 512:(qkc + 1) * 512], op=ALU.mult))
            st("vector", lambda e, qkc=qkc, qslot=qslot: e.tensor_tensor(
                out=qslot[0:64, qkc * 512:(qkc + 1) * 512], in0=sm2[0:64, 0:512],
                in1=krot[0:64, 0:512], op=ALU.add))
            hrope[(h, key, "v")] = st("vector", lambda e, qkc=qkc, qslot=qslot: e.tensor_copy(
                out=qslot[64:128, qkc * 512:(qkc + 1) * 512], in_=kraw[64:128, 0:512]))
        ev_done[kq_ev] = sMM.n
        ni = ev_index[kq_ev] + 2
        if ni < len(events):
            emit_ev_load(events[ni])
        if h == 0:
            if "d_k0" in DBG:
                st("gpsimd", lambda e, kslot=kslot: e.dma_start(out=DBG["d_k0"][:, :], in_=kslot[:, :]),
                   waits=[(sVE, hrope[(0, 3, "v")])], inc=sDBG2, inck=16)
            if "d_q0" in DBG:
                st("gpsimd", lambda e, qslot=qslot: e.dma_start(out=DBG["d_q0"][:, :], in_=qslot[:, :]),
                   waits=[(sVE, hrope[(0, "q1", "v")])], inc=sDBG2, inck=16)

        w_rope_all = max(hrope[(h, 3, "v")], hrope[(h, "q1", "v")],
                         hrope[(h, 0, "v")], hrope[(h, 1, "v")], hrope[(h, 2, "v")],
                         hrope[(h, "q0", "v")])
        emit_scale(24)
        for qh in range(2):
            ps_y = psC if qh == 0 else psD
            ps_l = psG if qh == 0 else psH

            def emit_lav(kkc, qh=qh, ps_y=ps_y, ps_l=ps_l):
                wexp = w_exp[(h, qh, kkc)]
                st("tensor", lambda e, kkc=kkc, ps_l=ps_l: e.matmul(
                    ps_l[:, 0:512], onesb[:, :], esb[:, kkc * 512:(kkc + 1) * 512],
                    start=(kkc == 0), stop=(kkc == NKC - 1)),
                   waits=[(sSE, wexp)] + ([(sVE, w_lrowcp)] if kkc == 0 else []),
                   inc=(sMM if kkc == NKC - 1 else None))
                st("tensor", lambda e, kkc=kkc, qh=qh, ps_y=ps_y, vslot=vslot,
                   h=h: e.matmul(
                       ps_y[:, 0:512],
                       vslot[:, kkc * 256 + (h % 2) * 128: kkc * 256 + (h % 2) * 128 + 128],
                       esb[:, kkc * 512:(kkc + 1) * 512],
                       start=(kkc == 0), stop=(kkc == NKC - 1)),
                   inc=(sMM if kkc == NKC - 1 else None))

            for kkc in range(NKC):
                ps_s = psA if kkc % 2 == 0 else psB
                st("tensor", lambda e, kkc=kkc, qh=qh, ps_s=ps_s, kslot=kslot,
                   qslot=qslot: e.matmul(
                       ps_s[:, 0:512], kslot[:, kkc * 128:(kkc + 1) * 128],
                       qslot[:, qh * 512:(qh + 1) * 512], start=True, stop=True),
                   waits=([(sVE, w_rope_all), (sVE, w_vgrp_evac[g]),
                           (sSE, w_yevac), (sVE, w_lrowcp)] if (qh == 0 and kkc == 0)
                          else [(sSE, w_exp.get((h, qh, kkc - 2),
                                                w_exp.get((h, 0, 14 + kkc), 0)))]),
                   inc=sMM)
                w_smm = sMM.n
                prev_avdone = (w_exp.get(("avdone", h, 0), 0) if qh == 1
                               else w_exp.get(("avdone", h - 1, 1), 0))
                w_exp[(h, qh, kkc)] = st("scalar", lambda e, kkc=kkc, ps_s=ps_s: e.activation(
                    out=esb[:, kkc * 512:(kkc + 1) * 512], in_=ps_s[:, 0:512],
                    func=AF.Exp, bias=consts[:, 0:1], scale=SCALE_S),
                    waits=([(sMM, w_smm), (sMM, prev_avdone), (sDBG2, sDBG2.n)]
                           if kkc == 0 else [(sMM, w_smm)]))
                if kkc >= 2:
                    emit_lav(kkc - 2)
            emit_lav(NKC - 2)
            emit_lav(NKC - 1)
            w_exp[("avdone", h, qh)] = sMM.n
            w_yevac = st("scalar", lambda e, h=h, qh=qh, ps_y=ps_y: e.activation(
                out=ysb[:, h * OWN + qh * 512: h * OWN + (qh + 1) * 512],
                in_=ps_y[:, 0:512], func=AF.Identity), waits=[(sMM, sMM.n)])
            w_lrowcp = st("vector", lambda e, qh=qh, ps_l=ps_l: e.tensor_copy(
                out=rows[0:1, qh * 512:(qh + 1) * 512], in_=ps_l[0:1, 0:512]),
                waits=[(sMM, sMM.n), (sLB, sLB.n)])
        st("sync", lambda e, h=h: e.dma_start(out=bnc_d[32 + h:33 + h, 0:1024],
                                              in_=rows[0:1, 0:1024]),
           waits=[(sVE, w_lrowcp)], inc=sLB, inck=16)
        if h == e_head and "d_e0" in DBG:
            st("gpsimd", lambda e: e.dma_start(out=DBG["d_e0"][:, :], in_=esb[:, :]),
               waits=[(sSE, w_exp[(e_head, 1, NKC - 1)])], inc=sDBG2, inck=16)

    st("sync", lambda e: e.dma_start(out=lrec[:, :], in_=bnc_d[32:48, 0:1024]),
       waits=[(sLB, sLB.n),
              (sVE, scale_state["fold"][-1] if scale_state["fold"] else 0)],
       inc=sLR, inck=16)
    w_lrl = sLR.n
    w_rec = st("vector", lambda e: e.reciprocal(out=lrec[:, :], in_=lrec[:, :]),
               waits=[(sLR, w_lrl)])
    st("sync", lambda e: e.dma_start(out=bnc_d[48:64, 0:1024], in_=lrec[:, :]),
       waits=[(sVE, w_rec)], inc=sLR, inck=16)
    w_lrw = sLR.n
    if "d_l" in DBG:
        st("sync", lambda e: e.dma_start(out=DBG["d_l"][:, :], in_=lrec[:, :]),
           waits=[(sVE, w_rec)], inc=sDBG, inck=16)
    w_ynorm = 0
    for h in range(NH):
        rb = rbc if h % 2 == 0 else sm2
        sRB = sRB0 if h % 2 == 0 else sRB1
        st("sync", lambda e, h=h, rb=rb: e.dma_start(
            out=rb[:, 0:1024],
            in_=bnc_d[48 + h:49 + h, 0:1024].broadcast_to((128, 1024))),
           waits=[(sLR, w_lrw)] + ([(sVE, w_ynorm)] if h >= 2 else
                                    [(sVE, scale_state["fold"][-1]
                                      if scale_state["fold"] else 0)]),
           inc=sRB, inck=16)
        wrb = sRB.n
        w_ynorm = st("vector", lambda e, h=h, rb=rb: e.tensor_tensor(
            out=ysb[:, h * OWN:(h + 1) * OWN], in0=ysb[:, h * OWN:(h + 1) * OWN],
            in1=rb[:, 0:1024], op=ALU.mult),
            waits=[(sRB, wrb), (sSE, w_yevac)])
    if "d_y" in DBG:
        st("gpsimd", lambda e: e.dma_start(out=DBG["d_y"][:, :], in_=ysb[:, :]),
           waits=[(sVE, w_ynorm)], inc=sDBG3, inck=16)
    w_dy_tap = sDBG3.n if "d_y" in DBG else 0

    emit_scale(n_scale)
    w_scale_fold = scale_state["fold"]
    if "d_ws" in dbg:
        DBG["d_ws"] = nc.dram_tensor("d_ws", [1, 16], F32, kind="ExternalOutput").ap()
        st("sync", lambda e: e.dma_start(out=DBG["d_ws"][0:1, 0:8], in_=wsc[0:1, :]),
           waits=[(sVE, w_scale_fold[-1])], inc=sDBG4, inck=16)

    wo_slot = [wsl[:, 0:2048], wsl[:, 2048:4096]]
    w_last_attn_ev = ev_done[events[-1]]
    wo_done = {}

    def emit_wo_load(dc):
        sl = dc % 2
        sA = sAs[sl]
        st("gpsimd", lambda e, dc=dc, sl=sl: e.dma_start(
            out=wo_slot[sl], in_=wo_d[dc]),
           waits=[(sMM, wo_done[dc - 2])] if dc >= 2 else
                 [(sMM, ev_done[events[-2 + dc]])], inc=sA, inck=16)
        ev_ld[("WO", dc)] = sA.n

    emit_wo_load(0)
    emit_wo_load(1)
    w_x2 = {}
    w_stmm = {}
    w_x2spill = []
    for dc in range(NKC):
        sl = dc % 2
        sXO = sXO0 if dc % 2 == 0 else sXO1
        xsl = xst[:, (dc % 2) * 1024:(dc % 2 + 1) * 1024]
        st("sync", lambda e, dc=dc, xsl=xsl: e.dma_start(
            out=xsl, in_=xc[dc * 128:(dc + 1) * 128, 0:OWN]),
           waits=[(sVE, w_x2.get(dc - 2, 0))] +
                 ([(sXS0 if dc % 2 == 0 else sXS1, w_x2spill[dc - 2])]
                  if dc >= 2 else []) +
                 ([(sMM, w_last_attn_ev)] if dc < 2 else []), inc=sXO, inck=16)
        w_xo = sXO.n
        for qh in range(2):
            ps_o = (psA if qh == 0 else psB) if dc % 2 == 0 else (psC if qh == 0 else psD)
            for kc in range(NKC):
                st("tensor", lambda e, dc=dc, qh=qh, kc=kc, sl=sl, ps_o=ps_o: e.matmul(
                    ps_o[:, 0:512], wo_slot[sl][:, kc * 128:(kc + 1) * 128],
                    ysb[:, kc * OWN + qh * 512: kc * OWN + (qh + 1) * 512],
                    start=(kc == 0), stop=(kc == NKC - 1)),
                   waits=([(sAs[sl], ev_ld[("WO", dc)]), (sVE, w_ynorm)] +
                          ([(sSE, w_x2.get(("se", dc - 2, qh), 0))] if dc >= 2 else []))
                   if kc == 0 else [],
                   inc=(sMM if kc == NKC - 1 else None))
            w_omm = sMM.n
            w_x2[("se", dc, qh)] = st("scalar", lambda e, dc=dc, qh=qh, ps_o=ps_o: e.activation(
                out=kraw[:, 0:512], in_=ps_o[:, 0:512], func=AF.Identity,
                scale=modv[:, 32 + dc:33 + dc]),
                waits=[(sMM, w_omm)] + ([(sVE, w_x2.get(("tt", dc, 0), 0))]
                                        if qh == 1 else
                                        [(sVE, w_x2.get(("tt", dc - 1, 1), 0))]))
            w_x2[("tt", dc, qh)] = st("vector", lambda e, dc=dc, qh=qh, xsl=xsl: e.tensor_tensor(
                out=xsl[:, qh * 512:(qh + 1) * 512], in0=xsl[:, qh * 512:(qh + 1) * 512],
                in1=kraw[:, 0:512], op=ALU.add),
                waits=[(sXO, w_xo), (sSE, w_x2[("se", dc, qh)])])
            st("vector", lambda e, qh=qh, xsl=xsl: e.tensor_copy(
                out=xbb[:, qh * 512:(qh + 1) * 512], in_=xsl[:, qh * 512:(qh + 1) * 512]),
               waits=[(sMM, w_stmm.get((dc - 1, qh), 0))])
            w_x2[dc] = sVE.n
            w_sq2 = st("scalar", lambda e, qh=qh: e.activation(
                out=sqb2[:, 0:512], in_=xbb[:, qh * 512:(qh + 1) * 512],
                func=AF.Square),
                waits=[(sVE, w_x2[dc]),
                       (sMM, w_stmm.get((dc, qh - 1),
                                        w_stmm.get((dc - 1, 1), 0)))])
            ps_sum = psE if qh == 0 else psF
            ps_sq = psG if qh == 0 else psH
            st("tensor", lambda e, dc=dc, qh=qh, ps_sum=ps_sum: e.matmul(
                ps_sum[:, 0:512], onesb[:, :], xbb[:, qh * 512:(qh + 1) * 512],
                start=(dc == 0), stop=(dc == NKC - 1)),
               waits=[(sVE, w_x2[dc])], inc=sMM)
            st("tensor", lambda e, dc=dc, qh=qh, ps_sq=ps_sq: e.matmul(
                ps_sq[:, 0:512], onesb[:, :], sqb2[:, 0:512],
                start=(dc == 0), stop=(dc == NKC - 1)),
               waits=[(sSE, w_sq2)], inc=sMM)
            w_stmm[(dc, qh)] = sMM.n
        wo_done[dc] = sMM.n
        if dc + 2 < NKC:
            emit_wo_load(dc + 2)
        sXSp = sXS0 if dc % 2 == 0 else sXS1
        st("sync", lambda e, dc=dc, xsl=xsl: e.dma_start(
            out=x2s_d[dc * 128:(dc + 1) * 128, :], in_=xsl),
           waits=[(sVE, w_x2[dc])], inc=sXSp, inck=16)
        w_x2spill.append(sXSp.n)
    w_ln2_mm = sMM.n
    if "d_x2" in DBG:
        st("sync", lambda e: e.dma_start(out=DBG["d_x2"][:, :], in_=x2s_d[:, :])
           if False else e.nop())
        for dc in range(NKC):
            st("gpsimd", lambda e, dc=dc: e.dma_start(
                out=DBG["d_x2"][dc * 128:(dc + 1) * 128, :],
                in_=x2s_d[dc * 128:(dc + 1) * 128, :]),
               waits=[(sXS0 if dc % 2 == 0 else sXS1, w_x2spill[dc])],
               inc=sDBG2, inck=16)

    if phases <= 2:
        for kc in range(NKC):
            st("gpsimd", lambda e, kc=kc: e.dma_start(
                out=out_d[kc * 128:(kc + 1) * 128, :], in_=ysb[:, kc * OWN:kc * OWN + OWN]),
               waits=[(sVE, w_ynorm)] if kc == 0 else [], inc=sOUT, inck=16)
        w_fin = sOUT.n
        st("sync", lambda e: e.nop(),
           waits=[(sOUT, w_fin), (sDBG, sDBG.n), (sDBG2, sDBG2.n),
                  (sXS0, sXS0.n), (sXS1, sXS1.n), (sMM, sMM.n)])
        K.DBG = DBG
        return K

    dbg_t("d_h2", [128, NKC * OWN]); dbg_t("d_q12", [128, NKC * OWN])
    dbg_t("d_hp", [128, 2 * OWN]); dbg_t("d_q3", [128, 2 * OWN])
    dbg_t("d_rows", [1, 2048])
    rr0 = rows[0:1, 0:1024]
    rr1 = rows[0:1, 1024:2048]
    rr2 = rows[0:1, 2048:3072]
    st("vector", lambda e: e.tensor_scalar_mul(out=rows[0:1, 0:512], in0=psE[0:1, 0:512],
                                               scalar1=1.0 / D),
       waits=[(sMM, w_ln2_mm), (sLR, sLR.n), (sRB0, sRB0.n), (sRB1, sRB1.n)])
    st("vector", lambda e: e.tensor_scalar_mul(out=rows[0:1, 512:1024],
                                               in0=psF[0:1, 0:512], scalar1=1.0 / D))
    st("vector", lambda e: e.tensor_scalar_mul(out=rows[0:1, 1024:1536],
                                               in0=psG[0:1, 0:512], scalar1=1.0 / D))
    w_r4 = st("vector", lambda e: e.tensor_scalar_mul(out=rows[0:1, 1536:2048],
                                                      in0=psH[0:1, 0:512], scalar1=1.0 / D))
    st("vector", lambda e: e.tensor_tensor(out=rr2, in0=rr0, in1=rr0, op=ALU.mult))
    st("vector", lambda e: e.tensor_tensor(out=rr1, in0=rr1, in1=rr2, op=ALU.subtract))
    w_var2 = sVE.n
    st("scalar", lambda e: e.activation(out=rr2, in_=rr1, func=AF.Sqrt,
                                        bias=consts[0:1, 2:3]), waits=[(sVE, w_var2)])
    w_std2 = sSE.n
    st("vector", lambda e: e.reciprocal(out=rr1, in_=rr2), waits=[(sSE, w_std2)])
    st("vector", lambda e: e.tensor_tensor(out=rr0, in0=rr0, in1=rr1, op=ALU.mult))
    w_ab2 = st("vector", lambda e: e.tensor_scalar_mul(out=rr0, in0=rr0, scalar1=-1.0))
    wbA2w, wbA2 = bounce(0, rr1, 1024, 16, rbc[:, 0:1024], waits=[(sVE, w_ab2)])
    wbB2w, wbB2 = bounce(1, rr0, 1024, 17, sm2[:, 0:1024], waits=[(sVE, w_ab2)])
    w_h2 = {}
    w_h2mm = {}
    w_maxinit = st("vector", lambda e: e.memset(maxac[:, :], 0.0))
    for dc in range(NKC):
        sXO = sXO0 if dc % 2 == 0 else sXO1
        xsl = xst[:, (dc % 2) * 1024:(dc % 2 + 1) * 1024]
        st("sync", lambda e, dc=dc, xsl=xsl: e.dma_start(
            out=xsl, in_=x2s_d[dc * 128:(dc + 1) * 128, :]),
           waits=([(sVE, w_h2.get(dc - 2, 0)), (sSE, w_h2.get(("aff", dc - 2), 0))]
                  if dc >= 2 else
                  [(sXS0, sXS0.n), (sXS1, sXS1.n)]), inc=sXO, inck=16)
        w_xo = sXO.n
        st("vector", lambda e, xsl=xsl: e.tensor_tensor(
            out=xsl, in0=xsl, in1=rbc[:, 0:1024], op=ALU.mult),
           waits=[(sXO, w_xo), wbA2, wbB2])
        w_t2 = st("vector", lambda e, xsl=xsl: e.tensor_tensor(
            out=xsl, in0=xsl, in1=sm2[:, 0:1024], op=ALU.add))
        w_aff = st("scalar", lambda e, dc=dc, xsl=xsl: e.activation(
            out=h2sb[:, dc * OWN:(dc + 1) * OWN], in_=xsl, func=AF.Identity,
            scale=modv[:, 64 + dc:65 + dc], bias=modv[:, 48 + dc:49 + dc]),
            waits=[(sVE, w_t2)] + ([(sDBG3, w_dy_tap)] if dc == 0 else []))
        w_h2[("aff", dc)] = w_aff
        w_h2[dc] = sVE.n
        for th in range(2):
            w_sqh = st("scalar", lambda e, dc=dc, th=th: e.activation(
                out=sqb2[:, 0:512],
                in_=h2sb[:, dc * OWN + th * 512: dc * OWN + (th + 1) * 512],
                func=AF.Square),
                waits=[(sMM, w_h2mm.get((dc, th - 1),
                                        w_h2mm.get((dc - 1, 1), 0))),
                       (sVE, w_h2.get(("mx", dc, th - 1),
                                      w_h2.get(("mx", dc - 1, 1), 0)))])
            ps_sq = psA if th == 0 else psB
            st("tensor", lambda e, dc=dc, th=th, ps_sq=ps_sq: e.matmul(
                ps_sq[:, 0:512], onesb[:, :], sqb2[:, 0:512],
                start=(dc == 0), stop=(dc == NKC - 1)),
               waits=[(sSE, w_sqh)], inc=sMM)
            w_h2mm[(dc, th)] = sMM.n
            w_h2[("mx", dc, th)] = st("vector", lambda e, th=th: e.tensor_tensor(
                out=maxac[:, th * 512:(th + 1) * 512], in0=maxac[:, th * 512:(th + 1) * 512],
                in1=sqb2[:, 0:512], op=ALU.max),
                waits=[(sSE, w_sqh)] + ([(sVE, w_maxinit)] if dc == 0 else []))
    if "d_h2" in DBG:
        st("gpsimd", lambda e: e.dma_start(out=DBG["d_h2"][:, :], in_=h2sb[:, :]),
           waits=[(sSE, sSE.n)], inc=sDBG2, inck=16)
    st("vector", lambda e: e.tensor_scalar_mul(
        out=rows[0:1, 1024:1536], in0=psA[0:1, 0:512], scalar1=1.0 / D),
       waits=[(sMM, w_h2mm[(NKC - 1, 0)]), wbA2w])
    w_rmsev = st("vector", lambda e: e.tensor_scalar_mul(
        out=rows[0:1, 1536:2048], in0=psB[0:1, 0:512], scalar1=1.0 / D),
        waits=[(sMM, w_h2mm[(NKC - 1, 1)])])
    st("vector", lambda e: e.tensor_copy(out=tmpb[0:64, :], in_=maxac[64:128, :]),
       waits=[(sVE, w_h2[("mx", NKC - 1, 1)])])
    st("vector", lambda e: e.tensor_tensor(out=maxac[0:64, :], in0=maxac[0:64, :],
                                           in1=tmpb[0:64, :], op=ALU.max))
    st("vector", lambda e: e.tensor_copy(out=tmpb[0:32, :], in_=maxac[32:64, :]))
    w_mx32 = st("vector", lambda e: e.tensor_tensor(out=maxac[0:32, :], in0=maxac[0:32, :],
                                                    in1=tmpb[0:32, :], op=ALU.max))
    wsh = w_mx32
    for npart in (16, 8, 4, 2, 1):
        st("sync", lambda e, npart=npart: e.dma_start(
            out=tmpb[0:npart, :], in_=maxac[npart:2 * npart, :]
            if npart >= 32 else maxac[npart:2 * npart, :]),
           waits=[(sVE, wsh)], inc=sB[4], inck=16)
        wdma = sB[4].n
        wsh = st("vector", lambda e, npart=npart: e.tensor_tensor(
            out=maxac[0:npart, :], in0=maxac[0:npart, :], in1=tmpb[0:npart, :],
            op=ALU.max), waits=[(sB[4], wdma)])
    st("scalar", lambda e: e.activation(out=rr2, in_=rr1, func=AF.Sqrt,
                                        bias=consts[0:1, 3:4]),
       waits=[(sVE, wsh), (sVE, w_rmsev)])
    w_s3 = sSE.n
    st("vector", lambda e: e.reciprocal(out=rr1, in_=rr2), waits=[(sSE, w_s3)])
    st("scalar", lambda e: e.activation(out=rr0, in_=maxac[0:1, 0:1024], func=AF.Sqrt),
       waits=[(sVE, sVE.n)])
    w_s4 = sSE.n
    st("vector", lambda e: e.tensor_tensor(out=rr0, in0=rr0, in1=rr1, op=ALU.mult),
       waits=[(sSE, w_s4)])
    st("vector", lambda e: e.tensor_scalar_max(out=rr0, in0=rr0, scalar1=1e-5))
    st("vector", lambda e: e.reciprocal(out=rr2, in_=rr0))
    st("vector", lambda e: e.tensor_tensor(out=rr2, in0=rr2, in1=rr1, op=ALU.mult))
    w_cq = st("vector", lambda e: e.tensor_scalar_mul(out=rr2, in0=rr2, scalar1=127.0))
    st("vector", lambda e: e.tensor_scalar(out=rr1, in0=rr0, scalar1=0.5 / 127.0,
                                           scalar2=wsc[0:1, 3:4],
                                           op0=ALU.mult, op1=ALU.mult))
    w_du1 = sVE.n
    st("vector", lambda e: e.tensor_scalar(out=rr0, in0=rr0, scalar1=0.5 / 127.0,
                                           scalar2=wsc[0:1, 4:5],
                                           op0=ALU.mult, op1=ALU.mult))
    w_du2 = sVE.n
    wbCQw, wbCQ = bounce(2, rr2, 1024, 18, maxac[:, 0:1024], waits=[(sVE, w_cq)])
    wbD1w, wbD1 = bounce(0, rr1, 1024, 19, du1b[:, 0:1024], waits=[(sVE, w_du1), wbA2])
    if "d_rows" in DBG:
        st("sync", lambda e: e.dma_start(out=DBG["d_rows"][0:1, 0:1024], in_=rr2),
           waits=[(sVE, w_cq)], inc=sDBG, inck=16)
        st("sync", lambda e: e.dma_start(out=DBG["d_rows"][0:1, 1024:2048], in_=rr0),
           waits=[(sVE, w_du2)], inc=sDBG, inck=16)
    if "d_cqb" in dbg:
        DBG["d_cqb"] = nc.dram_tensor("d_cqb", [128, 2048], F32, kind="ExternalOutput").ap()
        st("gpsimd", lambda e: e.dma_start(out=DBG["d_cqb"][:, 0:1024], in_=maxac[:, 0:1024]),
           waits=[wbCQ], inc=sDBG5, inck=16)
        st("gpsimd", lambda e: e.dma_start(out=DBG["d_cqb"][:, 1024:2048], in_=du1b[:, 0:1024]),
           waits=[wbD1], inc=sDBG5, inck=16)
    w_q12 = 0
    for dc in range(NKC):
        chains = []
        for th in range(2):
            chains.append(dict(
                src=h2sb[:, dc * OWN + th * 512: dc * OWN + (th + 1) * 512],
                cq=maxac[:, th * 512:(th + 1) * 512],
                ta=rbc[:, th * 512:(th + 1) * 512],
                tb=sm2[:, th * 512:(th + 1) * 512],
                bb=tmpb[:, th * 512:(th + 1) * 512],
                dst=q12[:, dc * OWN + th * 512: dc * OWN + (th + 1) * 512],
                mw=([wbCQ, wbD1] if (dc == 0 and th == 0) else [])))
        w_q12 = emit_ladder(chains)
    if "d_q12" in DBG:
        st("gpsimd", lambda e: e.dma_start(out=DBG["d_q12"][:, :], in_=q12[:, :]),
           waits=[(sVE, w_q12)], inc=sDBG2, inck=16)

    st("vector", lambda e: e.tensor_scalar_mul(out=wsc[:, 6:7], in0=wsc[:, 0:1],
                                               scalar1=-1.0),
       waits=[(sVE, w_scale_fold[-1])] + ([(sDBG4, sDBG4.n)] if "d_ws" in DBG else []))
    w_negt = st("vector", lambda e: e.tensor_scalar_mul(out=wsc[:, 7:8], in0=wsc[:, 1:2],
                                                        scalar1=-1.0))
    w_f1mm = {}
    w_tern = {}
    w_ternse = {}
    w_hspill = {}
    w_hv = {}
    w_maxi3 = st("vector", lambda e: e.memset(maxac[:, :], 0.0),
                 waits=[(sVE, w_q12)] + ([(sDBG5, 32)] if "d_cqb" in DBG else []))
    def f1_loads(dc2):
        sW = sW0 if dc2 % 2 == 0 else sW1
        w_dst1 = wfl if dc2 % 2 == 0 else wfl3
        w_dst2 = wfl2 if dc2 % 2 == 0 else wfl4
        for cc in range(4):
            st("sync", lambda e, dc2=dc2, cc=cc, w_dst1=w_dst1: e.dma_start(
                out=w_dst1[:, cc * 512:(cc + 1) * 512],
                in_=w1_d[dc2][:, 4 * cc:4 * cc + 4, :]),
               waits=([(sSE, w_ternse.get(dc2 - 2, 0))] if cc == 0 else []),
               inc=sW, inck=16)
            st("sync", lambda e, dc2=dc2, cc=cc, w_dst2=w_dst2: e.dma_start(
                out=w_dst2[:, cc * 512:(cc + 1) * 512],
                in_=w2_d[dc2][:, 4 * cc:4 * cc + 4, :]),
               inc=sW, inck=16)
        return sW.n

    def f1_tern(dc2, w_wld):
        sW = sW0 if dc2 % 2 == 0 else sW1
        wslc1 = wfl if dc2 % 2 == 0 else wfl3
        wslc2 = wfl2 if dc2 % 2 == 0 else wfl4
        tA = ternA if dc2 % 2 == 0 else ternA2
        tB = ternB if dc2 % 2 == 0 else ternB2
        st("scalar", lambda e, wslc1=wslc1: e.activation(
            out=terntmp[:, :], in_=wslc1[:, :], func=AF.Sign, bias=wsc[:, 6:7]),
           waits=[(sW, w_wld), (sVE, w_negt), (sVE, w_tern.get(dc2 - 1, 0))] +
                 ([(sDBG2, sDBG2.n)] if dc2 < 2 else []))
        st("scalar", lambda e, wslc1=wslc1: e.activation(
            out=sgntmp[:, :], in_=wslc1[:, :], func=AF.Sign, bias=wsc[:, 0:1]))
        w_s2 = sSE.n
        st("scalar", lambda e, wslc2=wslc2: e.activation(
            out=sgntmp2[:, :], in_=wslc2[:, :], func=AF.Sign, bias=wsc[:, 7:8]))
        w_ternse[dc2] = st("scalar", lambda e, wslc2=wslc2: e.activation(
            out=sgntmp3[:, :], in_=wslc2[:, :], func=AF.Sign, bias=wsc[:, 1:2]))
        wta = st("vector", lambda e, tA=tA: e.tensor_tensor(
            out=tA[:, :], in0=terntmp[:, :], in1=sgntmp[:, :], op=ALU.add),
            waits=[(sSE, w_s2), (sMM, w_f1mm.get(dc2 - 2, 0))])
        w_tern[dc2] = st("vector", lambda e, tB=tB: e.tensor_tensor(
            out=tB[:, :], in0=sgntmp2[:, :], in1=sgntmp3[:, :], op=ALU.add),
            waits=[(sSE, w_ternse[dc2])])

    w_ld = {0: f1_loads(0)}
    f1_tern(0, w_ld[0])
    for dc2 in range(NFC):
        if dc2 + 1 < NFC:
            w_ld[dc2 + 1] = f1_loads(dc2 + 1)
            f1_tern(dc2 + 1, w_ld[dc2 + 1])
        tA = ternA if dc2 % 2 == 0 else ternA2
        tB = ternB if dc2 % 2 == 0 else ternB2
        psU0 = psA if dc2 % 2 == 0 else psE
        psU1 = psB if dc2 % 2 == 0 else psF
        for th in range(2):
            for kc in range(NKC):
                st("tensor", lambda e, th=th, kc=kc, tA=tA, psU0=psU0, psU1=psU1:
                   e.matmul(
                       (psU0 if th == 0 else psU1)[:, 0:512],
                       tA[:, kc * 128:(kc + 1) * 128],
                       q12[:, kc * OWN + th * 512: kc * OWN + (th + 1) * 512],
                       start=(kc == 0), stop=(kc == NKC - 1)),
                   waits=([(sVE, w_tern[dc2]), (sVE, w_q12)] +
                          ([(sVE, w_hv.get(("udq", dc2 - 2, th), 0))])
                          if kc == 0 else []),
                   inc=(sMM if kc == NKC - 1 else None))
        for th in range(2):
            for kc in range(NKC):
                st("tensor", lambda e, th=th, kc=kc, tB=tB: e.matmul(
                    (psC if th == 0 else psD)[:, 0:512],
                    tB[:, kc * 128:(kc + 1) * 128],
                    q12[:, kc * OWN + th * 512: kc * OWN + (th + 1) * 512],
                    start=(kc == 0), stop=(kc == NKC - 1)),
                   waits=([(sVE, w_hv.get((dc2 - 1, th), 0))] if kc == 0 else []),
                   inc=(sMM if kc == NKC - 1 else None))
        w_f1mm[dc2] = sMM.n
        for th in range(2):
            psu = psU0 if th == 0 else psU1
            pst2 = psC if th == 0 else psD
            hsl = hch[:, ((2 * dc2 + th) % 4) * 512:(((2 * dc2 + th) % 4) + 1) * 512]
            sHSp = sHS[(2 * dc2 + th) % 4]
            st("vector", lambda e, psu=psu, th=th: e.tensor_tensor(
                out=rbc[:, th * 512:(th + 1) * 512], in0=psu[:, 0:512],
                in1=du1b[:, th * 512:(th + 1) * 512], op=ALU.mult),
               waits=[(sMM, w_f1mm[dc2])] +
                     ([(sHSp, sHSp.n)] if dc2 >= 2 else []))
            w_udq = sVE.n
            w_hv[("udq", dc2, th)] = w_udq
            st("scalar", lambda e, th=th: e.activation(
                out=sm2[:, th * 512:(th + 1) * 512], in_=rbc[:, th * 512:(th + 1) * 512],
                func=AF.Silu), waits=[(sVE, w_udq)])
            w_sil = sSE.n
            w_hv[(dc2, th)] = st("vector", lambda e, pst2=pst2, th=th, hsl=hsl: e.tensor_tensor(
                out=hsl, in0=sm2[:, th * 512:(th + 1) * 512], in1=pst2[:, 0:512],
                op=ALU.mult), waits=[(sSE, w_sil)])
            st("sync", lambda e, dc2=dc2, th=th, hsl=hsl: e.dma_start(
                out=hs_d[dc2, :, th * 512:(th + 1) * 512], in_=hsl),
               waits=[(sVE, w_hv[(dc2, th)])], inc=sHSp, inck=16)
            w_hspill[(dc2, th)] = sHSp.n
            w_sq3 = st("scalar", lambda e, hsl=hsl: e.activation(
                out=sqb2[:, 0:512], in_=hsl, func=AF.Square),
                waits=[(sVE, w_hv[(dc2, th)]),
                       (sMM, w_f1mm.get(("st", dc2, th - 1),
                                        w_f1mm.get(("st", dc2 - 1, 1), 0))),
                       (sVE, w_hv.get(("mx", dc2, th - 1),
                                      w_hv.get(("mx", dc2 - 1, 1), 0)))])
            ps_sq = psG if th == 0 else psH
            st("tensor", lambda e, dc2=dc2, th=th, ps_sq=ps_sq: e.matmul(
                ps_sq[:, 0:512], onesb[:, :], sqb2[:, 0:512],
                start=(dc2 == 0), stop=(dc2 == NFC - 1)),
               waits=[(sSE, w_sq3)], inc=sMM)
            w_f1mm[("st", dc2, th)] = sMM.n
            w_hv[("mx", dc2, th)] = st("vector", lambda e, th=th: e.tensor_tensor(
                out=maxac[:, th * 512:(th + 1) * 512],
                in0=maxac[:, th * 512:(th + 1) * 512], in1=sqb2[:, 0:512], op=ALU.max),
                waits=[(sSE, w_sq3)] + ([(sVE, w_maxi3)] if dc2 == 0 else []))
    if "d_hp" in DBG:
        st("gpsimd", lambda e: e.dma_start(out=DBG["d_hp"][:, 0:1024],
                                           in_=hs_d[0, :, :]),
           waits=[(sHS[0], w_hspill[(0, 0)]), (sHS[1], w_hspill[(0, 1)])],
           inc=sDBG2, inck=16)

    st("vector", lambda e: e.tensor_scalar_mul(out=rows[0:1, 1024:1536],
                                               in0=psG[0:1, 0:512], scalar1=1.0 / DFF),
       waits=[(sMM, w_f1mm[("st", NFC - 1, 0)]), (sMM, w_f1mm[("st", NFC - 1, 1)]),
              (sDBG, sDBG.n)])
    st("vector", lambda e: e.tensor_scalar_mul(out=rows[0:1, 1536:2048],
                                               in0=psH[0:1, 0:512], scalar1=1.0 / DFF))
    st("vector", lambda e: e.tensor_tensor(out=rr2, in0=rr0, in1=rr0, op=ALU.mult))
    st("vector", lambda e: e.tensor_tensor(out=rr1, in0=rr1, in1=rr2, op=ALU.mult))
    w_e3 = sVE.n
    st("scalar", lambda e: e.activation(out=rr2, in_=rr1, func=AF.Sqrt,
                                        bias=consts[0:1, 3:4]), waits=[(sVE, w_e3)])
    w_s5 = sSE.n
    st("vector", lambda e: e.reciprocal(out=rr1, in_=rr2), waits=[(sSE, w_s5)])
    st("vector", lambda e: e.tensor_copy(out=tmpb[0:64, :], in_=maxac[64:128, :]),
       waits=[(sVE, w_hv[("mx", NFC - 1, 1)])])
    st("vector", lambda e: e.tensor_tensor(out=maxac[0:64, :], in0=maxac[0:64, :],
                                           in1=tmpb[0:64, :], op=ALU.max))
    st("vector", lambda e: e.tensor_copy(out=tmpb[0:32, :], in_=maxac[32:64, :]))
    wsh = st("vector", lambda e: e.tensor_tensor(out=maxac[0:32, :], in0=maxac[0:32, :],
                                                 in1=tmpb[0:32, :], op=ALU.max))
    for npart in (16, 8, 4, 2, 1):
        st("sync", lambda e, npart=npart: e.dma_start(
            out=tmpb[0:npart, :], in_=maxac[npart:2 * npart, :]),
           waits=[(sVE, wsh)], inc=sB[4], inck=16)
        wdma = sB[4].n
        wsh = st("vector", lambda e, npart=npart: e.tensor_tensor(
            out=maxac[0:npart, :], in0=maxac[0:npart, :], in1=tmpb[0:npart, :],
            op=ALU.max), waits=[(sB[4], wdma)])
    st("scalar", lambda e: e.activation(out=rr2, in_=maxac[0:1, 0:1024], func=AF.Sqrt),
       waits=[(sVE, wsh)])
    w_s6 = sSE.n
    st("vector", lambda e: e.tensor_tensor(out=rr2, in0=rr2, in1=rr0, op=ALU.mult),
       waits=[(sSE, w_s6)])
    st("vector", lambda e: e.tensor_tensor(out=rr2, in0=rr2, in1=rr1, op=ALU.mult))
    st("vector", lambda e: e.tensor_scalar_max(out=rr2, in0=rr2, scalar1=1e-5))
    st("vector", lambda e: e.tensor_tensor(out=rr1, in0=rr1, in1=rr0, op=ALU.mult))
    st("vector", lambda e: e.reciprocal(out=rr0, in_=rr2))
    st("vector", lambda e: e.tensor_tensor(out=rr1, in0=rr1, in1=rr0, op=ALU.mult))
    w_cq3 = st("vector", lambda e: e.tensor_scalar_mul(out=rr1, in0=rr1, scalar1=127.0))
    st("vector", lambda e: e.tensor_scalar(out=rr2, in0=rr2, scalar1=0.5 / 127.0,
                                           scalar2=wsc[0:1, 5:6],
                                           op0=ALU.mult, op1=ALU.mult))
    w_do3 = sVE.n
    wbC3w, wbC3 = bounce(2, rr1, 1024, 20, cq3b[:, 0:1024],
                         waits=[(sVE, w_cq3), wbCQ])
    wbD3w, wbD3 = bounce(1, rr2, 1024, 21, do3b[:, 0:1024],
                         waits=[(sVE, w_do3), wbB2])
    w_negt3 = st("vector", lambda e: e.tensor_scalar_mul(out=wsc[:, 6:7],
                                                         in0=wsc[:, 2:3], scalar1=-1.0),
                 waits=[(sSE, w_ternse[NFC - 1])])

    w_f3 = {}
    w_q3sp = {}
    w_outev = {}
    w_x3st = []
    for dgq in range(4):
        douts = list(range(dgq * 4, (dgq + 1) * 4))
        for k3 in range(NFC):
            qsl = q3c[:, (k3 % 2) * 1024:((k3 % 2) + 1) * 1024]
            sHRp = sHR[k3 % 2]
            if dgq == 0:
                hsl2 = hch[:, (k3 % 2) * 1024:((k3 % 2) + 1) * 1024]
                for cc in range(2):
                    st("sync", lambda e, k3=k3, hsl2=hsl2, cc=cc: e.dma_start(
                        out=hsl2[:, cc * 512:(cc + 1) * 512],
                        in_=hs_d[k3, :, cc * 512:(cc + 1) * 512]),
                       waits=([(sVE, w_f3.get(("lad", k3 - 2), 0)),
                               (sHS[0], sHS[0].n), (sHS[1], sHS[1].n),
                               (sHS[2], sHS[2].n), (sHS[3], sHS[3].n)]
                              if k3 < 2 else
                              [(sVE, w_f3.get(("lad", k3 - 2), 0))])
                       if cc == 0 else [],
                       inc=sHRp, inck=16)
                w_hr = sHRp.n
                chains = []
                for th in range(2):
                    chains.append(dict(
                        src=hsl2[:, th * 512:(th + 1) * 512],
                        cq=cq3b[:, th * 512:(th + 1) * 512],
                        ta=rbc[:, th * 512:(th + 1) * 512],
                        tb=sm2[:, th * 512:(th + 1) * 512],
                        bb=tmpb[:, th * 512:(th + 1) * 512],
                        dst=qsl[:, th * 512:(th + 1) * 512],
                        mw=[(sHRp, w_hr)] +
                           ([wbC3, wbD3] if k3 == 0 and th == 0 else []) +
                           ([(sMM, w_f3.get(("mm", k3 - 2), 0)),
                             (sQS[k3 % 2], w_q3sp.get(k3 - 2, 0))]
                            if th == 0 else [])))
                w_f3[("lad", k3)] = emit_ladder(chains)
                sQSp = sQS[k3 % 2]
                st("sync", lambda e, k3=k3, qsl=qsl: e.dma_start(
                    out=q3s_d[k3, :, :], in_=qsl),
                   waits=[(sVE, w_f3[("lad", k3)])], inc=sQSp, inck=16)
                w_q3sp[k3] = sQSp.n
                w_qrdy = ("v", w_f3[("lad", k3)])
            else:
                for cc in range(2):
                    st("sync", lambda e, k3=k3, qsl=qsl, cc=cc: e.dma_start(
                        out=qsl[:, cc * 512:(cc + 1) * 512],
                        in_=q3s_d[k3, :, cc * 512:(cc + 1) * 512]),
                       waits=(([(sQS[0], sQS[0].n), (sQS[1], sQS[1].n)] if
                               (dgq == 1 and k3 < 2) else []) +
                              [(sMM, w_f3.get(("mm", k3 - 2), 0))])
                       if cc == 0 else [],
                       inc=sHRp, inck=16)
                w_qrdy = ("d", sHRp.n, sHRp)
            sW = sW0 if k3 % 2 == 0 else sW1
            wslc = wfl if k3 % 2 == 0 else wfl2
            for cc in range(2):
                st("sync", lambda e, dgq=dgq, k3=k3, wslc=wslc, cc=cc: e.dma_start(
                    out=wslc[:, cc * 256:(cc + 1) * 256],
                    in_=w3_d[dgq, k3][:, cc * 256:(cc + 1) * 256]),
                   waits=([(sSE, w_f3.get(("ternse", k3 - 2), 0))] if cc == 0 else []),
                   inc=sW, inck=16)
            w_w3 = sW.n
            tsl = ternA[:, (k3 % 2) * 512:((k3 % 2) + 1) * 512]
            st("scalar", lambda e, wslc=wslc: e.activation(
                out=terntmp[:, 0:512], in_=wslc[:, 0:512], func=AF.Sign,
                bias=wsc[:, 6:7]),
               waits=[(sW, w_w3), (sVE, w_f3.get(("tern", k3 - 1), 0))] +
                     ([(sVE, w_negt3)] if dgq == 0 and k3 == 0 else []))
            w_f3[("ternse", k3)] = st("scalar", lambda e, wslc=wslc: e.activation(
                out=sgntmp[:, 0:512], in_=wslc[:, 0:512], func=AF.Sign,
                bias=wsc[:, 2:3]))
            w_f3[("tern", k3)] = st("vector", lambda e, tsl=tsl: e.tensor_tensor(
                out=tsl, in0=terntmp[:, 0:512], in1=sgntmp[:, 0:512], op=ALU.add),
                waits=[(sSE, w_f3[("ternse", k3)]),
                       (sMM, w_f3.get(("mm", k3 - 2), 0))])
            for di, dout in enumerate(douts):
                for th in range(2):
                    psx = pss[di * 2 + th]
                    ww = []
                    if k3 == 0:
                        ww = [(sVE, w_f3[("tern", k3)])]
                        if w_qrdy[0] == "v":
                            ww.append((sVE, w_qrdy[1]))
                        else:
                            ww.append((w_qrdy[2], w_qrdy[1]))
                        if dgq > 0 and di == 0 and th == 0:
                            ww.append((sSE, w_outev.get(dgq - 1, 0)))
                    elif di == 0 and th == 0:
                        ww = [(sVE, w_f3[("tern", k3)])]
                        if w_qrdy[0] == "v":
                            ww.append((sVE, w_qrdy[1]))
                        else:
                            ww.append((w_qrdy[2], w_qrdy[1]))
                    st("tensor", lambda e, di=di, th=th, psx=psx, qsl=qsl, k3=k3,
                       tsl=tsl: e.matmul(
                           psx[:, 0:512], tsl[:, di * 128:(di + 1) * 128],
                           qsl[:, th * 512:(th + 1) * 512],
                           start=(k3 == 0), stop=(k3 == NFC - 1)),
                       waits=ww, inc=(sMM if (di == 3 and th == 1) else None))
            w_f3[("mm", k3)] = sMM.n
        for di, dout in enumerate(douts):
            sXO = sXO0 if dout % 2 == 0 else sXO1
            xsl = xst[:, (dout % 2) * 1024:(dout % 2 + 1) * 1024]
            st("sync", lambda e, dout=dout, xsl=xsl: e.dma_start(
                out=xsl, in_=x2s_d[dout * 128:(dout + 1) * 128, :]),
               waits=[(sVE, w_f3.get(("x3", dout - 2), 0))] +
                     ([w_f3[("ost", dout - 2)]] if ("ost", dout - 2) in w_f3 else []),
               inc=sXO, inck=16)
            w_xo = sXO.n
            for th in range(2):
                psx = pss[di * 2 + th]
                w_ev = st("scalar", lambda e, psx=psx, dout=dout: e.activation(
                    out=kraw[:, 0:512], in_=psx[:, 0:512], func=AF.Identity,
                    scale=modv[:, 80 + dout:81 + dout]),
                    waits=[(sMM, w_f3[("mm", NFC - 1)])] +
                          ([(sVE, w_f3.get(("x3v", dout, th - 1),
                                           w_f3.get(("x3v", dout - 1, 1), 0)))]))
                st("vector", lambda e, th=th: e.tensor_tensor(
                    out=kraw[:, 0:512], in0=kraw[:, 0:512],
                    in1=do3b[:, th * 512:(th + 1) * 512], op=ALU.mult),
                   waits=[(sSE, w_ev)])
                w_f3[("x3v", dout, th)] = st("vector", lambda e, th=th, xsl=xsl:
                                             e.tensor_tensor(
                    out=xsl[:, th * 512:(th + 1) * 512],
                    in0=xsl[:, th * 512:(th + 1) * 512],
                    in1=kraw[:, 0:512], op=ALU.add), waits=[(sXO, w_xo)])
            w_f3[("x3", dout)] = w_f3[("x3v", dout, 1)]
            sOUTp = sOUT if dout % 2 == 0 else sOUT1
            st("sync", lambda e, dout=dout, xsl=xsl: e.dma_start(
                out=out_d[dout * 128:(dout + 1) * 128, :], in_=xsl),
               waits=[(sVE, w_f3[("x3", dout)])], inc=sOUTp, inck=16)
            w_f3[("ost", dout)] = (sOUTp, sOUTp.n)
        w_outev[dgq] = sSE.n
    if "d_q3" in DBG:
        st("gpsimd", lambda e: e.dma_start(out=DBG["d_q3"][:, 0:1024],
                                           in_=q3s_d[0, :, :]),
           waits=[(sQS[0], w_q3sp.get(0, 0))], inc=sDBG2, inck=16)

    st("sync", lambda e: e.nop(),
       waits=[(sOUT, sOUT.n), (sOUT1, sOUT1.n), (sDBG, sDBG.n), (sDBG2, sDBG2.n), (sDBG3, sDBG3.n), (sDBG4, sDBG4.n)])
    K.DBG = DBG
    return K


def _host_prep(inputs, b, half):
    x = inputs["L0_attn_x"][b]
    own = half * OWN
    pos = (np.arange(T) + own) % T
    xfm = np.ascontiguousarray(x[pos].T).astype(np.float32)
    c = inputs["L0_c"][b]
    cvec = np.ascontiguousarray(np.asarray(c).reshape(NKC, 128).T).astype(np.float32)
    return {"xc": xfm, "cvec": cvec, "pos": pos.astype(np.float32).reshape(1, T)}


def _til(w, no, ni):
    return np.ascontiguousarray(
        np.asarray(w).reshape(no // 128, 128, ni // 128, 128).transpose(0, 3, 2, 1)
    ).astype(np.float32)


def _host_weights(inputs):
    wv = np.asarray(inputs["L0_attn_wv"])
    w3 = np.asarray(inputs["L0_ffn_w3"])
    wmod = np.asarray(inputs["L0_wmod"])
    return {
        "wq": _til(inputs["L0_attn_wq"], D, D),
        "wk": _til(inputs["L0_attn_wk"], D, D),
        "wo": _til(inputs["L0_attn_wo"], D, D),
        "wv": np.ascontiguousarray(
            wv.T.reshape(NKC, 128, 8, 256).transpose(2, 1, 0, 3)).astype(np.float32),
        "w1": _til(inputs["L0_ffn_w1"], DFF, D),
        "w2": _til(inputs["L0_ffn_w2"], DFF, D),
        "w3": np.ascontiguousarray(
            w3.reshape(NKC, 128, NFC, 128).transpose(2, 3, 0, 1)
            .reshape(NFC, 128, NKC * 128)
            .reshape(NFC, 128, 4, 4 * 128).transpose(2, 0, 1, 3)).astype(np.float32),
        "wmod": np.ascontiguousarray(
            wmod.reshape(NKC, 128, 24, 512).transpose(2, 1, 0, 3)).astype(np.float32),
    }


def kernel(**inputs):
    nc = bass.Bass("TRN2", target_bir_lowering=False)
    K = build(nc)
    K.emit()
    w = _host_weights(inputs)
    in_maps = []
    for core in range(8):
        b, half = core // 2, core % 2
        m = dict(w)
        m.update(_host_prep(inputs, b, half))
        in_maps.append(m)
    res = run_bass_kernel_spmd(nc, in_maps, core_ids=list(range(8)))
    out = np.zeros((B, T, D), np.float32)
    for core in range(8):
        b, half = core // 2, core % 2
        own = half * OWN
        out[b, own:own + OWN, :] = res.results[core]["out"].T
    return out
